# version 03
# baseline (speedup 1.0000x reference)
import numpy as np
import ml_dtypes
import concourse.bass as bass
import concourse.mybir as mybir
from concourse.bass_utils import run_bass_kernel_spmd

F32 = mybir.dt.float32
BF16 = mybir.dt.bfloat16
AF = mybir.ActivationFunctionType
OP = mybir.AluOpType
AX = mybir.AxisListType

T = 1024
NCH = 8
CH = 128
KD = 16
DM = 2048
FX = 34
NH = 64
NS = 128
DI = 4096
DIT = 32
FFH = 8192
FFT = 64
MAGIC = 12582912.0
EPS = 1e-5

DEV_PHASES = 99
DEV_TAPS = ()

A_XT = 0
A_B1 = 16384
A_B2 = 20480
A_B3 = 24576
A_B4 = 28672
A_B5 = 32768
A_BC = 36864
A_ACC = 38912
A_XSF = 39936
A_END = 40960
ARENA_N = A_END


class Sched:

    ENGS = ("sync", "tensor", "vector", "scalar", "gpsimd")

    def __init__(self):
        self.ops = {e: [] for e in self.ENGS}
        self.cnt = {}
        self.owner = {}
        self.selfgate = {}

    def sem(self, name, owner=None, selfgate=False):
        self.cnt[name] = 0
        self.owner[name] = owner
        self.selfgate[name] = selfgate

    SERIAL = {"vector": "v", "scalar": "s", "gpsimd": "g"}

    def op(self, eng, fn, waits=(), inc=None, n=1, is_dma=False):
        waits = list(waits)
        own = self.SERIAL.get(eng)
        if own is not None and self.cnt[own] > 0:
            waits.append((own, self.cnt[own]))
        for s, v in waits:
            assert v <= self.cnt[s], f"wait {s}>={v} but only {self.cnt[s]} emitted"
        self.ops[eng].append((fn, tuple(waits), inc, n, is_dma))
        if inc is not None:
            self.cnt[inc] += n
            return self.cnt[inc]
        return None

    def dma(self, eng, sem, out, in_, waits=()):
        def fn(e, out=out, in_=in_):
            return e.dma_start(out=out, in_=in_)
        waits = list(waits)
        if self.selfgate.get(sem):
            waits.append((sem, self.cnt[sem]))
        c = self.op(eng, fn, waits, inc=sem, n=16, is_dma=True)
        return (sem, c)

    def at(self, *names):
        return [(s, self.cnt[s]) for s in names]

    def emit(self, block, sems):
        for eng in self.ENGS:
            ops = self.ops[eng]
            if not ops:
                continue

            def body(e, eng=eng, ops=ops):
                last = {}
                for fn, waits, inc, n, is_dma in ops:
                    for s, v in waits:
                        if v <= 0 or last.get(s, 0) >= v:
                            continue
                        e.wait_ge(sems[s], v)
                        last[s] = v
                    inst = fn(e)
                    if inc is not None:
                        inst.then_inc(sems[inc], n)

            getattr(block, eng)(body)


def build_program():
    nc = bass.Bass("TRN2", target_bir_lowering=False)
    S = Sched()
    for name in ["t", "v", "s", "g"]:
        S.sem(name, owner=name if name != "t" else "tensor")
    S.owner["t"] = "tensor"
    S.owner["v"] = "vector"
    S.owner["s"] = "scalar"
    S.owner["g"] = "gpsimd"
    S.sem("cc")
    for name in ["dc", "dw0", "dw1", "dx", "dm0", "dm1", "dr", "drv", "dmo",
                 "dy", "dy0", "dy1", "dp0", "dp1", "dpm", "dxs", "dys", "dss",
                 "dmid", "dmi", "dyf", "dyf1"]:
        S.sem(name)
    for name in ["db", "dbs", "dg"]:
        S.sem(name, selfgate=True)

    I = {}
    def inp(name, shape, dt=F32):
        I[name] = nc.dram_tensor(name, shape, dt, kind="ExternalInput")
        return I[name]

    xT = inp("xT", [KD, 128, T])
    w_xbc = inp("w_xbc", [FX, 128, KD * 128])
    w_dt = inp("w_dt", [128, KD * 64])
    w_gate = inp("w_gate", [DIT, 128, KD * 128])
    w_out = inp("w_out", [DI, DM])
    conv_w = inp("conv_w", [128, FX * 4])
    conv_b = inp("conv_b", [128, FX])
    dt_bias = inp("dt_bias", [64, 1])
    a_log = inp("a_log", [64, 1])
    d_col = inp("d_col", [128, DIT])
    norm_w = inp("norm_w", [128, DIT])
    gw_sl = inp("gw_sl", [256, FFH])
    vw_sl = inp("vw_sl", [256, FFH])
    ow_sl = inp("ow_sl", [1024, DM])
    gate_b = inp("gate_b", [128, FFT])
    val_b = inp("val_b", [128, FFT])
    out_b = inp("out_b", [128, KD])
    state_w5 = inp("state_w5", [5, 4])
    tail_mask = inp("tail_mask", [128, 4])

    yout = nc.dram_tensor("yout", [DM, T], F32, kind="ExternalOutput")

    xs_dram = nc.dram_tensor("xs_dram", [DI, T], F32)
    y_dram = nc.dram_tensor("y_dram", [DI, T], F32)
    yf_dram = nc.dram_tensor("yf_dram", [DI, T], F32)
    st_dram = nc.dram_tensor("st_dram", [NCH, 128, DI], F32)
    a_dram = nc.dram_tensor("a_dram", [64, T], F32)
    alast_dram = nc.dram_tensor("alast_dram", [1, 512], F32)
    misc_dram = nc.dram_tensor("misc_dram", [128, 64], F32)
    ln_dram = nc.dram_tensor("ln_dram", [8, T], F32)
    mid_dram = nc.dram_tensor("mid_dram", [FFH, T], F32)
    mi_dram = nc.dram_tensor("mi_dram", [FFH, T], BF16)
    tail_in = nc.dram_tensor("tail_in", [128, 102], F32)
    tail_out = nc.dram_tensor("tail_out", [512, 102], F32)
    sag_in = nc.dram_tensor("sag_in", [128, DI], F32)
    sag_out = nc.dram_tensor("sag_out", [512, DI], F32)
    asum_in = nc.dram_tensor("asum_in", [64, 1], F32)
    asum_out = nc.dram_tensor("asum_out", [256, 1], F32)
    gam_in = nc.dram_tensor("gam_in", [1, 4], F32)
    gam_out = nc.dram_tensor("gam_out", [1, 4], F32)
    mg_in = nc.dram_tensor("mg_in", [256, FFH], BF16)
    mv_in = nc.dram_tensor("mv_in", [256, FFH], BF16)
    mo_in = nc.dram_tensor("mo_in", [1024, DM], BF16)
    mg_out = nc.dram_tensor("mg_out", [DM, FFH], BF16)
    mv_out = nc.dram_tensor("mv_out", [DM, FFH], BF16)
    mo_out = nc.dram_tensor("mo_out", [FFH, DM], BF16)

    taps = {}
    for tname in DEV_TAPS:
        shp = {"xs": [DI + 256, T], "dt": [64, T], "acsT": [64, T], "y": [DI, T],
               "yf": [DI, T], "mid": [DM, T], "state": [128, DI],
               "ffnmid": [FFH, T]}[tname]
        taps[tname] = nc.dram_tensor("tap_" + tname, shp, F32, kind="ExternalOutput")

    from contextlib import ExitStack
    with ExitStack() as stk:
        E = stk.enter_context
        AR = E(nc.sbuf_tensor("arena", [128, ARENA_N], F32))

        def ar(off, n, rows=128):
            return AR[0:rows, off:off + n]

        wq = [E(nc.sbuf_tensor(f"wq{i}", [128, KD * 128], F32)) for i in range(2)]
        dt_sb = E(nc.sbuf_tensor("dt_sb", [64, T], F32))
        acsT = E(nc.sbuf_tensor("acsT", [64, T], F32))
        dt_tm = E(nc.sbuf_tensor("dt_tm", [128, NCH * 64], F32))
        adt_tm = E(nc.sbuf_tensor("adt_tm", [128, NCH * 64], F32))
        acs_col = E(nc.sbuf_tensor("acs_col", [128, NCH * 64], F32))
        alast_sb = E(nc.sbuf_tensor("alast_sb", [64, NCH + 8], F32))
        ds_all = E(nc.sbuf_tensor("ds_all", [128, NCH * 64], F32))
        dch_rep = E(nc.sbuf_tensor("dch_rep", [128, NCH * 64], F32))
        arep = E(nc.sbuf_tensor("arep", [128, 64], F32))
        dcol = E(nc.sbuf_tensor("dcol", [128, DIT], F32))
        nw_col = E(nc.sbuf_tensor("nw_col", [128, DIT], F32))
        cw_sb = E(nc.sbuf_tensor("cw_sb", [128, FX * 4], F32))
        cb_sb = E(nc.sbuf_tensor("cb_sb", [128, FX], F32))
        ones_sb = E(nc.sbuf_tensor("ones_sb", [128, 128], F32))
        u01 = E(nc.sbuf_tensor("u01", [128, 128], F32))
        ident = E(nc.sbuf_tensor("ident", [128, 128], F32))
        gu = E(nc.sbuf_tensor("gu", [128, 128], F32))
        accfix = E(nc.sbuf_tensor("accfix", [128, 102], F32))
        tail_sb = E(nc.sbuf_tensor("tail_sb", [128, 102], F32))
        tailprev = E(nc.sbuf_tensor("tailprev", [128, 102], F32))
        xsfix = E(nc.sbuf_tensor("xsfix", [128, 102], F32))
        tmask_sb = E(nc.sbuf_tensor("tmask_sb", [128, 4], F32))
        fb_sb = E(nc.sbuf_tensor("fb_sb", [128, 144], F32))
        small = E(nc.sbuf_tensor("small", [128, 64], F32))
        pA = E(nc.psum_tensor("pA", [128, 1024], F32))
        pB = E(nc.psum_tensor("pB", [128, 1024], F32))
        pC = E(nc.psum_tensor("pC", [128, 1024], F32))
        pD = E(nc.psum_tensor("pD", [128, 1024], F32))

        sems = {name: E(nc.semaphore(name)) for name in S.cnt}

        xT_sb = ar(A_XT, KD * T)
        acc = ar(A_ACC, T)
        xs_f = ar(A_XSF, T)
        BC = ar(A_BC, 2 * T)

        ld0 = S.dma("sync", "dc", xT_sb, bass.AP(xT, 0, [[T, 128], [128 * T, KD], [1, T]]))
        S.dma("sync", "dc", cw_sb[:], conv_w[:, :])
        S.dma("sync", "dc", cb_sb[:], conv_b[:, :])
        S.dma("sync", "dc", tmask_sb[:], tail_mask[:, :])
        S.dma("sync", "dc", small[0:64, 0:1], dt_bias[:, :])
        S.dma("sync", "dc", small[0:64, 1:2], a_log[:, :])
        S.dma("sync", "dc", nw_col[:], norm_w[:, :])
        S.dma("sync", "dc", fb_sb[:, 0:64], gate_b[:, :])
        S.dma("sync", "dc", fb_sb[:, 64:128], val_b[:, :])
        S.dma("sync", "dc", fb_sb[:, 128:144], out_b[:, :])
        ld_const = ("dc", S.cnt["dc"])
        S.op("vector", lambda e: e.memset(ones_sb[:], 1.0), inc="v")
        S.op("gpsimd", lambda e: e.affine_select(
            u01[:], ones_sb[:], [[1, 128]], OP.is_ge, 0.0, base=0,
            channel_multiplier=-1), waits=S.at("v"), inc="g")
        S.op("gpsimd", lambda e: e.affine_select(
            ident[:], ones_sb[:], [[1, 128]], OP.is_equal, 0.0, base=0,
            channel_multiplier=-1), waits=S.at("v"), inc="g")
        g_const = S.cnt["g"]
        S.op("scalar", lambda e: e.activation(small[0:64, 4:5], small[0:64, 1:2],
                                              AF.Exp),
             waits=[ld_const], inc="s")
        S.op("vector", lambda e: e.tensor_scalar_mul(small[0:64, 5:6],
                                                     small[0:64, 4:5], -1.0),
             waits=S.at("s"), inc="v")
        S.dma("scalar", "dbs", misc_dram[0:1, 0:64], small[0:64, 5:6], waits=S.at("v"))
        bounce1 = ("dbs", S.cnt["dbs"])
        S.dma("sync", "dc", arep[:, 0:64],
              bass.AP(misc_dram, 0, [[0, 128], [1, 64]]),
              waits=[bounce1])
        S.dma("sync", "dc", dcol[:], I["d_col"][:, :])

        dc_all = ("dc", S.cnt["dc"])
        wacc = E(nc.sbuf_tensor("wacc", [128, 8], F32))
        grow = E(nc.sbuf_tensor("grow", [1, 8], F32))
        rgc = E(nc.sbuf_tensor("rgc", [128, 4], F32))
        p0b = [ar(A_B1, 2048), ar(A_B1 + 2048, 2048)]
        p0t = ar(A_B2, 2048)
        p0m_bf = ar(A_B2 + 2048, 1024).bitcast(BF16)
        subs = []
        for mat, src, dst in [(0, gw_sl, mg_in), (1, vw_sl, mv_in), (2, ow_sl, mo_in)]:
            for r in range(src.shape[0] // 128):
                for c in range(src.shape[1] // 2048):
                    subs.append((mat, src, dst, r, c))
        S.op("gpsimd", lambda e: e.memset(wacc[:, 0:6], 0.0), waits=[dc_all],
             inc="g")
        p0_consumed = [0, 0]
        nsub = [0, 0, 0]
        for i, (mat, src, dst, r, c) in enumerate(subs):
            bb = p0b[i % 2]
            ld = S.dma("sync", "dp0" if i % 2 == 0 else "dp1", bb,
                       src[128 * r:128 * (r + 1), 2048 * c:2048 * (c + 1)],
                       waits=[("s", p0_consumed[i % 2])])
            S.op("scalar", lambda e, bb=bb: e.activation(
                p0t, bb, AF.Abs, accum_out=wacc[:, 6:7]),
                waits=[ld, dc_all, ("g", S.cnt["g"])], inc="s")
            par = nsub[mat] % 2
            S.op("gpsimd", lambda e, mat=mat, par=par: e.tensor_add(
                wacc[:, 3 * (1 - par) + mat:3 * (1 - par) + mat + 1],
                wacc[:, 3 * par + mat:3 * par + mat + 1], wacc[:, 6:7]),
                waits=S.at("s"), inc="g")
            nsub[mat] += 1
            p0_consumed[i % 2] = S.cnt["s"]
        S.op("tensor", lambda e: e.matmul(pC[0:1, 0:4], ones_sb[:, 0:1],
                                          wacc[:, 0:4], start=True, stop=True),
             waits=S.at("g"), inc="t")
        S.op("vector", lambda e: e.tensor_copy(grow[:, 0:4], pC[0:1, 0:4]),
             waits=S.at("t"), inc="v")
        p0a_dma = [("dp0", S.cnt["dp0"]), ("dp1", S.cnt["dp1"])]

        wq_free = [0, 0]
        ps_free = [0, 0]
        last_silu = 0
        last_spill = ("dxs", 0)
        dt_ready_v = 0

        for f in range(FX + 1):
            buf = f % 2
            wsl = wq[buf]
            dwn = "dw0" if buf == 0 else "dw1"
            if f < FX:
                ldw = S.dma("sync", dwn, wsl[:], w_xbc[f, :, :],
                            waits=[("t", wq_free[buf])])
            else:
                ldw = S.dma("sync", dwn, wsl[:, 0:KD * 64], w_dt[:, :],
                            waits=[("t", wq_free[buf])])
            ps = [pA, pB][buf]
            rows = 128 if f < FX else 64

            def mm(e, ps=ps, wsl=wsl, rows=rows):
                last = None
                for cb in range(2):
                    for k in range(KD):
                        last = e.matmul(
                            ps[0:rows, cb * 512:(cb + 1) * 512],
                            wsl[:, k * rows:(k + 1) * rows],
                            xT_sb[:, k * T + cb * 512: k * T + (cb + 1) * 512],
                            start=(k == 0), stop=(k == KD - 1))
                return last
            tcnt = S.op("tensor", mm,
                        waits=[ldw, dc_all, ("v", ps_free[buf])],
                        inc="t")
            wq_free[buf] = tcnt

            if f < FX:
                pp = [acc, ar(A_B4, T)]

                def c0(e, ps=ps, f=f):
                    return e.tensor_scalar(pp[0], ps[:],
                                           cw_sb[:, 4 * f + 3:4 * f + 4],
                                           None, OP.mult)
                S.op("vector", c0, waits=[("t", tcnt), ("s", last_silu)], inc="v")
                for k in (2, 1, 0):
                    lo = 3 - k
                    src, dst2 = pp[(2 - k) % 2], pp[(3 - k) % 2]

                    def ck(e, ps=ps, f=f, k=k, lo=lo, src=src, dst2=dst2):
                        return e.scalar_tensor_tensor(
                            dst2[:, lo:T], ps[:, 0:T - lo],
                            cw_sb[:, 4 * f + k:4 * f + k + 1],
                            src[:, lo:T], OP.mult, OP.add)
                    S.op("vector", ck, inc="v")

                    def hd(e, lo=lo, src=src, dst2=dst2):
                        return e.tensor_copy(dst2[:, 0:lo], src[:, 0:lo])
                    S.op("vector", hd, inc="v")
                S.op("vector", lambda e, f=f: e.tensor_copy(
                    accfix[:, 3 * f:3 * f + 3], pp[1][:, 0:3]), inc="v")
                S.op("vector", lambda e, ps=ps, f=f: e.tensor_copy(
                    tail_sb[:, 3 * f:3 * f + 3], ps[:, T - 3:T]), inc="v")
                ps_free[buf] = S.cnt["v"]
                dst = xs_f if f < 32 else BC[:, (f - 32) * T:(f - 31) * T]

                def si(e, dst=dst, f=f):
                    return e.activation(dst, pp[1], AF.Silu, bias=cb_sb[:, f:f + 1])
                last_silu = S.op("scalar", si,
                                 waits=[("v", S.cnt["v"]), last_spill], inc="s")
                if f < 32:
                    last_spill = S.dma("scalar", "dxs",
                                       xs_dram[128 * f:128 * (f + 1), :],
                                       xs_f, waits=[("s", last_silu)])
            else:
                def d1(e, ps=ps):
                    return e.tensor_scalar(acc[0:64, :], ps[0:64, :],
                                           small[0:64, 0:1], None, OP.add)
                S.op("vector", d1, waits=[("t", tcnt), ("s", last_silu)]
                     + p0a_dma, inc="v")
                ps_free[buf] = S.cnt["v"]
                S.op("scalar", lambda e: e.activation(
                    ar(A_XSF, T, 64), acc[0:64, :], AF.Abs),
                    waits=[("v", S.cnt["v"]), last_spill], inc="s")
                S.op("scalar", lambda e: e.activation(
                    ar(A_B1, T, 64), ar(A_XSF, T, 64), AF.Exp, scale=-1.0), inc="s")
                S.op("vector", lambda e: e.tensor_scalar_max(
                    ar(A_B2 + 2048, T, 64), acc[0:64, :], 0.0),
                    waits=S.at("s"), inc="v")
                S.op("vector", lambda e: e.tensor_scalar_add(
                    ar(A_B2, T, 64), ar(A_B1, T, 64), 1.0), inc="v")
                S.op("scalar", lambda e: e.activation(
                    ar(A_B1, T, 64), ar(A_B2, T, 64), AF.Ln),
                    waits=S.at("v"), inc="s")
                S.op("vector", lambda e: e.tensor_add(
                    dt_sb[:], ar(A_B2 + 2048, T, 64), ar(A_B1, T, 64)),
                    waits=S.at("s"), inc="v")
                dt_ready_v = S.cnt["v"]

        p1_dsc = ("dxs", S.cnt["dxs"])

        gsum_d = S.dma("gpsimd", "dg", gam_in[:, :], grow[0:1, 0:4], waits=S.at("v"))

        def cc_gam(e):
            return e.collective_compute(
                "AllReduce", OP.add, replica_groups=[list(range(8))],
                ins=[gam_in.ap().opt()], outs=[gam_out.ap().opt()])
        S.op("gpsimd", cc_gam, waits=[gsum_d], inc="cc")
        cc_gam_n = S.cnt["cc"]
        ldg = S.dma("sync", "db", grow[0:1, 4:8], gam_out[:, :], waits=[("cc", cc_gam_n)])
        S.op("vector", lambda e: e.tensor_scalar(
            grow[0:1, 0:4], grow[0:1, 4:8], 1.0 / 16777216.0, 1e-5, OP.mult, OP.max),
            waits=[ldg], inc="v")
        S.op("vector", lambda e: e.reciprocal(grow[0:1, 4:8], grow[0:1, 0:4]),
             inc="v")
        S.dma("scalar", "dbs", misc_dram[6:7, 0:4], grow[0:1, 0:4], waits=S.at("v"))
        gam_bd = S.dma("scalar", "dbs", misc_dram[5:6, 0:4], grow[0:1, 4:8],
                       waits=S.at("v"))
        ldrg = S.dma("sync", "db", rgc[:, 0:4],
                     bass.AP(misc_dram, 5 * 64, [[0, 128], [1, 4]]),
                     waits=[gam_bd])
        p0_consumed = [S.cnt["v"], S.cnt["v"]]
        mdone = ("dpm", 0)
        for i, (mat, src, dst, r, c) in enumerate(subs):
            bb = p0b[i % 2]
            ld = S.dma("sync", "dp0" if i % 2 == 0 else "dp1", bb,
                       src[128 * r:128 * (r + 1), 2048 * c:2048 * (c + 1)],
                       waits=[("v", p0_consumed[i % 2]), ldrg])
            S.op("vector", lambda e, bb=bb, mat=mat: e.tensor_scalar(
                p0t, bb, rgc[:, mat:mat + 1], MAGIC, OP.mult, OP.add),
                waits=[ld, mdone], inc="v")
            for hh in range(2):
                S.op("vector", lambda e, hh=hh: e.tensor_scalar(
                    ar(A_B2 + 3072, 1024), p0t[:, 1024 * hh:1024 * (hh + 1)],
                    MAGIC, -1.0, OP.subtract, OP.max), inc="v")
                S.op("vector", lambda e, hh=hh: e.tensor_scalar_min(
                    p0m_bf[:, 1024 * hh:1024 * (hh + 1)], ar(A_B2 + 3072, 1024),
                    1.0), inc="v")
            p0_consumed[i % 2] = S.cnt["v"]
            mdone = S.dma("scalar", "dpm",
                          dst[128 * r:128 * (r + 1), 2048 * c:2048 * (c + 1)],
                          p0m_bf[:, 0:2048], waits=S.at("v"))

        p0b_dma = [mdone, ("dp0", S.cnt["dp0"]), ("dp1", S.cnt["dp1"])]

        S.dma("gpsimd", "dg", tail_in[:, :], tail_sb[:], waits=S.at("v"))

        def cc_tail(e):
            return e.collective_compute(
                "AllGather", OP.bypass, replica_groups=[[0, 1, 2, 3], [4, 5, 6, 7]],
                ins=[tail_in.ap().opt()], outs=[tail_out.ap().opt()])
        S.op("gpsimd", cc_tail, waits=S.at("dg"), inc="cc")
        cc_tail_n = S.cnt["cc"]
        S.op("vector", lambda e: e.memset(tailprev[:], 0.0), inc="v")
        tpp = [tailprev, tail_sb]
        for j in range(4):
            ldb = S.dma("sync", "db", xsfix[:], tail_out[128 * j:128 * (j + 1), :],
                        waits=[("cc", cc_tail_n), ("v", S.cnt["v"])])

            def addb(e, j=j):
                return e.scalar_tensor_tensor(tpp[(j + 1) % 2][:], xsfix[:],
                                              tmask_sb[:, j:j + 1], tpp[j % 2][:],
                                              OP.mult, OP.add)
            S.op("vector", addb, waits=[ldb], inc="v")
        for f in range(FX):
            S.op("vector", lambda e, f=f: e.scalar_tensor_tensor(
                xsfix[:, 3 * f:3 * f + 3], tail_sb[:, 3 * f:3 * f + 3],
                cw_sb[:, 4 * f + 0:4 * f + 1], accfix[:, 3 * f:3 * f + 3],
                OP.mult, OP.add), inc="v")
            S.op("vector", lambda e, f=f: e.scalar_tensor_tensor(
                tailprev[:, 3 * f:3 * f + 2], tail_sb[:, 3 * f + 1:3 * f + 3],
                cw_sb[:, 4 * f + 1:4 * f + 2], xsfix[:, 3 * f:3 * f + 2],
                OP.mult, OP.add), inc="v")
            S.op("vector", lambda e, f=f: e.tensor_copy(
                tailprev[:, 3 * f + 2:3 * f + 3], xsfix[:, 3 * f + 2:3 * f + 3]),
                inc="v")
            S.op("vector", lambda e, f=f: e.scalar_tensor_tensor(
                xsfix[:, 3 * f:3 * f + 1], tail_sb[:, 3 * f + 2:3 * f + 3],
                cw_sb[:, 4 * f + 2:4 * f + 3], tailprev[:, 3 * f:3 * f + 1],
                OP.mult, OP.add), inc="v")
            S.op("vector", lambda e, f=f: e.tensor_copy(
                xsfix[:, 3 * f + 1:3 * f + 3], tailprev[:, 3 * f + 1:3 * f + 3]),
                inc="v")

            def sif(e, f=f):
                return e.activation(tail_sb[:, 3 * f:3 * f + 3],
                                    xsfix[:, 3 * f:3 * f + 3],
                                    AF.Silu, bias=cb_sb[:, f:f + 1])
            S.op("scalar", sif, waits=S.at("v"), inc="s")
        fix_s = S.cnt["s"]
        for f in range(32):
            S.dma("gpsimd", "dg", xs_dram[128 * f:128 * (f + 1), 0:3],
                  tail_sb[:, 3 * f:3 * f + 3],
                  waits=[("s", fix_s), p1_dsc])
        for f in (32, 33):
            S.op("vector", lambda e, f=f: e.tensor_copy(
                BC[:, (f - 32) * T:(f - 32) * T + 3], tail_sb[:, 3 * f:3 * f + 3]),
                waits=[("s", fix_s)], inc="v")
        patch_g = ("dg", S.cnt["dg"])
        patch_v = S.cnt["v"]

        btm = E(nc.sbuf_tensor("btm", [128, 128], F32))
        facr = E(nc.sbuf_tensor("facr", [128, 64], F32))
        asw = E(nc.sbuf_tensor("asw", [5, 64], F32))
        w5_sb = E(nc.sbuf_tensor("w5_sb", [5, 4], F32))
        alr = E(nc.sbuf_tensor("alr", [128, 64], F32))
        asum_sb = E(nc.sbuf_tensor("asum_sb", [64, 1], F32))

        def mkap(x, off, freedims, parts=None):
            b = x if isinstance(x, bass.AP) else x[:]
            pd = list(b.ap[0])
            if parts is not None:
                pd = [pd[0], parts]
            return bass.AP(b.tensor, b.offset + off, [pd] + [list(d) for d in freedims])

        for m in range(NCH):
            S.op("tensor", lambda e, m=m: e.transpose(
                pC[0:128, 64 * m:64 * (m + 1)], dt_sb[:, 128 * m:128 * (m + 1)],
                ident[0:64, 0:64]),
                waits=[("v", dt_ready_v), ("g", g_const)], inc="t")
        S.op("vector", lambda e: e.tensor_copy(dt_tm[:, 0:512], pC[:, 0:512]),
             waits=S.at("t"), inc="v")
        S.op("vector", lambda e: e.tensor_tensor(
            adt_tm[:, 0:512], dt_tm[:, 0:512],
            mkap(arep, 0, [[0, NCH], [1, 64]]), OP.mult),
            waits=S.at("dc"), inc="v")
        adt_v = S.cnt["v"]
        for m in range(NCH):
            S.op("tensor", lambda e, m=m: e.matmul(
                pD[0:64, 128 * m:128 * (m + 1)], adt_tm[:, 64 * m:64 * (m + 1)],
                u01[:], start=True, stop=True), waits=[("v", adt_v)], inc="t")
            S.op("tensor", lambda e, m=m: e.matmul(
                pC[0:128, 512 + 64 * m:512 + 64 * (m + 1)], u01[:],
                adt_tm[:, 64 * m:64 * (m + 1)], start=True, stop=True), inc="t")
        acs_t = S.cnt["t"]
        S.op("vector", lambda e: e.tensor_copy(acsT[:, :], pD[0:64, 0:1024]),
             waits=[("t", acs_t)], inc="v")
        S.op("vector", lambda e: e.tensor_copy(acs_col[:, :], pC[:, 512:1024]),
             inc="v")
        acs_v = S.cnt["v"]
        acsT_d = S.dma("scalar", "dbs", a_dram[:, :], acsT[:, :], waits=[("v", acs_v)])
        alast_d = S.dma("scalar", "dbs", alast_dram[0:1, 0:512], acs_col[127:128, :],
                        waits=[("v", acs_v)])
        S.op("vector", lambda e: e.reduce_sum(
            asum_sb[:, 0:1], mkap(acsT, 127, [[128, NCH]], parts=64),
            axis=AX.X), waits=[("v", acs_v)], inc="v")
        asum_v = S.cnt["v"]
        for m in range(NCH):
            lda = S.dma("sync", "db", alr[:, 0:64],
                        bass.AP(alast_dram, 64 * m, [[0, 128], [1, 64]]),
                        waits=[alast_d, ("v", S.cnt["v"]),
                               ("s", S.cnt["s"])])
            S.op("vector", lambda e, m=m: e.tensor_sub(
                adt_tm[:, 64 * m:64 * (m + 1)], alr[:, 0:64],
                acs_col[:, 64 * m:64 * (m + 1)]), waits=[lda], inc="v")
            S.op("scalar", lambda e, m=m: e.activation(
                dch_rep[:, 64 * m:64 * (m + 1)], alr[:, 0:64], AF.Exp),
                waits=[lda], inc="s")
        S.op("scalar", lambda e: e.activation(
            ds_all[:, :], adt_tm[:, :], AF.Exp), waits=S.at("v"), inc="s")
        apath_s = S.cnt["s"]

        xs_ch = ar(A_B1, DI)
        xd = ar(A_B2, DI)
        y_sb = ar(A_B4, DI)
        s_run = ar(A_B5, DI)
        repA = ar(A_B3, 2048)
        repB = ar(A_B3 + 2048, 2048)
        S.op("vector", lambda e: e.memset(s_run, 0.0), inc="v")

        st_spill = [("dss", 0)]
        p2_y_d = []
        prev_t = 0
        for m in range(NCH):
            xw = [p1_dsc, ("t", prev_t), ("v", S.cnt["v"])]
            if m == 0:
                xw.append(patch_g)
                xw += p0b_dma
            ldx = S.dma("sync", "dx", xs_ch,
                        bass.AP(xs_dram, 128 * m,
                                [[T, 128], [128 * T, DIT], [1, 128]]), waits=xw)
            BTm = BC[:, 128 * m:128 * (m + 1)]
            CTm = BC[:, T + 128 * m:T + 128 * (m + 1)]
            bcw = [("v", patch_v)] if m == 0 else []
            gt = S.op("tensor", lambda e, BTm=BTm, CTm=CTm: e.matmul(
                pD[:, 0:128], BTm, CTm, start=True, stop=True),
                waits=bcw + [("v", S.cnt["v"])], inc="t")
            bt = S.op("tensor", lambda e, BTm=BTm: e.transpose(
                pD[:, 128:256], BTm, ident[:]), inc="t")
            S.op("vector", lambda e: e.tensor_tensor(gu[:], pD[:, 0:128], u01[:],
                                                     OP.mult),
                 waits=[("t", bt)], inc="v")
            S.op("vector", lambda e: e.tensor_copy(btm[:], pD[:, 128:256]), inc="v")

            for hf in range(2):
                tws = [ldx, ("v", S.cnt["v"])]
                for i in range(16):
                    ps = pA if i < 8 else pB
                    S.op("tensor", lambda e, i=i, hf=hf, ps=ps: e.transpose(
                        ps[:, 128 * (i % 8):128 * (i % 8 + 1)],
                        xs_ch[:, 2048 * hf + 128 * i: 2048 * hf + 128 * (i + 1)],
                        ident[:]), waits=tws if i == 0 else (), inc="t")
                txd = S.cnt["t"]
                for h2 in range(2):
                    ps = [pA, pB][h2]

                    def evx(e, ps=ps, hf=hf, h2=h2, m=m):
                        in1 = mkap(dt_tm, 64 * m + 32 * hf + 16 * h2,
                                   [[2, 8], [1, 2], [0, 64]])
                        return e.tensor_tensor(
                            xd[:, 2048 * hf + 1024 * h2:2048 * hf + 1024 * (h2 + 1)],
                            ps[:], in1, OP.mult)
                    S.op("vector", evx, waits=[("t", txd)], inc="v")
                xd_v = S.cnt["v"]

                for qr in range(2):
                    hq = 32 * hf + 16 * qr
                    ldr = S.dma("sync", "dr", repA,
                                bass.AP(a_dram, 1024 * hq + 128 * m,
                                        [[0, 128], [1024, 16], [1, 128]]),
                                waits=[acsT_d, st_spill[-1],
                                       ("t", S.cnt["t"]), ("v", S.cnt["v"])])

                    def mtsub(e, hq=hq, m=m):
                        in1 = mkap(acs_col, 64 * m + hq, [[1, 16], [0, 128]])
                        return e.tensor_tensor(repB, repA, in1, OP.subtract)
                    S.op("vector", mtsub,
                         waits=[ldr, ("v", acs_v), st_spill[-1]], inc="v")
                    S.op("vector", lambda e: e.tensor_scalar_min(repA, repB, 0.0),
                         inc="v")
                    S.op("scalar", lambda e: e.activation(repB, repA, AF.Exp),
                         waits=S.at("v"), inc="s")

                    def mtgu(e):
                        in1 = mkap(gu, 0, [[0, 16], [1, 128]])
                        return e.tensor_tensor(repA, repB, in1, OP.mult)
                    S.op("vector", mtgu, waits=S.at("s"), inc="v")
                    mt_v = S.cnt["v"]

                    psq = [pC, pD][qr]
                    for hh in range(16):
                        h = hq + hh
                        slot = hh // 2
                        base = 64 * (h % 2)

                        def ymm(e, h=h, hh=hh, psq=psq, slot=slot, base=base):
                            return e.matmul(
                                psq[base:base + 64, 128 * slot:128 * (slot + 1)],
                                xd[:, 64 * h:64 * (h + 1)],
                                repA[:, 128 * hh:128 * (hh + 1)],
                                start=True, stop=True)
                        S.op("tensor", ymm,
                             waits=[("v", mt_v)] if hh == 0 else (), inc="t")
                ydiag_t = S.cnt["t"]

                def dtmp(e, hf=hf):
                    in1 = mkap(dcol, 16 * hf, [[1, 16], [0, 128]])
                    return e.tensor_tensor(repB, xs_ch[:, 2048 * hf:2048 * (hf + 1)],
                                           in1, OP.mult)
                S.op("vector", dtmp, waits=[("t", ydiag_t),
                                            ("dys", S.cnt["dys"])], inc="v")
                for h2 in range(2):
                    ps = [pC, pD][h2]

                    def yev(e, ps=ps, hf=hf, h2=h2):
                        return e.tensor_tensor(
                            y_sb[:, 2048 * hf + 1024 * h2:2048 * hf + 1024 * (h2 + 1)],
                            repB[:, 1024 * h2:1024 * (h2 + 1)], ps[:], OP.add)
                    S.op("vector", yev, inc="v")
                yv = S.cnt["v"]
                p2_y_d.append(S.dma(
                    "scalar", "dys",
                    bass.AP(y_dram, (2048 * hf) * T + 128 * m,
                            [[T, 128], [128 * T, 16], [1, 128]]),
                    y_sb[:, 2048 * hf:2048 * (hf + 1)], waits=[("v", yv)]))

                def xd2f(e, hf=hf, m=m):
                    in1 = mkap(ds_all, 64 * m + 32 * hf, [[1, 32], [0, 64]])
                    return e.tensor_tensor(repB, xd[:, 2048 * hf:2048 * (hf + 1)],
                                           in1, OP.mult)
                S.op("vector", xd2f, waits=[("s", apath_s)], inc="v")
                xd2_v = S.cnt["v"]
                for i in range(4):
                    ps = [pA, pB][i // 2]

                    def smm(e, i=i, ps=ps):
                        return e.matmul(ps[:, 512 * (i % 2):512 * (i % 2 + 1)],
                                        btm[:], repB[:, 512 * i:512 * (i + 1)],
                                        start=True, stop=True)
                    S.op("tensor", smm,
                         waits=[("v", xd2_v)] if i == 0 else (), inc="t")
                prev_t = S.cnt["t"]
                for h2 in range(2):
                    ps = [pA, pB][h2]
                    S.op("vector", lambda e, ps=ps, h2=h2: e.tensor_copy(
                        repB[:, 1024 * h2:1024 * (h2 + 1)], ps[:]),
                        waits=[("t", prev_t)], inc="v")
                stv = S.cnt["v"]
                st_spill.append(S.dma(
                    "scalar", "dss", st_dram[m, :, 2048 * hf:2048 * (hf + 1)], repB,
                    waits=[("v", stv)]))
                tmpv = xd[:, 2048 * hf:2048 * (hf + 1)]

                def sdec(e, hf=hf, m=m, tmpv=tmpv):
                    in1 = mkap(dch_rep, 64 * m + 32 * hf, [[1, 32], [0, 64]])
                    return e.tensor_tensor(tmpv, s_run[:, 2048 * hf:2048 * (hf + 1)],
                                           in1, OP.mult)
                S.op("vector", sdec, waits=[("s", apath_s)], inc="v")
                S.op("vector", lambda e, hf=hf, tmpv=tmpv: e.tensor_add(
                    s_run[:, 2048 * hf:2048 * (hf + 1)], tmpv, repB), inc="v")

        sloc_d = S.dma("gpsimd", "dg", sag_in[:, :], s_run, waits=S.at("v"))
        asum_d = S.dma("gpsimd", "dg", asum_in[:, :], asum_sb[:, 0:1],
                       waits=[("v", asum_v)])

        def cc_st(e):
            return e.collective_compute(
                "AllGather", OP.bypass, replica_groups=[[0, 1, 2, 3], [4, 5, 6, 7]],
                ins=[sag_in.ap().opt()], outs=[sag_out.ap().opt()])
        S.op("gpsimd", cc_st, waits=[sloc_d], inc="cc")

        def cc_as(e):
            return e.collective_compute(
                "AllGather", OP.bypass, replica_groups=[[0, 1, 2, 3], [4, 5, 6, 7]],
                ins=[asum_in.ap().opt()], outs=[asum_out.ap().opt()])
        S.op("gpsimd", cc_as, waits=[asum_d], inc="cc")
        cc_state_n = S.cnt["cc"]
        def cc_ag(name_in, name_out):
            def f(e, name_in=name_in, name_out=name_out):
                return e.collective_compute(
                    "AllGather", OP.bypass, replica_groups=[list(range(8))],
                    ins=[name_in.ap().opt()], outs=[name_out.ap().opt()])
            return f
        S.op("gpsimd", cc_ag(mg_in, mg_out), waits=[mdone], inc="cc")
        S.op("gpsimd", cc_ag(mv_in, mv_out), inc="cc")
        S.op("gpsimd", cc_ag(mo_in, mo_out), inc="cc")
        cc_m_n = S.cnt["cc"]
        ld1r = S.dma("sync", "db", asw[4:5, :], ones_sb[0:1, 0:64],
                     waits=S.at("v"))
        ld5 = S.dma("sync", "db", asw[0:4, :],
                    bass.AP(asum_out, 0, [[64, 4], [1, 64]]),
                    waits=[("cc", cc_state_n)])
        ldw5 = S.dma("sync", "db", w5_sb[:, :], state_w5[:, :])
        fac_t = S.op("tensor", lambda e: e.matmul(
            pD[0:4, 0:64], w5_sb[:, :], asw[:, :], start=True, stop=True),
            waits=[ld5, ldw5, ld1r, ("v", S.cnt["v"])], inc="t")
        S.op("scalar", lambda e: e.activation(facr[0:4, 0:64], pD[0:4, 0:64],
                                              AF.Exp),
             waits=[("t", fac_t)], inc="s")
        facd = S.dma("scalar", "dbs", misc_dram[1:5, 0:64], facr[0:4, 0:64],
                     waits=S.at("s"))
        spp = [s_run, ar(A_B4, DI)]
        for j in range(4):
            ldb = S.dma("sync", "dx", xs_ch, sag_out[128 * j:128 * (j + 1), :],
                        waits=[("cc", cc_state_n), ("t", S.cnt["t"]),
                               ("v", S.cnt["v"]), sloc_d])
            ldf = S.dma("sync", "db", facr[:, 0:64],
                        bass.AP(misc_dram, 64 * (1 + j), [[0, 128], [1, 64]]),
                        waits=[facd, ("v", S.cnt["v"])])
            if j == 0:
                def sacc0(e):
                    in1 = mkap(facr, 0, [[1, 64], [0, 64]])
                    return e.tensor_tensor(spp[0], xs_ch, in1, OP.mult)
                S.op("vector", sacc0,
                     waits=[ldb, ldf, ("dys", S.cnt["dys"])], inc="v")
            else:
                def sacc(e):
                    in1 = mkap(facr, 0, [[1, 64], [0, 64]])
                    return e.tensor_tensor(xd, xs_ch, in1, OP.mult)
                S.op("vector", sacc, waits=[ldb, ldf], inc="v")
                S.op("vector", lambda e, j=j: e.tensor_add(
                    spp[j % 2], spp[(j + 1) % 2], xd), inc="v")
        S.op("vector", lambda e: e.tensor_copy(s_run, spp[1]), inc="v")
        sin_v = S.cnt["v"]

        hbuf = lambda base, p: ar(base + 2048 * p, 2048)
        ACX = ar(A_ACC, 2048)
        p4_y_d = [("dys", S.cnt["dys"])]
        dss_all = ("dss", S.cnt["dss"])
        yfree = [S.cnt["v"], S.cnt["v"]]
        stfree = [S.cnt["v"], S.cnt["v"]]
        erfree = [S.cnt["v"], S.cnt["v"]]
        erspill = [("dys", 0), ("dys", 0)]
        eefree = [S.cnt["v"], S.cnt["v"]]
        supd = [sin_v, sin_v]
        psum_free = [S.cnt["v"], S.cnt["v"]]
        for m in range(NCH):
            for hf in range(2):
                hc = 2 * m + hf
                p = hc % 2
                y_p, st_p = hbuf(A_B1, p), hbuf(A_B2, p)
                er_p, ee_p = hbuf(A_B3, p), hbuf(A_B4, p)
                pp0, pp1 = ([pA, pB], [pC, pD])[p]
                ldy = S.dma("sync", "dy0" if p == 0 else "dy1", y_p,
                            bass.AP(y_dram, (2048 * hf) * T + 128 * m,
                                    [[T, 128], [128 * T, 16], [1, 128]]),
                            waits=[("dys", S.cnt["dys"]), ("v", yfree[p])])
                lds = S.dma("sync", "dm0" if p == 0 else "dm1", st_p,
                            st_dram[m, :, 2048 * hf:2048 * (hf + 1)],
                            waits=[dss_all, ("v", stfree[p])])
                for ph in range(2):
                    lde = S.dma("sync", "dr" if p == 0 else "drv",
                                AR[64 * ph:64 * (ph + 1),
                                   A_B3 + 2048 * p:A_B3 + 2048 * (p + 1)],
                                bass.AP(a_dram, 1024 * ph + 32 * hf * 1024 + 128 * m,
                                        [[0, 64], [2048, 16], [1, 128]]),
                                waits=[acsT_d, ("v", erfree[p]),
                                       erspill[p]])
                exp_s = S.op("scalar", lambda e, er_p=er_p, ee_p=ee_p: e.activation(
                    ee_p, er_p, AF.Exp),
                    waits=[lde, ("v", eefree[p])], inc="s")
                CTm = BC[:, T + 128 * m:T + 128 * (m + 1)]
                for j2r in range(16):
                    j2 = 16 * hf + j2r
                    ps = [pp0, pp1][j2r // 8]

                    def omm(e, j2=j2, j2r=j2r, ps=ps, CTm=CTm):
                        return e.matmul(ps[:, 128 * (j2r % 8):128 * (j2r % 8 + 1)],
                                        s_run[:, 128 * j2:128 * (j2 + 1)], CTm,
                                        start=True, stop=True)
                    S.op("tensor", omm,
                         waits=[("v", max(supd[hf], psum_free[p]))]
                         if j2r == 0 else (), inc="t")
                yoff_t = S.cnt["t"]
                for q2 in range(2):
                    ps = [pp0, pp1][q2]

                    def oev(e, q2=q2, ps=ps, ee_p=ee_p):
                        return e.tensor_tensor(
                            ACX[:, 1024 * q2:1024 * (q2 + 1)], ps[:],
                            ee_p[:, 1024 * q2:1024 * (q2 + 1)], OP.mult)
                    S.op("vector", oev,
                         waits=[("t", yoff_t), ("s", exp_s), ldy], inc="v")
                psum_free[p] = S.cnt["v"]
                S.op("vector", lambda e, er_p=er_p, y_p=y_p: e.tensor_add(
                    er_p, y_p, ACX), inc="v")
                yfree[p] = S.cnt["v"]
                erfree[p] = S.cnt["v"]
                erspill[p] = S.dma(
                    "scalar", "dys",
                    bass.AP(y_dram, (2048 * hf) * T + 128 * m,
                            [[T, 128], [128 * T, 16], [1, 128]]),
                    er_p, waits=[("v", S.cnt["v"])])
                p4_y_d.append(erspill[p])

                def sup1(e, m=m, hf=hf, ee_p=ee_p):
                    in1 = mkap(dch_rep, 64 * m + 32 * hf, [[1, 32], [0, 64]])
                    return e.tensor_tensor(
                        ee_p, s_run[:, 2048 * hf:2048 * (hf + 1)], in1, OP.mult)
                S.op("vector", sup1, waits=[("t", yoff_t)], inc="v")

                def sup2(e, hf=hf, ee_p=ee_p, st_p=st_p):
                    return e.tensor_add(s_run[:, 2048 * hf:2048 * (hf + 1)],
                                        ee_p, st_p)
                S.op("vector", sup2, waits=[lds], inc="v")
                supd[hf] = S.cnt["v"]
                stfree[p] = S.cnt["v"]
                eefree[p] = S.cnt["v"]

        wq_free = [S.cnt["t"], S.cnt["t"]]
        ps_sfree = [S.cnt["s"], S.cnt["s"]]
        yf_dp = [("dyf", 0), ("dyf1", 0)]
        ydp_names = ["dy0", "dy1"]
        tt_p = [0, 0]
        sq_p = [0, 0]
        pend_sq = []
        for j in range(DIT):
            buf = j % 2
            wsl = wq[buf]
            gsil = ar(A_B3 + 1024 * buf, 1024)
            yj = ar(A_B3 + 2048 + 1024 * buf, 1024)
            yfb = ar(A_B2 + 1024 * buf, 1024)
            ldw = S.dma("sync", "dw0" if buf == 0 else "dw1", wsl[:],
                        w_gate[j, :, :], waits=[("t", wq_free[buf])])
            ps = [pA, pB][buf]

            def gmm(e, ps=ps, wsl=wsl):
                last = None
                for cb in range(2):
                    for k in range(KD):
                        last = e.matmul(
                            ps[:, cb * 512:(cb + 1) * 512],
                            wsl[:, k * 128:(k + 1) * 128],
                            xT_sb[:, k * T + cb * 512: k * T + (cb + 1) * 512],
                            start=(k == 0), stop=(k == KD - 1))
                return last
            gw_ = [ldw, ("s", ps_sfree[buf])]
            if j < 2:
                gw_.append(("v", S.cnt["v"]))
            tcnt = S.op("tensor", gmm, waits=gw_, inc="t")
            wq_free[buf] = tcnt
            while len(pend_sq) >= 3:
                q = pend_sq.pop(0)
                S.op("tensor", q[0], waits=[("s", q[1])], inc="t")
                S.op("tensor", q[2], inc="t")
            ldy = S.dma("sync", ydp_names[buf], yj,
                        y_dram[128 * j:128 * (j + 1), :],
                        waits=[p4_y_d[-1], ("v", tt_p[buf])])
            scnt = S.op("scalar", lambda e, ps=ps, gsil=gsil: e.activation(
                gsil, ps[:], AF.Silu),
                waits=[("t", tcnt), ("v", tt_p[buf])], inc="s")
            ps_sfree[buf] = scnt
            tt_p[buf] = S.op(
                "vector", lambda e, yfb=yfb, yj=yj, gsil=gsil: e.tensor_tensor(
                    yfb, yj, gsil, OP.mult),
                waits=[("s", max(scnt, sq_p[buf])), ldy, yf_dp[buf]], inc="v")
            yf_dp[buf] = S.dma("scalar", "dyf" if buf == 0 else "dyf1",
                               yf_dram[128 * j:128 * (j + 1), :], yfb,
                               waits=[("v", tt_p[buf])])
            sqb = ar(A_B4 + 1024 * (j % 4), 1024)
            sq_p[buf] = S.op("scalar", lambda e, yfb=yfb, sqb=sqb: e.activation(
                sqb, yfb, AF.Square),
                waits=[("v", tt_p[buf]), ("t", S.cnt["t"])], inc="s")

            def mk_sq(cb, j, sqb):
                def f(e, cb=cb, j=j, sqb=sqb):
                    return e.matmul(
                        pC[0:1, cb * 512:(cb + 1) * 512], ones_sb[:, 0:1],
                        sqb[:, cb * 512:(cb + 1) * 512],
                        start=(j == 0), stop=(j == DIT - 1))
                return f
            pend_sq.append((mk_sq(0, j, sqb), sq_p[buf], mk_sq(1, j, sqb)))
        for q in pend_sq:
            S.op("tensor", q[0], waits=[("s", q[1])], inc="t")
            S.op("tensor", q[2], inc="t")
        pend_sq = []
        ssq_t = S.cnt["t"]
        r_row = ar(A_B1, 1024, 1)
        r_row2 = ar(A_B1 + 1024, 1024, 1)
        S.op("vector", lambda e: e.tensor_scalar(
            r_row, pC[0:1, 0:1024], 1.0 / DI, EPS, OP.mult, OP.add),
            waits=[("t", ssq_t)], inc="v")
        S.op("vector", lambda e: e.reciprocal(r_row2, r_row), inc="v")
        S.op("scalar", lambda e: e.activation(r_row, r_row2, AF.Sqrt),
             waits=S.at("v"), inc="s")
        rbounce = S.dma("scalar", "dbs", ln_dram[0:1, :], r_row, waits=S.at("s"))
        r_bc = ar(A_B5, 1024)
        ldrbc = S.dma("sync", "db", r_bc, bass.AP(ln_dram, 0, [[0, 128], [1, T]]),
                      waits=[rbounce])

        yn_sb = ar(A_B1, 4 * DI)
        p6_done_v = S.cnt["v"]
        for cb in range(2):
            yn_v = 0
            for k in range(DIT):
                tb = ar(A_B5 + 1024 + 512 * (k % 2), 512)
                ldyf = S.dma("sync", "dy0" if k % 2 == 0 else "dy1", tb,
                             bass.AP(yf_dram, 128 * k * T + 512 * cb,
                                     [[T, 128], [1, 512]]),
                             waits=[yf_dp[0], yf_dp[1], ("v", S.cnt["v"])])

                def ynf(e, k=k, cb=cb, tb=tb):
                    return e.tensor_scalar(
                        yn_sb[:, 512 * k:512 * (k + 1)], tb,
                        nw_col[:, k:k + 1], None, OP.mult)
                yn_v = S.op("vector", ynf, waits=[ldyf], inc="v")
            for oh in range(2):
                for k in range(DIT):
                    buf = k % 2
                    wsl = wq[buf]
                    ldw = S.dma("sync", "dw0" if buf == 0 else "dw1",
                                wsl[:, 0:1024],
                                w_out[128 * k:128 * (k + 1), 1024 * oh:1024 * (oh + 1)],
                                waits=[("t", wq_free[buf])])

                    def pmm(e, k=k, wsl=wsl):
                        last = None
                        for o in range(8):
                            ps = [pA, pB, pC, pD][o // 2]
                            last = e.matmul(
                                ps[:, 512 * (o % 2):512 * (o % 2 + 1)],
                                wsl[:, 128 * o:128 * (o + 1)],
                                yn_sb[:, 512 * k:512 * (k + 1)],
                                start=(k == 0), stop=(k == DIT - 1))
                        return last
                    wq_free[buf] = S.op(
                        "tensor", pmm,
                        waits=[ldw, ("v", max(yn_v, p6_done_v) if k == 0 else 0)],
                        inc="t")
                op_t = S.cnt["t"]
                for o in range(8):
                    ps = [pA, pB, pC, pD][o // 2]
                    og = 8 * oh + o

                    def res_r(e, ps=ps, o=o, cb=cb):
                        return e.tensor_tensor(
                            ar(A_B5 + 2048, 512),
                            ps[:, 512 * (o % 2):512 * (o % 2 + 1)],
                            r_bc[:, 512 * cb:512 * (cb + 1)], OP.mult)
                    S.op("vector", res_r, waits=[("t", op_t), ldrbc], inc="v")

                    def res_f(e, og=og, cb=cb):
                        xsl = xT_sb[:, og * T + 512 * cb: og * T + 512 * (cb + 1)]
                        return e.tensor_tensor(
                            ar(A_B5 + 2560, 512), xsl, ar(A_B5 + 2048, 512),
                            OP.add)
                    S.op("vector", res_f, inc="v")

                    def res_c(e, og=og, cb=cb):
                        xsl = xT_sb[:, og * T + 512 * cb: og * T + 512 * (cb + 1)]
                        return e.tensor_copy(xsl, ar(A_B5 + 2560, 512))
                    S.op("vector", res_c, inc="v")
                p6_done_v = S.cnt["v"]
        sqt = ar(A_B5 + 1024, 512)
        for cb in range(2):
            for k in range(KD):
                sq_s = S.op("scalar", lambda e, k=k, cb=cb: e.activation(
                    sqt, xT_sb[:, k * T + 512 * cb: k * T + 512 * (cb + 1)],
                    AF.Square), waits=[("v", p6_done_v), ("t", S.cnt["t"])], inc="s")
                S.op("tensor", lambda e, k=k, cb=cb: e.matmul(
                    pA[0:1, 512 * cb:512 * (cb + 1)], ones_sb[:, 0:1],
                    xT_sb[:, k * T + 512 * cb: k * T + 512 * (cb + 1)],
                    start=(k == 0), stop=(k == KD - 1)),
                    waits=[("v", p6_done_v)], inc="t")
                S.op("tensor", lambda e, k=k, cb=cb: e.matmul(
                    pB[0:1, 512 * cb:512 * (cb + 1)], ones_sb[:, 0:1], sqt,
                    start=(k == 0), stop=(k == KD - 1)),
                    waits=[("s", sq_s)], inc="t")
        ln_t = S.cnt["t"]
        mu_row = ar(A_B1, 1024, 1)
        r1_row = ar(A_B1 + 1024, 1024, 1)
        t_row = ar(A_B1 + 2048, 1024, 1)
        t_row2 = ar(A_B1 + 3072, 1024, 1)
        S.op("vector", lambda e: e.tensor_scalar_mul(mu_row, pA[0:1, 0:1024],
                                                     1.0 / DM),
             waits=[("t", ln_t)], inc="v")
        S.op("vector", lambda e: e.tensor_scalar_mul(t_row, pB[0:1, 0:1024],
                                                     1.0 / DM), inc="v")
        S.op("vector", lambda e: e.tensor_tensor(r1_row, mu_row, mu_row, OP.mult),
             inc="v")
        S.op("vector", lambda e: e.tensor_sub(t_row2, t_row, r1_row), inc="v")
        S.op("vector", lambda e: e.tensor_scalar_add(t_row, t_row2, EPS), inc="v")
        S.op("vector", lambda e: e.reciprocal(r1_row, t_row), inc="v")
        S.op("scalar", lambda e: e.activation(t_row, r1_row, AF.Sqrt),
             waits=S.at("v"), inc="s")
        mu_d = S.dma("scalar", "dbs", ln_dram[1:2, :], mu_row, waits=S.at("v", "s"))
        rr_d = S.dma("scalar", "dbs", ln_dram[2:3, :], t_row, waits=S.at("s"))
        mu_bc = ar(A_B2, 1024)
        ldmu = S.dma("sync", "db", mu_bc, bass.AP(ln_dram, T, [[0, 128], [1, T]]),
                     waits=[mu_d, ("v", S.cnt["v"])])
        ldrr = S.dma("sync", "db", r_bc, bass.AP(ln_dram, 2 * T, [[0, 128], [1, T]]),
                     waits=[rr_d])
        ln1_v = 0
        for k in range(KD):
            def c1(e, k=k):
                xsl = xT_sb[:, k * T:(k + 1) * T]
                return e.tensor_tensor(ar(A_B1, T), xsl, mu_bc, OP.subtract)
            S.op("vector", c1, waits=[ldmu, ldrr], inc="v")

            def c2(e, k=k):
                xsl = xT_sb[:, k * T:(k + 1) * T]
                return e.tensor_tensor(xsl, ar(A_B1, T), r_bc, OP.mult)
            ln1_v = S.op("vector", c2, inc="v")

        xi_bf = AR[0:128, A_B1:A_B1 + 8192].bitcast(BF16)
        sg_bc = ar(A_B3, 1024)
        sv_bc = ar(A_B3 + 1024, 1024)
        qt1 = ar(A_B3 + 2048, 1024)
        qt2 = ar(A_B3 + 3072, 1024)
        sxq_bc = ar(A_B4, 1024)
        gm = ar(A_B4, 1024)
        grow_g = ar(A_B4 + 1024, 1024, 1)
        grow_v2 = ar(A_B4 + 2048, 1024, 1)
        gcol = ar(A_B4 + 3072, 8)
        gq = ar(A_B4 + 3080, 8)
        gr = ar(A_B4 + 3088, 8)
        ggam = ar(A_B4 + 3096, 4)

        maxt = ar(A_B2, 1024)
        absx = ar(A_B2 + 1024, 1024)
        S.op("scalar", lambda e: e.activation(maxt, xT_sb[:, 0:T], AF.Abs),
             waits=[("v", ln1_v), ("t", S.cnt["t"])], inc="s")
        for k in range(1, KD):
            ab_s = S.op("scalar", lambda e, k=k: e.activation(
                absx, xT_sb[:, k * T:(k + 1) * T], AF.Abs),
                waits=[("v", S.cnt["v"])], inc="s")
            S.op("vector", lambda e: e.tensor_tensor(maxt, maxt, absx, OP.max),
                 waits=[("s", ab_s)], inc="v")
        gx_v = S.cnt["v"]

        def gamma_pass(src, extra_tw=()):
            for c8 in range(8):
                S.op("tensor", lambda e, c8=c8, src=src: e.transpose(
                    pA[:, 128 * c8:128 * c8 + 128],
                    src[:, 128 * c8:128 * (c8 + 1)], ident[:]),
                    waits=([("v", S.cnt["v"])] + list(extra_tw)) if c8 == 0 else (),
                    inc="t")
            tt = S.cnt["t"]
            for c8 in range(8):
                S.op("vector", lambda e, c8=c8: e.reduce_max(
                    gcol[:, c8:c8 + 1], pA[:, 128 * c8:128 * (c8 + 1)], axis=AX.X),
                    waits=[("t", tt)] if c8 == 0 else (), inc="v")
            S.op("vector", lambda e: e.tensor_scalar_max(gcol, gcol, 1e-5), inc="v")
            S.op("vector", lambda e: e.reciprocal(gq, gcol), inc="v")
            S.op("vector", lambda e: e.tensor_scalar_mul(gr, gq, 127.0), inc="v")
            vq = S.cnt["v"]
            for c8 in range(8):
                S.dma("scalar", "dbs", ln_dram[3:4, 128 * c8:128 * (c8 + 1)],
                      gr[:, c8:c8 + 1], waits=[("v", vq)])
                dd = S.dma("scalar", "dbs", ln_dram[4:5, 128 * c8:128 * (c8 + 1)],
                           gcol[:, c8:c8 + 1], waits=[("v", vq)])
            return vq, dd

        gx2_v, gq_d = gamma_pass(maxt)
        ldsx = S.dma("sync", "db", sxq_bc, bass.AP(ln_dram, 3 * T, [[0, 128], [1, T]]),
                     waits=[gq_d, ("v", gx2_v)])
        ldgw = S.dma("sync", "db", ggam, bass.AP(misc_dram, 6 * 64, [[0, 128], [1, 4]]),
                     waits=[gam_bd, ("v", gx2_v)])
        ldg4 = S.dma("sync", "db", grow_g, bass.AP(ln_dram, 4 * T, [[1, 1], [1, T]]),
                     waits=[gq_d, ("v", gx2_v)])
        S.op("vector", lambda e: e.tensor_scalar(
            grow_v2, grow_g, ggam[0:1, 0:1], 1.0 / 127.0, OP.mult, OP.mult),
            waits=[ldgw, ldg4], inc="v")
        sgq_d = S.dma("scalar", "dbs", ln_dram[5:6, :], grow_v2, waits=S.at("v"))
        S.op("vector", lambda e: e.tensor_scalar(
            grow_v2, grow_g, ggam[0:1, 1:2], 1.0 / 127.0, OP.mult, OP.mult),
            waits=[sgq_d], inc="v")
        sgq_d = S.dma("scalar", "dbs", ln_dram[6:7, :], grow_v2, waits=S.at("v"))
        ldsg = S.dma("sync", "db", sg_bc, bass.AP(ln_dram, 5 * T, [[0, 128], [1, T]]),
                     waits=[sgq_d])
        ldsv = S.dma("sync", "db", sv_bc, bass.AP(ln_dram, 6 * T, [[0, 128], [1, T]]),
                     waits=[sgq_d])
        xi_v = 0
        for k in range(KD):
            S.op("vector", lambda e, k=k: e.tensor_tensor(
                qt1, xT_sb[:, k * T:(k + 1) * T], sxq_bc, OP.mult),
                waits=[ldsx] if k == 0 else (), inc="v")
            S.op("vector", lambda e: e.tensor_scalar(
                qt2, qt1, MAGIC, -MAGIC, OP.add, OP.add), inc="v")
            S.op("vector", lambda e: e.tensor_scalar_max(qt1, qt2, -128.0), inc="v")
            xi_v = S.op("vector", lambda e, k=k: e.tensor_scalar_min(
                xi_bf[:, k * T:(k + 1) * T], qt1, 127.0), inc="v")

        bandg = ar(A_B5, 2048).bitcast(BF16)
        bandv = ar(A_B5 + 2048, 2048).bitcast(BF16)
        gmp = [gm, ar(A_B4 + 1024, 1024)]
        S.op("vector", lambda e: e.memset(gm, 0.0), waits=[("v", xi_v)], inc="v")
        S.op("vector", lambda e: e.memset(gmp[1], 0.0), inc="v")
        gm0_v = S.cnt["v"]
        ngm = [0, 0]
        mid_d = ("dmid", 0)
        bg_free = S.cnt["t"]
        bv_free = S.cnt["t"]
        for fb in range(32):
            bw = [("cc", cc_m_n), ("t", bg_free)]
            if fb == 0:
                bw += [("dy0", S.cnt["dy0"]), ("dy1", S.cnt["dy1"]),
                       ("db", S.cnt["db"]), ("dy", S.cnt["dy"])]
            ldbg = S.dma("sync", "dr", bandg,
                         bass.AP(mg_out, 256 * fb,
                                 [[FFH, 128], [FFH * 128, KD], [1, 256]]),
                         waits=bw)
            ldbv = S.dma("sync", "drv", bandv,
                         bass.AP(mv_out, 256 * fb,
                                 [[FFH, 128], [FFH * 128, KD], [1, 256]]),
                         waits=[("cc", cc_m_n), ("t", bv_free)])

            def gvmm(e, band, pss):
                last = None
                for ft in range(2):
                    for cb in range(2):
                        ps = pss[ft]
                        for k in range(KD):
                            last = e.matmul(
                                ps[:, 512 * cb:512 * (cb + 1)],
                                band[:, 256 * k + 128 * ft:256 * k + 128 * (ft + 1)],
                                xi_bf[:, k * T + 512 * cb:k * T + 512 * (cb + 1)],
                                start=(k == 0), stop=(k == KD - 1))
                return last
            bg_free = S.op("tensor", lambda e: gvmm(e, bandg, [pA, pB]),
                           waits=[ldbg, ("v", S.cnt["v"])], inc="t")
            bv_free = S.op("tensor", lambda e: gvmm(e, bandv, [pC, pD]),
                           waits=[ldbv, ("v", S.cnt["v"])], inc="t")
            for ft in range(2):
                for cb in range(2):
                    psg, psv = [pA, pB][ft], [pC, pD][ft]
                    fgl = 2 * fb + ft

                    def sgm(e, psg=psg, cb=cb):
                        return e.tensor_tensor(
                            qt1[:, 0:512], psg[:, 512 * cb:512 * (cb + 1)],
                            sg_bc[:, 512 * cb:512 * (cb + 1)], OP.mult)
                    S.op("vector", sgm,
                         waits=[("t", bv_free), ldsg, ldsv, mid_d], inc="v")
                    S.op("scalar", lambda e, fgl=fgl: e.activation(
                        qt2[:, 0:512], qt1[:, 0:512], AF.Silu,
                        bias=fb_sb[:, fgl:fgl + 1]), waits=S.at("v"), inc="s")

                    def svm(e, psv=psv, cb=cb):
                        return e.tensor_tensor(
                            qt1[:, 512:1024], psv[:, 512 * cb:512 * (cb + 1)],
                            sv_bc[:, 512 * cb:512 * (cb + 1)], OP.mult)
                    S.op("vector", svm, inc="v")
                    S.op("vector", lambda e, fgl=fgl: e.tensor_scalar_add(
                        qt1[:, 0:512], qt1[:, 512:1024],
                        fb_sb[:, 64 + fgl:64 + fgl + 1]),
                        waits=S.at("s"), inc="v")
                    S.op("vector", lambda e: e.tensor_tensor(
                        qt1[:, 512:1024], qt1[:, 0:512], qt2[:, 0:512], OP.mult),
                        waits=S.at("s"), inc="v")
                    am_s = S.op("scalar", lambda e: e.activation(
                        qt2[:, 512:1024], qt1[:, 512:1024], AF.Abs),
                        waits=S.at("v"), inc="s")
                    par = ngm[cb] % 2
                    S.op("vector", lambda e, cb=cb, par=par: e.tensor_tensor(
                        gmp[1 - par][:, 512 * cb:512 * (cb + 1)],
                        gmp[par][:, 512 * cb:512 * (cb + 1)],
                        qt2[:, 512:1024], OP.max), waits=[("s", am_s)], inc="v")
                    ngm[cb] += 1
                    midv = S.cnt["v"]
                    mid_d = S.dma(
                        "scalar", "dmid",
                        mid_dram[128 * fgl:128 * (fgl + 1), 512 * cb:512 * (cb + 1)],
                        qt1[:, 512:1024], waits=[("v", midv)])

        go_v, go_d = gamma_pass(gm)
        ldso = S.dma("sync", "db", sxq_bc, bass.AP(ln_dram, 3 * T, [[0, 128], [1, T]]),
                     waits=[go_d, ("v", go_v)])
        ldg5 = S.dma("sync", "db", grow_g, bass.AP(ln_dram, 4 * T, [[1, 1], [1, T]]),
                     waits=[go_d, ("v", go_v)])
        S.op("vector", lambda e: e.tensor_scalar(
            grow_v2, grow_g, ggam[0:1, 2:3], 1.0 / 127.0, OP.mult, OP.mult),
            waits=[ldg5, ldgw], inc="v")
        soo_d = S.dma("scalar", "dbs", ln_dram[5:6, :], grow_v2, waits=S.at("v"))
        ldsoo = S.dma("sync", "db", sg_bc, bass.AP(ln_dram, 5 * T, [[0, 128], [1, T]]),
                      waits=[soo_d, ("v", S.cnt["v"])])
        mi_v = 0
        mi_d = ("dmi", 0)
        mo_bf = ar(A_B5, 4096).bitcast(BF16)
        mi_sb = [AR[0:128, A_B1 + 2048 * i:A_B1 + 2048 * (i + 1)].bitcast(BF16)
                 for i in range(2)]
        p7_done_v = S.cnt["v"]
        mo_free_t = S.cnt["t"]
        for cb in range(2):
            for oh in range(2):
                for kc in range(8):
                    if cb == 0 and oh == 0:
                        for ft8 in range(8):
                            ft = 8 * kc + ft8
                            ldm = S.dma("sync", "dy", qt2,
                                        mid_dram[128 * ft:128 * (ft + 1), :],
                                        waits=[mid_d, mi_d, ("v", mi_v)])
                            S.op("vector", lambda e: e.tensor_tensor(
                                qt1, qt2, sxq_bc, OP.mult),
                                waits=[ldm, ldso], inc="v")
                            S.op("vector", lambda e: e.tensor_scalar(
                                qt2, qt1, MAGIC, -MAGIC, OP.add, OP.add), inc="v")
                            S.op("vector", lambda e: e.tensor_scalar_max(
                                qt1, qt2, -128.0), inc="v")
                            mi_v = S.op("vector", lambda e: e.tensor_scalar_min(
                                qt2.bitcast(BF16)[:, 0:1024], qt1, 127.0), inc="v")
                            mi_d = S.dma("scalar", "dmi",
                                         mi_dram[128 * ft:128 * (ft + 1), :],
                                         qt2.bitcast(BF16)[:, 0:1024],
                                         waits=[("v", mi_v)])
                    mw = [("cc", cc_m_n), ("t", mo_free_t)]
                    if cb == 0 and oh == 0 and kc == 0:
                        mw += [("dr", S.cnt["dr"]), ("drv", S.cnt["drv"])]
                    ldmo = S.dma("sync", "dmo", mo_bf,
                                 bass.AP(mo_out, (1024 * kc) * DM + 1024 * oh,
                                         [[DM, 128], [DM * 128, 8], [1, 1024]]),
                                 waits=mw)
                    miw = [mi_d, ("t", mo_free_t)]
                    if cb == 0 and oh == 0 and kc < 2:
                        miw += [("dx", S.cnt["dx"]), ("v", S.cnt["v"])]
                    ldmi = S.dma("sync", "dm0" if kc % 2 == 0 else "dm1",
                                 mi_sb[kc % 2],
                                 bass.AP(mi_dram, (1024 * kc) * T + 512 * cb,
                                         [[T, 128], [128 * T, 8], [1, 512]]),
                                 waits=miw)

                    def omm7(e, kc=kc):
                        last = None
                        for k8 in range(8):
                            for o in range(8):
                                ps = [pA, pB, pC, pD][o // 2]
                                last = e.matmul(
                                    ps[:, 512 * (o % 2):512 * (o % 2 + 1)],
                                    mo_bf[:, 1024 * k8 + 128 * o:
                                          1024 * k8 + 128 * (o + 1)],
                                    mi_sb[kc % 2][:, 512 * k8:512 * (k8 + 1)],
                                    start=(kc == 0 and k8 == 0),
                                    stop=(kc == 7 and k8 == 7))
                        return last
                    mo_free_t = S.op("tensor", omm7,
                                     waits=[ldmo, ldmi,
                                            ("v", p7_done_v if kc == 0 else 0)],
                                     inc="t")
                for o in range(8):
                    ps = [pA, pB, pC, pD][o // 2]
                    og = 8 * oh + o

                    def fev(e, ps=ps, o=o, cb=cb):
                        return e.tensor_tensor(
                            qt1[:, 0:512], ps[:, 512 * (o % 2):512 * (o % 2 + 1)],
                            sg_bc[:, 512 * cb:512 * (cb + 1)], OP.mult)
                    S.op("vector", fev,
                         waits=[("t", mo_free_t), ldsoo], inc="v")
                    S.op("vector", lambda e, og=og: e.tensor_scalar_add(
                        qt2[:, 0:512], qt1[:, 0:512], fb_sb[:, 128 + og:129 + og]),
                        inc="v")

                    def fres(e, og=og, cb=cb):
                        xsl = xT_sb[:, og * T + 512 * cb: og * T + 512 * (cb + 1)]
                        return e.tensor_tensor(qt1[:, 512:1024], xsl,
                                               qt2[:, 0:512], OP.add)
                    S.op("vector", fres, inc="v")

                    def fres_c(e, og=og, cb=cb):
                        xsl = xT_sb[:, og * T + 512 * cb: og * T + 512 * (cb + 1)]
                        return e.tensor_copy(xsl, qt1[:, 512:1024])
                    S.op("vector", fres_c, inc="v")
                p7_done_v = S.cnt["v"]
        sqt2 = ar(A_B3 + 2048, 512)
        for cb in range(2):
            for k in range(KD):
                sq_s = S.op("scalar", lambda e, k=k, cb=cb: e.activation(
                    sqt2, xT_sb[:, k * T + 512 * cb: k * T + 512 * (cb + 1)],
                    AF.Square), waits=[("v", p7_done_v), ("t", S.cnt["t"])], inc="s")
                S.op("tensor", lambda e, k=k, cb=cb: e.matmul(
                    pA[0:1, 512 * cb:512 * (cb + 1)], ones_sb[:, 0:1],
                    xT_sb[:, k * T + 512 * cb: k * T + 512 * (cb + 1)],
                    start=(k == 0), stop=(k == KD - 1)),
                    waits=[("v", p7_done_v)], inc="t")
                S.op("tensor", lambda e, k=k, cb=cb: e.matmul(
                    pB[0:1, 512 * cb:512 * (cb + 1)], ones_sb[:, 0:1], sqt2,
                    start=(k == 0), stop=(k == KD - 1)),
                    waits=[("s", sq_s)], inc="t")
        ln2_t = S.cnt["t"]
        mu2 = ar(A_B4 + 1024, 1024, 1)
        t2r = ar(A_B4 + 2048, 1024, 1)
        r2r = ar(A_B4 + 3072, 1024, 1)
        S.op("vector", lambda e: e.tensor_scalar_mul(mu2, pA[0:1, 0:1024], 1.0 / DM),
             waits=[("t", ln2_t)], inc="v")
        S.op("vector", lambda e: e.tensor_scalar_mul(t2r, pB[0:1, 0:1024], 1.0 / DM),
             inc="v")
        S.op("vector", lambda e: e.tensor_tensor(r2r, mu2, mu2, OP.mult), inc="v")
        S.op("vector", lambda e: e.tensor_sub(t2r, t2r, r2r), inc="v")
        S.op("vector", lambda e: e.tensor_scalar_add(t2r, t2r, EPS), inc="v")
        S.op("vector", lambda e: e.reciprocal(r2r, t2r), inc="v")
        S.op("scalar", lambda e: e.activation(t2r, r2r, AF.Sqrt),
             waits=S.at("v"), inc="s")
        mu2_d = S.dma("scalar", "dbs", ln_dram[1:2, :], mu2, waits=S.at("v", "s"))
        rr2_d = S.dma("scalar", "dbs", ln_dram[2:3, :], t2r, waits=S.at("s"))
        mu_bc2 = ar(A_B2, 1024)
        r_bc2 = ar(A_B2 + 1024, 1024)
        ldmu2 = S.dma("sync", "db", mu_bc2, bass.AP(ln_dram, T, [[0, 128], [1, T]]),
                      waits=[mu2_d, ("v", S.cnt["v"])])
        ldrr2 = S.dma("sync", "db", r_bc2, bass.AP(ln_dram, 2 * T, [[0, 128], [1, T]]),
                      waits=[rr2_d, ("v", S.cnt["v"])])
        for k in range(KD):
            def d1f(e, k=k):
                xsl = xT_sb[:, k * T:(k + 1) * T]
                return e.tensor_tensor(xsl, xsl, mu_bc2, OP.subtract)
            S.op("vector", d1f, waits=[ldmu2, ldrr2], inc="v")

            def d2f(e, k=k):
                xsl = xT_sb[:, k * T:(k + 1) * T]
                return e.tensor_tensor(xsl, xsl, r_bc2, OP.mult)
            S.op("vector", d2f, inc="v")
        fin_v = S.cnt["v"]
        yout_d = S.dma("gpsimd", "dg",
                       bass.AP(yout, 0, [[T, 128], [128 * T, KD], [1, T]]),
                       xT_sb, waits=[("v", fin_v)])

        if "mid" in taps:
            for k in range(KD):
                S.dma("gpsimd", "dg", taps["mid"][128 * k:128 * (k + 1), :],
                      AR[0:128, A_XT + k * T:A_XT + (k + 1) * T],
                      waits=[("v", ln1_v)])
        if "ffnmid" in taps:
            for ft in range(FFT):
                S.dma("gpsimd", "dg", taps["ffnmid"][128 * ft:128 * (ft + 1), :],
                      mid_dram[128 * ft:128 * (ft + 1), :],
                      waits=[("dmid", S.cnt["dmid"])])
        if "acsT" in taps:
            S.dma("gpsimd", "dg", taps["acsT"][:, :], acsT[:, :],
                  waits=[("v", acs_v)])
        if "state" in taps:
            S.dma("gpsimd", "dg", taps["state"][:, :], s_run, waits=S.at("v"))
        if "y" in taps:
            for f in range(DIT):
                S.dma("gpsimd", "dg", taps["y"][128 * f:128 * (f + 1), :],
                      y_dram[128 * f:128 * (f + 1), :],
                      waits=[("dys", S.cnt["dys"])])
        if "xs" in taps:
            for f in range(32):
                S.dma("gpsimd", "dg", taps["xs"][128 * f:128 * (f + 1), :],
                      xs_dram[128 * f:128 * (f + 1), :],
                      waits=[patch_g, p1_dsc])
            for f in (32, 33):
                S.dma("gpsimd", "dg", taps["xs"][128 * f:128 * (f + 1), :],
                      BC[:, (f - 32) * T:(f - 31) * T],
                      waits=[("v", patch_v)])
        if "dt" in taps:
            S.dma("gpsimd", "dg", taps["dt"][:, :], dt_sb[:],
                  waits=[("v", dt_ready_v)])

        S.op("gpsimd", lambda e: e.nop(),
             waits=S.at("dg"), inc="g")

        block = E(nc.Block())
        S.emit(block, sems)
    return nc



def _stage(core, ins):
    b, q = core // 4, core % 4
    hs = ins["L0_attn_hidden_states"]
    xTa = np.ascontiguousarray(hs[b, q * T:(q + 1) * T, :].T).reshape(KD, 128, T)
    WT = np.ascontiguousarray(ins["L0_attn_in_proj_w"].T)
    wx = WT[:, DI:DI + 4352]
    w_xbc = np.ascontiguousarray(
        wx.reshape(KD, 128, FX, 128).transpose(2, 1, 0, 3)).reshape(FX, 128, KD * 128)
    wd = WT[:, DI + 4352:]
    w_dt = np.ascontiguousarray(
        wd.reshape(KD, 128, 64).transpose(1, 0, 2)).reshape(128, KD * 64)
    w_gate = np.ascontiguousarray(
        WT[:, :DI].reshape(KD, 128, DIT, 128).transpose(2, 1, 0, 3)).reshape(
        DIT, 128, KD * 128)
    w_out = np.ascontiguousarray(ins["L0_attn_out_proj_w"].T)
    conv_w = np.ascontiguousarray(
        ins["L0_attn_conv_w"].reshape(FX, 128, 4).transpose(1, 0, 2)).reshape(128, FX * 4)
    conv_b = np.ascontiguousarray(ins["L0_attn_conv_b"].reshape(FX, 128).T)
    state_w5 = np.zeros((5, 4), np.float32)
    for j in range(4):
        if j < q:
            for i in range(4):
                state_w5[i, j] = 1.0 if (j < i < q) else 0.0
        else:
            state_w5[4, j] = -1e30
    tail_mask = np.zeros((128, 4), np.float32)
    if q > 0:
        tail_mask[:, q - 1] = 1.0
    f32 = np.float32
    return {
        "xT": xTa.astype(f32), "w_xbc": w_xbc.astype(f32), "w_dt": w_dt.astype(f32),
        "w_gate": w_gate.astype(f32), "w_out": w_out.astype(f32),
        "conv_w": conv_w.astype(f32), "conv_b": conv_b.astype(f32),
        "dt_bias": np.ascontiguousarray(ins["L0_attn_dt_bias"].reshape(64, 1)).astype(f32),
        "a_log": np.ascontiguousarray(ins["L0_attn_A_log"].reshape(64, 1)).astype(f32),
        "d_col": np.ascontiguousarray(
            np.repeat(ins["L0_attn_D"].reshape(DIT, 2).T, 64, axis=0)).astype(f32),
        "norm_w": np.ascontiguousarray(ins["L0_attn_norm_w"].reshape(DIT, 128).T).astype(f32),
        "gw_sl": np.ascontiguousarray(ins["L0_ffn_gate_w"].T[256 * core:256 * (core + 1)]).astype(f32),
        "vw_sl": np.ascontiguousarray(ins["L0_ffn_val_w"].T[256 * core:256 * (core + 1)]).astype(f32),
        "ow_sl": np.ascontiguousarray(ins["L0_ffn_out_w"].T[1024 * core:1024 * (core + 1)]).astype(f32),
        "gate_b": np.ascontiguousarray(ins["L0_ffn_gate_b"].reshape(FFT, 128).T).astype(f32),
        "val_b": np.ascontiguousarray(ins["L0_ffn_val_b"].reshape(FFT, 128).T).astype(f32),
        "out_b": np.ascontiguousarray(ins["L0_ffn_out_b"].reshape(KD, 128).T).astype(f32),
        "state_w5": state_w5, "tail_mask": tail_mask,
    }


def kernel(**inputs):
    ins = {k: np.asarray(v) for k, v in inputs.items()}
    nc = build_program()
    in_maps = [_stage(c, ins) for c in range(8)]
    res = run_bass_kernel_spmd(nc, in_maps, core_ids=list(range(8)))
    out = np.empty((2, 4096, 2048), np.float32)
    for c in range(8):
        b, q = c // 4, c % 4
        out[b, q * T:(q + 1) * T, :] = res.results[c]["yout"].T
    return out


# version 04
# speedup vs baseline: 1.0801x; 1.0801x over previous
import numpy as np
import ml_dtypes
import concourse.bass as bass
import concourse.mybir as mybir
from concourse.bass_utils import run_bass_kernel_spmd

F32 = mybir.dt.float32
BF16 = mybir.dt.bfloat16
AF = mybir.ActivationFunctionType
OP = mybir.AluOpType
AX = mybir.AxisListType

T = 1024
NCH = 8
CH = 128
KD = 16
DM = 2048
FX = 34
NH = 64
NS = 128
DI = 4096
DIT = 32
FFH = 8192
FFT = 64
MAGIC = 12582912.0
EPS = 1e-5

DEV_PHASES = 99
DEV_TAPS = ()

A_XT = 0
A_B1 = 16384
A_B2 = 20480
A_B3 = 24576
A_B4 = 28672
A_B5 = 32768
A_BC = 36864
A_ACC = 38912
A_XSF = 39936
A_END = 40960
ARENA_N = A_END


class Sched:

    ENGS = ("sync", "tensor", "vector", "scalar", "gpsimd")

    def __init__(self):
        self.ops = {e: [] for e in self.ENGS}
        self.cnt = {}
        self.owner = {}
        self.selfgate = {}

    def sem(self, name, owner=None, selfgate=False):
        self.cnt[name] = 0
        self.owner[name] = owner
        self.selfgate[name] = selfgate

    SERIAL = {"vector": "v", "scalar": "s", "gpsimd": "g"}

    def op(self, eng, fn, waits=(), inc=None, n=1, is_dma=False):
        waits = list(waits)
        own = self.SERIAL.get(eng)
        if own is not None and self.cnt[own] > 0:
            waits.append((own, self.cnt[own]))
        for s, v in waits:
            assert v <= self.cnt[s], f"wait {s}>={v} but only {self.cnt[s]} emitted"
        self.ops[eng].append((fn, tuple(waits), inc, n, is_dma))
        if inc is not None:
            self.cnt[inc] += n
            return self.cnt[inc]
        return None

    def dma(self, eng, sem, out, in_, waits=()):
        def fn(e, out=out, in_=in_):
            return e.dma_start(out=out, in_=in_)
        waits = list(waits)
        if self.selfgate.get(sem):
            waits.append((sem, self.cnt[sem]))
        c = self.op(eng, fn, waits, inc=sem, n=16, is_dma=True)
        return (sem, c)

    def at(self, *names):
        return [(s, self.cnt[s]) for s in names]

    def emit(self, block, sems):
        for eng in self.ENGS:
            ops = self.ops[eng]
            if not ops:
                continue

            def body(e, eng=eng, ops=ops):
                last = {}
                for fn, waits, inc, n, is_dma in ops:
                    for s, v in waits:
                        if v <= 0 or last.get(s, 0) >= v:
                            continue
                        e.wait_ge(sems[s], v)
                        last[s] = v
                    inst = fn(e)
                    if inc is not None:
                        inst.then_inc(sems[inc], n)

            getattr(block, eng)(body)


def build_program():
    nc = bass.Bass("TRN2", target_bir_lowering=False)
    S = Sched()
    for name in ["t", "v", "s", "g"]:
        S.sem(name, owner=name if name != "t" else "tensor")
    S.owner["t"] = "tensor"
    S.owner["v"] = "vector"
    S.owner["s"] = "scalar"
    S.owner["g"] = "gpsimd"
    S.sem("cc")
    for name in ["dc", "dw0", "dw1", "dx", "dm0", "dm1", "dr", "drv", "dmo",
                 "dy", "dy0", "dy1", "dp0", "dp1", "dpm", "dxs", "dys", "dss",
                 "dmid", "dmi", "dyf", "dyf1"]:
        S.sem(name)
    for name in ["db", "dbs", "dg"]:
        S.sem(name, selfgate=True)

    I = {}
    def inp(name, shape, dt=F32):
        I[name] = nc.dram_tensor(name, shape, dt, kind="ExternalInput")
        return I[name]

    xT = inp("xT", [KD, 128, T])
    w_xbc = inp("w_xbc", [FX, 128, KD * 128])
    w_dt = inp("w_dt", [128, KD * 64])
    w_gate = inp("w_gate", [DIT, 128, KD * 128])
    w_out = inp("w_out", [DI, DM])
    conv_w = inp("conv_w", [128, FX * 4])
    conv_b = inp("conv_b", [128, FX])
    dt_bias = inp("dt_bias", [64, 1])
    a_log = inp("a_log", [64, 1])
    d_col = inp("d_col", [128, DIT])
    norm_w = inp("norm_w", [128, DIT])
    gw_sl = inp("gw_sl", [256, FFH])
    vw_sl = inp("vw_sl", [256, FFH])
    ow_sl = inp("ow_sl", [1024, DM])
    gate_b = inp("gate_b", [128, FFT])
    val_b = inp("val_b", [128, FFT])
    out_b = inp("out_b", [128, KD])
    state_w5 = inp("state_w5", [5, 4])
    tail_mask = inp("tail_mask", [128, 4])
    perm64 = inp("perm64", [64, 64])

    yout = nc.dram_tensor("yout", [DM, T], F32, kind="ExternalOutput")

    xs_dram = nc.dram_tensor("xs_dram", [NCH * 128, DIT * 128], F32)
    y_dram = nc.dram_tensor("y_dram", [NCH * 128, DIT * 128], F32)
    yf_dram = nc.dram_tensor("yf_dram", [DI, T], F32)
    st_dram = nc.dram_tensor("st_dram", [NCH, 128, DI], F32)
    a_dram = nc.dram_tensor("a_dram", [NCH, 64 * 128], F32)
    a4_dram = nc.dram_tensor("a4_dram", [NCH, 64 * 128], F32)
    alast_dram = nc.dram_tensor("alast_dram", [1, 512], F32)
    misc_dram = nc.dram_tensor("misc_dram", [128, 64], F32)
    ln_dram = nc.dram_tensor("ln_dram", [8, T], F32)
    mid_dram = nc.dram_tensor("mid_dram", [FFH, T], F32)
    mi_dram = nc.dram_tensor("mi_dram", [FFH, T], BF16)
    tail_in = nc.dram_tensor("tail_in", [128, 102], F32)
    tail_out = nc.dram_tensor("tail_out", [512, 102], F32)
    sag_in = nc.dram_tensor("sag_in", [128, DI], F32)
    sag_out = nc.dram_tensor("sag_out", [512, DI], F32)
    asum_in = nc.dram_tensor("asum_in", [64, 1], F32)
    asum_out = nc.dram_tensor("asum_out", [256, 1], F32)
    gam_in = nc.dram_tensor("gam_in", [1, 4], F32)
    gam_out = nc.dram_tensor("gam_out", [1, 4], F32)
    mg_in = nc.dram_tensor("mg_in", [256, FFH], BF16)
    mv_in = nc.dram_tensor("mv_in", [256, FFH], BF16)
    mo_in = nc.dram_tensor("mo_in", [1024, DM], BF16)
    mg_out = nc.dram_tensor("mg_out", [DM, FFH], BF16)
    mv_out = nc.dram_tensor("mv_out", [DM, FFH], BF16)
    mo_out = nc.dram_tensor("mo_out", [FFH, DM], BF16)

    taps = {}
    for tname in DEV_TAPS:
        shp = {"xs": [DI + 256, T], "dt": [64, T], "acsT": [64, T], "y": [DI, T],
               "yf": [DI, T], "mid": [DM, T], "state": [128, DI],
               "ffnmid": [FFH, T]}[tname]
        taps[tname] = nc.dram_tensor("tap_" + tname, shp, F32, kind="ExternalOutput")

    from contextlib import ExitStack
    with ExitStack() as stk:
        E = stk.enter_context
        AR = E(nc.sbuf_tensor("arena", [128, ARENA_N], F32))

        def ar(off, n, rows=128):
            return AR[0:rows, off:off + n]

        wq = [E(nc.sbuf_tensor(f"wq{i}", [128, KD * 128], F32)) for i in range(2)]
        dt_sb = E(nc.sbuf_tensor("dt_sb", [64, T], F32))
        acsT = E(nc.sbuf_tensor("acsT", [64, T], F32))
        dt_tm = E(nc.sbuf_tensor("dt_tm", [128, NCH * 64], F32))
        adt_tm = E(nc.sbuf_tensor("adt_tm", [128, NCH * 64], F32))
        acs_col = E(nc.sbuf_tensor("acs_col", [128, NCH * 64], F32))
        alast_sb = E(nc.sbuf_tensor("alast_sb", [64, NCH + 8], F32))
        ds_all = E(nc.sbuf_tensor("ds_all", [128, NCH * 64], F32))
        dch_rep = E(nc.sbuf_tensor("dch_rep", [128, NCH * 64], F32))
        arep = E(nc.sbuf_tensor("arep", [128, 64], F32))
        dcol = E(nc.sbuf_tensor("dcol", [128, DIT], F32))
        nw_col = E(nc.sbuf_tensor("nw_col", [128, DIT], F32))
        cw_sb = E(nc.sbuf_tensor("cw_sb", [128, FX * 4], F32))
        cb_sb = E(nc.sbuf_tensor("cb_sb", [128, FX], F32))
        ones_sb = E(nc.sbuf_tensor("ones_sb", [128, 128], F32))
        u01 = E(nc.sbuf_tensor("u01", [128, 128], F32))
        ident = E(nc.sbuf_tensor("ident", [128, 128], F32))
        gu = E(nc.sbuf_tensor("gu", [128, 128], F32))
        accfix = E(nc.sbuf_tensor("accfix", [128, 102], F32))
        tail_sb = E(nc.sbuf_tensor("tail_sb", [128, 102], F32))
        tailprev = E(nc.sbuf_tensor("tailprev", [128, 102], F32))
        xsfix = E(nc.sbuf_tensor("xsfix", [128, 102], F32))
        tmask_sb = E(nc.sbuf_tensor("tmask_sb", [128, 4], F32))
        fb_sb = E(nc.sbuf_tensor("fb_sb", [128, 144], F32))
        small = E(nc.sbuf_tensor("small", [128, 64], F32))
        pm_sb = E(nc.sbuf_tensor("pm_sb", [64, 64], F32))
        ap64 = E(nc.sbuf_tensor("ap64", [64, T], F32))
        pA = E(nc.psum_tensor("pA", [128, 1024], F32))
        pB = E(nc.psum_tensor("pB", [128, 1024], F32))
        pC = E(nc.psum_tensor("pC", [128, 1024], F32))
        pD = E(nc.psum_tensor("pD", [128, 1024], F32))

        sems = {name: E(nc.semaphore(name)) for name in S.cnt}

        xT_sb = ar(A_XT, KD * T)
        acc = ar(A_ACC, T)
        xs_f = ar(A_XSF, T)
        BC = ar(A_BC, 2 * T)

        ld0 = S.dma("sync", "dc", xT_sb, bass.AP(xT, 0, [[T, 128], [128 * T, KD], [1, T]]))
        S.dma("sync", "dc", cw_sb[:], conv_w[:, :])
        S.dma("sync", "dc", cb_sb[:], conv_b[:, :])
        S.dma("sync", "dc", tmask_sb[:], tail_mask[:, :])
        S.dma("sync", "dc", small[0:64, 0:1], dt_bias[:, :])
        S.dma("sync", "dc", small[0:64, 1:2], a_log[:, :])
        S.dma("sync", "dc", nw_col[:], norm_w[:, :])
        S.dma("sync", "dc", fb_sb[:, 0:64], gate_b[:, :])
        S.dma("sync", "dc", fb_sb[:, 64:128], val_b[:, :])
        S.dma("sync", "dc", fb_sb[:, 128:144], out_b[:, :])
        S.dma("sync", "dc", pm_sb[:], perm64[:, :])
        ld_const = ("dc", S.cnt["dc"])
        S.op("vector", lambda e: e.memset(ones_sb[:], 1.0), inc="v")
        S.op("gpsimd", lambda e: e.affine_select(
            u01[:], ones_sb[:], [[1, 128]], OP.is_ge, 0.0, base=0,
            channel_multiplier=-1), waits=S.at("v"), inc="g")
        S.op("gpsimd", lambda e: e.affine_select(
            ident[:], ones_sb[:], [[1, 128]], OP.is_equal, 0.0, base=0,
            channel_multiplier=-1), waits=S.at("v"), inc="g")
        g_const = S.cnt["g"]
        S.op("scalar", lambda e: e.activation(small[0:64, 4:5], small[0:64, 1:2],
                                              AF.Exp),
             waits=[ld_const], inc="s")
        S.op("vector", lambda e: e.tensor_scalar_mul(small[0:64, 5:6],
                                                     small[0:64, 4:5], -1.0),
             waits=S.at("s"), inc="v")
        S.dma("scalar", "dbs", misc_dram[0:1, 0:64], small[0:64, 5:6], waits=S.at("v"))
        bounce1 = ("dbs", S.cnt["dbs"])
        S.dma("sync", "dc", arep[:, 0:64],
              bass.AP(misc_dram, 0, [[0, 128], [1, 64]]),
              waits=[bounce1])
        S.dma("sync", "dc", dcol[:], I["d_col"][:, :])

        dc_all = ("dc", S.cnt["dc"])
        wacc = E(nc.sbuf_tensor("wacc", [128, 8], F32))
        grow = E(nc.sbuf_tensor("grow", [1, 8], F32))
        rgc = E(nc.sbuf_tensor("rgc", [128, 4], F32))
        p0b = [ar(A_B1, 2048), ar(A_B1 + 2048, 2048)]
        p0t = ar(A_B2, 2048)
        p0m_bf = ar(A_B2 + 2048, 1024).bitcast(BF16)
        subs = []
        for mat, src, dst in [(0, gw_sl, mg_in), (1, vw_sl, mv_in), (2, ow_sl, mo_in)]:
            for r in range(src.shape[0] // 128):
                for c in range(src.shape[1] // 2048):
                    subs.append((mat, src, dst, r, c))
        S.op("gpsimd", lambda e: e.memset(wacc[:, 0:6], 0.0), waits=[dc_all],
             inc="g")
        p0_consumed = [0, 0]
        nsub = [0, 0, 0]
        for i, (mat, src, dst, r, c) in enumerate(subs):
            bb = p0b[i % 2]
            ld = S.dma("sync", "dp0" if i % 2 == 0 else "dp1", bb,
                       src[128 * r:128 * (r + 1), 2048 * c:2048 * (c + 1)],
                       waits=[("s", p0_consumed[i % 2])])
            S.op("scalar", lambda e, bb=bb: e.activation(
                p0t, bb, AF.Abs, accum_out=wacc[:, 6:7]),
                waits=[ld, dc_all, ("g", S.cnt["g"])], inc="s")
            par = nsub[mat] % 2
            S.op("gpsimd", lambda e, mat=mat, par=par: e.tensor_add(
                wacc[:, 3 * (1 - par) + mat:3 * (1 - par) + mat + 1],
                wacc[:, 3 * par + mat:3 * par + mat + 1], wacc[:, 6:7]),
                waits=S.at("s"), inc="g")
            nsub[mat] += 1
            p0_consumed[i % 2] = S.cnt["s"]
        S.op("tensor", lambda e: e.matmul(pC[0:1, 0:4], ones_sb[:, 0:1],
                                          wacc[:, 0:4], start=True, stop=True),
             waits=S.at("g"), inc="t")
        S.op("vector", lambda e: e.tensor_copy(grow[:, 0:4], pC[0:1, 0:4]),
             waits=S.at("t"), inc="v")
        p0a_dma = [("dp0", S.cnt["dp0"]), ("dp1", S.cnt["dp1"])]

        wq_free = [0, 0]
        ps_free = [0, 0]
        last_silu = 0
        last_spill = ("dxs", 0)
        dt_ready_v = 0

        for f in range(FX + 1):
            buf = f % 2
            wsl = wq[buf]
            dwn = "dw0" if buf == 0 else "dw1"
            if f < FX:
                ldw = S.dma("sync", dwn, wsl[:], w_xbc[f, :, :],
                            waits=[("t", wq_free[buf])])
            else:
                ldw = S.dma("sync", dwn, wsl[:, 0:KD * 64], w_dt[:, :],
                            waits=[("t", wq_free[buf])])
            ps = [pA, pB][buf]
            rows = 128 if f < FX else 64

            def mm(e, ps=ps, wsl=wsl, rows=rows):
                last = None
                for cb in range(2):
                    for k in range(KD):
                        last = e.matmul(
                            ps[0:rows, cb * 512:(cb + 1) * 512],
                            wsl[:, k * rows:(k + 1) * rows],
                            xT_sb[:, k * T + cb * 512: k * T + (cb + 1) * 512],
                            start=(k == 0), stop=(k == KD - 1))
                return last
            tcnt = S.op("tensor", mm,
                        waits=[ldw, dc_all, ("v", ps_free[buf])],
                        inc="t")
            wq_free[buf] = tcnt

            if f < FX:
                pp = [acc, ar(A_B4, T)]

                def c0(e, ps=ps, f=f):
                    return e.tensor_scalar(pp[0], ps[:],
                                           cw_sb[:, 4 * f + 3:4 * f + 4],
                                           None, OP.mult)
                S.op("vector", c0, waits=[("t", tcnt), ("s", last_silu)], inc="v")
                for k in (2, 1, 0):
                    lo = 3 - k
                    src, dst2 = pp[(2 - k) % 2], pp[(3 - k) % 2]

                    def ck(e, ps=ps, f=f, k=k, lo=lo, src=src, dst2=dst2):
                        return e.scalar_tensor_tensor(
                            dst2[:, lo:T], ps[:, 0:T - lo],
                            cw_sb[:, 4 * f + k:4 * f + k + 1],
                            src[:, lo:T], OP.mult, OP.add)
                    S.op("vector", ck, inc="v")

                    def hd(e, lo=lo, src=src, dst2=dst2):
                        return e.tensor_copy(dst2[:, 0:lo], src[:, 0:lo])
                    S.op("vector", hd, inc="v")
                S.op("vector", lambda e, f=f: e.tensor_copy(
                    accfix[:, 3 * f:3 * f + 3], pp[1][:, 0:3]), inc="v")
                S.op("vector", lambda e, ps=ps, f=f: e.tensor_copy(
                    tail_sb[:, 3 * f:3 * f + 3], ps[:, T - 3:T]), inc="v")
                ps_free[buf] = S.cnt["v"]
                dst = xs_f if f < 32 else BC[:, (f - 32) * T:(f - 31) * T]

                def si(e, dst=dst, f=f):
                    return e.activation(dst, pp[1], AF.Silu, bias=cb_sb[:, f:f + 1])
                last_silu = S.op("scalar", si,
                                 waits=[("v", S.cnt["v"]), last_spill], inc="s")
                if f < 32:
                    last_spill = S.dma("scalar", "dxs",
                                       bass.AP(xs_dram, 128 * f,
                                               [[4096, 128], [524288, 8],
                                                [1, 128]]),
                                       xs_f, waits=[("s", last_silu)])
            else:
                def d1(e, ps=ps):
                    return e.tensor_scalar(acc[0:64, :], ps[0:64, :],
                                           small[0:64, 0:1], None, OP.add)
                S.op("vector", d1, waits=[("t", tcnt), ("s", last_silu)]
                     + p0a_dma, inc="v")
                ps_free[buf] = S.cnt["v"]
                S.op("scalar", lambda e: e.activation(
                    ar(A_XSF, T, 64), acc[0:64, :], AF.Abs),
                    waits=[("v", S.cnt["v"]), last_spill], inc="s")
                S.op("scalar", lambda e: e.activation(
                    ar(A_B1, T, 64), ar(A_XSF, T, 64), AF.Exp, scale=-1.0), inc="s")
                S.op("vector", lambda e: e.tensor_scalar_max(
                    ar(A_B2 + 2048, T, 64), acc[0:64, :], 0.0),
                    waits=S.at("s"), inc="v")
                S.op("vector", lambda e: e.tensor_scalar_add(
                    ar(A_B2, T, 64), ar(A_B1, T, 64), 1.0), inc="v")
                S.op("scalar", lambda e: e.activation(
                    ar(A_B1, T, 64), ar(A_B2, T, 64), AF.Ln),
                    waits=S.at("v"), inc="s")
                S.op("vector", lambda e: e.tensor_add(
                    dt_sb[:], ar(A_B2 + 2048, T, 64), ar(A_B1, T, 64)),
                    waits=S.at("s"), inc="v")
                dt_ready_v = S.cnt["v"]

        p1_dsc = ("dxs", S.cnt["dxs"])

        gsum_d = S.dma("gpsimd", "dg", gam_in[:, :], grow[0:1, 0:4], waits=S.at("v"))

        def cc_gam(e):
            return e.collective_compute(
                "AllReduce", OP.add, replica_groups=[list(range(8))],
                ins=[gam_in.ap().opt()], outs=[gam_out.ap().opt()])
        S.op("gpsimd", cc_gam, waits=[gsum_d], inc="cc")
        cc_gam_n = S.cnt["cc"]
        ldg = S.dma("sync", "db", grow[0:1, 4:8], gam_out[:, :], waits=[("cc", cc_gam_n)])
        S.op("vector", lambda e: e.tensor_scalar(
            grow[0:1, 0:4], grow[0:1, 4:8], 1.0 / 16777216.0, 1e-5, OP.mult, OP.max),
            waits=[ldg], inc="v")
        S.op("vector", lambda e: e.reciprocal(grow[0:1, 4:8], grow[0:1, 0:4]),
             inc="v")
        S.dma("scalar", "dbs", misc_dram[6:7, 0:4], grow[0:1, 0:4], waits=S.at("v"))
        gam_bd = S.dma("scalar", "dbs", misc_dram[5:6, 0:4], grow[0:1, 4:8],
                       waits=S.at("v"))
        ldrg = S.dma("sync", "db", rgc[:, 0:4],
                     bass.AP(misc_dram, 5 * 64, [[0, 128], [1, 4]]),
                     waits=[gam_bd])
        p0_consumed = [S.cnt["v"], S.cnt["v"]]
        mdone = ("dpm", 0)
        for i, (mat, src, dst, r, c) in enumerate(subs):
            bb = p0b[i % 2]
            ld = S.dma("sync", "dp0" if i % 2 == 0 else "dp1", bb,
                       src[128 * r:128 * (r + 1), 2048 * c:2048 * (c + 1)],
                       waits=[("v", p0_consumed[i % 2]), ldrg])
            S.op("vector", lambda e, bb=bb, mat=mat: e.tensor_scalar(
                p0t, bb, rgc[:, mat:mat + 1], MAGIC, OP.mult, OP.add),
                waits=[ld, mdone], inc="v")
            for hh in range(2):
                S.op("vector", lambda e, hh=hh: e.tensor_scalar(
                    ar(A_B2 + 3072, 1024), p0t[:, 1024 * hh:1024 * (hh + 1)],
                    MAGIC, -1.0, OP.subtract, OP.max), inc="v")
                S.op("vector", lambda e, hh=hh: e.tensor_scalar_min(
                    p0m_bf[:, 1024 * hh:1024 * (hh + 1)], ar(A_B2 + 3072, 1024),
                    1.0), inc="v")
            p0_consumed[i % 2] = S.cnt["v"]
            mdone = S.dma("scalar", "dpm",
                          dst[128 * r:128 * (r + 1), 2048 * c:2048 * (c + 1)],
                          p0m_bf[:, 0:2048], waits=S.at("v"))

        p0b_dma = [mdone, ("dp0", S.cnt["dp0"]), ("dp1", S.cnt["dp1"])]

        S.dma("gpsimd", "dg", tail_in[:, :], tail_sb[:], waits=S.at("v"))

        def cc_tail(e):
            return e.collective_compute(
                "AllGather", OP.bypass, replica_groups=[[0, 1, 2, 3], [4, 5, 6, 7]],
                ins=[tail_in.ap().opt()], outs=[tail_out.ap().opt()])
        S.op("gpsimd", cc_tail, waits=S.at("dg"), inc="cc")
        cc_tail_n = S.cnt["cc"]
        S.op("vector", lambda e: e.memset(tailprev[:], 0.0), inc="v")
        tpp = [tailprev, tail_sb]
        for j in range(4):
            ldb = S.dma("sync", "db", xsfix[:], tail_out[128 * j:128 * (j + 1), :],
                        waits=[("cc", cc_tail_n), ("v", S.cnt["v"])])

            def addb(e, j=j):
                return e.scalar_tensor_tensor(tpp[(j + 1) % 2][:], xsfix[:],
                                              tmask_sb[:, j:j + 1], tpp[j % 2][:],
                                              OP.mult, OP.add)
            S.op("vector", addb, waits=[ldb], inc="v")
        for f in range(FX):
            S.op("vector", lambda e, f=f: e.scalar_tensor_tensor(
                xsfix[:, 3 * f:3 * f + 3], tail_sb[:, 3 * f:3 * f + 3],
                cw_sb[:, 4 * f + 0:4 * f + 1], accfix[:, 3 * f:3 * f + 3],
                OP.mult, OP.add), inc="v")
            S.op("vector", lambda e, f=f: e.scalar_tensor_tensor(
                tailprev[:, 3 * f:3 * f + 2], tail_sb[:, 3 * f + 1:3 * f + 3],
                cw_sb[:, 4 * f + 1:4 * f + 2], xsfix[:, 3 * f:3 * f + 2],
                OP.mult, OP.add), inc="v")
            S.op("vector", lambda e, f=f: e.tensor_copy(
                tailprev[:, 3 * f + 2:3 * f + 3], xsfix[:, 3 * f + 2:3 * f + 3]),
                inc="v")
            S.op("vector", lambda e, f=f: e.scalar_tensor_tensor(
                xsfix[:, 3 * f:3 * f + 1], tail_sb[:, 3 * f + 2:3 * f + 3],
                cw_sb[:, 4 * f + 2:4 * f + 3], tailprev[:, 3 * f:3 * f + 1],
                OP.mult, OP.add), inc="v")
            S.op("vector", lambda e, f=f: e.tensor_copy(
                xsfix[:, 3 * f + 1:3 * f + 3], tailprev[:, 3 * f + 1:3 * f + 3]),
                inc="v")

            def sif(e, f=f):
                return e.activation(tail_sb[:, 3 * f:3 * f + 3],
                                    xsfix[:, 3 * f:3 * f + 3],
                                    AF.Silu, bias=cb_sb[:, f:f + 1])
            S.op("scalar", sif, waits=S.at("v"), inc="s")
        fix_s = S.cnt["s"]
        for f in range(32):
            S.dma("gpsimd", "dg",
                  bass.AP(xs_dram, 128 * f, [[4096, 128], [1, 3]]),
                  tail_sb[:, 3 * f:3 * f + 3],
                  waits=[("s", fix_s), p1_dsc])
        for f in (32, 33):
            S.op("vector", lambda e, f=f: e.tensor_copy(
                BC[:, (f - 32) * T:(f - 32) * T + 3], tail_sb[:, 3 * f:3 * f + 3]),
                waits=[("s", fix_s)], inc="v")
        patch_g = ("dg", S.cnt["dg"])
        patch_v = S.cnt["v"]

        btm = E(nc.sbuf_tensor("btm", [128, 128], F32))
        facr = E(nc.sbuf_tensor("facr", [128, 64], F32))
        asw = E(nc.sbuf_tensor("asw", [5, 64], F32))
        w5_sb = E(nc.sbuf_tensor("w5_sb", [5, 4], F32))
        alr = E(nc.sbuf_tensor("alr", [128, 64], F32))
        asum_sb = E(nc.sbuf_tensor("asum_sb", [64, 1], F32))

        def mkap(x, off, freedims, parts=None):
            b = x if isinstance(x, bass.AP) else x[:]
            pd = list(b.ap[0])
            if parts is not None:
                pd = [pd[0], parts]
            return bass.AP(b.tensor, b.offset + off, [pd] + [list(d) for d in freedims])

        for m in range(NCH):
            S.op("tensor", lambda e, m=m: e.transpose(
                pC[0:128, 64 * m:64 * (m + 1)], dt_sb[:, 128 * m:128 * (m + 1)],
                ident[0:64, 0:64]),
                waits=[("v", dt_ready_v), ("g", g_const)], inc="t")
        S.op("vector", lambda e: e.tensor_copy(dt_tm[:, 0:512], pC[:, 0:512]),
             waits=S.at("t"), inc="v")
        S.op("vector", lambda e: e.tensor_tensor(
            adt_tm[:, 0:512], dt_tm[:, 0:512],
            mkap(arep, 0, [[0, NCH], [1, 64]]), OP.mult),
            waits=S.at("dc"), inc="v")
        adt_v = S.cnt["v"]
        for m in range(NCH):
            S.op("tensor", lambda e, m=m: e.matmul(
                pD[0:64, 128 * m:128 * (m + 1)], adt_tm[:, 64 * m:64 * (m + 1)],
                u01[:], start=True, stop=True), waits=[("v", adt_v)], inc="t")
            S.op("tensor", lambda e, m=m: e.matmul(
                pC[0:128, 512 + 64 * m:512 + 64 * (m + 1)], u01[:],
                adt_tm[:, 64 * m:64 * (m + 1)], start=True, stop=True), inc="t")
        acs_t = S.cnt["t"]
        S.op("vector", lambda e: e.tensor_copy(acsT[:, :], pD[0:64, 0:1024]),
             waits=[("t", acs_t)], inc="v")
        S.op("vector", lambda e: e.tensor_copy(acs_col[:, :], pC[:, 512:1024]),
             inc="v")
        acs_v = S.cnt["v"]
        acsT_d = S.dma("scalar", "dbs",
                       bass.AP(a_dram, 0, [[128, 64], [8192, 8], [1, 128]]),
                       acsT[:, :], waits=[("v", acs_v)])
        for cb2 in range(2):
            S.op("tensor", lambda e, cb2=cb2: e.matmul(
                pD[0:64, 512 * cb2:512 * (cb2 + 1)], pm_sb[:],
                acsT[:, 512 * cb2:512 * (cb2 + 1)], start=True, stop=True),
                waits=([("v", acs_v), dc_all] if cb2 == 0 else ()), inc="t")
        S.op("vector", lambda e: e.tensor_copy(ap64[:, :], pD[0:64, 0:1024]),
             waits=S.at("t"), inc="v")
        a4_d = S.dma("scalar", "dbs",
                     bass.AP(a4_dram, 0, [[128, 64], [8192, 8], [1, 128]]),
                     ap64[:, :], waits=[("v", S.cnt["v"])])
        alast_d = S.dma("scalar", "dbs", alast_dram[0:1, 0:512], acs_col[127:128, :],
                        waits=[("v", acs_v)])
        S.op("vector", lambda e: e.reduce_sum(
            asum_sb[:, 0:1], mkap(acsT, 127, [[128, NCH]], parts=64),
            axis=AX.X), waits=[("v", acs_v)], inc="v")
        asum_v = S.cnt["v"]
        for m in range(NCH):
            lda = S.dma("sync", "db", alr[:, 0:64],
                        bass.AP(alast_dram, 64 * m, [[0, 128], [1, 64]]),
                        waits=[alast_d, ("v", S.cnt["v"]),
                               ("s", S.cnt["s"])])
            S.op("vector", lambda e, m=m: e.tensor_sub(
                adt_tm[:, 64 * m:64 * (m + 1)], alr[:, 0:64],
                acs_col[:, 64 * m:64 * (m + 1)]), waits=[lda], inc="v")
            S.op("scalar", lambda e, m=m: e.activation(
                dch_rep[:, 64 * m:64 * (m + 1)], alr[:, 0:64], AF.Exp),
                waits=[lda], inc="s")
        S.op("scalar", lambda e: e.activation(
            ds_all[:, :], adt_tm[:, :], AF.Exp), waits=S.at("v"), inc="s")
        apath_s = S.cnt["s"]

        xs_ch = ar(A_B1, DI)
        xd = ar(A_B2, DI)
        y_sb = ar(A_B4, DI)
        s_run = ar(A_B5, DI)
        repA = ar(A_B3, 2048)
        repB = ar(A_B3 + 2048, 2048)
        S.op("vector", lambda e: e.memset(s_run, 0.0), inc="v")

        st_spill = [("dss", 0)]
        p2_y_d = []
        prev_t = 0
        for m in range(NCH):
            xw = [p1_dsc, ("t", prev_t), ("v", S.cnt["v"])]
            if m == 0:
                xw.append(patch_g)
                xw += p0b_dma
            ldx = S.dma("sync", "dx", xs_ch,
                        bass.AP(xs_dram, 524288 * m, [[4096, 128], [1, 4096]]),
                        waits=xw)
            BTm = BC[:, 128 * m:128 * (m + 1)]
            CTm = BC[:, T + 128 * m:T + 128 * (m + 1)]
            bcw = [("v", patch_v)] if m == 0 else []
            gt = S.op("tensor", lambda e, BTm=BTm, CTm=CTm: e.matmul(
                pD[:, 0:128], BTm, CTm, start=True, stop=True),
                waits=bcw + [("v", S.cnt["v"])], inc="t")
            bt = S.op("tensor", lambda e, BTm=BTm: e.transpose(
                pD[:, 128:256], BTm, ident[:]), inc="t")
            S.op("vector", lambda e: e.tensor_tensor(gu[:], pD[:, 0:128], u01[:],
                                                     OP.mult),
                 waits=[("t", bt)], inc="v")
            S.op("vector", lambda e: e.tensor_copy(btm[:], pD[:, 128:256]), inc="v")

            for hf in range(2):
                tws = [ldx, ("v", S.cnt["v"])]
                for i in range(16):
                    ps = pA if i < 8 else pB
                    S.op("tensor", lambda e, i=i, hf=hf, ps=ps: e.transpose(
                        ps[:, 128 * (i % 8):128 * (i % 8 + 1)],
                        xs_ch[:, 2048 * hf + 128 * i: 2048 * hf + 128 * (i + 1)],
                        ident[:]), waits=tws if i == 0 else (), inc="t")
                txd = S.cnt["t"]
                for h2 in range(2):
                    ps = [pA, pB][h2]

                    def evx(e, ps=ps, hf=hf, h2=h2, m=m):
                        in1 = mkap(dt_tm, 64 * m + 32 * hf + 16 * h2,
                                   [[2, 8], [1, 2], [0, 64]])
                        return e.tensor_tensor(
                            xd[:, 2048 * hf + 1024 * h2:2048 * hf + 1024 * (h2 + 1)],
                            ps[:], in1, OP.mult)
                    S.op("vector", evx, waits=[("t", txd)], inc="v")
                xd_v = S.cnt["v"]

                for qr in range(2):
                    hq = 32 * hf + 16 * qr
                    ldr = S.dma("sync", "dr", repA,
                                bass.AP(a_dram, 8192 * m + 128 * hq,
                                        [[0, 128], [1, 2048]]),
                                waits=[acsT_d, st_spill[-1],
                                       ("t", S.cnt["t"]), ("v", S.cnt["v"])])

                    def mtsub(e, hq=hq, m=m):
                        in1 = mkap(acs_col, 64 * m + hq, [[1, 16], [0, 128]])
                        return e.tensor_tensor(repB, repA, in1, OP.subtract)
                    S.op("vector", mtsub,
                         waits=[ldr, ("v", acs_v), st_spill[-1]], inc="v")
                    S.op("vector", lambda e: e.tensor_scalar_min(repA, repB, 0.0),
                         inc="v")
                    S.op("scalar", lambda e: e.activation(repB, repA, AF.Exp),
                         waits=S.at("v"), inc="s")

                    def mtgu(e):
                        in1 = mkap(gu, 0, [[0, 16], [1, 128]])
                        return e.tensor_tensor(repA, repB, in1, OP.mult)
                    S.op("vector", mtgu, waits=S.at("s"), inc="v")
                    mt_v = S.cnt["v"]

                    psq = [pC, pD][qr]
                    for hh in range(16):
                        h = hq + hh
                        slot = hh // 2
                        base = 64 * (h % 2)

                        def ymm(e, h=h, hh=hh, psq=psq, slot=slot, base=base):
                            return e.matmul(
                                psq[base:base + 64, 128 * slot:128 * (slot + 1)],
                                xd[:, 64 * h:64 * (h + 1)],
                                repA[:, 128 * hh:128 * (hh + 1)],
                                start=True, stop=True)
                        S.op("tensor", ymm,
                             waits=[("v", mt_v)] if hh == 0 else (), inc="t")
                ydiag_t = S.cnt["t"]

                def dtmp(e, hf=hf):
                    in1 = mkap(dcol, 16 * hf, [[1, 16], [0, 128]])
                    return e.tensor_tensor(repB, xs_ch[:, 2048 * hf:2048 * (hf + 1)],
                                           in1, OP.mult)
                S.op("vector", dtmp, waits=[("t", ydiag_t),
                                            ("dys", S.cnt["dys"])], inc="v")
                for h2 in range(2):
                    ps = [pC, pD][h2]

                    def yev(e, ps=ps, hf=hf, h2=h2):
                        return e.tensor_tensor(
                            y_sb[:, 2048 * hf + 1024 * h2:2048 * hf + 1024 * (h2 + 1)],
                            repB[:, 1024 * h2:1024 * (h2 + 1)], ps[:], OP.add)
                    S.op("vector", yev, inc="v")
                yv = S.cnt["v"]
                p2_y_d.append(S.dma(
                    "scalar", "dys",
                    bass.AP(y_dram, 524288 * m + 2048 * hf,
                            [[4096, 128], [1, 2048]]),
                    y_sb[:, 2048 * hf:2048 * (hf + 1)], waits=[("v", yv)]))

                def xd2f(e, hf=hf, m=m):
                    in1 = mkap(ds_all, 64 * m + 32 * hf, [[1, 32], [0, 64]])
                    return e.tensor_tensor(repB, xd[:, 2048 * hf:2048 * (hf + 1)],
                                           in1, OP.mult)
                S.op("vector", xd2f, waits=[("s", apath_s)], inc="v")
                xd2_v = S.cnt["v"]
                for i in range(4):
                    ps = [pA, pB][i // 2]

                    def smm(e, i=i, ps=ps):
                        return e.matmul(ps[:, 512 * (i % 2):512 * (i % 2 + 1)],
                                        btm[:], repB[:, 512 * i:512 * (i + 1)],
                                        start=True, stop=True)
                    S.op("tensor", smm,
                         waits=[("v", xd2_v)] if i == 0 else (), inc="t")
                prev_t = S.cnt["t"]
                for h2 in range(2):
                    ps = [pA, pB][h2]
                    S.op("vector", lambda e, ps=ps, h2=h2: e.tensor_copy(
                        repB[:, 1024 * h2:1024 * (h2 + 1)], ps[:]),
                        waits=[("t", prev_t)], inc="v")
                stv = S.cnt["v"]
                st_spill.append(S.dma(
                    "scalar", "dss", st_dram[m, :, 2048 * hf:2048 * (hf + 1)], repB,
                    waits=[("v", stv)]))
                tmpv = xd[:, 2048 * hf:2048 * (hf + 1)]

                def sdec(e, hf=hf, m=m, tmpv=tmpv):
                    in1 = mkap(dch_rep, 64 * m + 32 * hf, [[1, 32], [0, 64]])
                    return e.tensor_tensor(tmpv, s_run[:, 2048 * hf:2048 * (hf + 1)],
                                           in1, OP.mult)
                S.op("vector", sdec, waits=[("s", apath_s)], inc="v")
                S.op("vector", lambda e, hf=hf, tmpv=tmpv: e.tensor_add(
                    s_run[:, 2048 * hf:2048 * (hf + 1)], tmpv, repB), inc="v")

        sloc_d = S.dma("gpsimd", "dg", sag_in[:, :], s_run, waits=S.at("v"))
        asum_d = S.dma("gpsimd", "dg", asum_in[:, :], asum_sb[:, 0:1],
                       waits=[("v", asum_v)])

        def cc_st(e):
            return e.collective_compute(
                "AllGather", OP.bypass, replica_groups=[[0, 1, 2, 3], [4, 5, 6, 7]],
                ins=[sag_in.ap().opt()], outs=[sag_out.ap().opt()])
        S.op("gpsimd", cc_st, waits=[sloc_d], inc="cc")

        def cc_as(e):
            return e.collective_compute(
                "AllGather", OP.bypass, replica_groups=[[0, 1, 2, 3], [4, 5, 6, 7]],
                ins=[asum_in.ap().opt()], outs=[asum_out.ap().opt()])
        S.op("gpsimd", cc_as, waits=[asum_d], inc="cc")
        cc_state_n = S.cnt["cc"]
        def cc_ag(name_in, name_out):
            def f(e, name_in=name_in, name_out=name_out):
                return e.collective_compute(
                    "AllGather", OP.bypass, replica_groups=[list(range(8))],
                    ins=[name_in.ap().opt()], outs=[name_out.ap().opt()])
            return f
        S.op("gpsimd", cc_ag(mg_in, mg_out), waits=[mdone], inc="cc")
        S.op("gpsimd", cc_ag(mv_in, mv_out), inc="cc")
        S.op("gpsimd", cc_ag(mo_in, mo_out), inc="cc")
        cc_m_n = S.cnt["cc"]
        ld1r = S.dma("sync", "db", asw[4:5, :], ones_sb[0:1, 0:64],
                     waits=S.at("v"))
        ld5 = S.dma("sync", "db", asw[0:4, :],
                    bass.AP(asum_out, 0, [[64, 4], [1, 64]]),
                    waits=[("cc", cc_state_n)])
        ldw5 = S.dma("sync", "db", w5_sb[:, :], state_w5[:, :])
        fac_t = S.op("tensor", lambda e: e.matmul(
            pD[0:4, 0:64], w5_sb[:, :], asw[:, :], start=True, stop=True),
            waits=[ld5, ldw5, ld1r, ("v", S.cnt["v"])], inc="t")
        S.op("scalar", lambda e: e.activation(facr[0:4, 0:64], pD[0:4, 0:64],
                                              AF.Exp),
             waits=[("t", fac_t)], inc="s")
        facd = S.dma("scalar", "dbs", misc_dram[1:5, 0:64], facr[0:4, 0:64],
                     waits=S.at("s"))
        spp = [s_run, ar(A_B4, DI)]
        for j in range(4):
            ldb = S.dma("sync", "dx", xs_ch, sag_out[128 * j:128 * (j + 1), :],
                        waits=[("cc", cc_state_n), ("t", S.cnt["t"]),
                               ("v", S.cnt["v"]), sloc_d])
            ldf = S.dma("sync", "db", facr[:, 0:64],
                        bass.AP(misc_dram, 64 * (1 + j), [[0, 128], [1, 64]]),
                        waits=[facd, ("v", S.cnt["v"])])
            if j == 0:
                def sacc0(e):
                    in1 = mkap(facr, 0, [[1, 64], [0, 64]])
                    return e.tensor_tensor(spp[0], xs_ch, in1, OP.mult)
                S.op("vector", sacc0,
                     waits=[ldb, ldf, ("dys", S.cnt["dys"])], inc="v")
            else:
                def sacc(e):
                    in1 = mkap(facr, 0, [[1, 64], [0, 64]])
                    return e.tensor_tensor(xd, xs_ch, in1, OP.mult)
                S.op("vector", sacc, waits=[ldb, ldf], inc="v")
                S.op("vector", lambda e, j=j: e.tensor_add(
                    spp[j % 2], spp[(j + 1) % 2], xd), inc="v")
        S.op("vector", lambda e: e.tensor_copy(s_run, spp[1]), inc="v")
        sin_v = S.cnt["v"]

        hbuf = lambda base, p: ar(base + 2048 * p, 2048)
        ACX = ar(A_ACC, 2048)
        p4_y_d = [("dys", S.cnt["dys"])]
        dss_all = ("dss", S.cnt["dss"])
        yfree = [S.cnt["v"], S.cnt["v"]]
        stfree = [S.cnt["v"], S.cnt["v"]]
        erfree = [S.cnt["v"], S.cnt["v"]]
        erspill = [("dys", 0), ("dys", 0)]
        eefree = [S.cnt["v"], S.cnt["v"]]
        supd = [sin_v, sin_v]
        psum_free = [S.cnt["v"], S.cnt["v"]]
        for m in range(NCH):
            for hf in range(2):
                hc = 2 * m + hf
                p = hc % 2
                y_p, st_p = hbuf(A_B1, p), hbuf(A_B2, p)
                er_p, ee_p = hbuf(A_B3, p), hbuf(A_B4, p)
                pp0, pp1 = ([pA, pB], [pC, pD])[p]
                ldy = S.dma("sync", "dy0" if p == 0 else "dy1", y_p,
                            bass.AP(y_dram, 524288 * m + 2048 * hf,
                                    [[4096, 128], [1, 2048]]),
                            waits=[("dys", S.cnt["dys"]), ("v", yfree[p])])
                lds = S.dma("sync", "dm0" if p == 0 else "dm1", st_p,
                            st_dram[m, :, 2048 * hf:2048 * (hf + 1)],
                            waits=[dss_all, ("v", stfree[p])])
                for ph in range(2):
                    lde = S.dma("sync", "dr" if p == 0 else "drv",
                                AR[64 * ph:64 * (ph + 1),
                                   A_B3 + 2048 * p:A_B3 + 2048 * (p + 1)],
                                bass.AP(a4_dram,
                                        8192 * m + (32 * ph + 16 * hf) * 128,
                                        [[0, 64], [1, 2048]]),
                                waits=[a4_d, ("v", erfree[p]),
                                       erspill[p]])
                exp_s = S.op("scalar", lambda e, er_p=er_p, ee_p=ee_p: e.activation(
                    ee_p, er_p, AF.Exp),
                    waits=[lde, ("v", eefree[p])], inc="s")
                CTm = BC[:, T + 128 * m:T + 128 * (m + 1)]
                for j2r in range(16):
                    j2 = 16 * hf + j2r
                    ps = [pp0, pp1][j2r // 8]

                    def omm(e, j2=j2, j2r=j2r, ps=ps, CTm=CTm):
                        return e.matmul(ps[:, 128 * (j2r % 8):128 * (j2r % 8 + 1)],
                                        s_run[:, 128 * j2:128 * (j2 + 1)], CTm,
                                        start=True, stop=True)
                    S.op("tensor", omm,
                         waits=[("v", max(supd[hf], psum_free[p]))]
                         if j2r == 0 else (), inc="t")
                yoff_t = S.cnt["t"]
                for q2 in range(2):
                    ps = [pp0, pp1][q2]

                    def oev(e, q2=q2, ps=ps, ee_p=ee_p):
                        return e.tensor_tensor(
                            ACX[:, 1024 * q2:1024 * (q2 + 1)], ps[:],
                            ee_p[:, 1024 * q2:1024 * (q2 + 1)], OP.mult)
                    S.op("vector", oev,
                         waits=[("t", yoff_t), ("s", exp_s), ldy], inc="v")
                psum_free[p] = S.cnt["v"]
                S.op("vector", lambda e, er_p=er_p, y_p=y_p: e.tensor_add(
                    er_p, y_p, ACX), inc="v")
                yfree[p] = S.cnt["v"]
                erfree[p] = S.cnt["v"]
                erspill[p] = S.dma(
                    "scalar", "dys",
                    bass.AP(y_dram, 524288 * m + 2048 * hf,
                            [[4096, 128], [1, 2048]]),
                    er_p, waits=[("v", S.cnt["v"])])
                p4_y_d.append(erspill[p])

                def sup1(e, m=m, hf=hf, ee_p=ee_p):
                    in1 = mkap(dch_rep, 64 * m + 32 * hf, [[1, 32], [0, 64]])
                    return e.tensor_tensor(
                        ee_p, s_run[:, 2048 * hf:2048 * (hf + 1)], in1, OP.mult)
                S.op("vector", sup1, waits=[("t", yoff_t)], inc="v")

                def sup2(e, hf=hf, ee_p=ee_p, st_p=st_p):
                    return e.tensor_add(s_run[:, 2048 * hf:2048 * (hf + 1)],
                                        ee_p, st_p)
                S.op("vector", sup2, waits=[lds], inc="v")
                supd[hf] = S.cnt["v"]
                stfree[p] = S.cnt["v"]
                eefree[p] = S.cnt["v"]

        wq_free = [S.cnt["t"], S.cnt["t"]]
        ps_sfree = [S.cnt["s"], S.cnt["s"]]
        yf_dp = [("dyf", 0), ("dyf1", 0)]
        ydp_names = ["dy0", "dy1"]
        tt_p = [0, 0]
        sq_p = [0, 0]
        pend_sq = []
        for j in range(DIT):
            buf = j % 2
            wsl = wq[buf]
            gsil = ar(A_B3 + 1024 * buf, 1024)
            yj = ar(A_B3 + 2048 + 1024 * buf, 1024)
            yfb = ar(A_B2 + 1024 * buf, 1024)
            ldw = S.dma("sync", "dw0" if buf == 0 else "dw1", wsl[:],
                        w_gate[j, :, :], waits=[("t", wq_free[buf])])
            ps = [pA, pB][buf]

            def gmm(e, ps=ps, wsl=wsl):
                last = None
                for cb in range(2):
                    for k in range(KD):
                        last = e.matmul(
                            ps[:, cb * 512:(cb + 1) * 512],
                            wsl[:, k * 128:(k + 1) * 128],
                            xT_sb[:, k * T + cb * 512: k * T + (cb + 1) * 512],
                            start=(k == 0), stop=(k == KD - 1))
                return last
            gw_ = [ldw, ("s", ps_sfree[buf])]
            if j < 2:
                gw_.append(("v", S.cnt["v"]))
            tcnt = S.op("tensor", gmm, waits=gw_, inc="t")
            wq_free[buf] = tcnt
            while len(pend_sq) >= 3:
                q = pend_sq.pop(0)
                S.op("tensor", q[0], waits=[("s", q[1])], inc="t")
                S.op("tensor", q[2], inc="t")
            ldy = S.dma("sync", ydp_names[buf], yj,
                        bass.AP(y_dram, 128 * j,
                                [[4096, 128], [524288, 8], [1, 128]]),
                        waits=[p4_y_d[-1], ("v", tt_p[buf])])
            scnt = S.op("scalar", lambda e, ps=ps, gsil=gsil: e.activation(
                gsil, ps[:], AF.Silu),
                waits=[("t", tcnt), ("v", tt_p[buf])], inc="s")
            ps_sfree[buf] = scnt
            tt_p[buf] = S.op(
                "vector", lambda e, yfb=yfb, yj=yj, gsil=gsil: e.tensor_tensor(
                    yfb, yj, gsil, OP.mult),
                waits=[("s", max(scnt, sq_p[buf])), ldy, yf_dp[buf]], inc="v")
            yf_dp[buf] = S.dma("scalar", "dyf" if buf == 0 else "dyf1",
                               yf_dram[128 * j:128 * (j + 1), :], yfb,
                               waits=[("v", tt_p[buf])])
            sqb = ar(A_B4 + 1024 * (j % 4), 1024)
            sq_p[buf] = S.op("scalar", lambda e, yfb=yfb, sqb=sqb: e.activation(
                sqb, yfb, AF.Square),
                waits=[("v", tt_p[buf]), ("t", S.cnt["t"])], inc="s")

            def mk_sq(cb, j, sqb):
                def f(e, cb=cb, j=j, sqb=sqb):
                    return e.matmul(
                        pC[0:1, cb * 512:(cb + 1) * 512], ones_sb[:, 0:1],
                        sqb[:, cb * 512:(cb + 1) * 512],
                        start=(j == 0), stop=(j == DIT - 1))
                return f
            pend_sq.append((mk_sq(0, j, sqb), sq_p[buf], mk_sq(1, j, sqb)))
        for q in pend_sq:
            S.op("tensor", q[0], waits=[("s", q[1])], inc="t")
            S.op("tensor", q[2], inc="t")
        pend_sq = []
        ssq_t = S.cnt["t"]
        r_row = ar(A_B1, 1024, 1)
        r_row2 = ar(A_B1 + 1024, 1024, 1)
        S.op("vector", lambda e: e.tensor_scalar(
            r_row, pC[0:1, 0:1024], 1.0 / DI, EPS, OP.mult, OP.add),
            waits=[("t", ssq_t)], inc="v")
        S.op("vector", lambda e: e.reciprocal(r_row2, r_row), inc="v")
        S.op("scalar", lambda e: e.activation(r_row, r_row2, AF.Sqrt),
             waits=S.at("v"), inc="s")
        rbounce = S.dma("scalar", "dbs", ln_dram[0:1, :], r_row, waits=S.at("s"))
        r_bc = ar(A_B5, 1024)
        ldrbc = S.dma("sync", "db", r_bc, bass.AP(ln_dram, 0, [[0, 128], [1, T]]),
                      waits=[rbounce])

        yn_sb = ar(A_B1, 4 * DI)
        p6_done_v = S.cnt["v"]
        for cb in range(2):
            yn_v = 0
            for k in range(DIT):
                tb = ar(A_B5 + 1024 + 512 * (k % 2), 512)
                ldyf = S.dma("sync", "dy0" if k % 2 == 0 else "dy1", tb,
                             bass.AP(yf_dram, 128 * k * T + 512 * cb,
                                     [[T, 128], [1, 512]]),
                             waits=[yf_dp[0], yf_dp[1], ("v", S.cnt["v"])])

                def ynf(e, k=k, cb=cb, tb=tb):
                    return e.tensor_scalar(
                        yn_sb[:, 512 * k:512 * (k + 1)], tb,
                        nw_col[:, k:k + 1], None, OP.mult)
                yn_v = S.op("vector", ynf, waits=[ldyf], inc="v")
            for oh in range(2):
                for k in range(DIT):
                    buf = k % 2
                    wsl = wq[buf]
                    ldw = S.dma("sync", "dw0" if buf == 0 else "dw1",
                                wsl[:, 0:1024],
                                w_out[128 * k:128 * (k + 1), 1024 * oh:1024 * (oh + 1)],
                                waits=[("t", wq_free[buf])])

                    def pmm(e, k=k, wsl=wsl):
                        last = None
                        for o in range(8):
                            ps = [pA, pB, pC, pD][o // 2]
                            last = e.matmul(
                                ps[:, 512 * (o % 2):512 * (o % 2 + 1)],
                                wsl[:, 128 * o:128 * (o + 1)],
                                yn_sb[:, 512 * k:512 * (k + 1)],
                                start=(k == 0), stop=(k == DIT - 1))
                        return last
                    wq_free[buf] = S.op(
                        "tensor", pmm,
                        waits=[ldw, ("v", max(yn_v, p6_done_v) if k == 0 else 0)],
                        inc="t")
                op_t = S.cnt["t"]
                for o in range(8):
                    ps = [pA, pB, pC, pD][o // 2]
                    og = 8 * oh + o

                    def res_r(e, ps=ps, o=o, cb=cb):
                        return e.tensor_tensor(
                            ar(A_B5 + 2048, 512),
                            ps[:, 512 * (o % 2):512 * (o % 2 + 1)],
                            r_bc[:, 512 * cb:512 * (cb + 1)], OP.mult)
                    S.op("vector", res_r, waits=[("t", op_t), ldrbc], inc="v")

                    def res_f(e, og=og, cb=cb):
                        xsl = xT_sb[:, og * T + 512 * cb: og * T + 512 * (cb + 1)]
                        return e.tensor_tensor(
                            ar(A_B5 + 2560, 512), xsl, ar(A_B5 + 2048, 512),
                            OP.add)
                    S.op("vector", res_f, inc="v")

                    def res_c(e, og=og, cb=cb):
                        xsl = xT_sb[:, og * T + 512 * cb: og * T + 512 * (cb + 1)]
                        return e.tensor_copy(xsl, ar(A_B5 + 2560, 512))
                    S.op("vector", res_c, inc="v")
                p6_done_v = S.cnt["v"]
        sqt = ar(A_B5 + 1024, 512)
        for cb in range(2):
            for k in range(KD):
                sq_s = S.op("scalar", lambda e, k=k, cb=cb: e.activation(
                    sqt, xT_sb[:, k * T + 512 * cb: k * T + 512 * (cb + 1)],
                    AF.Square), waits=[("v", p6_done_v), ("t", S.cnt["t"])], inc="s")
                S.op("tensor", lambda e, k=k, cb=cb: e.matmul(
                    pA[0:1, 512 * cb:512 * (cb + 1)], ones_sb[:, 0:1],
                    xT_sb[:, k * T + 512 * cb: k * T + 512 * (cb + 1)],
                    start=(k == 0), stop=(k == KD - 1)),
                    waits=[("v", p6_done_v)], inc="t")
                S.op("tensor", lambda e, k=k, cb=cb: e.matmul(
                    pB[0:1, 512 * cb:512 * (cb + 1)], ones_sb[:, 0:1], sqt,
                    start=(k == 0), stop=(k == KD - 1)),
                    waits=[("s", sq_s)], inc="t")
        ln_t = S.cnt["t"]
        mu_row = ar(A_B1, 1024, 1)
        r1_row = ar(A_B1 + 1024, 1024, 1)
        t_row = ar(A_B1 + 2048, 1024, 1)
        t_row2 = ar(A_B1 + 3072, 1024, 1)
        S.op("vector", lambda e: e.tensor_scalar_mul(mu_row, pA[0:1, 0:1024],
                                                     1.0 / DM),
             waits=[("t", ln_t)], inc="v")
        S.op("vector", lambda e: e.tensor_scalar_mul(t_row, pB[0:1, 0:1024],
                                                     1.0 / DM), inc="v")
        S.op("vector", lambda e: e.tensor_tensor(r1_row, mu_row, mu_row, OP.mult),
             inc="v")
        S.op("vector", lambda e: e.tensor_sub(t_row2, t_row, r1_row), inc="v")
        S.op("vector", lambda e: e.tensor_scalar_add(t_row, t_row2, EPS), inc="v")
        S.op("vector", lambda e: e.reciprocal(r1_row, t_row), inc="v")
        S.op("scalar", lambda e: e.activation(t_row, r1_row, AF.Sqrt),
             waits=S.at("v"), inc="s")
        mu_d = S.dma("scalar", "dbs", ln_dram[1:2, :], mu_row, waits=S.at("v", "s"))
        rr_d = S.dma("scalar", "dbs", ln_dram[2:3, :], t_row, waits=S.at("s"))
        mu_bc = ar(A_B2, 1024)
        ldmu = S.dma("sync", "db", mu_bc, bass.AP(ln_dram, T, [[0, 128], [1, T]]),
                     waits=[mu_d, ("v", S.cnt["v"])])
        ldrr = S.dma("sync", "db", r_bc, bass.AP(ln_dram, 2 * T, [[0, 128], [1, T]]),
                     waits=[rr_d])
        ln1_v = 0
        for k in range(KD):
            def c1(e, k=k):
                xsl = xT_sb[:, k * T:(k + 1) * T]
                return e.tensor_tensor(ar(A_B1, T), xsl, mu_bc, OP.subtract)
            S.op("vector", c1, waits=[ldmu, ldrr], inc="v")

            def c2(e, k=k):
                xsl = xT_sb[:, k * T:(k + 1) * T]
                return e.tensor_tensor(xsl, ar(A_B1, T), r_bc, OP.mult)
            ln1_v = S.op("vector", c2, inc="v")

        xi_bf = AR[0:128, A_B1:A_B1 + 8192].bitcast(BF16)
        sg_bc = ar(A_B3, 1024)
        sv_bc = ar(A_B3 + 1024, 1024)
        qt1 = ar(A_B3 + 2048, 1024)
        qt2 = ar(A_B3 + 3072, 1024)
        sxq_bc = ar(A_B4, 1024)
        gm = ar(A_B4, 1024)
        grow_g = ar(A_B4 + 1024, 1024, 1)
        grow_v2 = ar(A_B4 + 2048, 1024, 1)
        gcol = ar(A_B4 + 3072, 8)
        gq = ar(A_B4 + 3080, 8)
        gr = ar(A_B4 + 3088, 8)
        ggam = ar(A_B4 + 3096, 4)

        maxt = ar(A_B2, 1024)
        absx = ar(A_B2 + 1024, 1024)
        S.op("scalar", lambda e: e.activation(maxt, xT_sb[:, 0:T], AF.Abs),
             waits=[("v", ln1_v), ("t", S.cnt["t"])], inc="s")
        for k in range(1, KD):
            ab_s = S.op("scalar", lambda e, k=k: e.activation(
                absx, xT_sb[:, k * T:(k + 1) * T], AF.Abs),
                waits=[("v", S.cnt["v"])], inc="s")
            S.op("vector", lambda e: e.tensor_tensor(maxt, maxt, absx, OP.max),
                 waits=[("s", ab_s)], inc="v")
        gx_v = S.cnt["v"]

        def gamma_pass(src, extra_tw=()):
            for c8 in range(8):
                S.op("tensor", lambda e, c8=c8, src=src: e.transpose(
                    pA[:, 128 * c8:128 * c8 + 128],
                    src[:, 128 * c8:128 * (c8 + 1)], ident[:]),
                    waits=([("v", S.cnt["v"])] + list(extra_tw)) if c8 == 0 else (),
                    inc="t")
            tt = S.cnt["t"]
            for c8 in range(8):
                S.op("vector", lambda e, c8=c8: e.reduce_max(
                    gcol[:, c8:c8 + 1], pA[:, 128 * c8:128 * (c8 + 1)], axis=AX.X),
                    waits=[("t", tt)] if c8 == 0 else (), inc="v")
            S.op("vector", lambda e: e.tensor_scalar_max(gcol, gcol, 1e-5), inc="v")
            S.op("vector", lambda e: e.reciprocal(gq, gcol), inc="v")
            S.op("vector", lambda e: e.tensor_scalar_mul(gr, gq, 127.0), inc="v")
            vq = S.cnt["v"]
            for c8 in range(8):
                S.dma("scalar", "dbs", ln_dram[3:4, 128 * c8:128 * (c8 + 1)],
                      gr[:, c8:c8 + 1], waits=[("v", vq)])
                dd = S.dma("scalar", "dbs", ln_dram[4:5, 128 * c8:128 * (c8 + 1)],
                           gcol[:, c8:c8 + 1], waits=[("v", vq)])
            return vq, dd

        gx2_v, gq_d = gamma_pass(maxt)
        ldsx = S.dma("sync", "db", sxq_bc, bass.AP(ln_dram, 3 * T, [[0, 128], [1, T]]),
                     waits=[gq_d, ("v", gx2_v)])
        ldgw = S.dma("sync", "db", ggam, bass.AP(misc_dram, 6 * 64, [[0, 128], [1, 4]]),
                     waits=[gam_bd, ("v", gx2_v)])
        ldg4 = S.dma("sync", "db", grow_g, bass.AP(ln_dram, 4 * T, [[1, 1], [1, T]]),
                     waits=[gq_d, ("v", gx2_v)])
        S.op("vector", lambda e: e.tensor_scalar(
            grow_v2, grow_g, ggam[0:1, 0:1], 1.0 / 127.0, OP.mult, OP.mult),
            waits=[ldgw, ldg4], inc="v")
        sgq_d = S.dma("scalar", "dbs", ln_dram[5:6, :], grow_v2, waits=S.at("v"))
        S.op("vector", lambda e: e.tensor_scalar(
            grow_v2, grow_g, ggam[0:1, 1:2], 1.0 / 127.0, OP.mult, OP.mult),
            waits=[sgq_d], inc="v")
        sgq_d = S.dma("scalar", "dbs", ln_dram[6:7, :], grow_v2, waits=S.at("v"))
        ldsg = S.dma("sync", "db", sg_bc, bass.AP(ln_dram, 5 * T, [[0, 128], [1, T]]),
                     waits=[sgq_d])
        ldsv = S.dma("sync", "db", sv_bc, bass.AP(ln_dram, 6 * T, [[0, 128], [1, T]]),
                     waits=[sgq_d])
        xi_v = 0
        for k in range(KD):
            S.op("vector", lambda e, k=k: e.tensor_tensor(
                qt1, xT_sb[:, k * T:(k + 1) * T], sxq_bc, OP.mult),
                waits=[ldsx] if k == 0 else (), inc="v")
            S.op("vector", lambda e: e.tensor_scalar(
                qt2, qt1, MAGIC, -MAGIC, OP.add, OP.add), inc="v")
            S.op("vector", lambda e: e.tensor_scalar_max(qt1, qt2, -128.0), inc="v")
            xi_v = S.op("vector", lambda e, k=k: e.tensor_scalar_min(
                xi_bf[:, k * T:(k + 1) * T], qt1, 127.0), inc="v")

        bandg = ar(A_B5, 2048).bitcast(BF16)
        bandv = ar(A_B5 + 2048, 2048).bitcast(BF16)
        gmp = [gm, ar(A_B4 + 1024, 1024)]
        S.op("vector", lambda e: e.memset(gm, 0.0), waits=[("v", xi_v)], inc="v")
        S.op("vector", lambda e: e.memset(gmp[1], 0.0), inc="v")
        gm0_v = S.cnt["v"]
        ngm = [0, 0]
        mid_d = ("dmid", 0)
        bg_free = S.cnt["t"]
        bv_free = S.cnt["t"]
        for fb in range(32):
            bw = [("cc", cc_m_n), ("t", bg_free)]
            if fb == 0:
                bw += [("dy0", S.cnt["dy0"]), ("dy1", S.cnt["dy1"]),
                       ("db", S.cnt["db"]), ("dy", S.cnt["dy"])]
            ldbg = S.dma("sync", "dr", bandg,
                         bass.AP(mg_out, 256 * fb,
                                 [[FFH, 128], [FFH * 128, KD], [1, 256]]),
                         waits=bw)
            ldbv = S.dma("sync", "drv", bandv,
                         bass.AP(mv_out, 256 * fb,
                                 [[FFH, 128], [FFH * 128, KD], [1, 256]]),
                         waits=[("cc", cc_m_n), ("t", bv_free)])

            def gvmm(e, band, pss):
                last = None
                for ft in range(2):
                    for cb in range(2):
                        ps = pss[ft]
                        for k in range(KD):
                            last = e.matmul(
                                ps[:, 512 * cb:512 * (cb + 1)],
                                band[:, 256 * k + 128 * ft:256 * k + 128 * (ft + 1)],
                                xi_bf[:, k * T + 512 * cb:k * T + 512 * (cb + 1)],
                                start=(k == 0), stop=(k == KD - 1))
                return last
            bg_free = S.op("tensor", lambda e: gvmm(e, bandg, [pA, pB]),
                           waits=[ldbg, ("v", S.cnt["v"])], inc="t")
            bv_free = S.op("tensor", lambda e: gvmm(e, bandv, [pC, pD]),
                           waits=[ldbv, ("v", S.cnt["v"])], inc="t")
            for ft in range(2):
                for cb in range(2):
                    psg, psv = [pA, pB][ft], [pC, pD][ft]
                    fgl = 2 * fb + ft

                    def sgm(e, psg=psg, cb=cb):
                        return e.tensor_tensor(
                            qt1[:, 0:512], psg[:, 512 * cb:512 * (cb + 1)],
                            sg_bc[:, 512 * cb:512 * (cb + 1)], OP.mult)
                    S.op("vector", sgm,
                         waits=[("t", bv_free), ldsg, ldsv, mid_d], inc="v")
                    S.op("scalar", lambda e, fgl=fgl: e.activation(
                        qt2[:, 0:512], qt1[:, 0:512], AF.Silu,
                        bias=fb_sb[:, fgl:fgl + 1]), waits=S.at("v"), inc="s")

                    def svm(e, psv=psv, cb=cb):
                        return e.tensor_tensor(
                            qt1[:, 512:1024], psv[:, 512 * cb:512 * (cb + 1)],
                            sv_bc[:, 512 * cb:512 * (cb + 1)], OP.mult)
                    S.op("vector", svm, inc="v")
                    S.op("vector", lambda e, fgl=fgl: e.tensor_scalar_add(
                        qt1[:, 0:512], qt1[:, 512:1024],
                        fb_sb[:, 64 + fgl:64 + fgl + 1]),
                        waits=S.at("s"), inc="v")
                    S.op("vector", lambda e: e.tensor_tensor(
                        qt1[:, 512:1024], qt1[:, 0:512], qt2[:, 0:512], OP.mult),
                        waits=S.at("s"), inc="v")
                    am_s = S.op("scalar", lambda e: e.activation(
                        qt2[:, 512:1024], qt1[:, 512:1024], AF.Abs),
                        waits=S.at("v"), inc="s")
                    par = ngm[cb] % 2
                    S.op("vector", lambda e, cb=cb, par=par: e.tensor_tensor(
                        gmp[1 - par][:, 512 * cb:512 * (cb + 1)],
                        gmp[par][:, 512 * cb:512 * (cb + 1)],
                        qt2[:, 512:1024], OP.max), waits=[("s", am_s)], inc="v")
                    ngm[cb] += 1
                    midv = S.cnt["v"]
                    mid_d = S.dma(
                        "scalar", "dmid",
                        mid_dram[128 * fgl:128 * (fgl + 1), 512 * cb:512 * (cb + 1)],
                        qt1[:, 512:1024], waits=[("v", midv)])

        go_v, go_d = gamma_pass(gm)
        ldso = S.dma("sync", "db", sxq_bc, bass.AP(ln_dram, 3 * T, [[0, 128], [1, T]]),
                     waits=[go_d, ("v", go_v)])
        ldg5 = S.dma("sync", "db", grow_g, bass.AP(ln_dram, 4 * T, [[1, 1], [1, T]]),
                     waits=[go_d, ("v", go_v)])
        S.op("vector", lambda e: e.tensor_scalar(
            grow_v2, grow_g, ggam[0:1, 2:3], 1.0 / 127.0, OP.mult, OP.mult),
            waits=[ldg5, ldgw], inc="v")
        soo_d = S.dma("scalar", "dbs", ln_dram[5:6, :], grow_v2, waits=S.at("v"))
        ldsoo = S.dma("sync", "db", sg_bc, bass.AP(ln_dram, 5 * T, [[0, 128], [1, T]]),
                      waits=[soo_d, ("v", S.cnt["v"])])
        mi_v = 0
        mi_d = ("dmi", 0)
        mo_bf = ar(A_B5, 4096).bitcast(BF16)
        mi_sb = [AR[0:128, A_B1 + 2048 * i:A_B1 + 2048 * (i + 1)].bitcast(BF16)
                 for i in range(2)]
        p7_done_v = S.cnt["v"]
        mo_free_t = S.cnt["t"]
        for cb in range(2):
            for oh in range(2):
                for kc in range(8):
                    if cb == 0 and oh == 0:
                        for ft8 in range(8):
                            ft = 8 * kc + ft8
                            ldm = S.dma("sync", "dy", qt2,
                                        mid_dram[128 * ft:128 * (ft + 1), :],
                                        waits=[mid_d, mi_d, ("v", mi_v)])
                            S.op("vector", lambda e: e.tensor_tensor(
                                qt1, qt2, sxq_bc, OP.mult),
                                waits=[ldm, ldso], inc="v")
                            S.op("vector", lambda e: e.tensor_scalar(
                                qt2, qt1, MAGIC, -MAGIC, OP.add, OP.add), inc="v")
                            S.op("vector", lambda e: e.tensor_scalar_max(
                                qt1, qt2, -128.0), inc="v")
                            mi_v = S.op("vector", lambda e: e.tensor_scalar_min(
                                qt2.bitcast(BF16)[:, 0:1024], qt1, 127.0), inc="v")
                            mi_d = S.dma("scalar", "dmi",
                                         mi_dram[128 * ft:128 * (ft + 1), :],
                                         qt2.bitcast(BF16)[:, 0:1024],
                                         waits=[("v", mi_v)])
                    mw = [("cc", cc_m_n), ("t", mo_free_t)]
                    if cb == 0 and oh == 0 and kc == 0:
                        mw += [("dr", S.cnt["dr"]), ("drv", S.cnt["drv"])]
                    ldmo = S.dma("sync", "dmo", mo_bf,
                                 bass.AP(mo_out, (1024 * kc) * DM + 1024 * oh,
                                         [[DM, 128], [DM * 128, 8], [1, 1024]]),
                                 waits=mw)
                    miw = [mi_d, ("t", mo_free_t)]
                    if cb == 0 and oh == 0 and kc < 2:
                        miw += [("dx", S.cnt["dx"]), ("v", S.cnt["v"])]
                    ldmi = S.dma("sync", "dm0" if kc % 2 == 0 else "dm1",
                                 mi_sb[kc % 2],
                                 bass.AP(mi_dram, (1024 * kc) * T + 512 * cb,
                                         [[T, 128], [128 * T, 8], [1, 512]]),
                                 waits=miw)

                    def omm7(e, kc=kc):
                        last = None
                        for k8 in range(8):
                            for o in range(8):
                                ps = [pA, pB, pC, pD][o // 2]
                                last = e.matmul(
                                    ps[:, 512 * (o % 2):512 * (o % 2 + 1)],
                                    mo_bf[:, 1024 * k8 + 128 * o:
                                          1024 * k8 + 128 * (o + 1)],
                                    mi_sb[kc % 2][:, 512 * k8:512 * (k8 + 1)],
                                    start=(kc == 0 and k8 == 0),
                                    stop=(kc == 7 and k8 == 7))
                        return last
                    mo_free_t = S.op("tensor", omm7,
                                     waits=[ldmo, ldmi,
                                            ("v", p7_done_v if kc == 0 else 0)],
                                     inc="t")
                for o in range(8):
                    ps = [pA, pB, pC, pD][o // 2]
                    og = 8 * oh + o

                    def fev(e, ps=ps, o=o, cb=cb):
                        return e.tensor_tensor(
                            qt1[:, 0:512], ps[:, 512 * (o % 2):512 * (o % 2 + 1)],
                            sg_bc[:, 512 * cb:512 * (cb + 1)], OP.mult)
                    S.op("vector", fev,
                         waits=[("t", mo_free_t), ldsoo], inc="v")
                    S.op("vector", lambda e, og=og: e.tensor_scalar_add(
                        qt2[:, 0:512], qt1[:, 0:512], fb_sb[:, 128 + og:129 + og]),
                        inc="v")

                    def fres(e, og=og, cb=cb):
                        xsl = xT_sb[:, og * T + 512 * cb: og * T + 512 * (cb + 1)]
                        return e.tensor_tensor(qt1[:, 512:1024], xsl,
                                               qt2[:, 0:512], OP.add)
                    S.op("vector", fres, inc="v")

                    def fres_c(e, og=og, cb=cb):
                        xsl = xT_sb[:, og * T + 512 * cb: og * T + 512 * (cb + 1)]
                        return e.tensor_copy(xsl, qt1[:, 512:1024])
                    S.op("vector", fres_c, inc="v")
                p7_done_v = S.cnt["v"]
        sqt2 = ar(A_B3 + 2048, 512)
        for cb in range(2):
            for k in range(KD):
                sq_s = S.op("scalar", lambda e, k=k, cb=cb: e.activation(
                    sqt2, xT_sb[:, k * T + 512 * cb: k * T + 512 * (cb + 1)],
                    AF.Square), waits=[("v", p7_done_v), ("t", S.cnt["t"])], inc="s")
                S.op("tensor", lambda e, k=k, cb=cb: e.matmul(
                    pA[0:1, 512 * cb:512 * (cb + 1)], ones_sb[:, 0:1],
                    xT_sb[:, k * T + 512 * cb: k * T + 512 * (cb + 1)],
                    start=(k == 0), stop=(k == KD - 1)),
                    waits=[("v", p7_done_v)], inc="t")
                S.op("tensor", lambda e, k=k, cb=cb: e.matmul(
                    pB[0:1, 512 * cb:512 * (cb + 1)], ones_sb[:, 0:1], sqt2,
                    start=(k == 0), stop=(k == KD - 1)),
                    waits=[("s", sq_s)], inc="t")
        ln2_t = S.cnt["t"]
        mu2 = ar(A_B4 + 1024, 1024, 1)
        t2r = ar(A_B4 + 2048, 1024, 1)
        r2r = ar(A_B4 + 3072, 1024, 1)
        S.op("vector", lambda e: e.tensor_scalar_mul(mu2, pA[0:1, 0:1024], 1.0 / DM),
             waits=[("t", ln2_t)], inc="v")
        S.op("vector", lambda e: e.tensor_scalar_mul(t2r, pB[0:1, 0:1024], 1.0 / DM),
             inc="v")
        S.op("vector", lambda e: e.tensor_tensor(r2r, mu2, mu2, OP.mult), inc="v")
        S.op("vector", lambda e: e.tensor_sub(t2r, t2r, r2r), inc="v")
        S.op("vector", lambda e: e.tensor_scalar_add(t2r, t2r, EPS), inc="v")
        S.op("vector", lambda e: e.reciprocal(r2r, t2r), inc="v")
        S.op("scalar", lambda e: e.activation(t2r, r2r, AF.Sqrt),
             waits=S.at("v"), inc="s")
        mu2_d = S.dma("scalar", "dbs", ln_dram[1:2, :], mu2, waits=S.at("v", "s"))
        rr2_d = S.dma("scalar", "dbs", ln_dram[2:3, :], t2r, waits=S.at("s"))
        mu_bc2 = ar(A_B2, 1024)
        r_bc2 = ar(A_B2 + 1024, 1024)
        ldmu2 = S.dma("sync", "db", mu_bc2, bass.AP(ln_dram, T, [[0, 128], [1, T]]),
                      waits=[mu2_d, ("v", S.cnt["v"])])
        ldrr2 = S.dma("sync", "db", r_bc2, bass.AP(ln_dram, 2 * T, [[0, 128], [1, T]]),
                      waits=[rr2_d, ("v", S.cnt["v"])])
        for k in range(KD):
            def d1f(e, k=k):
                xsl = xT_sb[:, k * T:(k + 1) * T]
                return e.tensor_tensor(xsl, xsl, mu_bc2, OP.subtract)
            S.op("vector", d1f, waits=[ldmu2, ldrr2], inc="v")

            def d2f(e, k=k):
                xsl = xT_sb[:, k * T:(k + 1) * T]
                return e.tensor_tensor(xsl, xsl, r_bc2, OP.mult)
            S.op("vector", d2f, inc="v")
        fin_v = S.cnt["v"]
        yout_d = S.dma("gpsimd", "dg",
                       bass.AP(yout, 0, [[T, 128], [128 * T, KD], [1, T]]),
                       xT_sb, waits=[("v", fin_v)])

        if "mid" in taps:
            for k in range(KD):
                S.dma("gpsimd", "dg", taps["mid"][128 * k:128 * (k + 1), :],
                      AR[0:128, A_XT + k * T:A_XT + (k + 1) * T],
                      waits=[("v", ln1_v)])
        if "ffnmid" in taps:
            for ft in range(FFT):
                S.dma("gpsimd", "dg", taps["ffnmid"][128 * ft:128 * (ft + 1), :],
                      mid_dram[128 * ft:128 * (ft + 1), :],
                      waits=[("dmid", S.cnt["dmid"])])
        if "acsT" in taps:
            S.dma("gpsimd", "dg", taps["acsT"][:, :], acsT[:, :],
                  waits=[("v", acs_v)])
        if "state" in taps:
            S.dma("gpsimd", "dg", taps["state"][:, :], s_run, waits=S.at("v"))
        if "y" in taps:
            for f in range(DIT):
                S.dma("gpsimd", "dg", taps["y"][128 * f:128 * (f + 1), :],
                      bass.AP(y_dram, 128 * f,
                              [[4096, 128], [524288, 8], [1, 128]]),
                      waits=[("dys", S.cnt["dys"])])
        if "xs" in taps:
            for f in range(32):
                S.dma("gpsimd", "dg", taps["xs"][128 * f:128 * (f + 1), :],
                      bass.AP(xs_dram, 128 * f,
                              [[4096, 128], [524288, 8], [1, 128]]),
                      waits=[patch_g, p1_dsc])
            for f in (32, 33):
                S.dma("gpsimd", "dg", taps["xs"][128 * f:128 * (f + 1), :],
                      BC[:, (f - 32) * T:(f - 31) * T],
                      waits=[("v", patch_v)])
        if "dt" in taps:
            S.dma("gpsimd", "dg", taps["dt"][:, :], dt_sb[:],
                  waits=[("v", dt_ready_v)])

        S.op("gpsimd", lambda e: e.nop(),
             waits=S.at("dg"), inc="g")

        block = E(nc.Block())
        S.emit(block, sems)
    return nc



def _stage(core, ins):
    b, q = core // 4, core % 4
    hs = ins["L0_attn_hidden_states"]
    xTa = np.ascontiguousarray(hs[b, q * T:(q + 1) * T, :].T).reshape(KD, 128, T)
    WT = np.ascontiguousarray(ins["L0_attn_in_proj_w"].T)
    wx = WT[:, DI:DI + 4352]
    w_xbc = np.ascontiguousarray(
        wx.reshape(KD, 128, FX, 128).transpose(2, 1, 0, 3)).reshape(FX, 128, KD * 128)
    wd = WT[:, DI + 4352:]
    w_dt = np.ascontiguousarray(
        wd.reshape(KD, 128, 64).transpose(1, 0, 2)).reshape(128, KD * 64)
    w_gate = np.ascontiguousarray(
        WT[:, :DI].reshape(KD, 128, DIT, 128).transpose(2, 1, 0, 3)).reshape(
        DIT, 128, KD * 128)
    w_out = np.ascontiguousarray(ins["L0_attn_out_proj_w"].T)
    conv_w = np.ascontiguousarray(
        ins["L0_attn_conv_w"].reshape(FX, 128, 4).transpose(1, 0, 2)).reshape(128, FX * 4)
    conv_b = np.ascontiguousarray(ins["L0_attn_conv_b"].reshape(FX, 128).T)
    state_w5 = np.zeros((5, 4), np.float32)
    for j in range(4):
        if j < q:
            for i in range(4):
                state_w5[i, j] = 1.0 if (j < i < q) else 0.0
        else:
            state_w5[4, j] = -1e30
    tail_mask = np.zeros((128, 4), np.float32)
    if q > 0:
        tail_mask[:, q - 1] = 1.0
    perm64 = np.zeros((64, 64), np.float32)
    for r in range(64):
        ph, hf, i = r // 32, (r % 32) // 16, r % 16
        perm64[32 * hf + 2 * i + ph, r] = 1.0
    f32 = np.float32
    return {
        "xT": xTa.astype(f32), "w_xbc": w_xbc.astype(f32), "w_dt": w_dt.astype(f32),
        "w_gate": w_gate.astype(f32), "w_out": w_out.astype(f32),
        "conv_w": conv_w.astype(f32), "conv_b": conv_b.astype(f32),
        "dt_bias": np.ascontiguousarray(ins["L0_attn_dt_bias"].reshape(64, 1)).astype(f32),
        "a_log": np.ascontiguousarray(ins["L0_attn_A_log"].reshape(64, 1)).astype(f32),
        "d_col": np.ascontiguousarray(
            np.repeat(ins["L0_attn_D"].reshape(DIT, 2).T, 64, axis=0)).astype(f32),
        "norm_w": np.ascontiguousarray(ins["L0_attn_norm_w"].reshape(DIT, 128).T).astype(f32),
        "gw_sl": np.ascontiguousarray(ins["L0_ffn_gate_w"].T[256 * core:256 * (core + 1)]).astype(f32),
        "vw_sl": np.ascontiguousarray(ins["L0_ffn_val_w"].T[256 * core:256 * (core + 1)]).astype(f32),
        "ow_sl": np.ascontiguousarray(ins["L0_ffn_out_w"].T[1024 * core:1024 * (core + 1)]).astype(f32),
        "gate_b": np.ascontiguousarray(ins["L0_ffn_gate_b"].reshape(FFT, 128).T).astype(f32),
        "val_b": np.ascontiguousarray(ins["L0_ffn_val_b"].reshape(FFT, 128).T).astype(f32),
        "out_b": np.ascontiguousarray(ins["L0_ffn_out_b"].reshape(KD, 128).T).astype(f32),
        "state_w5": state_w5, "tail_mask": tail_mask, "perm64": perm64,
    }


def kernel(**inputs):
    ins = {k: np.asarray(v) for k, v in inputs.items()}
    nc = build_program()
    in_maps = [_stage(c, ins) for c in range(8)]
    res = run_bass_kernel_spmd(nc, in_maps, core_ids=list(range(8)))
    out = np.empty((2, 4096, 2048), np.float32)
    for c in range(8):
        b, q = c // 4, c % 4
        out[b, q * T:(q + 1) * T, :] = res.results[c]["yout"].T
    return out


# version 05
# speedup vs baseline: 1.0867x; 1.0061x over previous
import numpy as np
import ml_dtypes
import concourse.bass as bass
import concourse.mybir as mybir
from concourse.bass_utils import run_bass_kernel_spmd

F32 = mybir.dt.float32
BF16 = mybir.dt.bfloat16
AF = mybir.ActivationFunctionType
OP = mybir.AluOpType
AX = mybir.AxisListType

T = 1024
NCH = 8
CH = 128
KD = 16
DM = 2048
FX = 34
NH = 64
NS = 128
DI = 4096
DIT = 32
FFH = 8192
FFT = 64
MAGIC = 12582912.0
EPS = 1e-5

DEV_PHASES = 99
DEV_TAPS = ()

A_XT = 0
A_B1 = 16384
A_B2 = 20480
A_B3 = 24576
A_B4 = 28672
A_B5 = 32768
A_BC = 36864
A_ACC = 38912
A_XSF = 39936
A_END = 40960
ARENA_N = A_END


class Sched:

    ENGS = ("sync", "tensor", "vector", "scalar", "gpsimd")

    def __init__(self):
        self.ops = {e: [] for e in self.ENGS}
        self.cnt = {}
        self.owner = {}
        self.selfgate = {}

    def sem(self, name, owner=None, selfgate=False):
        self.cnt[name] = 0
        self.owner[name] = owner
        self.selfgate[name] = selfgate

    SERIAL = {"vector": "v", "scalar": "s", "gpsimd": "g"}

    def op(self, eng, fn, waits=(), inc=None, n=1, is_dma=False):
        waits = list(waits)
        own = self.SERIAL.get(eng)
        if own is not None and self.cnt[own] > 0:
            waits.append((own, self.cnt[own]))
        for s, v in waits:
            assert v <= self.cnt[s], f"wait {s}>={v} but only {self.cnt[s]} emitted"
        self.ops[eng].append((fn, tuple(waits), inc, n, is_dma))
        if inc is not None:
            self.cnt[inc] += n
            return self.cnt[inc]
        return None

    def dma(self, eng, sem, out, in_, waits=()):
        def fn(e, out=out, in_=in_):
            return e.dma_start(out=out, in_=in_)
        waits = list(waits)
        if self.selfgate.get(sem):
            waits.append((sem, self.cnt[sem]))
        c = self.op(eng, fn, waits, inc=sem, n=16, is_dma=True)
        return (sem, c)

    def at(self, *names):
        return [(s, self.cnt[s]) for s in names]

    def emit(self, block, sems):
        for eng in self.ENGS:
            ops = self.ops[eng]
            if not ops:
                continue

            def body(e, eng=eng, ops=ops):
                last = {}
                for fn, waits, inc, n, is_dma in ops:
                    for s, v in waits:
                        if v <= 0 or last.get(s, 0) >= v:
                            continue
                        e.wait_ge(sems[s], v)
                        last[s] = v
                    inst = fn(e)
                    if inc is not None:
                        inst.then_inc(sems[inc], n)

            getattr(block, eng)(body)


def build_program():
    nc = bass.Bass("TRN2", target_bir_lowering=False)
    S = Sched()
    for name in ["t", "v", "s", "g"]:
        S.sem(name, owner=name if name != "t" else "tensor")
    S.owner["t"] = "tensor"
    S.owner["v"] = "vector"
    S.owner["s"] = "scalar"
    S.owner["g"] = "gpsimd"
    S.sem("cc")
    for name in ["dc", "dw0", "dw1", "dx", "dm0", "dm1", "dr", "drv", "dmo",
                 "dy", "dy0", "dy1", "dp0", "dp1", "dpm", "dxs", "dys", "dss",
                 "dmid", "dmi", "dyf", "dyf1"]:
        S.sem(name)
    for name in ["db", "dbs", "dg"]:
        S.sem(name, selfgate=True)

    I = {}
    def inp(name, shape, dt=F32):
        I[name] = nc.dram_tensor(name, shape, dt, kind="ExternalInput")
        return I[name]

    xT = inp("xT", [KD, 128, T])
    w_xbc = inp("w_xbc", [FX, 128, KD * 128])
    w_dt = inp("w_dt", [128, KD * 64])
    w_gate = inp("w_gate", [DIT, 128, KD * 128])
    w_out = inp("w_out", [DI, DM])
    conv_w = inp("conv_w", [128, FX * 4])
    conv_b = inp("conv_b", [128, FX])
    dt_bias = inp("dt_bias", [64, 1])
    a_log = inp("a_log", [64, 1])
    d_col = inp("d_col", [128, DIT])
    norm_w = inp("norm_w", [128, DIT])
    gw_sl = inp("gw_sl", [128, 16384])
    vw_sl = inp("vw_sl", [128, 16384])
    ow_sl = inp("ow_sl", [128, 16384])
    gate_b = inp("gate_b", [128, FFT])
    val_b = inp("val_b", [128, FFT])
    out_b = inp("out_b", [128, KD])
    state_w5 = inp("state_w5", [5, 4])
    tail_mask = inp("tail_mask", [128, 4])
    perm64 = inp("perm64", [64, 64])

    yout = nc.dram_tensor("yout", [DM, T], F32, kind="ExternalOutput")

    xs_dram = nc.dram_tensor("xs_dram", [NCH * 128, DIT * 128], F32)
    y_dram = nc.dram_tensor("y_dram", [NCH * 128, DIT * 128], F32)
    yf_dram = nc.dram_tensor("yf_dram", [DI, T], F32)
    st_dram = nc.dram_tensor("st_dram", [NCH, 128, DI], F32)
    a_dram = nc.dram_tensor("a_dram", [NCH, 64 * 128], F32)
    a4_dram = nc.dram_tensor("a4_dram", [NCH, 64 * 128], F32)
    alast_dram = nc.dram_tensor("alast_dram", [1, 512], F32)
    misc_dram = nc.dram_tensor("misc_dram", [128, 64], F32)
    ln_dram = nc.dram_tensor("ln_dram", [8, T], F32)
    mid_dram = nc.dram_tensor("mid_dram", [FFH, T], F32)
    mi_dram = nc.dram_tensor("mi_dram", [16 * 128, 4096], BF16)
    tail_in = nc.dram_tensor("tail_in", [128, 102], F32)
    tail_out = nc.dram_tensor("tail_out", [512, 102], F32)
    sag_in = nc.dram_tensor("sag_in", [128, DI], F32)
    sag_out = nc.dram_tensor("sag_out", [512, DI], F32)
    asum_in = nc.dram_tensor("asum_in", [64, 1], F32)
    asum_out = nc.dram_tensor("asum_out", [256, 1], F32)
    gam_in = nc.dram_tensor("gam_in", [1, 4], F32)
    gam_out = nc.dram_tensor("gam_out", [1, 4], F32)
    mg_in = nc.dram_tensor("mg_in", [128, 16384], BF16)
    mv_in = nc.dram_tensor("mv_in", [128, 16384], BF16)
    mo_in = nc.dram_tensor("mo_in", [128, 16384], BF16)
    mg_out = nc.dram_tensor("mg_out", [1024, 16384], BF16)
    mv_out = nc.dram_tensor("mv_out", [1024, 16384], BF16)
    mo_out = nc.dram_tensor("mo_out", [1024, 16384], BF16)

    taps = {}
    for tname in DEV_TAPS:
        shp = {"xs": [DI + 256, T], "dt": [64, T], "acsT": [64, T], "y": [DI, T],
               "yf": [DI, T], "mid": [DM, T], "state": [128, DI],
               "ffnmid": [FFH, T]}[tname]
        taps[tname] = nc.dram_tensor("tap_" + tname, shp, F32, kind="ExternalOutput")

    from contextlib import ExitStack
    with ExitStack() as stk:
        E = stk.enter_context
        AR = E(nc.sbuf_tensor("arena", [128, ARENA_N], F32))

        def ar(off, n, rows=128):
            return AR[0:rows, off:off + n]

        wq = [E(nc.sbuf_tensor(f"wq{i}", [128, KD * 128], F32)) for i in range(2)]
        dt_sb = E(nc.sbuf_tensor("dt_sb", [64, T], F32))
        acsT = E(nc.sbuf_tensor("acsT", [64, T], F32))
        dt_tm = E(nc.sbuf_tensor("dt_tm", [128, NCH * 64], F32))
        adt_tm = E(nc.sbuf_tensor("adt_tm", [128, NCH * 64], F32))
        acs_col = E(nc.sbuf_tensor("acs_col", [128, NCH * 64], F32))
        alast_sb = E(nc.sbuf_tensor("alast_sb", [64, NCH + 8], F32))
        ds_all = E(nc.sbuf_tensor("ds_all", [128, NCH * 64], F32))
        dch_rep = E(nc.sbuf_tensor("dch_rep", [128, NCH * 64], F32))
        arep = E(nc.sbuf_tensor("arep", [128, 64], F32))
        dcol = E(nc.sbuf_tensor("dcol", [128, DIT], F32))
        nw_col = E(nc.sbuf_tensor("nw_col", [128, DIT], F32))
        cw_sb = E(nc.sbuf_tensor("cw_sb", [128, FX * 4], F32))
        cb_sb = E(nc.sbuf_tensor("cb_sb", [128, FX], F32))
        ones_sb = E(nc.sbuf_tensor("ones_sb", [128, 128], F32))
        u01 = E(nc.sbuf_tensor("u01", [128, 128], F32))
        ident = E(nc.sbuf_tensor("ident", [128, 128], F32))
        gu = E(nc.sbuf_tensor("gu", [128, 128], F32))
        accfix = E(nc.sbuf_tensor("accfix", [128, 102], F32))
        tail_sb = E(nc.sbuf_tensor("tail_sb", [128, 102], F32))
        tailprev = E(nc.sbuf_tensor("tailprev", [128, 102], F32))
        xsfix = E(nc.sbuf_tensor("xsfix", [128, 102], F32))
        tmask_sb = E(nc.sbuf_tensor("tmask_sb", [128, 4], F32))
        fb_sb = E(nc.sbuf_tensor("fb_sb", [128, 144], F32))
        small = E(nc.sbuf_tensor("small", [128, 64], F32))
        pm_sb = E(nc.sbuf_tensor("pm_sb", [64, 64], F32))
        ap64 = E(nc.sbuf_tensor("ap64", [64, T], F32))
        pA = E(nc.psum_tensor("pA", [128, 1024], F32))
        pB = E(nc.psum_tensor("pB", [128, 1024], F32))
        pC = E(nc.psum_tensor("pC", [128, 1024], F32))
        pD = E(nc.psum_tensor("pD", [128, 1024], F32))

        sems = {name: E(nc.semaphore(name)) for name in S.cnt}

        xT_sb = ar(A_XT, KD * T)
        acc = ar(A_ACC, T)
        xs_f = ar(A_XSF, T)
        BC = ar(A_BC, 2 * T)

        ld0 = S.dma("sync", "dc", xT_sb, bass.AP(xT, 0, [[T, 128], [128 * T, KD], [1, T]]))
        S.dma("sync", "dc", cw_sb[:], conv_w[:, :])
        S.dma("sync", "dc", cb_sb[:], conv_b[:, :])
        S.dma("sync", "dc", tmask_sb[:], tail_mask[:, :])
        S.dma("sync", "dc", small[0:64, 0:1], dt_bias[:, :])
        S.dma("sync", "dc", small[0:64, 1:2], a_log[:, :])
        S.dma("sync", "dc", nw_col[:], norm_w[:, :])
        S.dma("sync", "dc", fb_sb[:, 0:64], gate_b[:, :])
        S.dma("sync", "dc", fb_sb[:, 64:128], val_b[:, :])
        S.dma("sync", "dc", fb_sb[:, 128:144], out_b[:, :])
        S.dma("sync", "dc", pm_sb[:], perm64[:, :])
        ld_const = ("dc", S.cnt["dc"])
        S.op("vector", lambda e: e.memset(ones_sb[:], 1.0), inc="v")
        S.op("gpsimd", lambda e: e.affine_select(
            u01[:], ones_sb[:], [[1, 128]], OP.is_ge, 0.0, base=0,
            channel_multiplier=-1), waits=S.at("v"), inc="g")
        S.op("gpsimd", lambda e: e.affine_select(
            ident[:], ones_sb[:], [[1, 128]], OP.is_equal, 0.0, base=0,
            channel_multiplier=-1), waits=S.at("v"), inc="g")
        g_const = S.cnt["g"]
        S.op("scalar", lambda e: e.activation(small[0:64, 4:5], small[0:64, 1:2],
                                              AF.Exp),
             waits=[ld_const], inc="s")
        S.op("vector", lambda e: e.tensor_scalar_mul(small[0:64, 5:6],
                                                     small[0:64, 4:5], -1.0),
             waits=S.at("s"), inc="v")
        S.dma("scalar", "dbs", misc_dram[0:1, 0:64], small[0:64, 5:6], waits=S.at("v"))
        bounce1 = ("dbs", S.cnt["dbs"])
        S.dma("sync", "dc", arep[:, 0:64],
              bass.AP(misc_dram, 0, [[0, 128], [1, 64]]),
              waits=[bounce1])
        S.dma("sync", "dc", dcol[:], I["d_col"][:, :])

        dc_all = ("dc", S.cnt["dc"])
        wacc = E(nc.sbuf_tensor("wacc", [128, 8], F32))
        grow = E(nc.sbuf_tensor("grow", [1, 8], F32))
        rgc = E(nc.sbuf_tensor("rgc", [128, 4], F32))
        p0b = [ar(A_B1, 2048), ar(A_B1 + 2048, 2048)]
        p0t = ar(A_B2, 2048)
        p0m_bf = ar(A_B2 + 2048, 1024).bitcast(BF16)
        subs = []
        for mat, src, dst in [(0, gw_sl, mg_in), (1, vw_sl, mv_in), (2, ow_sl, mo_in)]:
            for r in range(src.shape[0] // 128):
                for c in range(src.shape[1] // 2048):
                    subs.append((mat, src, dst, r, c))
        S.op("gpsimd", lambda e: e.memset(wacc[:, 0:6], 0.0), waits=[dc_all],
             inc="g")
        p0_consumed = [0, 0]
        nsub = [0, 0, 0]
        for i, (mat, src, dst, r, c) in enumerate(subs):
            bb = p0b[i % 2]
            ld = S.dma("sync", "dp0" if i % 2 == 0 else "dp1", bb,
                       src[128 * r:128 * (r + 1), 2048 * c:2048 * (c + 1)],
                       waits=[("s", p0_consumed[i % 2])])
            S.op("scalar", lambda e, bb=bb: e.activation(
                p0t, bb, AF.Abs, accum_out=wacc[:, 6:7]),
                waits=[ld, dc_all, ("g", S.cnt["g"])], inc="s")
            par = nsub[mat] % 2
            S.op("gpsimd", lambda e, mat=mat, par=par: e.tensor_add(
                wacc[:, 3 * (1 - par) + mat:3 * (1 - par) + mat + 1],
                wacc[:, 3 * par + mat:3 * par + mat + 1], wacc[:, 6:7]),
                waits=S.at("s"), inc="g")
            nsub[mat] += 1
            p0_consumed[i % 2] = S.cnt["s"]
        S.op("tensor", lambda e: e.matmul(pC[0:1, 0:4], ones_sb[:, 0:1],
                                          wacc[:, 0:4], start=True, stop=True),
             waits=S.at("g"), inc="t")
        S.op("vector", lambda e: e.tensor_copy(grow[:, 0:4], pC[0:1, 0:4]),
             waits=S.at("t"), inc="v")
        p0a_dma = [("dp0", S.cnt["dp0"]), ("dp1", S.cnt["dp1"])]

        wq_free = [0, 0]
        ps_free = [0, 0]
        last_silu = 0
        last_spill = ("dxs", 0)
        dt_ready_v = 0

        for f in range(FX + 1):
            buf = f % 2
            wsl = wq[buf]
            dwn = "dw0" if buf == 0 else "dw1"
            if f < FX:
                ldw = S.dma("sync", dwn, wsl[:], w_xbc[f, :, :],
                            waits=[("t", wq_free[buf])])
            else:
                ldw = S.dma("sync", dwn, wsl[:, 0:KD * 64], w_dt[:, :],
                            waits=[("t", wq_free[buf])])
            ps = [pA, pB][buf]
            rows = 128 if f < FX else 64

            def mm(e, ps=ps, wsl=wsl, rows=rows):
                last = None
                for cb in range(2):
                    for k in range(KD):
                        last = e.matmul(
                            ps[0:rows, cb * 512:(cb + 1) * 512],
                            wsl[:, k * rows:(k + 1) * rows],
                            xT_sb[:, k * T + cb * 512: k * T + (cb + 1) * 512],
                            start=(k == 0), stop=(k == KD - 1))
                return last
            tcnt = S.op("tensor", mm,
                        waits=[ldw, dc_all, ("v", ps_free[buf])],
                        inc="t")
            wq_free[buf] = tcnt

            if f < FX:
                pp = [acc, ar(A_B4, T)]

                def c0(e, ps=ps, f=f):
                    return e.tensor_scalar(pp[0], ps[:],
                                           cw_sb[:, 4 * f + 3:4 * f + 4],
                                           None, OP.mult)
                S.op("vector", c0, waits=[("t", tcnt), ("s", last_silu)], inc="v")
                for k in (2, 1, 0):
                    lo = 3 - k
                    src, dst2 = pp[(2 - k) % 2], pp[(3 - k) % 2]

                    def ck(e, ps=ps, f=f, k=k, lo=lo, src=src, dst2=dst2):
                        return e.scalar_tensor_tensor(
                            dst2[:, lo:T], ps[:, 0:T - lo],
                            cw_sb[:, 4 * f + k:4 * f + k + 1],
                            src[:, lo:T], OP.mult, OP.add)
                    S.op("vector", ck, inc="v")

                    def hd(e, lo=lo, src=src, dst2=dst2):
                        return e.tensor_copy(dst2[:, 0:lo], src[:, 0:lo])
                    S.op("vector", hd, inc="v")
                S.op("vector", lambda e, f=f: e.tensor_copy(
                    accfix[:, 3 * f:3 * f + 3], pp[1][:, 0:3]), inc="v")
                S.op("vector", lambda e, ps=ps, f=f: e.tensor_copy(
                    tail_sb[:, 3 * f:3 * f + 3], ps[:, T - 3:T]), inc="v")
                ps_free[buf] = S.cnt["v"]
                dst = xs_f if f < 32 else BC[:, (f - 32) * T:(f - 31) * T]

                def si(e, dst=dst, f=f):
                    return e.activation(dst, pp[1], AF.Silu, bias=cb_sb[:, f:f + 1])
                last_silu = S.op("scalar", si,
                                 waits=[("v", S.cnt["v"]), last_spill], inc="s")
                if f < 32:
                    last_spill = S.dma("scalar", "dxs",
                                       bass.AP(xs_dram, 128 * f,
                                               [[4096, 128], [524288, 8],
                                                [1, 128]]),
                                       xs_f, waits=[("s", last_silu)])
            else:
                def d1(e, ps=ps):
                    return e.tensor_scalar(acc[0:64, :], ps[0:64, :],
                                           small[0:64, 0:1], None, OP.add)
                S.op("vector", d1, waits=[("t", tcnt), ("s", last_silu)]
                     + p0a_dma, inc="v")
                ps_free[buf] = S.cnt["v"]
                S.op("scalar", lambda e: e.activation(
                    ar(A_XSF, T, 64), acc[0:64, :], AF.Abs),
                    waits=[("v", S.cnt["v"]), last_spill], inc="s")
                S.op("scalar", lambda e: e.activation(
                    ar(A_B1, T, 64), ar(A_XSF, T, 64), AF.Exp, scale=-1.0), inc="s")
                S.op("vector", lambda e: e.tensor_scalar_max(
                    ar(A_B2 + 2048, T, 64), acc[0:64, :], 0.0),
                    waits=S.at("s"), inc="v")
                S.op("vector", lambda e: e.tensor_scalar_add(
                    ar(A_B2, T, 64), ar(A_B1, T, 64), 1.0), inc="v")
                S.op("scalar", lambda e: e.activation(
                    ar(A_B1, T, 64), ar(A_B2, T, 64), AF.Ln),
                    waits=S.at("v"), inc="s")
                S.op("vector", lambda e: e.tensor_add(
                    dt_sb[:], ar(A_B2 + 2048, T, 64), ar(A_B1, T, 64)),
                    waits=S.at("s"), inc="v")
                dt_ready_v = S.cnt["v"]

        p1_dsc = ("dxs", S.cnt["dxs"])

        gsum_d = S.dma("gpsimd", "dg", gam_in[:, :], grow[0:1, 0:4], waits=S.at("v"))

        def cc_gam(e):
            return e.collective_compute(
                "AllReduce", OP.add, replica_groups=[list(range(8))],
                ins=[gam_in.ap().opt()], outs=[gam_out.ap().opt()])
        S.op("gpsimd", cc_gam, waits=[gsum_d], inc="cc")
        cc_gam_n = S.cnt["cc"]
        ldg = S.dma("sync", "db", grow[0:1, 4:8], gam_out[:, :], waits=[("cc", cc_gam_n)])
        S.op("vector", lambda e: e.tensor_scalar(
            grow[0:1, 0:4], grow[0:1, 4:8], 1.0 / 16777216.0, 1e-5, OP.mult, OP.max),
            waits=[ldg], inc="v")
        S.op("vector", lambda e: e.reciprocal(grow[0:1, 4:8], grow[0:1, 0:4]),
             inc="v")
        S.dma("scalar", "dbs", misc_dram[6:7, 0:4], grow[0:1, 0:4], waits=S.at("v"))
        gam_bd = S.dma("scalar", "dbs", misc_dram[5:6, 0:4], grow[0:1, 4:8],
                       waits=S.at("v"))
        ldrg = S.dma("sync", "db", rgc[:, 0:4],
                     bass.AP(misc_dram, 5 * 64, [[0, 128], [1, 4]]),
                     waits=[gam_bd])
        p0_consumed = [S.cnt["v"], S.cnt["v"]]
        mdone = ("dpm", 0)
        for i, (mat, src, dst, r, c) in enumerate(subs):
            bb = p0b[i % 2]
            ld = S.dma("sync", "dp0" if i % 2 == 0 else "dp1", bb,
                       src[128 * r:128 * (r + 1), 2048 * c:2048 * (c + 1)],
                       waits=[("v", p0_consumed[i % 2]), ldrg])
            S.op("vector", lambda e, bb=bb, mat=mat: e.tensor_scalar(
                p0t, bb, rgc[:, mat:mat + 1], MAGIC, OP.mult, OP.add),
                waits=[ld, mdone], inc="v")

            def bfap(off, freedims):
                b = p0m_bf
                return bass.AP(b.tensor, b.offset + off,
                               [list(b.ap[0])] + freedims)
            for hh in range(2):
                S.op("vector", lambda e, hh=hh: e.tensor_scalar(
                    ar(A_B2 + 3072, 1024), p0t[:, 1024 * hh:1024 * (hh + 1)],
                    MAGIC, -1.0, OP.subtract, OP.max), inc="v")
                if mat < 2:
                    mo_ap = bfap(256 * hh, [[512, 4], [1, 256]])
                else:
                    mo_ap = p0m_bf[:, 1024 * hh:1024 * (hh + 1)]
                S.op("vector", lambda e, mo_ap=mo_ap: e.tensor_scalar_min(
                    mo_ap, ar(A_B2 + 3072, 1024), 1.0), inc="v")
            p0_consumed[i % 2] = S.cnt["v"]
            if mat < 2:
                dst_ap = bass.AP(dst, 512 * c, [[16384, 128], [4096, 4], [1, 512]])
            else:
                dst_ap = bass.AP(dst, 1024 * c, [[16384, 128], [8192, 2], [1, 1024]])
            mdone = S.dma("scalar", "dpm", dst_ap,
                          p0m_bf[:, 0:2048], waits=S.at("v"))

        p0b_dma = [mdone, ("dp0", S.cnt["dp0"]), ("dp1", S.cnt["dp1"])]

        S.dma("gpsimd", "dg", tail_in[:, :], tail_sb[:], waits=S.at("v"))

        def cc_tail(e):
            return e.collective_compute(
                "AllGather", OP.bypass, replica_groups=[[0, 1, 2, 3], [4, 5, 6, 7]],
                ins=[tail_in.ap().opt()], outs=[tail_out.ap().opt()])
        S.op("gpsimd", cc_tail, waits=S.at("dg"), inc="cc")
        cc_tail_n = S.cnt["cc"]
        S.op("vector", lambda e: e.memset(tailprev[:], 0.0), inc="v")
        tpp = [tailprev, tail_sb]
        for j in range(4):
            ldb = S.dma("sync", "db", xsfix[:], tail_out[128 * j:128 * (j + 1), :],
                        waits=[("cc", cc_tail_n), ("v", S.cnt["v"])])

            def addb(e, j=j):
                return e.scalar_tensor_tensor(tpp[(j + 1) % 2][:], xsfix[:],
                                              tmask_sb[:, j:j + 1], tpp[j % 2][:],
                                              OP.mult, OP.add)
            S.op("vector", addb, waits=[ldb], inc="v")
        for f in range(FX):
            S.op("vector", lambda e, f=f: e.scalar_tensor_tensor(
                xsfix[:, 3 * f:3 * f + 3], tail_sb[:, 3 * f:3 * f + 3],
                cw_sb[:, 4 * f + 0:4 * f + 1], accfix[:, 3 * f:3 * f + 3],
                OP.mult, OP.add), inc="v")
            S.op("vector", lambda e, f=f: e.scalar_tensor_tensor(
                tailprev[:, 3 * f:3 * f + 2], tail_sb[:, 3 * f + 1:3 * f + 3],
                cw_sb[:, 4 * f + 1:4 * f + 2], xsfix[:, 3 * f:3 * f + 2],
                OP.mult, OP.add), inc="v")
            S.op("vector", lambda e, f=f: e.tensor_copy(
                tailprev[:, 3 * f + 2:3 * f + 3], xsfix[:, 3 * f + 2:3 * f + 3]),
                inc="v")
            S.op("vector", lambda e, f=f: e.scalar_tensor_tensor(
                xsfix[:, 3 * f:3 * f + 1], tail_sb[:, 3 * f + 2:3 * f + 3],
                cw_sb[:, 4 * f + 2:4 * f + 3], tailprev[:, 3 * f:3 * f + 1],
                OP.mult, OP.add), inc="v")
            S.op("vector", lambda e, f=f: e.tensor_copy(
                xsfix[:, 3 * f + 1:3 * f + 3], tailprev[:, 3 * f + 1:3 * f + 3]),
                inc="v")

            def sif(e, f=f):
                return e.activation(tail_sb[:, 3 * f:3 * f + 3],
                                    xsfix[:, 3 * f:3 * f + 3],
                                    AF.Silu, bias=cb_sb[:, f:f + 1])
            S.op("scalar", sif, waits=S.at("v"), inc="s")
        fix_s = S.cnt["s"]
        for f in range(32):
            S.dma("gpsimd", "dg",
                  bass.AP(xs_dram, 128 * f, [[4096, 128], [1, 3]]),
                  tail_sb[:, 3 * f:3 * f + 3],
                  waits=[("s", fix_s), p1_dsc])
        for f in (32, 33):
            S.op("vector", lambda e, f=f: e.tensor_copy(
                BC[:, (f - 32) * T:(f - 32) * T + 3], tail_sb[:, 3 * f:3 * f + 3]),
                waits=[("s", fix_s)], inc="v")
        patch_g = ("dg", S.cnt["dg"])
        patch_v = S.cnt["v"]

        btm = E(nc.sbuf_tensor("btm", [128, 128], F32))
        facr = E(nc.sbuf_tensor("facr", [128, 64], F32))
        asw = E(nc.sbuf_tensor("asw", [5, 64], F32))
        w5_sb = E(nc.sbuf_tensor("w5_sb", [5, 4], F32))
        alr = E(nc.sbuf_tensor("alr", [128, 64], F32))
        asum_sb = E(nc.sbuf_tensor("asum_sb", [64, 1], F32))

        def mkap(x, off, freedims, parts=None):
            b = x if isinstance(x, bass.AP) else x[:]
            pd = list(b.ap[0])
            if parts is not None:
                pd = [pd[0], parts]
            return bass.AP(b.tensor, b.offset + off, [pd] + [list(d) for d in freedims])

        for m in range(NCH):
            S.op("tensor", lambda e, m=m: e.transpose(
                pC[0:128, 64 * m:64 * (m + 1)], dt_sb[:, 128 * m:128 * (m + 1)],
                ident[0:64, 0:64]),
                waits=[("v", dt_ready_v), ("g", g_const)], inc="t")
        S.op("vector", lambda e: e.tensor_copy(dt_tm[:, 0:512], pC[:, 0:512]),
             waits=S.at("t"), inc="v")
        S.op("vector", lambda e: e.tensor_tensor(
            adt_tm[:, 0:512], dt_tm[:, 0:512],
            mkap(arep, 0, [[0, NCH], [1, 64]]), OP.mult),
            waits=S.at("dc"), inc="v")
        adt_v = S.cnt["v"]
        for m in range(NCH):
            S.op("tensor", lambda e, m=m: e.matmul(
                pD[0:64, 128 * m:128 * (m + 1)], adt_tm[:, 64 * m:64 * (m + 1)],
                u01[:], start=True, stop=True), waits=[("v", adt_v)], inc="t")
            S.op("tensor", lambda e, m=m: e.matmul(
                pC[0:128, 512 + 64 * m:512 + 64 * (m + 1)], u01[:],
                adt_tm[:, 64 * m:64 * (m + 1)], start=True, stop=True), inc="t")
        acs_t = S.cnt["t"]
        S.op("vector", lambda e: e.tensor_copy(acsT[:, :], pD[0:64, 0:1024]),
             waits=[("t", acs_t)], inc="v")
        S.op("vector", lambda e: e.tensor_copy(acs_col[:, :], pC[:, 512:1024]),
             inc="v")
        acs_v = S.cnt["v"]
        acsT_d = S.dma("scalar", "dbs",
                       bass.AP(a_dram, 0, [[128, 64], [8192, 8], [1, 128]]),
                       acsT[:, :], waits=[("v", acs_v)])
        for cb2 in range(2):
            S.op("tensor", lambda e, cb2=cb2: e.matmul(
                pD[0:64, 512 * cb2:512 * (cb2 + 1)], pm_sb[:],
                acsT[:, 512 * cb2:512 * (cb2 + 1)], start=True, stop=True),
                waits=([("v", acs_v), dc_all] if cb2 == 0 else ()), inc="t")
        S.op("vector", lambda e: e.tensor_copy(ap64[:, :], pD[0:64, 0:1024]),
             waits=S.at("t"), inc="v")
        a4_d = S.dma("scalar", "dbs",
                     bass.AP(a4_dram, 0, [[128, 64], [8192, 8], [1, 128]]),
                     ap64[:, :], waits=[("v", S.cnt["v"])])
        alast_d = S.dma("scalar", "dbs", alast_dram[0:1, 0:512], acs_col[127:128, :],
                        waits=[("v", acs_v)])
        S.op("vector", lambda e: e.reduce_sum(
            asum_sb[:, 0:1], mkap(acsT, 127, [[128, NCH]], parts=64),
            axis=AX.X), waits=[("v", acs_v)], inc="v")
        asum_v = S.cnt["v"]
        for m in range(NCH):
            lda = S.dma("sync", "db", alr[:, 0:64],
                        bass.AP(alast_dram, 64 * m, [[0, 128], [1, 64]]),
                        waits=[alast_d, ("v", S.cnt["v"]),
                               ("s", S.cnt["s"])])
            S.op("vector", lambda e, m=m: e.tensor_sub(
                adt_tm[:, 64 * m:64 * (m + 1)], alr[:, 0:64],
                acs_col[:, 64 * m:64 * (m + 1)]), waits=[lda], inc="v")
            S.op("scalar", lambda e, m=m: e.activation(
                dch_rep[:, 64 * m:64 * (m + 1)], alr[:, 0:64], AF.Exp),
                waits=[lda], inc="s")
        S.op("scalar", lambda e: e.activation(
            ds_all[:, :], adt_tm[:, :], AF.Exp), waits=S.at("v"), inc="s")
        apath_s = S.cnt["s"]

        xs_ch = ar(A_B1, DI)
        xd = ar(A_B2, DI)
        y_sb = ar(A_B4, DI)
        s_run = ar(A_B5, DI)
        repA = ar(A_B3, 2048)
        repB = ar(A_B3 + 2048, 2048)
        S.op("vector", lambda e: e.memset(s_run, 0.0), inc="v")

        st_spill = [("dss", 0)]
        p2_y_d = []
        prev_t = 0
        for m in range(NCH):
            xw = [p1_dsc, ("t", prev_t), ("v", S.cnt["v"])]
            if m == 0:
                xw.append(patch_g)
                xw += p0b_dma
            ldx = S.dma("sync", "dx", xs_ch,
                        bass.AP(xs_dram, 524288 * m, [[4096, 128], [1, 4096]]),
                        waits=xw)
            BTm = BC[:, 128 * m:128 * (m + 1)]
            CTm = BC[:, T + 128 * m:T + 128 * (m + 1)]
            bcw = [("v", patch_v)] if m == 0 else []
            gt = S.op("tensor", lambda e, BTm=BTm, CTm=CTm: e.matmul(
                pD[:, 0:128], BTm, CTm, start=True, stop=True),
                waits=bcw + [("v", S.cnt["v"])], inc="t")
            bt = S.op("tensor", lambda e, BTm=BTm: e.transpose(
                pD[:, 128:256], BTm, ident[:]), inc="t")
            S.op("vector", lambda e: e.tensor_tensor(gu[:], pD[:, 0:128], u01[:],
                                                     OP.mult),
                 waits=[("t", bt)], inc="v")
            S.op("vector", lambda e: e.tensor_copy(btm[:], pD[:, 128:256]), inc="v")

            for hf in range(2):
                tws = [ldx, ("v", S.cnt["v"])]
                for i in range(16):
                    ps = pA if i < 8 else pB
                    S.op("tensor", lambda e, i=i, hf=hf, ps=ps: e.transpose(
                        ps[:, 128 * (i % 8):128 * (i % 8 + 1)],
                        xs_ch[:, 2048 * hf + 128 * i: 2048 * hf + 128 * (i + 1)],
                        ident[:]), waits=tws if i == 0 else (), inc="t")
                txd = S.cnt["t"]
                for h2 in range(2):
                    ps = [pA, pB][h2]

                    def evx(e, ps=ps, hf=hf, h2=h2, m=m):
                        in1 = mkap(dt_tm, 64 * m + 32 * hf + 16 * h2,
                                   [[2, 8], [1, 2], [0, 64]])
                        return e.tensor_tensor(
                            xd[:, 2048 * hf + 1024 * h2:2048 * hf + 1024 * (h2 + 1)],
                            ps[:], in1, OP.mult)
                    S.op("vector", evx, waits=[("t", txd)], inc="v")
                xd_v = S.cnt["v"]

                for qr in range(2):
                    hq = 32 * hf + 16 * qr
                    ldr = S.dma("sync", "dr", repA,
                                bass.AP(a_dram, 8192 * m + 128 * hq,
                                        [[0, 128], [1, 2048]]),
                                waits=[acsT_d, st_spill[-1],
                                       ("t", S.cnt["t"]), ("v", S.cnt["v"])])

                    def mtsub(e, hq=hq, m=m):
                        in1 = mkap(acs_col, 64 * m + hq, [[1, 16], [0, 128]])
                        return e.tensor_tensor(repB, repA, in1, OP.subtract)
                    S.op("vector", mtsub,
                         waits=[ldr, ("v", acs_v), st_spill[-1]], inc="v")
                    S.op("vector", lambda e: e.tensor_scalar_min(repA, repB, 0.0),
                         inc="v")
                    S.op("scalar", lambda e: e.activation(repB, repA, AF.Exp),
                         waits=S.at("v"), inc="s")

                    def mtgu(e):
                        in1 = mkap(gu, 0, [[0, 16], [1, 128]])
                        return e.tensor_tensor(repA, repB, in1, OP.mult)
                    S.op("vector", mtgu, waits=S.at("s"), inc="v")
                    mt_v = S.cnt["v"]

                    psq = [pC, pD][qr]
                    for hh in range(16):
                        h = hq + hh
                        slot = hh // 2
                        base = 64 * (h % 2)

                        def ymm(e, h=h, hh=hh, psq=psq, slot=slot, base=base):
                            return e.matmul(
                                psq[base:base + 64, 128 * slot:128 * (slot + 1)],
                                xd[:, 64 * h:64 * (h + 1)],
                                repA[:, 128 * hh:128 * (hh + 1)],
                                start=True, stop=True)
                        S.op("tensor", ymm,
                             waits=[("v", mt_v)] if hh == 0 else (), inc="t")
                ydiag_t = S.cnt["t"]

                def dtmp(e, hf=hf):
                    in1 = mkap(dcol, 16 * hf, [[1, 16], [0, 128]])
                    return e.tensor_tensor(repB, xs_ch[:, 2048 * hf:2048 * (hf + 1)],
                                           in1, OP.mult)
                S.op("vector", dtmp, waits=[("t", ydiag_t),
                                            ("dys", S.cnt["dys"])], inc="v")
                for h2 in range(2):
                    ps = [pC, pD][h2]

                    def yev(e, ps=ps, hf=hf, h2=h2):
                        return e.tensor_tensor(
                            y_sb[:, 2048 * hf + 1024 * h2:2048 * hf + 1024 * (h2 + 1)],
                            repB[:, 1024 * h2:1024 * (h2 + 1)], ps[:], OP.add)
                    S.op("vector", yev, inc="v")
                yv = S.cnt["v"]
                p2_y_d.append(S.dma(
                    "scalar", "dys",
                    bass.AP(y_dram, 524288 * m + 2048 * hf,
                            [[4096, 128], [1, 2048]]),
                    y_sb[:, 2048 * hf:2048 * (hf + 1)], waits=[("v", yv)]))

                def xd2f(e, hf=hf, m=m):
                    in1 = mkap(ds_all, 64 * m + 32 * hf, [[1, 32], [0, 64]])
                    return e.tensor_tensor(repB, xd[:, 2048 * hf:2048 * (hf + 1)],
                                           in1, OP.mult)
                S.op("vector", xd2f, waits=[("s", apath_s)], inc="v")
                xd2_v = S.cnt["v"]
                for i in range(4):
                    ps = [pA, pB][i // 2]

                    def smm(e, i=i, ps=ps):
                        return e.matmul(ps[:, 512 * (i % 2):512 * (i % 2 + 1)],
                                        btm[:], repB[:, 512 * i:512 * (i + 1)],
                                        start=True, stop=True)
                    S.op("tensor", smm,
                         waits=[("v", xd2_v)] if i == 0 else (), inc="t")
                prev_t = S.cnt["t"]
                for h2 in range(2):
                    ps = [pA, pB][h2]
                    S.op("vector", lambda e, ps=ps, h2=h2: e.tensor_copy(
                        repB[:, 1024 * h2:1024 * (h2 + 1)], ps[:]),
                        waits=[("t", prev_t)], inc="v")
                stv = S.cnt["v"]
                st_spill.append(S.dma(
                    "scalar", "dss", st_dram[m, :, 2048 * hf:2048 * (hf + 1)], repB,
                    waits=[("v", stv)]))
                tmpv = xd[:, 2048 * hf:2048 * (hf + 1)]

                def sdec(e, hf=hf, m=m, tmpv=tmpv):
                    in1 = mkap(dch_rep, 64 * m + 32 * hf, [[1, 32], [0, 64]])
                    return e.tensor_tensor(tmpv, s_run[:, 2048 * hf:2048 * (hf + 1)],
                                           in1, OP.mult)
                S.op("vector", sdec, waits=[("s", apath_s)], inc="v")
                S.op("vector", lambda e, hf=hf, tmpv=tmpv: e.tensor_add(
                    s_run[:, 2048 * hf:2048 * (hf + 1)], tmpv, repB), inc="v")

        sloc_d = S.dma("gpsimd", "dg", sag_in[:, :], s_run, waits=S.at("v"))
        asum_d = S.dma("gpsimd", "dg", asum_in[:, :], asum_sb[:, 0:1],
                       waits=[("v", asum_v)])

        def cc_st(e):
            return e.collective_compute(
                "AllGather", OP.bypass, replica_groups=[[0, 1, 2, 3], [4, 5, 6, 7]],
                ins=[sag_in.ap().opt()], outs=[sag_out.ap().opt()])
        S.op("gpsimd", cc_st, waits=[sloc_d], inc="cc")

        def cc_as(e):
            return e.collective_compute(
                "AllGather", OP.bypass, replica_groups=[[0, 1, 2, 3], [4, 5, 6, 7]],
                ins=[asum_in.ap().opt()], outs=[asum_out.ap().opt()])
        S.op("gpsimd", cc_as, waits=[asum_d], inc="cc")
        cc_state_n = S.cnt["cc"]
        def cc_ag(name_in, name_out):
            def f(e, name_in=name_in, name_out=name_out):
                return e.collective_compute(
                    "AllGather", OP.bypass, replica_groups=[list(range(8))],
                    ins=[name_in.ap().opt()], outs=[name_out.ap().opt()])
            return f
        S.op("gpsimd", cc_ag(mg_in, mg_out), waits=[mdone], inc="cc")
        S.op("gpsimd", cc_ag(mv_in, mv_out), inc="cc")
        S.op("gpsimd", cc_ag(mo_in, mo_out), inc="cc")
        cc_m_n = S.cnt["cc"]
        ld1r = S.dma("sync", "db", asw[4:5, :], ones_sb[0:1, 0:64],
                     waits=S.at("v"))
        ld5 = S.dma("sync", "db", asw[0:4, :],
                    bass.AP(asum_out, 0, [[64, 4], [1, 64]]),
                    waits=[("cc", cc_state_n)])
        ldw5 = S.dma("sync", "db", w5_sb[:, :], state_w5[:, :])
        fac_t = S.op("tensor", lambda e: e.matmul(
            pD[0:4, 0:64], w5_sb[:, :], asw[:, :], start=True, stop=True),
            waits=[ld5, ldw5, ld1r, ("v", S.cnt["v"])], inc="t")
        S.op("scalar", lambda e: e.activation(facr[0:4, 0:64], pD[0:4, 0:64],
                                              AF.Exp),
             waits=[("t", fac_t)], inc="s")
        facd = S.dma("scalar", "dbs", misc_dram[1:5, 0:64], facr[0:4, 0:64],
                     waits=S.at("s"))
        spp = [s_run, ar(A_B4, DI)]
        for j in range(4):
            ldb = S.dma("sync", "dx", xs_ch, sag_out[128 * j:128 * (j + 1), :],
                        waits=[("cc", cc_state_n), ("t", S.cnt["t"]),
                               ("v", S.cnt["v"]), sloc_d])
            ldf = S.dma("sync", "db", facr[:, 0:64],
                        bass.AP(misc_dram, 64 * (1 + j), [[0, 128], [1, 64]]),
                        waits=[facd, ("v", S.cnt["v"])])
            if j == 0:
                def sacc0(e):
                    in1 = mkap(facr, 0, [[1, 64], [0, 64]])
                    return e.tensor_tensor(spp[0], xs_ch, in1, OP.mult)
                S.op("vector", sacc0,
                     waits=[ldb, ldf, ("dys", S.cnt["dys"])], inc="v")
            else:
                def sacc(e):
                    in1 = mkap(facr, 0, [[1, 64], [0, 64]])
                    return e.tensor_tensor(xd, xs_ch, in1, OP.mult)
                S.op("vector", sacc, waits=[ldb, ldf], inc="v")
                S.op("vector", lambda e, j=j: e.tensor_add(
                    spp[j % 2], spp[(j + 1) % 2], xd), inc="v")
        S.op("vector", lambda e: e.tensor_copy(s_run, spp[1]), inc="v")
        sin_v = S.cnt["v"]

        hbuf = lambda base, p: ar(base + 2048 * p, 2048)
        ACX = ar(A_ACC, 2048)
        p4_y_d = [("dys", S.cnt["dys"])]
        dss_all = ("dss", S.cnt["dss"])
        yfree = [S.cnt["v"], S.cnt["v"]]
        stfree = [S.cnt["v"], S.cnt["v"]]
        erfree = [S.cnt["v"], S.cnt["v"]]
        erspill = [("dys", 0), ("dys", 0)]
        eefree = [S.cnt["v"], S.cnt["v"]]
        supd = [sin_v, sin_v]
        psum_free = [S.cnt["v"], S.cnt["v"]]
        for m in range(NCH):
            for hf in range(2):
                hc = 2 * m + hf
                p = hc % 2
                y_p, st_p = hbuf(A_B1, p), hbuf(A_B2, p)
                er_p, ee_p = hbuf(A_B3, p), hbuf(A_B4, p)
                pp0, pp1 = ([pA, pB], [pC, pD])[p]
                ldy = S.dma("sync", "dy0" if p == 0 else "dy1", y_p,
                            bass.AP(y_dram, 524288 * m + 2048 * hf,
                                    [[4096, 128], [1, 2048]]),
                            waits=[("dys", S.cnt["dys"]), ("v", yfree[p])])
                lds = S.dma("sync", "dm0" if p == 0 else "dm1", st_p,
                            st_dram[m, :, 2048 * hf:2048 * (hf + 1)],
                            waits=[dss_all, ("v", stfree[p])])
                for ph in range(2):
                    lde = S.dma("sync", "dr" if p == 0 else "drv",
                                AR[64 * ph:64 * (ph + 1),
                                   A_B3 + 2048 * p:A_B3 + 2048 * (p + 1)],
                                bass.AP(a4_dram,
                                        8192 * m + (32 * ph + 16 * hf) * 128,
                                        [[0, 64], [1, 2048]]),
                                waits=[a4_d, ("v", erfree[p]),
                                       erspill[p]])
                exp_s = S.op("scalar", lambda e, er_p=er_p, ee_p=ee_p: e.activation(
                    ee_p, er_p, AF.Exp),
                    waits=[lde, ("v", eefree[p])], inc="s")
                CTm = BC[:, T + 128 * m:T + 128 * (m + 1)]
                for j2r in range(16):
                    j2 = 16 * hf + j2r
                    ps = [pp0, pp1][j2r // 8]

                    def omm(e, j2=j2, j2r=j2r, ps=ps, CTm=CTm):
                        return e.matmul(ps[:, 128 * (j2r % 8):128 * (j2r % 8 + 1)],
                                        s_run[:, 128 * j2:128 * (j2 + 1)], CTm,
                                        start=True, stop=True)
                    S.op("tensor", omm,
                         waits=[("v", max(supd[hf], psum_free[p]))]
                         if j2r == 0 else (), inc="t")
                yoff_t = S.cnt["t"]
                for q2 in range(2):
                    ps = [pp0, pp1][q2]

                    def oev(e, q2=q2, ps=ps, ee_p=ee_p):
                        return e.tensor_tensor(
                            ACX[:, 1024 * q2:1024 * (q2 + 1)], ps[:],
                            ee_p[:, 1024 * q2:1024 * (q2 + 1)], OP.mult)
                    S.op("vector", oev,
                         waits=[("t", yoff_t), ("s", exp_s), ldy], inc="v")
                psum_free[p] = S.cnt["v"]
                S.op("vector", lambda e, er_p=er_p, y_p=y_p: e.tensor_add(
                    er_p, y_p, ACX), inc="v")
                yfree[p] = S.cnt["v"]
                erfree[p] = S.cnt["v"]
                erspill[p] = S.dma(
                    "scalar", "dys",
                    bass.AP(y_dram, 524288 * m + 2048 * hf,
                            [[4096, 128], [1, 2048]]),
                    er_p, waits=[("v", S.cnt["v"])])
                p4_y_d.append(erspill[p])

                def sup1(e, m=m, hf=hf, ee_p=ee_p):
                    in1 = mkap(dch_rep, 64 * m + 32 * hf, [[1, 32], [0, 64]])
                    return e.tensor_tensor(
                        ee_p, s_run[:, 2048 * hf:2048 * (hf + 1)], in1, OP.mult)
                S.op("vector", sup1, waits=[("t", yoff_t)], inc="v")

                def sup2(e, hf=hf, ee_p=ee_p, st_p=st_p):
                    return e.tensor_add(s_run[:, 2048 * hf:2048 * (hf + 1)],
                                        ee_p, st_p)
                S.op("vector", sup2, waits=[lds], inc="v")
                supd[hf] = S.cnt["v"]
                stfree[p] = S.cnt["v"]
                eefree[p] = S.cnt["v"]

        wq_free = [S.cnt["t"], S.cnt["t"]]
        ps_sfree = [S.cnt["s"], S.cnt["s"]]
        yf_dp = [("dyf", 0), ("dyf1", 0)]
        ydp_names = ["dy0", "dy1"]
        tt_p = [0, 0]
        sq_p = [0, 0]
        pend_sq = []
        for j in range(DIT):
            buf = j % 2
            wsl = wq[buf]
            gsil = ar(A_B3 + 1024 * buf, 1024)
            yj = ar(A_B3 + 2048 + 1024 * buf, 1024)
            yfb = ar(A_B2 + 1024 * buf, 1024)
            ldw = S.dma("sync", "dw0" if buf == 0 else "dw1", wsl[:],
                        w_gate[j, :, :], waits=[("t", wq_free[buf])])
            ps = [pA, pB][buf]

            def gmm(e, ps=ps, wsl=wsl):
                last = None
                for cb in range(2):
                    for k in range(KD):
                        last = e.matmul(
                            ps[:, cb * 512:(cb + 1) * 512],
                            wsl[:, k * 128:(k + 1) * 128],
                            xT_sb[:, k * T + cb * 512: k * T + (cb + 1) * 512],
                            start=(k == 0), stop=(k == KD - 1))
                return last
            gw_ = [ldw, ("s", ps_sfree[buf])]
            if j < 2:
                gw_.append(("v", S.cnt["v"]))
            tcnt = S.op("tensor", gmm, waits=gw_, inc="t")
            wq_free[buf] = tcnt
            while len(pend_sq) >= 3:
                q = pend_sq.pop(0)
                S.op("tensor", q[0], waits=[("s", q[1])], inc="t")
                S.op("tensor", q[2], inc="t")
            ldy = S.dma("sync", ydp_names[buf], yj,
                        bass.AP(y_dram, 128 * j,
                                [[4096, 128], [524288, 8], [1, 128]]),
                        waits=[p4_y_d[-1], ("v", tt_p[buf])])
            scnt = S.op("scalar", lambda e, ps=ps, gsil=gsil: e.activation(
                gsil, ps[:], AF.Silu),
                waits=[("t", tcnt), ("v", tt_p[buf])], inc="s")
            ps_sfree[buf] = scnt
            tt_p[buf] = S.op(
                "vector", lambda e, yfb=yfb, yj=yj, gsil=gsil: e.tensor_tensor(
                    yfb, yj, gsil, OP.mult),
                waits=[("s", max(scnt, sq_p[buf])), ldy, yf_dp[buf]], inc="v")
            yf_dp[buf] = S.dma("scalar", "dyf" if buf == 0 else "dyf1",
                               yf_dram[128 * j:128 * (j + 1), :], yfb,
                               waits=[("v", tt_p[buf])])
            sqb = ar(A_B4 + 1024 * (j % 4), 1024)
            sq_p[buf] = S.op("scalar", lambda e, yfb=yfb, sqb=sqb: e.activation(
                sqb, yfb, AF.Square),
                waits=[("v", tt_p[buf]), ("t", S.cnt["t"])], inc="s")

            def mk_sq(cb, j, sqb):
                def f(e, cb=cb, j=j, sqb=sqb):
                    return e.matmul(
                        pC[0:1, cb * 512:(cb + 1) * 512], ones_sb[:, 0:1],
                        sqb[:, cb * 512:(cb + 1) * 512],
                        start=(j == 0), stop=(j == DIT - 1))
                return f
            pend_sq.append((mk_sq(0, j, sqb), sq_p[buf], mk_sq(1, j, sqb)))
        for q in pend_sq:
            S.op("tensor", q[0], waits=[("s", q[1])], inc="t")
            S.op("tensor", q[2], inc="t")
        pend_sq = []
        ssq_t = S.cnt["t"]
        r_row = ar(A_B1, 1024, 1)
        r_row2 = ar(A_B1 + 1024, 1024, 1)
        S.op("vector", lambda e: e.tensor_scalar(
            r_row, pC[0:1, 0:1024], 1.0 / DI, EPS, OP.mult, OP.add),
            waits=[("t", ssq_t)], inc="v")
        S.op("vector", lambda e: e.reciprocal(r_row2, r_row), inc="v")
        S.op("scalar", lambda e: e.activation(r_row, r_row2, AF.Sqrt),
             waits=S.at("v"), inc="s")
        rbounce = S.dma("scalar", "dbs", ln_dram[0:1, :], r_row, waits=S.at("s"))
        r_bc = ar(A_B5, 1024)
        ldrbc = S.dma("sync", "db", r_bc, bass.AP(ln_dram, 0, [[0, 128], [1, T]]),
                      waits=[rbounce])

        yn_sb = ar(A_B1, 4 * DI)
        p6_done_v = S.cnt["v"]
        for cb in range(2):
            yn_v = 0
            for k in range(DIT):
                tb = ar(A_B5 + 1024 + 512 * (k % 2), 512)
                ldyf = S.dma("sync", "dy0" if k % 2 == 0 else "dy1", tb,
                             bass.AP(yf_dram, 128 * k * T + 512 * cb,
                                     [[T, 128], [1, 512]]),
                             waits=[yf_dp[0], yf_dp[1], ("v", S.cnt["v"])])

                def ynf(e, k=k, cb=cb, tb=tb):
                    return e.tensor_scalar(
                        yn_sb[:, 512 * k:512 * (k + 1)], tb,
                        nw_col[:, k:k + 1], None, OP.mult)
                yn_v = S.op("vector", ynf, waits=[ldyf], inc="v")
            for oh in range(2):
                for k in range(DIT):
                    buf = k % 2
                    wsl = wq[buf]
                    ldw = S.dma("sync", "dw0" if buf == 0 else "dw1",
                                wsl[:, 0:1024],
                                w_out[128 * k:128 * (k + 1), 1024 * oh:1024 * (oh + 1)],
                                waits=[("t", wq_free[buf])])

                    def pmm(e, k=k, wsl=wsl):
                        last = None
                        for o in range(8):
                            ps = [pA, pB, pC, pD][o // 2]
                            last = e.matmul(
                                ps[:, 512 * (o % 2):512 * (o % 2 + 1)],
                                wsl[:, 128 * o:128 * (o + 1)],
                                yn_sb[:, 512 * k:512 * (k + 1)],
                                start=(k == 0), stop=(k == DIT - 1))
                        return last
                    wq_free[buf] = S.op(
                        "tensor", pmm,
                        waits=[ldw, ("v", max(yn_v, p6_done_v) if k == 0 else 0)],
                        inc="t")
                op_t = S.cnt["t"]
                for o in range(8):
                    ps = [pA, pB, pC, pD][o // 2]
                    og = 8 * oh + o

                    def res_r(e, ps=ps, o=o, cb=cb):
                        return e.tensor_tensor(
                            ar(A_B5 + 2048, 512),
                            ps[:, 512 * (o % 2):512 * (o % 2 + 1)],
                            r_bc[:, 512 * cb:512 * (cb + 1)], OP.mult)
                    S.op("vector", res_r, waits=[("t", op_t), ldrbc], inc="v")

                    def res_f(e, og=og, cb=cb):
                        xsl = xT_sb[:, og * T + 512 * cb: og * T + 512 * (cb + 1)]
                        return e.tensor_tensor(
                            ar(A_B5 + 2560, 512), xsl, ar(A_B5 + 2048, 512),
                            OP.add)
                    S.op("vector", res_f, inc="v")

                    def res_c(e, og=og, cb=cb):
                        xsl = xT_sb[:, og * T + 512 * cb: og * T + 512 * (cb + 1)]
                        return e.tensor_copy(xsl, ar(A_B5 + 2560, 512))
                    S.op("vector", res_c, inc="v")
                p6_done_v = S.cnt["v"]
        sqt = ar(A_B5 + 1024, 512)
        for cb in range(2):
            for k in range(KD):
                sq_s = S.op("scalar", lambda e, k=k, cb=cb: e.activation(
                    sqt, xT_sb[:, k * T + 512 * cb: k * T + 512 * (cb + 1)],
                    AF.Square), waits=[("v", p6_done_v), ("t", S.cnt["t"])], inc="s")
                S.op("tensor", lambda e, k=k, cb=cb: e.matmul(
                    pA[0:1, 512 * cb:512 * (cb + 1)], ones_sb[:, 0:1],
                    xT_sb[:, k * T + 512 * cb: k * T + 512 * (cb + 1)],
                    start=(k == 0), stop=(k == KD - 1)),
                    waits=[("v", p6_done_v)], inc="t")
                S.op("tensor", lambda e, k=k, cb=cb: e.matmul(
                    pB[0:1, 512 * cb:512 * (cb + 1)], ones_sb[:, 0:1], sqt,
                    start=(k == 0), stop=(k == KD - 1)),
                    waits=[("s", sq_s)], inc="t")
        ln_t = S.cnt["t"]
        mu_row = ar(A_B1, 1024, 1)
        r1_row = ar(A_B1 + 1024, 1024, 1)
        t_row = ar(A_B1 + 2048, 1024, 1)
        t_row2 = ar(A_B1 + 3072, 1024, 1)
        S.op("vector", lambda e: e.tensor_scalar_mul(mu_row, pA[0:1, 0:1024],
                                                     1.0 / DM),
             waits=[("t", ln_t)], inc="v")
        S.op("vector", lambda e: e.tensor_scalar_mul(t_row, pB[0:1, 0:1024],
                                                     1.0 / DM), inc="v")
        S.op("vector", lambda e: e.tensor_tensor(r1_row, mu_row, mu_row, OP.mult),
             inc="v")
        S.op("vector", lambda e: e.tensor_sub(t_row2, t_row, r1_row), inc="v")
        S.op("vector", lambda e: e.tensor_scalar_add(t_row, t_row2, EPS), inc="v")
        S.op("vector", lambda e: e.reciprocal(r1_row, t_row), inc="v")
        S.op("scalar", lambda e: e.activation(t_row, r1_row, AF.Sqrt),
             waits=S.at("v"), inc="s")
        mu_d = S.dma("scalar", "dbs", ln_dram[1:2, :], mu_row, waits=S.at("v", "s"))
        rr_d = S.dma("scalar", "dbs", ln_dram[2:3, :], t_row, waits=S.at("s"))
        mu_bc = ar(A_B2, 1024)
        ldmu = S.dma("sync", "db", mu_bc, bass.AP(ln_dram, T, [[0, 128], [1, T]]),
                     waits=[mu_d, ("v", S.cnt["v"])])
        ldrr = S.dma("sync", "db", r_bc, bass.AP(ln_dram, 2 * T, [[0, 128], [1, T]]),
                     waits=[rr_d])
        ln1_v = 0
        for k in range(KD):
            def c1(e, k=k):
                xsl = xT_sb[:, k * T:(k + 1) * T]
                return e.tensor_tensor(ar(A_B1, T), xsl, mu_bc, OP.subtract)
            S.op("vector", c1, waits=[ldmu, ldrr], inc="v")

            def c2(e, k=k):
                xsl = xT_sb[:, k * T:(k + 1) * T]
                return e.tensor_tensor(xsl, ar(A_B1, T), r_bc, OP.mult)
            ln1_v = S.op("vector", c2, inc="v")

        xi_bf = AR[0:128, A_B1:A_B1 + 8192].bitcast(BF16)
        sg_bc = ar(A_B3, 1024)
        sv_bc = ar(A_B3 + 1024, 1024)
        qt1 = ar(A_B3 + 2048, 1024)
        qt2 = ar(A_B3 + 3072, 1024)
        sxq_bc = ar(A_B4, 1024)
        gm = ar(A_B4, 1024)
        grow_g = ar(A_B4 + 1024, 1024, 1)
        grow_v2 = ar(A_B4 + 2048, 1024, 1)
        gcol = ar(A_B4 + 3072, 8)
        gq = ar(A_B4 + 3080, 8)
        gr = ar(A_B4 + 3088, 8)
        ggam = ar(A_B4 + 3096, 4)

        maxt = ar(A_B2, 1024)
        absx = ar(A_B2 + 1024, 1024)
        S.op("scalar", lambda e: e.activation(maxt, xT_sb[:, 0:T], AF.Abs),
             waits=[("v", ln1_v), ("t", S.cnt["t"])], inc="s")
        for k in range(1, KD):
            ab_s = S.op("scalar", lambda e, k=k: e.activation(
                absx, xT_sb[:, k * T:(k + 1) * T], AF.Abs),
                waits=[("v", S.cnt["v"])], inc="s")
            S.op("vector", lambda e: e.tensor_tensor(maxt, maxt, absx, OP.max),
                 waits=[("s", ab_s)], inc="v")
        gx_v = S.cnt["v"]

        def gamma_pass(src, extra_tw=()):
            for c8 in range(8):
                S.op("tensor", lambda e, c8=c8, src=src: e.transpose(
                    pA[:, 128 * c8:128 * c8 + 128],
                    src[:, 128 * c8:128 * (c8 + 1)], ident[:]),
                    waits=([("v", S.cnt["v"])] + list(extra_tw)) if c8 == 0 else (),
                    inc="t")
            tt = S.cnt["t"]
            for c8 in range(8):
                S.op("vector", lambda e, c8=c8: e.reduce_max(
                    gcol[:, c8:c8 + 1], pA[:, 128 * c8:128 * (c8 + 1)], axis=AX.X),
                    waits=[("t", tt)] if c8 == 0 else (), inc="v")
            S.op("vector", lambda e: e.tensor_scalar_max(gcol, gcol, 1e-5), inc="v")
            S.op("vector", lambda e: e.reciprocal(gq, gcol), inc="v")
            S.op("vector", lambda e: e.tensor_scalar_mul(gr, gq, 127.0), inc="v")
            vq = S.cnt["v"]
            for c8 in range(8):
                S.dma("scalar", "dbs", ln_dram[3:4, 128 * c8:128 * (c8 + 1)],
                      gr[:, c8:c8 + 1], waits=[("v", vq)])
                dd = S.dma("scalar", "dbs", ln_dram[4:5, 128 * c8:128 * (c8 + 1)],
                           gcol[:, c8:c8 + 1], waits=[("v", vq)])
            return vq, dd

        gx2_v, gq_d = gamma_pass(maxt)
        ldsx = S.dma("sync", "db", sxq_bc, bass.AP(ln_dram, 3 * T, [[0, 128], [1, T]]),
                     waits=[gq_d, ("v", gx2_v)])
        ldgw = S.dma("sync", "db", ggam, bass.AP(misc_dram, 6 * 64, [[0, 128], [1, 4]]),
                     waits=[gam_bd, ("v", gx2_v)])
        ldg4 = S.dma("sync", "db", grow_g, bass.AP(ln_dram, 4 * T, [[1, 1], [1, T]]),
                     waits=[gq_d, ("v", gx2_v)])
        S.op("vector", lambda e: e.tensor_scalar(
            grow_v2, grow_g, ggam[0:1, 0:1], 1.0 / 127.0, OP.mult, OP.mult),
            waits=[ldgw, ldg4], inc="v")
        sgq_d = S.dma("scalar", "dbs", ln_dram[5:6, :], grow_v2, waits=S.at("v"))
        S.op("vector", lambda e: e.tensor_scalar(
            grow_v2, grow_g, ggam[0:1, 1:2], 1.0 / 127.0, OP.mult, OP.mult),
            waits=[sgq_d], inc="v")
        sgq_d = S.dma("scalar", "dbs", ln_dram[6:7, :], grow_v2, waits=S.at("v"))
        ldsg = S.dma("sync", "db", sg_bc, bass.AP(ln_dram, 5 * T, [[0, 128], [1, T]]),
                     waits=[sgq_d])
        ldsv = S.dma("sync", "db", sv_bc, bass.AP(ln_dram, 6 * T, [[0, 128], [1, T]]),
                     waits=[sgq_d])
        xi_v = 0
        for k in range(KD):
            S.op("vector", lambda e, k=k: e.tensor_tensor(
                qt1, xT_sb[:, k * T:(k + 1) * T], sxq_bc, OP.mult),
                waits=[ldsx] if k == 0 else (), inc="v")
            S.op("vector", lambda e: e.tensor_scalar(
                qt2, qt1, MAGIC, -MAGIC, OP.add, OP.add), inc="v")
            S.op("vector", lambda e: e.tensor_scalar_max(qt1, qt2, -128.0), inc="v")
            xi_v = S.op("vector", lambda e, k=k: e.tensor_scalar_min(
                xi_bf[:, k * T:(k + 1) * T], qt1, 127.0), inc="v")

        bandg = ar(A_B5, 2048).bitcast(BF16)
        bandv = ar(A_B5 + 2048, 2048).bitcast(BF16)
        gmp = [gm, ar(A_B4 + 1024, 1024)]
        S.op("vector", lambda e: e.memset(gm, 0.0), waits=[("v", xi_v)], inc="v")
        S.op("vector", lambda e: e.memset(gmp[1], 0.0), inc="v")
        gm0_v = S.cnt["v"]
        ngm = [0, 0]
        mid_d = ("dmid", 0)
        bg_free = S.cnt["t"]
        bv_free = S.cnt["t"]
        for fb in range(32):
            bw = [("cc", cc_m_n), ("t", bg_free)]
            if fb == 0:
                bw += [("dy0", S.cnt["dy0"]), ("dy1", S.cnt["dy1"]),
                       ("db", S.cnt["db"]), ("dy", S.cnt["dy"])]
            boff = (128 * (fb // 4)) * 16384 + (fb % 4) * 4096
            ldbg = S.dma("sync", "dr", bandg,
                         bass.AP(mg_out, boff, [[16384, 128], [1, 4096]]),
                         waits=bw)
            ldbv = S.dma("sync", "drv", bandv,
                         bass.AP(mv_out, boff, [[16384, 128], [1, 4096]]),
                         waits=[("cc", cc_m_n), ("t", bv_free)])

            def gvmm(e, band, pss):
                last = None
                for ft in range(2):
                    for cb in range(2):
                        ps = pss[ft]
                        for k in range(KD):
                            last = e.matmul(
                                ps[:, 512 * cb:512 * (cb + 1)],
                                band[:, 256 * k + 128 * ft:256 * k + 128 * (ft + 1)],
                                xi_bf[:, k * T + 512 * cb:k * T + 512 * (cb + 1)],
                                start=(k == 0), stop=(k == KD - 1))
                return last
            bg_free = S.op("tensor", lambda e: gvmm(e, bandg, [pA, pB]),
                           waits=[ldbg, ("v", S.cnt["v"])], inc="t")
            bv_free = S.op("tensor", lambda e: gvmm(e, bandv, [pC, pD]),
                           waits=[ldbv, ("v", S.cnt["v"])], inc="t")
            for ft in range(2):
                for cb in range(2):
                    psg, psv = [pA, pB][ft], [pC, pD][ft]
                    fgl = 2 * fb + ft

                    def sgm(e, psg=psg, cb=cb):
                        return e.tensor_tensor(
                            qt1[:, 0:512], psg[:, 512 * cb:512 * (cb + 1)],
                            sg_bc[:, 512 * cb:512 * (cb + 1)], OP.mult)
                    S.op("vector", sgm,
                         waits=[("t", bv_free), ldsg, ldsv, mid_d], inc="v")
                    S.op("scalar", lambda e, fgl=fgl: e.activation(
                        qt2[:, 0:512], qt1[:, 0:512], AF.Silu,
                        bias=fb_sb[:, fgl:fgl + 1]), waits=S.at("v"), inc="s")

                    def svm(e, psv=psv, cb=cb):
                        return e.tensor_tensor(
                            qt1[:, 512:1024], psv[:, 512 * cb:512 * (cb + 1)],
                            sv_bc[:, 512 * cb:512 * (cb + 1)], OP.mult)
                    S.op("vector", svm, inc="v")
                    S.op("vector", lambda e, fgl=fgl: e.tensor_scalar_add(
                        qt1[:, 0:512], qt1[:, 512:1024],
                        fb_sb[:, 64 + fgl:64 + fgl + 1]),
                        waits=S.at("s"), inc="v")
                    S.op("vector", lambda e: e.tensor_tensor(
                        qt1[:, 512:1024], qt1[:, 0:512], qt2[:, 0:512], OP.mult),
                        waits=S.at("s"), inc="v")
                    am_s = S.op("scalar", lambda e: e.activation(
                        qt2[:, 512:1024], qt1[:, 512:1024], AF.Abs),
                        waits=S.at("v"), inc="s")
                    par = ngm[cb] % 2
                    S.op("vector", lambda e, cb=cb, par=par: e.tensor_tensor(
                        gmp[1 - par][:, 512 * cb:512 * (cb + 1)],
                        gmp[par][:, 512 * cb:512 * (cb + 1)],
                        qt2[:, 512:1024], OP.max), waits=[("s", am_s)], inc="v")
                    ngm[cb] += 1
                    midv = S.cnt["v"]
                    mid_d = S.dma(
                        "scalar", "dmid",
                        mid_dram[128 * fgl:128 * (fgl + 1), 512 * cb:512 * (cb + 1)],
                        qt1[:, 512:1024], waits=[("v", midv)])

        go_v, go_d = gamma_pass(gm)
        ldso = S.dma("sync", "db", sxq_bc, bass.AP(ln_dram, 3 * T, [[0, 128], [1, T]]),
                     waits=[go_d, ("v", go_v)])
        ldg5 = S.dma("sync", "db", grow_g, bass.AP(ln_dram, 4 * T, [[1, 1], [1, T]]),
                     waits=[go_d, ("v", go_v)])
        S.op("vector", lambda e: e.tensor_scalar(
            grow_v2, grow_g, ggam[0:1, 2:3], 1.0 / 127.0, OP.mult, OP.mult),
            waits=[ldg5, ldgw], inc="v")
        soo_d = S.dma("scalar", "dbs", ln_dram[5:6, :], grow_v2, waits=S.at("v"))
        ldsoo = S.dma("sync", "db", sg_bc, bass.AP(ln_dram, 5 * T, [[0, 128], [1, T]]),
                      waits=[soo_d, ("v", S.cnt["v"])])
        mi_v = 0
        mi_d = ("dmi", 0)
        mo_bf = ar(A_B5, 4096).bitcast(BF16)
        mi_sb = [AR[0:128, A_B1 + 2048 * i:A_B1 + 2048 * (i + 1)].bitcast(BF16)
                 for i in range(2)]
        p7_done_v = S.cnt["v"]
        mo_free_t = S.cnt["t"]
        for cb in range(2):
            for oh in range(2):
                for kc in range(8):
                    if cb == 0 and oh == 0:
                        for ft8 in range(8):
                            ft = 8 * kc + ft8
                            ldm = S.dma("sync", "dy", qt2,
                                        mid_dram[128 * ft:128 * (ft + 1), :],
                                        waits=[mid_d, mi_d, ("v", mi_v)])
                            S.op("vector", lambda e: e.tensor_tensor(
                                qt1, qt2, sxq_bc, OP.mult),
                                waits=[ldm, ldso], inc="v")
                            S.op("vector", lambda e: e.tensor_scalar(
                                qt2, qt1, MAGIC, -MAGIC, OP.add, OP.add), inc="v")
                            S.op("vector", lambda e: e.tensor_scalar_max(
                                qt1, qt2, -128.0), inc="v")
                            mi_v = S.op("vector", lambda e: e.tensor_scalar_min(
                                qt2.bitcast(BF16)[:, 0:1024], qt1, 127.0), inc="v")
                            S.dma("scalar", "dmi",
                                  bass.AP(mi_dram,
                                          (2 * (ft // 8)) * 524288
                                          + (ft % 8) * 512,
                                          [[4096, 128], [1, 512]]),
                                  qt2.bitcast(BF16)[:, 0:512],
                                  waits=[("v", mi_v)])
                            mi_d = S.dma("scalar", "dmi",
                                         bass.AP(mi_dram,
                                                 (2 * (ft // 8) + 1) * 524288
                                                 + (ft % 8) * 512,
                                                 [[4096, 128], [1, 512]]),
                                         qt2.bitcast(BF16)[:, 512:1024],
                                         waits=[("v", mi_v)])
                    mw = [("cc", cc_m_n), ("t", mo_free_t)]
                    if cb == 0 and oh == 0 and kc == 0:
                        mw += [("dr", S.cnt["dr"]), ("drv", S.cnt["drv"])]
                    ldmo = S.dma("sync", "dmo", mo_bf,
                                 bass.AP(mo_out,
                                         (128 * kc) * 16384 + 8192 * oh,
                                         [[16384, 128], [1, 8192]]),
                                 waits=mw)
                    miw = [mi_d, ("t", mo_free_t)]
                    if cb == 0 and oh == 0 and kc < 2:
                        miw += [("dx", S.cnt["dx"]), ("v", S.cnt["v"])]
                    ldmi = S.dma("sync", "dm0" if kc % 2 == 0 else "dm1",
                                 mi_sb[kc % 2],
                                 bass.AP(mi_dram, (2 * kc + cb) * 524288,
                                         [[4096, 128], [1, 4096]]),
                                 waits=miw)

                    def omm7(e, kc=kc):
                        last = None
                        for k8 in range(8):
                            for o in range(8):
                                ps = [pA, pB, pC, pD][o // 2]
                                last = e.matmul(
                                    ps[:, 512 * (o % 2):512 * (o % 2 + 1)],
                                    mo_bf[:, 1024 * k8 + 128 * o:
                                          1024 * k8 + 128 * (o + 1)],
                                    mi_sb[kc % 2][:, 512 * k8:512 * (k8 + 1)],
                                    start=(kc == 0 and k8 == 0),
                                    stop=(kc == 7 and k8 == 7))
                        return last
                    mo_free_t = S.op("tensor", omm7,
                                     waits=[ldmo, ldmi,
                                            ("v", p7_done_v if kc == 0 else 0)],
                                     inc="t")
                for o in range(8):
                    ps = [pA, pB, pC, pD][o // 2]
                    og = 8 * oh + o

                    def fev(e, ps=ps, o=o, cb=cb):
                        return e.tensor_tensor(
                            qt1[:, 0:512], ps[:, 512 * (o % 2):512 * (o % 2 + 1)],
                            sg_bc[:, 512 * cb:512 * (cb + 1)], OP.mult)
                    S.op("vector", fev,
                         waits=[("t", mo_free_t), ldsoo], inc="v")
                    S.op("vector", lambda e, og=og: e.tensor_scalar_add(
                        qt2[:, 0:512], qt1[:, 0:512], fb_sb[:, 128 + og:129 + og]),
                        inc="v")

                    def fres(e, og=og, cb=cb):
                        xsl = xT_sb[:, og * T + 512 * cb: og * T + 512 * (cb + 1)]
                        return e.tensor_tensor(qt1[:, 512:1024], xsl,
                                               qt2[:, 0:512], OP.add)
                    S.op("vector", fres, inc="v")

                    def fres_c(e, og=og, cb=cb):
                        xsl = xT_sb[:, og * T + 512 * cb: og * T + 512 * (cb + 1)]
                        return e.tensor_copy(xsl, qt1[:, 512:1024])
                    S.op("vector", fres_c, inc="v")
                p7_done_v = S.cnt["v"]
        sqt2 = ar(A_B3 + 2048, 512)
        for cb in range(2):
            for k in range(KD):
                sq_s = S.op("scalar", lambda e, k=k, cb=cb: e.activation(
                    sqt2, xT_sb[:, k * T + 512 * cb: k * T + 512 * (cb + 1)],
                    AF.Square), waits=[("v", p7_done_v), ("t", S.cnt["t"])], inc="s")
                S.op("tensor", lambda e, k=k, cb=cb: e.matmul(
                    pA[0:1, 512 * cb:512 * (cb + 1)], ones_sb[:, 0:1],
                    xT_sb[:, k * T + 512 * cb: k * T + 512 * (cb + 1)],
                    start=(k == 0), stop=(k == KD - 1)),
                    waits=[("v", p7_done_v)], inc="t")
                S.op("tensor", lambda e, k=k, cb=cb: e.matmul(
                    pB[0:1, 512 * cb:512 * (cb + 1)], ones_sb[:, 0:1], sqt2,
                    start=(k == 0), stop=(k == KD - 1)),
                    waits=[("s", sq_s)], inc="t")
        ln2_t = S.cnt["t"]
        mu2 = ar(A_B4 + 1024, 1024, 1)
        t2r = ar(A_B4 + 2048, 1024, 1)
        r2r = ar(A_B4 + 3072, 1024, 1)
        S.op("vector", lambda e: e.tensor_scalar_mul(mu2, pA[0:1, 0:1024], 1.0 / DM),
             waits=[("t", ln2_t)], inc="v")
        S.op("vector", lambda e: e.tensor_scalar_mul(t2r, pB[0:1, 0:1024], 1.0 / DM),
             inc="v")
        S.op("vector", lambda e: e.tensor_tensor(r2r, mu2, mu2, OP.mult), inc="v")
        S.op("vector", lambda e: e.tensor_sub(t2r, t2r, r2r), inc="v")
        S.op("vector", lambda e: e.tensor_scalar_add(t2r, t2r, EPS), inc="v")
        S.op("vector", lambda e: e.reciprocal(r2r, t2r), inc="v")
        S.op("scalar", lambda e: e.activation(t2r, r2r, AF.Sqrt),
             waits=S.at("v"), inc="s")
        mu2_d = S.dma("scalar", "dbs", ln_dram[1:2, :], mu2, waits=S.at("v", "s"))
        rr2_d = S.dma("scalar", "dbs", ln_dram[2:3, :], t2r, waits=S.at("s"))
        mu_bc2 = ar(A_B2, 1024)
        r_bc2 = ar(A_B2 + 1024, 1024)
        ldmu2 = S.dma("sync", "db", mu_bc2, bass.AP(ln_dram, T, [[0, 128], [1, T]]),
                      waits=[mu2_d, ("v", S.cnt["v"])])
        ldrr2 = S.dma("sync", "db", r_bc2, bass.AP(ln_dram, 2 * T, [[0, 128], [1, T]]),
                      waits=[rr2_d, ("v", S.cnt["v"])])
        for k in range(KD):
            def d1f(e, k=k):
                xsl = xT_sb[:, k * T:(k + 1) * T]
                return e.tensor_tensor(xsl, xsl, mu_bc2, OP.subtract)
            S.op("vector", d1f, waits=[ldmu2, ldrr2], inc="v")

            def d2f(e, k=k):
                xsl = xT_sb[:, k * T:(k + 1) * T]
                return e.tensor_tensor(xsl, xsl, r_bc2, OP.mult)
            S.op("vector", d2f, inc="v")
        fin_v = S.cnt["v"]
        yout_d = S.dma("gpsimd", "dg",
                       bass.AP(yout, 0, [[T, 128], [128 * T, KD], [1, T]]),
                       xT_sb, waits=[("v", fin_v)])

        if "mid" in taps:
            for k in range(KD):
                S.dma("gpsimd", "dg", taps["mid"][128 * k:128 * (k + 1), :],
                      AR[0:128, A_XT + k * T:A_XT + (k + 1) * T],
                      waits=[("v", ln1_v)])
        if "ffnmid" in taps:
            for ft in range(FFT):
                S.dma("gpsimd", "dg", taps["ffnmid"][128 * ft:128 * (ft + 1), :],
                      mid_dram[128 * ft:128 * (ft + 1), :],
                      waits=[("dmid", S.cnt["dmid"])])
        if "acsT" in taps:
            S.dma("gpsimd", "dg", taps["acsT"][:, :], acsT[:, :],
                  waits=[("v", acs_v)])
        if "state" in taps:
            S.dma("gpsimd", "dg", taps["state"][:, :], s_run, waits=S.at("v"))
        if "y" in taps:
            for f in range(DIT):
                S.dma("gpsimd", "dg", taps["y"][128 * f:128 * (f + 1), :],
                      bass.AP(y_dram, 128 * f,
                              [[4096, 128], [524288, 8], [1, 128]]),
                      waits=[("dys", S.cnt["dys"])])
        if "xs" in taps:
            for f in range(32):
                S.dma("gpsimd", "dg", taps["xs"][128 * f:128 * (f + 1), :],
                      bass.AP(xs_dram, 128 * f,
                              [[4096, 128], [524288, 8], [1, 128]]),
                      waits=[patch_g, p1_dsc])
            for f in (32, 33):
                S.dma("gpsimd", "dg", taps["xs"][128 * f:128 * (f + 1), :],
                      BC[:, (f - 32) * T:(f - 31) * T],
                      waits=[("v", patch_v)])
        if "dt" in taps:
            S.dma("gpsimd", "dg", taps["dt"][:, :], dt_sb[:],
                  waits=[("v", dt_ready_v)])

        S.op("gpsimd", lambda e: e.nop(),
             waits=S.at("dg"), inc="g")

        block = E(nc.Block())
        S.emit(block, sems)
    return nc



def _stage(core, ins):
    b, q = core // 4, core % 4
    hs = ins["L0_attn_hidden_states"]
    xTa = np.ascontiguousarray(hs[b, q * T:(q + 1) * T, :].T).reshape(KD, 128, T)
    WT = np.ascontiguousarray(ins["L0_attn_in_proj_w"].T)
    wx = WT[:, DI:DI + 4352]
    w_xbc = np.ascontiguousarray(
        wx.reshape(KD, 128, FX, 128).transpose(2, 1, 0, 3)).reshape(FX, 128, KD * 128)
    wd = WT[:, DI + 4352:]
    w_dt = np.ascontiguousarray(
        wd.reshape(KD, 128, 64).transpose(1, 0, 2)).reshape(128, KD * 64)
    w_gate = np.ascontiguousarray(
        WT[:, :DI].reshape(KD, 128, DIT, 128).transpose(2, 1, 0, 3)).reshape(
        DIT, 128, KD * 128)
    w_out = np.ascontiguousarray(ins["L0_attn_out_proj_w"].T)
    conv_w = np.ascontiguousarray(
        ins["L0_attn_conv_w"].reshape(FX, 128, 4).transpose(1, 0, 2)).reshape(128, FX * 4)
    conv_b = np.ascontiguousarray(ins["L0_attn_conv_b"].reshape(FX, 128).T)
    state_w5 = np.zeros((5, 4), np.float32)
    for j in range(4):
        if j < q:
            for i in range(4):
                state_w5[i, j] = 1.0 if (j < i < q) else 0.0
        else:
            state_w5[4, j] = -1e30
    tail_mask = np.zeros((128, 4), np.float32)
    if q > 0:
        tail_mask[:, q - 1] = 1.0
    perm64 = np.zeros((64, 64), np.float32)
    for r in range(64):
        ph, hf, i = r // 32, (r % 32) // 16, r % 16
        perm64[32 * hf + 2 * i + ph, r] = 1.0
    f32 = np.float32
    return {
        "xT": xTa.astype(f32), "w_xbc": w_xbc.astype(f32), "w_dt": w_dt.astype(f32),
        "w_gate": w_gate.astype(f32), "w_out": w_out.astype(f32),
        "conv_w": conv_w.astype(f32), "conv_b": conv_b.astype(f32),
        "dt_bias": np.ascontiguousarray(ins["L0_attn_dt_bias"].reshape(64, 1)).astype(f32),
        "a_log": np.ascontiguousarray(ins["L0_attn_A_log"].reshape(64, 1)).astype(f32),
        "d_col": np.ascontiguousarray(
            np.repeat(ins["L0_attn_D"].reshape(DIT, 2).T, 64, axis=0)).astype(f32),
        "norm_w": np.ascontiguousarray(ins["L0_attn_norm_w"].reshape(DIT, 128).T).astype(f32),
        "gw_sl": np.ascontiguousarray(
            ins["L0_ffn_gate_w"].T[:, 1024 * core:1024 * (core + 1)]
            .reshape(16, 128, 1024).transpose(1, 0, 2).reshape(128, 16384)
        ).astype(f32),
        "vw_sl": np.ascontiguousarray(
            ins["L0_ffn_val_w"].T[:, 1024 * core:1024 * (core + 1)]
            .reshape(16, 128, 1024).transpose(1, 0, 2).reshape(128, 16384)
        ).astype(f32),
        "ow_sl": np.ascontiguousarray(
            ins["L0_ffn_out_w"].T[1024 * core:1024 * (core + 1), :]
            .reshape(8, 128, 2048).transpose(1, 0, 2).reshape(128, 16384)
        ).astype(f32),
        "gate_b": np.ascontiguousarray(ins["L0_ffn_gate_b"].reshape(FFT, 128).T).astype(f32),
        "val_b": np.ascontiguousarray(ins["L0_ffn_val_b"].reshape(FFT, 128).T).astype(f32),
        "out_b": np.ascontiguousarray(ins["L0_ffn_out_b"].reshape(KD, 128).T).astype(f32),
        "state_w5": state_w5, "tail_mask": tail_mask, "perm64": perm64,
    }


def kernel(**inputs):
    ins = {k: np.asarray(v) for k, v in inputs.items()}
    nc = build_program()
    in_maps = [_stage(c, ins) for c in range(8)]
    res = run_bass_kernel_spmd(nc, in_maps, core_ids=list(range(8)))
    out = np.empty((2, 4096, 2048), np.float32)
    for c in range(8):
        b, q = c // 4, c % 4
        out[b, q * T:(q + 1) * T, :] = res.results[c]["yout"].T
    return out


# version 06
# speedup vs baseline: 1.1197x; 1.0303x over previous
import numpy as np
import ml_dtypes
import concourse.bass as bass
import concourse.mybir as mybir
from concourse.bass_utils import run_bass_kernel_spmd

F32 = mybir.dt.float32
BF16 = mybir.dt.bfloat16
AF = mybir.ActivationFunctionType
OP = mybir.AluOpType
AX = mybir.AxisListType

T = 1024
NCH = 8
CH = 128
KD = 16
DM = 2048
FX = 34
NH = 64
NS = 128
DI = 4096
DIT = 32
FFH = 8192
FFT = 64
MAGIC = 12582912.0
EPS = 1e-5

DEV_PHASES = 99
DEV_TAPS = ()

A_XT = 0
A_B1 = 16384
A_B2 = 20480
A_B3 = 24576
A_B4 = 28672
A_B5 = 32768
A_BC = 36864
A_ACC = 38912
A_XSF = 39936
A_END = 40960
ARENA_N = A_END


class Sched:

    ENGS = ("sync", "tensor", "vector", "scalar", "gpsimd")

    def __init__(self):
        self.ops = {e: [] for e in self.ENGS}
        self.cnt = {}
        self.owner = {}
        self.selfgate = {}

    def sem(self, name, owner=None, selfgate=False):
        self.cnt[name] = 0
        self.owner[name] = owner
        self.selfgate[name] = selfgate

    SERIAL = {"vector": "v", "scalar": "s", "gpsimd": "g"}

    def op(self, eng, fn, waits=(), inc=None, n=1, is_dma=False):
        waits = list(waits)
        own = self.SERIAL.get(eng)
        if own is not None and self.cnt[own] > 0:
            waits.append((own, self.cnt[own]))
        for s, v in waits:
            assert v <= self.cnt[s], f"wait {s}>={v} but only {self.cnt[s]} emitted"
        self.ops[eng].append((fn, tuple(waits), inc, n, is_dma))
        if inc is not None:
            self.cnt[inc] += n
            return self.cnt[inc]
        return None

    def dma(self, eng, sem, out, in_, waits=()):
        def fn(e, out=out, in_=in_):
            return e.dma_start(out=out, in_=in_)
        waits = list(waits)
        if self.selfgate.get(sem):
            waits.append((sem, self.cnt[sem]))
        c = self.op(eng, fn, waits, inc=sem, n=16, is_dma=True)
        return (sem, c)

    def at(self, *names):
        return [(s, self.cnt[s]) for s in names]

    def emit(self, block, sems):
        for eng in self.ENGS:
            ops = self.ops[eng]
            if not ops:
                continue

            def body(e, eng=eng, ops=ops):
                last = {}
                for fn, waits, inc, n, is_dma in ops:
                    for s, v in waits:
                        if v <= 0 or last.get(s, 0) >= v:
                            continue
                        e.wait_ge(sems[s], v)
                        last[s] = v
                    inst = fn(e)
                    if inc is not None:
                        inst.then_inc(sems[inc], n)

            getattr(block, eng)(body)


def build_program():
    nc = bass.Bass("TRN2", target_bir_lowering=False)
    S = Sched()
    for name in ["t", "v", "s", "g"]:
        S.sem(name, owner=name if name != "t" else "tensor")
    S.owner["t"] = "tensor"
    S.owner["v"] = "vector"
    S.owner["s"] = "scalar"
    S.owner["g"] = "gpsimd"
    S.sem("cc")
    for name in ["dc", "dw0", "dw1", "dx", "dm0", "dm1", "dr", "drv", "dmo",
                 "dy", "dy0", "dy1", "dp0", "dp1", "dpm", "dxs", "dys", "dss",
                 "dmid", "dmi", "dyf", "dyf1", "dr2", "drv2", "dmo2"]:
        S.sem(name)
    for name in ["db", "dbs", "dg"]:
        S.sem(name, selfgate=True)

    I = {}
    def inp(name, shape, dt=F32):
        I[name] = nc.dram_tensor(name, shape, dt, kind="ExternalInput")
        return I[name]

    xT = inp("xT", [KD, 128, T])
    w_xbc = inp("w_xbc", [FX, 128, KD * 128])
    w_dt = inp("w_dt", [128, KD * 64])
    w_gate = inp("w_gate", [DIT, 128, KD * 128])
    w_out = inp("w_out", [DI, DM])
    conv_w = inp("conv_w", [128, FX * 4])
    conv_b = inp("conv_b", [128, FX])
    dt_bias = inp("dt_bias", [64, 1])
    a_log = inp("a_log", [64, 1])
    d_col = inp("d_col", [128, DIT])
    norm_w = inp("norm_w", [128, DIT])
    gw_sl = inp("gw_sl", [128, 16384])
    vw_sl = inp("vw_sl", [128, 16384])
    ow_sl = inp("ow_sl", [128, 16384])
    gate_b = inp("gate_b", [128, FFT])
    val_b = inp("val_b", [128, FFT])
    out_b = inp("out_b", [128, KD])
    state_w5 = inp("state_w5", [5, 4])
    tail_mask = inp("tail_mask", [128, 4])
    perm64 = inp("perm64", [64, 64])

    yout = nc.dram_tensor("yout", [DM, T], F32, kind="ExternalOutput")

    xs_dram = nc.dram_tensor("xs_dram", [NCH * 128, DIT * 128], F32)
    y_dram = nc.dram_tensor("y_dram", [NCH * 128, DIT * 128], F32)
    yf_dram = nc.dram_tensor("yf_dram", [DI, T], F32)
    st_dram = nc.dram_tensor("st_dram", [NCH, 128, DI], F32)
    a_dram = nc.dram_tensor("a_dram", [NCH, 64 * 128], F32)
    a4_dram = nc.dram_tensor("a4_dram", [NCH, 64 * 128], F32)
    alast_dram = nc.dram_tensor("alast_dram", [1, 512], F32)
    misc_dram = nc.dram_tensor("misc_dram", [128, 64], F32)
    ln_dram = nc.dram_tensor("ln_dram", [8, T], F32)
    mid_dram = nc.dram_tensor("mid_dram", [FFH, T], F32)
    mi_dram = nc.dram_tensor("mi_dram", [16 * 128, 4096], BF16)
    tail_in = nc.dram_tensor("tail_in", [128, 102], F32)
    tail_out = nc.dram_tensor("tail_out", [512, 102], F32)
    sag_in = nc.dram_tensor("sag_in", [128, DI], F32)
    sag_out = nc.dram_tensor("sag_out", [512, DI], F32)
    asum_in = nc.dram_tensor("asum_in", [64, 1], F32)
    asum_out = nc.dram_tensor("asum_out", [256, 1], F32)
    gam_in = nc.dram_tensor("gam_in", [1, 4], F32)
    gam_out = nc.dram_tensor("gam_out", [1, 4], F32)
    mg_in = nc.dram_tensor("mg_in", [128, 16384], BF16)
    mv_in = nc.dram_tensor("mv_in", [128, 16384], BF16)
    mo_in = nc.dram_tensor("mo_in", [128, 16384], BF16)
    mg_out = nc.dram_tensor("mg_out", [1024, 16384], BF16)
    mv_out = nc.dram_tensor("mv_out", [1024, 16384], BF16)
    mo_out = nc.dram_tensor("mo_out", [1024, 16384], BF16)

    taps = {}
    for tname in DEV_TAPS:
        shp = {"xs": [DI + 256, T], "dt": [64, T], "acsT": [64, T], "y": [DI, T],
               "yf": [DI, T], "mid": [DM, T], "state": [128, DI],
               "ffnmid": [FFH, T]}[tname]
        taps[tname] = nc.dram_tensor("tap_" + tname, shp, F32, kind="ExternalOutput")

    from contextlib import ExitStack
    with ExitStack() as stk:
        E = stk.enter_context
        AR = E(nc.sbuf_tensor("arena", [128, ARENA_N], F32))

        def ar(off, n, rows=128):
            return AR[0:rows, off:off + n]

        wq = [E(nc.sbuf_tensor(f"wq{i}", [128, KD * 128], F32)) for i in range(2)]
        dt_sb = E(nc.sbuf_tensor("dt_sb", [64, T], F32))
        acsT = E(nc.sbuf_tensor("acsT", [64, T], F32))
        dt_tm = E(nc.sbuf_tensor("dt_tm", [128, NCH * 64], F32))
        adt_tm = E(nc.sbuf_tensor("adt_tm", [128, NCH * 64], F32))
        acs_col = E(nc.sbuf_tensor("acs_col", [128, NCH * 64], F32))
        alast_sb = E(nc.sbuf_tensor("alast_sb", [64, NCH + 8], F32))
        ds_all = E(nc.sbuf_tensor("ds_all", [128, NCH * 64], F32))
        dch_rep = E(nc.sbuf_tensor("dch_rep", [128, NCH * 64], F32))
        arep = E(nc.sbuf_tensor("arep", [128, 64], F32))
        dcol = E(nc.sbuf_tensor("dcol", [128, DIT], F32))
        nw_col = E(nc.sbuf_tensor("nw_col", [128, DIT], F32))
        cw_sb = E(nc.sbuf_tensor("cw_sb", [128, FX * 4], F32))
        cb_sb = E(nc.sbuf_tensor("cb_sb", [128, FX], F32))
        ones_sb = E(nc.sbuf_tensor("ones_sb", [128, 128], F32))
        u01 = E(nc.sbuf_tensor("u01", [128, 128], F32))
        ident = E(nc.sbuf_tensor("ident", [128, 128], F32))
        gu = E(nc.sbuf_tensor("gu", [128, 128], F32))
        accfix = E(nc.sbuf_tensor("accfix", [128, 102], F32))
        tail_sb = E(nc.sbuf_tensor("tail_sb", [128, 102], F32))
        tailprev = E(nc.sbuf_tensor("tailprev", [128, 102], F32))
        xsfix = E(nc.sbuf_tensor("xsfix", [128, 102], F32))
        tmask_sb = E(nc.sbuf_tensor("tmask_sb", [128, 4], F32))
        fb_sb = E(nc.sbuf_tensor("fb_sb", [128, 144], F32))
        small = E(nc.sbuf_tensor("small", [128, 64], F32))
        pm_sb = E(nc.sbuf_tensor("pm_sb", [64, 64], F32))
        ap64 = E(nc.sbuf_tensor("ap64", [64, T], F32))
        pA = E(nc.psum_tensor("pA", [128, 1024], F32))
        pB = E(nc.psum_tensor("pB", [128, 1024], F32))
        pC = E(nc.psum_tensor("pC", [128, 1024], F32))
        pD = E(nc.psum_tensor("pD", [128, 1024], F32))

        sems = {name: E(nc.semaphore(name)) for name in S.cnt}

        xT_sb = ar(A_XT, KD * T)
        acc = ar(A_ACC, T)
        xs_f = ar(A_XSF, T)
        BC = ar(A_BC, 2 * T)

        ld0 = S.dma("sync", "dc", xT_sb, bass.AP(xT, 0, [[T, 128], [128 * T, KD], [1, T]]))
        S.dma("sync", "dc", cw_sb[:], conv_w[:, :])
        S.dma("sync", "dc", cb_sb[:], conv_b[:, :])
        S.dma("sync", "dc", tmask_sb[:], tail_mask[:, :])
        S.dma("sync", "dc", small[0:64, 0:1], dt_bias[:, :])
        S.dma("sync", "dc", small[0:64, 1:2], a_log[:, :])
        S.dma("sync", "dc", nw_col[:], norm_w[:, :])
        S.dma("sync", "dc", fb_sb[:, 0:64], gate_b[:, :])
        S.dma("sync", "dc", fb_sb[:, 64:128], val_b[:, :])
        S.dma("sync", "dc", fb_sb[:, 128:144], out_b[:, :])
        S.dma("sync", "dc", pm_sb[:], perm64[:, :])
        ld_const = ("dc", S.cnt["dc"])
        S.op("vector", lambda e: e.memset(ones_sb[:], 1.0), inc="v")
        S.op("gpsimd", lambda e: e.affine_select(
            u01[:], ones_sb[:], [[1, 128]], OP.is_ge, 0.0, base=0,
            channel_multiplier=-1), waits=S.at("v"), inc="g")
        S.op("gpsimd", lambda e: e.affine_select(
            ident[:], ones_sb[:], [[1, 128]], OP.is_equal, 0.0, base=0,
            channel_multiplier=-1), waits=S.at("v"), inc="g")
        g_const = S.cnt["g"]
        S.op("scalar", lambda e: e.activation(small[0:64, 4:5], small[0:64, 1:2],
                                              AF.Exp),
             waits=[ld_const], inc="s")
        S.op("vector", lambda e: e.tensor_scalar_mul(small[0:64, 5:6],
                                                     small[0:64, 4:5], -1.0),
             waits=S.at("s"), inc="v")
        S.dma("scalar", "dbs", misc_dram[0:1, 0:64], small[0:64, 5:6], waits=S.at("v"))
        bounce1 = ("dbs", S.cnt["dbs"])
        S.dma("sync", "dc", arep[:, 0:64],
              bass.AP(misc_dram, 0, [[0, 128], [1, 64]]),
              waits=[bounce1])
        S.dma("sync", "dc", dcol[:], I["d_col"][:, :])

        dc_all = ("dc", S.cnt["dc"])
        wacc = E(nc.sbuf_tensor("wacc", [128, 8], F32))
        grow = E(nc.sbuf_tensor("grow", [1, 8], F32))
        rgc = E(nc.sbuf_tensor("rgc", [128, 4], F32))
        p0b = [ar(A_B1, 2048), ar(A_B1 + 2048, 2048)]
        p0t = ar(A_B2, 2048)
        p0m_bf = ar(A_B2 + 2048, 1024).bitcast(BF16)
        subs = []
        for mat, src, dst in [(0, gw_sl, mg_in), (1, vw_sl, mv_in), (2, ow_sl, mo_in)]:
            for r in range(src.shape[0] // 128):
                for c in range(src.shape[1] // 2048):
                    subs.append((mat, src, dst, r, c))
        S.op("gpsimd", lambda e: e.memset(wacc[:, 0:6], 0.0), waits=[dc_all],
             inc="g")
        p0_consumed = [0, 0]
        nsub = [0, 0, 0]
        for i, (mat, src, dst, r, c) in enumerate(subs):
            bb = p0b[i % 2]
            ld = S.dma("sync", "dp0" if i % 2 == 0 else "dp1", bb,
                       src[128 * r:128 * (r + 1), 2048 * c:2048 * (c + 1)],
                       waits=[("s", p0_consumed[i % 2])])
            S.op("scalar", lambda e, bb=bb: e.activation(
                p0t, bb, AF.Abs, accum_out=wacc[:, 6:7]),
                waits=[ld, dc_all, ("g", S.cnt["g"])], inc="s")
            par = nsub[mat] % 2
            S.op("gpsimd", lambda e, mat=mat, par=par: e.tensor_add(
                wacc[:, 3 * (1 - par) + mat:3 * (1 - par) + mat + 1],
                wacc[:, 3 * par + mat:3 * par + mat + 1], wacc[:, 6:7]),
                waits=S.at("s"), inc="g")
            nsub[mat] += 1
            p0_consumed[i % 2] = S.cnt["s"]
        S.op("tensor", lambda e: e.matmul(pC[0:1, 0:4], ones_sb[:, 0:1],
                                          wacc[:, 0:4], start=True, stop=True),
             waits=S.at("g"), inc="t")
        S.op("vector", lambda e: e.tensor_copy(grow[:, 0:4], pC[0:1, 0:4]),
             waits=S.at("t"), inc="v")
        p0a_dma = [("dp0", S.cnt["dp0"]), ("dp1", S.cnt["dp1"])]

        wq_free = [0, 0]
        ps_free = [0, 0]
        last_silu = 0
        last_spill = ("dxs", 0)
        dt_ready_v = 0

        for f in range(FX + 1):
            buf = f % 2
            wsl = wq[buf]
            dwn = "dw0" if buf == 0 else "dw1"
            if f < FX:
                ldw = S.dma("sync", dwn, wsl[:], w_xbc[f, :, :],
                            waits=[("t", wq_free[buf])])
            else:
                ldw = S.dma("sync", dwn, wsl[:, 0:KD * 64], w_dt[:, :],
                            waits=[("t", wq_free[buf])])
            ps = [pA, pB][buf]
            rows = 128 if f < FX else 64

            def mm(e, ps=ps, wsl=wsl, rows=rows):
                last = None
                for cb in range(2):
                    for k in range(KD):
                        last = e.matmul(
                            ps[0:rows, cb * 512:(cb + 1) * 512],
                            wsl[:, k * rows:(k + 1) * rows],
                            xT_sb[:, k * T + cb * 512: k * T + (cb + 1) * 512],
                            start=(k == 0), stop=(k == KD - 1))
                return last
            tcnt = S.op("tensor", mm,
                        waits=[ldw, dc_all, ("v", ps_free[buf])],
                        inc="t")
            wq_free[buf] = tcnt

            if f < FX:
                pp = [acc, ar(A_B4, T)]

                def c0(e, ps=ps, f=f):
                    return e.tensor_scalar(pp[0], ps[:],
                                           cw_sb[:, 4 * f + 3:4 * f + 4],
                                           None, OP.mult)
                S.op("vector", c0, waits=[("t", tcnt), ("s", last_silu)], inc="v")
                for k in (2, 1, 0):
                    lo = 3 - k
                    src, dst2 = pp[(2 - k) % 2], pp[(3 - k) % 2]

                    def ck(e, ps=ps, f=f, k=k, lo=lo, src=src, dst2=dst2):
                        return e.scalar_tensor_tensor(
                            dst2[:, lo:T], ps[:, 0:T - lo],
                            cw_sb[:, 4 * f + k:4 * f + k + 1],
                            src[:, lo:T], OP.mult, OP.add)
                    S.op("vector", ck, inc="v")

                    def hd(e, lo=lo, src=src, dst2=dst2):
                        return e.tensor_copy(dst2[:, 0:lo], src[:, 0:lo])
                    S.op("vector", hd, inc="v")
                S.op("vector", lambda e, f=f: e.tensor_copy(
                    accfix[:, 3 * f:3 * f + 3], pp[1][:, 0:3]), inc="v")
                S.op("vector", lambda e, ps=ps, f=f: e.tensor_copy(
                    tail_sb[:, 3 * f:3 * f + 3], ps[:, T - 3:T]), inc="v")
                ps_free[buf] = S.cnt["v"]
                dst = xs_f if f < 32 else BC[:, (f - 32) * T:(f - 31) * T]

                def si(e, dst=dst, f=f):
                    return e.activation(dst, pp[1], AF.Silu, bias=cb_sb[:, f:f + 1])
                last_silu = S.op("scalar", si,
                                 waits=[("v", S.cnt["v"]), last_spill], inc="s")
                if f < 32:
                    last_spill = S.dma("scalar", "dxs",
                                       bass.AP(xs_dram, 128 * f,
                                               [[4096, 128], [524288, 8],
                                                [1, 128]]),
                                       xs_f, waits=[("s", last_silu)])
            else:
                def d1(e, ps=ps):
                    return e.tensor_scalar(acc[0:64, :], ps[0:64, :],
                                           small[0:64, 0:1], None, OP.add)
                S.op("vector", d1, waits=[("t", tcnt), ("s", last_silu)]
                     + p0a_dma, inc="v")
                ps_free[buf] = S.cnt["v"]
                S.op("scalar", lambda e: e.activation(
                    ar(A_XSF, T, 64), acc[0:64, :], AF.Abs),
                    waits=[("v", S.cnt["v"]), last_spill], inc="s")
                S.op("scalar", lambda e: e.activation(
                    ar(A_B1, T, 64), ar(A_XSF, T, 64), AF.Exp, scale=-1.0), inc="s")
                S.op("vector", lambda e: e.tensor_scalar_max(
                    ar(A_B2 + 2048, T, 64), acc[0:64, :], 0.0),
                    waits=S.at("s"), inc="v")
                S.op("vector", lambda e: e.tensor_scalar_add(
                    ar(A_B2, T, 64), ar(A_B1, T, 64), 1.0), inc="v")
                S.op("scalar", lambda e: e.activation(
                    ar(A_B1, T, 64), ar(A_B2, T, 64), AF.Ln),
                    waits=S.at("v"), inc="s")
                S.op("vector", lambda e: e.tensor_add(
                    dt_sb[:], ar(A_B2 + 2048, T, 64), ar(A_B1, T, 64)),
                    waits=S.at("s"), inc="v")
                dt_ready_v = S.cnt["v"]

        p1_dsc = ("dxs", S.cnt["dxs"])

        gsum_d = S.dma("gpsimd", "dg", gam_in[:, :], grow[0:1, 0:4], waits=S.at("v"))

        def cc_gam(e):
            return e.collective_compute(
                "AllReduce", OP.add, replica_groups=[list(range(8))],
                ins=[gam_in.ap().opt()], outs=[gam_out.ap().opt()])
        S.op("gpsimd", cc_gam, waits=[gsum_d], inc="cc")
        cc_gam_n = S.cnt["cc"]
        ldg = S.dma("sync", "db", grow[0:1, 4:8], gam_out[:, :], waits=[("cc", cc_gam_n)])
        S.op("vector", lambda e: e.tensor_scalar(
            grow[0:1, 0:4], grow[0:1, 4:8], 1.0 / 16777216.0, 1e-5, OP.mult, OP.max),
            waits=[ldg], inc="v")
        S.op("vector", lambda e: e.reciprocal(grow[0:1, 4:8], grow[0:1, 0:4]),
             inc="v")
        S.dma("scalar", "dbs", misc_dram[6:7, 0:4], grow[0:1, 0:4], waits=S.at("v"))
        gam_bd = S.dma("scalar", "dbs", misc_dram[5:6, 0:4], grow[0:1, 4:8],
                       waits=S.at("v"))
        ldrg = S.dma("sync", "db", rgc[:, 0:4],
                     bass.AP(misc_dram, 5 * 64, [[0, 128], [1, 4]]),
                     waits=[gam_bd])
        p0_consumed = [S.cnt["v"], S.cnt["v"]]
        mdone = ("dpm", 0)
        for i, (mat, src, dst, r, c) in enumerate(subs):
            bb = p0b[i % 2]
            ld = S.dma("sync", "dp0" if i % 2 == 0 else "dp1", bb,
                       src[128 * r:128 * (r + 1), 2048 * c:2048 * (c + 1)],
                       waits=[("v", p0_consumed[i % 2]), ldrg])
            S.op("vector", lambda e, bb=bb, mat=mat: e.tensor_scalar(
                p0t, bb, rgc[:, mat:mat + 1], MAGIC, OP.mult, OP.add),
                waits=[ld, mdone], inc="v")

            def bfap(off, freedims):
                b = p0m_bf
                return bass.AP(b.tensor, b.offset + off,
                               [list(b.ap[0])] + freedims)
            for hh in range(2):
                S.op("vector", lambda e, hh=hh: e.tensor_scalar(
                    ar(A_B2 + 3072, 1024), p0t[:, 1024 * hh:1024 * (hh + 1)],
                    MAGIC, -1.0, OP.subtract, OP.max), inc="v")
                if mat < 2:
                    mo_ap = bfap(256 * hh, [[512, 4], [1, 256]])
                else:
                    mo_ap = p0m_bf[:, 1024 * hh:1024 * (hh + 1)]
                S.op("vector", lambda e, mo_ap=mo_ap: e.tensor_scalar_min(
                    mo_ap, ar(A_B2 + 3072, 1024), 1.0), inc="v")
            p0_consumed[i % 2] = S.cnt["v"]
            if mat < 2:
                dst_ap = bass.AP(dst, 512 * c, [[16384, 128], [4096, 4], [1, 512]])
            else:
                dst_ap = bass.AP(dst, 1024 * c, [[16384, 128], [8192, 2], [1, 1024]])
            mdone = S.dma("scalar", "dpm", dst_ap,
                          p0m_bf[:, 0:2048], waits=S.at("v"))

        p0b_dma = [mdone, ("dp0", S.cnt["dp0"]), ("dp1", S.cnt["dp1"])]

        S.dma("gpsimd", "dg", tail_in[:, :], tail_sb[:], waits=S.at("v"))

        def cc_tail(e):
            return e.collective_compute(
                "AllGather", OP.bypass, replica_groups=[[0, 1, 2, 3], [4, 5, 6, 7]],
                ins=[tail_in.ap().opt()], outs=[tail_out.ap().opt()])
        S.op("gpsimd", cc_tail, waits=S.at("dg"), inc="cc")
        cc_tail_n = S.cnt["cc"]
        S.op("vector", lambda e: e.memset(tailprev[:], 0.0), inc="v")
        tpp = [tailprev, tail_sb]
        for j in range(4):
            ldb = S.dma("sync", "db", xsfix[:], tail_out[128 * j:128 * (j + 1), :],
                        waits=[("cc", cc_tail_n), ("v", S.cnt["v"])])

            def addb(e, j=j):
                return e.scalar_tensor_tensor(tpp[(j + 1) % 2][:], xsfix[:],
                                              tmask_sb[:, j:j + 1], tpp[j % 2][:],
                                              OP.mult, OP.add)
            S.op("vector", addb, waits=[ldb], inc="v")
        for f in range(FX):
            S.op("vector", lambda e, f=f: e.scalar_tensor_tensor(
                xsfix[:, 3 * f:3 * f + 3], tail_sb[:, 3 * f:3 * f + 3],
                cw_sb[:, 4 * f + 0:4 * f + 1], accfix[:, 3 * f:3 * f + 3],
                OP.mult, OP.add), inc="v")
            S.op("vector", lambda e, f=f: e.scalar_tensor_tensor(
                tailprev[:, 3 * f:3 * f + 2], tail_sb[:, 3 * f + 1:3 * f + 3],
                cw_sb[:, 4 * f + 1:4 * f + 2], xsfix[:, 3 * f:3 * f + 2],
                OP.mult, OP.add), inc="v")
            S.op("vector", lambda e, f=f: e.tensor_copy(
                tailprev[:, 3 * f + 2:3 * f + 3], xsfix[:, 3 * f + 2:3 * f + 3]),
                inc="v")
            S.op("vector", lambda e, f=f: e.scalar_tensor_tensor(
                xsfix[:, 3 * f:3 * f + 1], tail_sb[:, 3 * f + 2:3 * f + 3],
                cw_sb[:, 4 * f + 2:4 * f + 3], tailprev[:, 3 * f:3 * f + 1],
                OP.mult, OP.add), inc="v")
            S.op("vector", lambda e, f=f: e.tensor_copy(
                xsfix[:, 3 * f + 1:3 * f + 3], tailprev[:, 3 * f + 1:3 * f + 3]),
                inc="v")

            def sif(e, f=f):
                return e.activation(tail_sb[:, 3 * f:3 * f + 3],
                                    xsfix[:, 3 * f:3 * f + 3],
                                    AF.Silu, bias=cb_sb[:, f:f + 1])
            S.op("scalar", sif, waits=S.at("v"), inc="s")
        fix_s = S.cnt["s"]
        for f in range(32):
            S.dma("gpsimd", "dg",
                  bass.AP(xs_dram, 128 * f, [[4096, 128], [1, 3]]),
                  tail_sb[:, 3 * f:3 * f + 3],
                  waits=[("s", fix_s), p1_dsc])
        for f in (32, 33):
            S.op("vector", lambda e, f=f: e.tensor_copy(
                BC[:, (f - 32) * T:(f - 32) * T + 3], tail_sb[:, 3 * f:3 * f + 3]),
                waits=[("s", fix_s)], inc="v")
        patch_g = ("dg", S.cnt["dg"])
        patch_v = S.cnt["v"]

        btm = E(nc.sbuf_tensor("btm", [128, 128], F32))
        facr = E(nc.sbuf_tensor("facr", [128, 64], F32))
        asw = E(nc.sbuf_tensor("asw", [5, 64], F32))
        w5_sb = E(nc.sbuf_tensor("w5_sb", [5, 4], F32))
        alr = E(nc.sbuf_tensor("alr", [128, 64], F32))
        asum_sb = E(nc.sbuf_tensor("asum_sb", [64, 1], F32))

        def mkap(x, off, freedims, parts=None):
            b = x if isinstance(x, bass.AP) else x[:]
            pd = list(b.ap[0])
            if parts is not None:
                pd = [pd[0], parts]
            return bass.AP(b.tensor, b.offset + off, [pd] + [list(d) for d in freedims])

        for m in range(NCH):
            S.op("tensor", lambda e, m=m: e.transpose(
                pC[0:128, 64 * m:64 * (m + 1)], dt_sb[:, 128 * m:128 * (m + 1)],
                ident[0:64, 0:64]),
                waits=[("v", dt_ready_v), ("g", g_const)], inc="t")
        S.op("vector", lambda e: e.tensor_copy(dt_tm[:, 0:512], pC[:, 0:512]),
             waits=S.at("t"), inc="v")
        S.op("vector", lambda e: e.tensor_tensor(
            adt_tm[:, 0:512], dt_tm[:, 0:512],
            mkap(arep, 0, [[0, NCH], [1, 64]]), OP.mult),
            waits=S.at("dc"), inc="v")
        adt_v = S.cnt["v"]
        for m in range(NCH):
            S.op("tensor", lambda e, m=m: e.matmul(
                pD[0:64, 128 * m:128 * (m + 1)], adt_tm[:, 64 * m:64 * (m + 1)],
                u01[:], start=True, stop=True), waits=[("v", adt_v)], inc="t")
            S.op("tensor", lambda e, m=m: e.matmul(
                pC[0:128, 512 + 64 * m:512 + 64 * (m + 1)], u01[:],
                adt_tm[:, 64 * m:64 * (m + 1)], start=True, stop=True), inc="t")
        acs_t = S.cnt["t"]
        S.op("vector", lambda e: e.tensor_copy(acsT[:, :], pD[0:64, 0:1024]),
             waits=[("t", acs_t)], inc="v")
        S.op("vector", lambda e: e.tensor_copy(acs_col[:, :], pC[:, 512:1024]),
             inc="v")
        acs_v = S.cnt["v"]
        acsT_d = S.dma("scalar", "dbs",
                       bass.AP(a_dram, 0, [[128, 64], [8192, 8], [1, 128]]),
                       acsT[:, :], waits=[("v", acs_v)])
        for cb2 in range(2):
            S.op("tensor", lambda e, cb2=cb2: e.matmul(
                pD[0:64, 512 * cb2:512 * (cb2 + 1)], pm_sb[:],
                acsT[:, 512 * cb2:512 * (cb2 + 1)], start=True, stop=True),
                waits=([("v", acs_v), dc_all] if cb2 == 0 else ()), inc="t")
        S.op("vector", lambda e: e.tensor_copy(ap64[:, :], pD[0:64, 0:1024]),
             waits=S.at("t"), inc="v")
        a4_d = S.dma("scalar", "dbs",
                     bass.AP(a4_dram, 0, [[128, 64], [8192, 8], [1, 128]]),
                     ap64[:, :], waits=[("v", S.cnt["v"])])
        alast_d = S.dma("scalar", "dbs", alast_dram[0:1, 0:512], acs_col[127:128, :],
                        waits=[("v", acs_v)])
        S.op("vector", lambda e: e.reduce_sum(
            asum_sb[:, 0:1], mkap(acsT, 127, [[128, NCH]], parts=64),
            axis=AX.X), waits=[("v", acs_v)], inc="v")
        asum_v = S.cnt["v"]
        for m in range(NCH):
            lda = S.dma("sync", "db", alr[:, 0:64],
                        bass.AP(alast_dram, 64 * m, [[0, 128], [1, 64]]),
                        waits=[alast_d, ("v", S.cnt["v"]),
                               ("s", S.cnt["s"])])
            S.op("vector", lambda e, m=m: e.tensor_sub(
                adt_tm[:, 64 * m:64 * (m + 1)], alr[:, 0:64],
                acs_col[:, 64 * m:64 * (m + 1)]), waits=[lda], inc="v")
            S.op("scalar", lambda e, m=m: e.activation(
                dch_rep[:, 64 * m:64 * (m + 1)], alr[:, 0:64], AF.Exp),
                waits=[lda], inc="s")
        S.op("scalar", lambda e: e.activation(
            ds_all[:, :], adt_tm[:, :], AF.Exp), waits=S.at("v"), inc="s")
        apath_s = S.cnt["s"]

        xs_ch = ar(A_B1, DI)
        xd = ar(A_B2, DI)
        y_sb = ar(A_B4, DI)
        s_run = ar(A_B5, DI)
        repA = ar(A_B3, 2048)
        repB = ar(A_B3 + 2048, 2048)
        S.op("vector", lambda e: e.memset(s_run, 0.0), inc="v")

        st_spill = [("dss", 0)]
        p2_y_d = []
        prev_t = 0
        for m in range(NCH):
            xw = [p1_dsc, ("t", prev_t), ("v", S.cnt["v"])]
            if m == 0:
                xw.append(patch_g)
                xw += p0b_dma
            ldx = S.dma("sync", "dx", xs_ch,
                        bass.AP(xs_dram, 524288 * m, [[4096, 128], [1, 4096]]),
                        waits=xw)
            BTm = BC[:, 128 * m:128 * (m + 1)]
            CTm = BC[:, T + 128 * m:T + 128 * (m + 1)]
            bcw = [("v", patch_v)] if m == 0 else []
            gt = S.op("tensor", lambda e, BTm=BTm, CTm=CTm: e.matmul(
                pD[:, 0:128], BTm, CTm, start=True, stop=True),
                waits=bcw + [("v", S.cnt["v"])], inc="t")
            bt = S.op("tensor", lambda e, BTm=BTm: e.transpose(
                pD[:, 128:256], BTm, ident[:]), inc="t")
            S.op("vector", lambda e: e.tensor_tensor(gu[:], pD[:, 0:128], u01[:],
                                                     OP.mult),
                 waits=[("t", bt)], inc="v")
            S.op("vector", lambda e: e.tensor_copy(btm[:], pD[:, 128:256]), inc="v")

            for hf in range(2):
                tws = [ldx, ("v", S.cnt["v"])]
                for i in range(16):
                    ps = pA if i < 8 else pB
                    S.op("tensor", lambda e, i=i, hf=hf, ps=ps: e.transpose(
                        ps[:, 128 * (i % 8):128 * (i % 8 + 1)],
                        xs_ch[:, 2048 * hf + 128 * i: 2048 * hf + 128 * (i + 1)],
                        ident[:]), waits=tws if i == 0 else (), inc="t")
                txd = S.cnt["t"]
                for h2 in range(2):
                    ps = [pA, pB][h2]

                    def evx(e, ps=ps, hf=hf, h2=h2, m=m):
                        in1 = mkap(dt_tm, 64 * m + 32 * hf + 16 * h2,
                                   [[2, 8], [1, 2], [0, 64]])
                        return e.tensor_tensor(
                            xd[:, 2048 * hf + 1024 * h2:2048 * hf + 1024 * (h2 + 1)],
                            ps[:], in1, OP.mult)
                    S.op("vector", evx, waits=[("t", txd)], inc="v")
                xd_v = S.cnt["v"]

                for qr in range(2):
                    hq = 32 * hf + 16 * qr
                    ldr = S.dma("sync", "dr", repA,
                                bass.AP(a_dram, 8192 * m + 128 * hq,
                                        [[0, 128], [1, 2048]]),
                                waits=[acsT_d, st_spill[-1],
                                       ("t", S.cnt["t"]), ("v", S.cnt["v"])])

                    def mtsub(e, hq=hq, m=m):
                        in1 = mkap(acs_col, 64 * m + hq, [[1, 16], [0, 128]])
                        return e.tensor_tensor(repB, repA, in1, OP.subtract)
                    S.op("vector", mtsub,
                         waits=[ldr, ("v", acs_v), st_spill[-1]], inc="v")
                    S.op("vector", lambda e: e.tensor_scalar_min(repA, repB, 0.0),
                         inc="v")
                    S.op("scalar", lambda e: e.activation(repB, repA, AF.Exp),
                         waits=S.at("v"), inc="s")

                    def mtgu(e):
                        in1 = mkap(gu, 0, [[0, 16], [1, 128]])
                        return e.tensor_tensor(repA, repB, in1, OP.mult)
                    S.op("vector", mtgu, waits=S.at("s"), inc="v")
                    mt_v = S.cnt["v"]

                    psq = [pC, pD][qr]
                    for hh in range(16):
                        h = hq + hh
                        slot = hh // 2
                        base = 64 * (h % 2)

                        def ymm(e, h=h, hh=hh, psq=psq, slot=slot, base=base):
                            return e.matmul(
                                psq[base:base + 64, 128 * slot:128 * (slot + 1)],
                                xd[:, 64 * h:64 * (h + 1)],
                                repA[:, 128 * hh:128 * (hh + 1)],
                                start=True, stop=True)
                        S.op("tensor", ymm,
                             waits=[("v", mt_v)] if hh == 0 else (), inc="t")
                ydiag_t = S.cnt["t"]

                def dtmp(e, hf=hf):
                    in1 = mkap(dcol, 16 * hf, [[1, 16], [0, 128]])
                    return e.tensor_tensor(repB, xs_ch[:, 2048 * hf:2048 * (hf + 1)],
                                           in1, OP.mult)
                S.op("vector", dtmp, waits=[("t", ydiag_t),
                                            ("dys", S.cnt["dys"])], inc="v")
                for h2 in range(2):
                    ps = [pC, pD][h2]

                    def yev(e, ps=ps, hf=hf, h2=h2):
                        return e.tensor_tensor(
                            y_sb[:, 2048 * hf + 1024 * h2:2048 * hf + 1024 * (h2 + 1)],
                            repB[:, 1024 * h2:1024 * (h2 + 1)], ps[:], OP.add)
                    S.op("vector", yev, inc="v")
                yv = S.cnt["v"]
                p2_y_d.append(S.dma(
                    "scalar", "dys",
                    bass.AP(y_dram, 524288 * m + 2048 * hf,
                            [[4096, 128], [1, 2048]]),
                    y_sb[:, 2048 * hf:2048 * (hf + 1)], waits=[("v", yv)]))

                def xd2f(e, hf=hf, m=m):
                    in1 = mkap(ds_all, 64 * m + 32 * hf, [[1, 32], [0, 64]])
                    return e.tensor_tensor(repB, xd[:, 2048 * hf:2048 * (hf + 1)],
                                           in1, OP.mult)
                S.op("vector", xd2f, waits=[("s", apath_s)], inc="v")
                xd2_v = S.cnt["v"]
                for i in range(4):
                    ps = [pA, pB][i // 2]

                    def smm(e, i=i, ps=ps):
                        return e.matmul(ps[:, 512 * (i % 2):512 * (i % 2 + 1)],
                                        btm[:], repB[:, 512 * i:512 * (i + 1)],
                                        start=True, stop=True)
                    S.op("tensor", smm,
                         waits=[("v", xd2_v)] if i == 0 else (), inc="t")
                prev_t = S.cnt["t"]
                for h2 in range(2):
                    ps = [pA, pB][h2]
                    S.op("vector", lambda e, ps=ps, h2=h2: e.tensor_copy(
                        repB[:, 1024 * h2:1024 * (h2 + 1)], ps[:]),
                        waits=[("t", prev_t)], inc="v")
                stv = S.cnt["v"]
                st_spill.append(S.dma(
                    "scalar", "dss", st_dram[m, :, 2048 * hf:2048 * (hf + 1)], repB,
                    waits=[("v", stv)]))
                tmpv = xd[:, 2048 * hf:2048 * (hf + 1)]

                def sdec(e, hf=hf, m=m, tmpv=tmpv):
                    in1 = mkap(dch_rep, 64 * m + 32 * hf, [[1, 32], [0, 64]])
                    return e.tensor_tensor(tmpv, s_run[:, 2048 * hf:2048 * (hf + 1)],
                                           in1, OP.mult)
                S.op("vector", sdec, waits=[("s", apath_s)], inc="v")
                S.op("vector", lambda e, hf=hf, tmpv=tmpv: e.tensor_add(
                    s_run[:, 2048 * hf:2048 * (hf + 1)], tmpv, repB), inc="v")

        sloc_d = S.dma("gpsimd", "dg", sag_in[:, :], s_run, waits=S.at("v"))
        asum_d = S.dma("gpsimd", "dg", asum_in[:, :], asum_sb[:, 0:1],
                       waits=[("v", asum_v)])

        def cc_st(e):
            return e.collective_compute(
                "AllGather", OP.bypass, replica_groups=[[0, 1, 2, 3], [4, 5, 6, 7]],
                ins=[sag_in.ap().opt()], outs=[sag_out.ap().opt()])
        S.op("gpsimd", cc_st, waits=[sloc_d], inc="cc")

        def cc_as(e):
            return e.collective_compute(
                "AllGather", OP.bypass, replica_groups=[[0, 1, 2, 3], [4, 5, 6, 7]],
                ins=[asum_in.ap().opt()], outs=[asum_out.ap().opt()])
        S.op("gpsimd", cc_as, waits=[asum_d], inc="cc")
        cc_state_n = S.cnt["cc"]
        def cc_ag(name_in, name_out):
            def f(e, name_in=name_in, name_out=name_out):
                return e.collective_compute(
                    "AllGather", OP.bypass, replica_groups=[list(range(8))],
                    ins=[name_in.ap().opt()], outs=[name_out.ap().opt()])
            return f
        S.op("gpsimd", cc_ag(mg_in, mg_out), waits=[mdone], inc="cc")
        S.op("gpsimd", cc_ag(mv_in, mv_out), inc="cc")
        S.op("gpsimd", cc_ag(mo_in, mo_out), inc="cc")
        cc_m_n = S.cnt["cc"]
        ld1r = S.dma("sync", "db", asw[4:5, :], ones_sb[0:1, 0:64],
                     waits=S.at("v"))
        ld5 = S.dma("sync", "db", asw[0:4, :],
                    bass.AP(asum_out, 0, [[64, 4], [1, 64]]),
                    waits=[("cc", cc_state_n)])
        ldw5 = S.dma("sync", "db", w5_sb[:, :], state_w5[:, :])
        fac_t = S.op("tensor", lambda e: e.matmul(
            pD[0:4, 0:64], w5_sb[:, :], asw[:, :], start=True, stop=True),
            waits=[ld5, ldw5, ld1r, ("v", S.cnt["v"])], inc="t")
        S.op("scalar", lambda e: e.activation(facr[0:4, 0:64], pD[0:4, 0:64],
                                              AF.Exp),
             waits=[("t", fac_t)], inc="s")
        facd = S.dma("scalar", "dbs", misc_dram[1:5, 0:64], facr[0:4, 0:64],
                     waits=S.at("s"))
        spp = [s_run, ar(A_B4, DI)]
        for j in range(4):
            ldb = S.dma("sync", "dx", xs_ch, sag_out[128 * j:128 * (j + 1), :],
                        waits=[("cc", cc_state_n), ("t", S.cnt["t"]),
                               ("v", S.cnt["v"]), sloc_d])
            ldf = S.dma("sync", "db", facr[:, 0:64],
                        bass.AP(misc_dram, 64 * (1 + j), [[0, 128], [1, 64]]),
                        waits=[facd, ("v", S.cnt["v"])])
            if j == 0:
                def sacc0(e):
                    in1 = mkap(facr, 0, [[1, 64], [0, 64]])
                    return e.tensor_tensor(spp[0], xs_ch, in1, OP.mult)
                S.op("vector", sacc0,
                     waits=[ldb, ldf, ("dys", S.cnt["dys"])], inc="v")
            else:
                def sacc(e):
                    in1 = mkap(facr, 0, [[1, 64], [0, 64]])
                    return e.tensor_tensor(xd, xs_ch, in1, OP.mult)
                S.op("vector", sacc, waits=[ldb, ldf], inc="v")
                S.op("vector", lambda e, j=j: e.tensor_add(
                    spp[j % 2], spp[(j + 1) % 2], xd), inc="v")
        S.op("vector", lambda e: e.tensor_copy(s_run, spp[1]), inc="v")
        sin_v = S.cnt["v"]

        hbuf = lambda base, p: ar(base + 2048 * p, 2048)
        ACX = ar(A_ACC, 2048)
        p4_y_d = [("dys", S.cnt["dys"])]
        dss_all = ("dss", S.cnt["dss"])
        yfree = [S.cnt["v"], S.cnt["v"]]
        stfree = [S.cnt["v"], S.cnt["v"]]
        erfree = [S.cnt["v"], S.cnt["v"]]
        erspill = [("dys", 0), ("dys", 0)]
        eefree = [S.cnt["v"], S.cnt["v"]]
        supd = [sin_v, sin_v]
        psum_free = [S.cnt["v"], S.cnt["v"]]
        for m in range(NCH):
            for hf in range(2):
                hc = 2 * m + hf
                p = hc % 2
                y_p, st_p = hbuf(A_B1, p), hbuf(A_B2, p)
                er_p, ee_p = hbuf(A_B3, p), hbuf(A_B4, p)
                pp0, pp1 = ([pA, pB], [pC, pD])[p]
                ldy = S.dma("sync", "dy0" if p == 0 else "dy1", y_p,
                            bass.AP(y_dram, 524288 * m + 2048 * hf,
                                    [[4096, 128], [1, 2048]]),
                            waits=[("dys", S.cnt["dys"]), ("v", yfree[p])])
                lds = S.dma("sync", "dm0" if p == 0 else "dm1", st_p,
                            st_dram[m, :, 2048 * hf:2048 * (hf + 1)],
                            waits=[dss_all, ("v", stfree[p])])
                for ph in range(2):
                    lde = S.dma("sync", "dr" if p == 0 else "drv",
                                AR[64 * ph:64 * (ph + 1),
                                   A_B3 + 2048 * p:A_B3 + 2048 * (p + 1)],
                                bass.AP(a4_dram,
                                        8192 * m + (32 * ph + 16 * hf) * 128,
                                        [[0, 64], [1, 2048]]),
                                waits=[a4_d, ("v", erfree[p]),
                                       erspill[p]])
                exp_s = S.op("scalar", lambda e, er_p=er_p, ee_p=ee_p: e.activation(
                    ee_p, er_p, AF.Exp),
                    waits=[lde, ("v", eefree[p])], inc="s")
                CTm = BC[:, T + 128 * m:T + 128 * (m + 1)]
                for j2r in range(16):
                    j2 = 16 * hf + j2r
                    ps = [pp0, pp1][j2r // 8]

                    def omm(e, j2=j2, j2r=j2r, ps=ps, CTm=CTm):
                        return e.matmul(ps[:, 128 * (j2r % 8):128 * (j2r % 8 + 1)],
                                        s_run[:, 128 * j2:128 * (j2 + 1)], CTm,
                                        start=True, stop=True)
                    S.op("tensor", omm,
                         waits=[("v", max(supd[hf], psum_free[p]))]
                         if j2r == 0 else (), inc="t")
                yoff_t = S.cnt["t"]
                for q2 in range(2):
                    ps = [pp0, pp1][q2]

                    def oev(e, q2=q2, ps=ps, ee_p=ee_p):
                        return e.tensor_tensor(
                            ACX[:, 1024 * q2:1024 * (q2 + 1)], ps[:],
                            ee_p[:, 1024 * q2:1024 * (q2 + 1)], OP.mult)
                    S.op("vector", oev,
                         waits=[("t", yoff_t), ("s", exp_s), ldy], inc="v")
                psum_free[p] = S.cnt["v"]
                S.op("vector", lambda e, er_p=er_p, y_p=y_p: e.tensor_add(
                    er_p, y_p, ACX), inc="v")
                yfree[p] = S.cnt["v"]
                erfree[p] = S.cnt["v"]
                erspill[p] = S.dma(
                    "scalar", "dys",
                    bass.AP(y_dram, 524288 * m + 2048 * hf,
                            [[4096, 128], [1, 2048]]),
                    er_p, waits=[("v", S.cnt["v"])])
                p4_y_d.append(erspill[p])

                def sup1(e, m=m, hf=hf, ee_p=ee_p):
                    in1 = mkap(dch_rep, 64 * m + 32 * hf, [[1, 32], [0, 64]])
                    return e.tensor_tensor(
                        ee_p, s_run[:, 2048 * hf:2048 * (hf + 1)], in1, OP.mult)
                S.op("vector", sup1, waits=[("t", yoff_t)], inc="v")

                def sup2(e, hf=hf, ee_p=ee_p, st_p=st_p):
                    return e.tensor_add(s_run[:, 2048 * hf:2048 * (hf + 1)],
                                        ee_p, st_p)
                S.op("vector", sup2, waits=[lds], inc="v")
                supd[hf] = S.cnt["v"]
                stfree[p] = S.cnt["v"]
                eefree[p] = S.cnt["v"]

        wq_free = [S.cnt["t"], S.cnt["t"]]
        ps_sfree = [S.cnt["s"], S.cnt["s"]]
        yf_dp = [("dyf", 0), ("dyf1", 0)]
        ydp_names = ["dy0", "dy1"]
        tt_p = [0, 0]
        sq_p = [0, 0]
        pend_sq = []
        for j in range(DIT):
            buf = j % 2
            wsl = wq[buf]
            gsil = ar(A_B3 + 1024 * buf, 1024)
            yj = ar(A_B3 + 2048 + 1024 * buf, 1024)
            yfb = ar(A_B2 + 1024 * buf, 1024)
            ldw = S.dma("sync", "dw0" if buf == 0 else "dw1", wsl[:],
                        w_gate[j, :, :], waits=[("t", wq_free[buf])])
            ps = [pA, pB][buf]

            def gmm(e, ps=ps, wsl=wsl):
                last = None
                for cb in range(2):
                    for k in range(KD):
                        last = e.matmul(
                            ps[:, cb * 512:(cb + 1) * 512],
                            wsl[:, k * 128:(k + 1) * 128],
                            xT_sb[:, k * T + cb * 512: k * T + (cb + 1) * 512],
                            start=(k == 0), stop=(k == KD - 1))
                return last
            gw_ = [ldw, ("s", ps_sfree[buf])]
            if j < 2:
                gw_.append(("v", S.cnt["v"]))
            tcnt = S.op("tensor", gmm, waits=gw_, inc="t")
            wq_free[buf] = tcnt
            while len(pend_sq) >= 3:
                q = pend_sq.pop(0)
                S.op("tensor", q[0], waits=[("s", q[1])], inc="t")
                S.op("tensor", q[2], inc="t")
            ldy = S.dma("sync", ydp_names[buf], yj,
                        bass.AP(y_dram, 128 * j,
                                [[4096, 128], [524288, 8], [1, 128]]),
                        waits=[p4_y_d[-1], ("v", tt_p[buf])])
            scnt = S.op("scalar", lambda e, ps=ps, gsil=gsil: e.activation(
                gsil, ps[:], AF.Silu),
                waits=[("t", tcnt), ("v", tt_p[buf])], inc="s")
            ps_sfree[buf] = scnt
            tt_p[buf] = S.op(
                "vector", lambda e, yfb=yfb, yj=yj, gsil=gsil: e.tensor_tensor(
                    yfb, yj, gsil, OP.mult),
                waits=[("s", max(scnt, sq_p[buf])), ldy, yf_dp[buf]], inc="v")
            yf_dp[buf] = S.dma("scalar", "dyf" if buf == 0 else "dyf1",
                               yf_dram[128 * j:128 * (j + 1), :], yfb,
                               waits=[("v", tt_p[buf])])
            sqb = ar(A_B4 + 1024 * (j % 4), 1024)
            sq_p[buf] = S.op("scalar", lambda e, yfb=yfb, sqb=sqb: e.activation(
                sqb, yfb, AF.Square),
                waits=[("v", tt_p[buf]), ("t", S.cnt["t"])], inc="s")

            def mk_sq(cb, j, sqb):
                def f(e, cb=cb, j=j, sqb=sqb):
                    return e.matmul(
                        pC[0:1, cb * 512:(cb + 1) * 512], ones_sb[:, 0:1],
                        sqb[:, cb * 512:(cb + 1) * 512],
                        start=(j == 0), stop=(j == DIT - 1))
                return f
            pend_sq.append((mk_sq(0, j, sqb), sq_p[buf], mk_sq(1, j, sqb)))
        for q in pend_sq:
            S.op("tensor", q[0], waits=[("s", q[1])], inc="t")
            S.op("tensor", q[2], inc="t")
        pend_sq = []
        ssq_t = S.cnt["t"]
        r_row = ar(A_B1, 1024, 1)
        r_row2 = ar(A_B1 + 1024, 1024, 1)
        S.op("vector", lambda e: e.tensor_scalar(
            r_row, pC[0:1, 0:1024], 1.0 / DI, EPS, OP.mult, OP.add),
            waits=[("t", ssq_t)], inc="v")
        S.op("vector", lambda e: e.reciprocal(r_row2, r_row), inc="v")
        S.op("scalar", lambda e: e.activation(r_row, r_row2, AF.Sqrt),
             waits=S.at("v"), inc="s")
        rbounce = S.dma("scalar", "dbs", ln_dram[0:1, :], r_row, waits=S.at("s"))
        r_bc = ar(A_B5, 1024)
        ldrbc = S.dma("sync", "db", r_bc, bass.AP(ln_dram, 0, [[0, 128], [1, T]]),
                      waits=[rbounce])

        yn_sb = ar(A_B1, 4 * DI)
        p6_done_v = S.cnt["v"]
        for cb in range(2):
            yn_v = 0
            for k in range(DIT):
                tb = ar(A_B5 + 1024 + 512 * (k % 2), 512)
                ldyf = S.dma("sync", "dy0" if k % 2 == 0 else "dy1", tb,
                             bass.AP(yf_dram, 128 * k * T + 512 * cb,
                                     [[T, 128], [1, 512]]),
                             waits=[yf_dp[0], yf_dp[1], ("v", S.cnt["v"])])

                def ynf(e, k=k, cb=cb, tb=tb):
                    return e.tensor_scalar(
                        yn_sb[:, 512 * k:512 * (k + 1)], tb,
                        nw_col[:, k:k + 1], None, OP.mult)
                yn_v = S.op("vector", ynf, waits=[ldyf], inc="v")
            for oh in range(2):
                for k in range(DIT):
                    buf = k % 2
                    wsl = wq[buf]
                    ldw = S.dma("sync", "dw0" if buf == 0 else "dw1",
                                wsl[:, 0:1024],
                                w_out[128 * k:128 * (k + 1), 1024 * oh:1024 * (oh + 1)],
                                waits=[("t", wq_free[buf])])

                    def pmm(e, k=k, wsl=wsl):
                        last = None
                        for o in range(8):
                            ps = [pA, pB, pC, pD][o // 2]
                            last = e.matmul(
                                ps[:, 512 * (o % 2):512 * (o % 2 + 1)],
                                wsl[:, 128 * o:128 * (o + 1)],
                                yn_sb[:, 512 * k:512 * (k + 1)],
                                start=(k == 0), stop=(k == DIT - 1))
                        return last
                    wq_free[buf] = S.op(
                        "tensor", pmm,
                        waits=[ldw, ("v", max(yn_v, p6_done_v) if k == 0 else 0)],
                        inc="t")
                op_t = S.cnt["t"]
                for o in range(8):
                    ps = [pA, pB, pC, pD][o // 2]
                    og = 8 * oh + o

                    def res_r(e, ps=ps, o=o, cb=cb):
                        return e.tensor_tensor(
                            ar(A_B5 + 2048, 512),
                            ps[:, 512 * (o % 2):512 * (o % 2 + 1)],
                            r_bc[:, 512 * cb:512 * (cb + 1)], OP.mult)
                    S.op("vector", res_r, waits=[("t", op_t), ldrbc], inc="v")

                    def res_f(e, og=og, cb=cb):
                        xsl = xT_sb[:, og * T + 512 * cb: og * T + 512 * (cb + 1)]
                        return e.tensor_tensor(
                            ar(A_B5 + 2560, 512), xsl, ar(A_B5 + 2048, 512),
                            OP.add)
                    S.op("vector", res_f, inc="v")

                    def res_c(e, og=og, cb=cb):
                        xsl = xT_sb[:, og * T + 512 * cb: og * T + 512 * (cb + 1)]
                        return e.tensor_copy(xsl, ar(A_B5 + 2560, 512))
                    S.op("vector", res_c, inc="v")
                p6_done_v = S.cnt["v"]
        sqt = ar(A_B5 + 1024, 512)
        for cb in range(2):
            for k in range(KD):
                sq_s = S.op("scalar", lambda e, k=k, cb=cb: e.activation(
                    sqt, xT_sb[:, k * T + 512 * cb: k * T + 512 * (cb + 1)],
                    AF.Square), waits=[("v", p6_done_v), ("t", S.cnt["t"])], inc="s")
                S.op("tensor", lambda e, k=k, cb=cb: e.matmul(
                    pA[0:1, 512 * cb:512 * (cb + 1)], ones_sb[:, 0:1],
                    xT_sb[:, k * T + 512 * cb: k * T + 512 * (cb + 1)],
                    start=(k == 0), stop=(k == KD - 1)),
                    waits=[("v", p6_done_v)], inc="t")
                S.op("tensor", lambda e, k=k, cb=cb: e.matmul(
                    pB[0:1, 512 * cb:512 * (cb + 1)], ones_sb[:, 0:1], sqt,
                    start=(k == 0), stop=(k == KD - 1)),
                    waits=[("s", sq_s)], inc="t")
        ln_t = S.cnt["t"]
        mu_row = ar(A_B1, 1024, 1)
        r1_row = ar(A_B1 + 1024, 1024, 1)
        t_row = ar(A_B1 + 2048, 1024, 1)
        t_row2 = ar(A_B1 + 3072, 1024, 1)
        S.op("vector", lambda e: e.tensor_scalar_mul(mu_row, pA[0:1, 0:1024],
                                                     1.0 / DM),
             waits=[("t", ln_t)], inc="v")
        S.op("vector", lambda e: e.tensor_scalar_mul(t_row, pB[0:1, 0:1024],
                                                     1.0 / DM), inc="v")
        S.op("vector", lambda e: e.tensor_tensor(r1_row, mu_row, mu_row, OP.mult),
             inc="v")
        S.op("vector", lambda e: e.tensor_sub(t_row2, t_row, r1_row), inc="v")
        S.op("vector", lambda e: e.tensor_scalar_add(t_row, t_row2, EPS), inc="v")
        S.op("vector", lambda e: e.reciprocal(r1_row, t_row), inc="v")
        S.op("scalar", lambda e: e.activation(t_row, r1_row, AF.Sqrt),
             waits=S.at("v"), inc="s")
        mu_d = S.dma("scalar", "dbs", ln_dram[1:2, :], mu_row, waits=S.at("v", "s"))
        rr_d = S.dma("scalar", "dbs", ln_dram[2:3, :], t_row, waits=S.at("s"))
        mu_bc = ar(A_B2, 1024)
        ldmu = S.dma("sync", "db", mu_bc, bass.AP(ln_dram, T, [[0, 128], [1, T]]),
                     waits=[mu_d, ("v", S.cnt["v"])])
        ldrr = S.dma("sync", "db", r_bc, bass.AP(ln_dram, 2 * T, [[0, 128], [1, T]]),
                     waits=[rr_d])
        ln1_v = 0
        for k in range(KD):
            def c1(e, k=k):
                xsl = xT_sb[:, k * T:(k + 1) * T]
                return e.tensor_tensor(ar(A_B1, T), xsl, mu_bc, OP.subtract)
            S.op("vector", c1, waits=[ldmu, ldrr], inc="v")

            def c2(e, k=k):
                xsl = xT_sb[:, k * T:(k + 1) * T]
                return e.tensor_tensor(xsl, ar(A_B1, T), r_bc, OP.mult)
            ln1_v = S.op("vector", c2, inc="v")

        xi_bf = AR[0:128, A_B1:A_B1 + 8192].bitcast(BF16)
        sg_bc = ar(A_B3, 1024)
        sv_bc = ar(A_B3 + 1024, 1024)
        qt1 = ar(A_B3 + 2048, 1024)
        qt2 = ar(A_B3 + 3072, 1024)
        sxq_bc = ar(A_B4, 1024)
        gm = ar(A_B4, 1024)
        grow_g = ar(A_B4 + 1024, 1024, 1)
        grow_v2 = ar(A_B4 + 2048, 1024, 1)
        gcol = ar(A_B4 + 3072, 8)
        gq = ar(A_B4 + 3080, 8)
        gr = ar(A_B4 + 3088, 8)
        ggam = ar(A_B4 + 3096, 4)

        maxt = ar(A_B2, 1024)
        absx = ar(A_B2 + 1024, 1024)
        S.op("scalar", lambda e: e.activation(maxt, xT_sb[:, 0:T], AF.Abs),
             waits=[("v", ln1_v), ("t", S.cnt["t"])], inc="s")
        for k in range(1, KD):
            ab_s = S.op("scalar", lambda e, k=k: e.activation(
                absx, xT_sb[:, k * T:(k + 1) * T], AF.Abs),
                waits=[("v", S.cnt["v"])], inc="s")
            S.op("vector", lambda e: e.tensor_tensor(maxt, maxt, absx, OP.max),
                 waits=[("s", ab_s)], inc="v")
        gx_v = S.cnt["v"]

        def gamma_pass(src, extra_tw=()):
            for c8 in range(8):
                S.op("tensor", lambda e, c8=c8, src=src: e.transpose(
                    pA[:, 128 * c8:128 * c8 + 128],
                    src[:, 128 * c8:128 * (c8 + 1)], ident[:]),
                    waits=([("v", S.cnt["v"])] + list(extra_tw)) if c8 == 0 else (),
                    inc="t")
            tt = S.cnt["t"]
            for c8 in range(8):
                S.op("vector", lambda e, c8=c8: e.reduce_max(
                    gcol[:, c8:c8 + 1], pA[:, 128 * c8:128 * (c8 + 1)], axis=AX.X),
                    waits=[("t", tt)] if c8 == 0 else (), inc="v")
            S.op("vector", lambda e: e.tensor_scalar_max(gcol, gcol, 1e-5), inc="v")
            S.op("vector", lambda e: e.reciprocal(gq, gcol), inc="v")
            S.op("vector", lambda e: e.tensor_scalar_mul(gr, gq, 127.0), inc="v")
            vq = S.cnt["v"]
            for c8 in range(8):
                S.dma("scalar", "dbs", ln_dram[3:4, 128 * c8:128 * (c8 + 1)],
                      gr[:, c8:c8 + 1], waits=[("v", vq)])
                dd = S.dma("scalar", "dbs", ln_dram[4:5, 128 * c8:128 * (c8 + 1)],
                           gcol[:, c8:c8 + 1], waits=[("v", vq)])
            return vq, dd

        gx2_v, gq_d = gamma_pass(maxt)
        ldsx = S.dma("sync", "db", sxq_bc, bass.AP(ln_dram, 3 * T, [[0, 128], [1, T]]),
                     waits=[gq_d, ("v", gx2_v)])
        ldgw = S.dma("sync", "db", ggam, bass.AP(misc_dram, 6 * 64, [[0, 128], [1, 4]]),
                     waits=[gam_bd, ("v", gx2_v)])
        ldg4 = S.dma("sync", "db", grow_g, bass.AP(ln_dram, 4 * T, [[1, 1], [1, T]]),
                     waits=[gq_d, ("v", gx2_v)])
        S.op("vector", lambda e: e.tensor_scalar(
            grow_v2, grow_g, ggam[0:1, 0:1], 1.0 / 127.0, OP.mult, OP.mult),
            waits=[ldgw, ldg4], inc="v")
        sgq_d = S.dma("scalar", "dbs", ln_dram[5:6, :], grow_v2, waits=S.at("v"))
        S.op("vector", lambda e: e.tensor_scalar(
            grow_v2, grow_g, ggam[0:1, 1:2], 1.0 / 127.0, OP.mult, OP.mult),
            waits=[sgq_d], inc="v")
        sgq_d = S.dma("scalar", "dbs", ln_dram[6:7, :], grow_v2, waits=S.at("v"))
        ldsg = S.dma("sync", "db", sg_bc, bass.AP(ln_dram, 5 * T, [[0, 128], [1, T]]),
                     waits=[sgq_d])
        ldsv = S.dma("sync", "db", sv_bc, bass.AP(ln_dram, 6 * T, [[0, 128], [1, T]]),
                     waits=[sgq_d])
        xi_v = 0
        for k in range(KD):
            S.op("vector", lambda e, k=k: e.tensor_tensor(
                qt1, xT_sb[:, k * T:(k + 1) * T], sxq_bc, OP.mult),
                waits=[ldsx] if k == 0 else (), inc="v")
            S.op("vector", lambda e: e.tensor_scalar(
                qt2, qt1, MAGIC, -MAGIC, OP.add, OP.add), inc="v")
            S.op("vector", lambda e: e.tensor_scalar_max(qt1, qt2, -128.0), inc="v")
            xi_v = S.op("vector", lambda e, k=k: e.tensor_scalar_min(
                xi_bf[:, k * T:(k + 1) * T], qt1, 127.0), inc="v")

        bandgs = [ar(A_B5, 2048).bitcast(BF16),
                  ar(A_BC, 2048).bitcast(BF16)]
        bandvs = [ar(A_B5 + 2048, 2048).bitcast(BF16),
                  ar(A_BC + 2048, 2048).bitcast(BF16)]
        gmp = [gm, ar(A_B4 + 1024, 1024)]
        S.op("vector", lambda e: e.memset(gm, 0.0), waits=[("v", xi_v)], inc="v")
        S.op("vector", lambda e: e.memset(gmp[1], 0.0), inc="v")
        gm0_v = S.cnt["v"]
        ngm = [0, 0]
        mid_d = ("dmid", 0)
        p7a_t0 = S.cnt["t"]
        bg_free = [p7a_t0, p7a_t0]
        bv_free = [p7a_t0, p7a_t0]
        psum_read_v = S.cnt["v"]
        for fb in range(32):
            par7 = fb % 2
            bandg = bandgs[par7]
            bandv = bandvs[par7]
            bw = [("cc", cc_m_n), ("t", bg_free[par7])]
            if fb == 0:
                bw += [("dy0", S.cnt["dy0"]), ("dy1", S.cnt["dy1"]),
                       ("db", S.cnt["db"]), ("dy", S.cnt["dy"])]
            if fb == 1:
                bw += [("v", psum_read_v)]
            boff = (128 * (fb // 4)) * 16384 + (fb % 4) * 4096
            ldbg = S.dma("sync", "dr" if par7 == 0 else "dr2", bandg,
                         bass.AP(mg_out, boff, [[16384, 128], [1, 4096]]),
                         waits=bw)
            ldbv = S.dma("sync", "drv" if par7 == 0 else "drv2", bandv,
                         bass.AP(mv_out, boff, [[16384, 128], [1, 4096]]),
                         waits=[("cc", cc_m_n), ("t", bv_free[par7])])

            def gvmm(e, band, pss):
                last = None
                for ft in range(2):
                    for cb in range(2):
                        ps = pss[ft]
                        for k in range(KD):
                            last = e.matmul(
                                ps[:, 512 * cb:512 * (cb + 1)],
                                band[:, 256 * k + 128 * ft:256 * k + 128 * (ft + 1)],
                                xi_bf[:, k * T + 512 * cb:k * T + 512 * (cb + 1)],
                                start=(k == 0), stop=(k == KD - 1))
                return last
            cur_bg = S.op("tensor",
                          lambda e, bandg=bandg: gvmm(e, bandg, [pA, pB]),
                          waits=[ldbg, ("v", psum_read_v)], inc="t")
            cur_bv = S.op("tensor",
                          lambda e, bandv=bandv: gvmm(e, bandv, [pC, pD]),
                          waits=[ldbv], inc="t")
            bg_free[par7] = cur_bg
            bv_free[par7] = cur_bv
            for ft in range(2):
                for cb in range(2):
                    psg, psv = [pA, pB][ft], [pC, pD][ft]
                    fgl = 2 * fb + ft

                    def sgm(e, psg=psg, cb=cb):
                        return e.tensor_tensor(
                            qt1[:, 0:512], psg[:, 512 * cb:512 * (cb + 1)],
                            sg_bc[:, 512 * cb:512 * (cb + 1)], OP.mult)
                    S.op("vector", sgm,
                         waits=[("t", cur_bv), ldsg, ldsv, mid_d], inc="v")
                    S.op("scalar", lambda e, fgl=fgl: e.activation(
                        qt2[:, 0:512], qt1[:, 0:512], AF.Silu,
                        bias=fb_sb[:, fgl:fgl + 1]), waits=S.at("v"), inc="s")

                    def svm(e, psv=psv, cb=cb):
                        return e.tensor_tensor(
                            qt1[:, 512:1024], psv[:, 512 * cb:512 * (cb + 1)],
                            sv_bc[:, 512 * cb:512 * (cb + 1)], OP.mult)
                    sv_cnt = S.op("vector", svm, inc="v")
                    if ft == 1 and cb == 1:
                        psum_read_v = sv_cnt
                    S.op("vector", lambda e, fgl=fgl: e.tensor_scalar_add(
                        qt1[:, 0:512], qt1[:, 512:1024],
                        fb_sb[:, 64 + fgl:64 + fgl + 1]),
                        waits=S.at("s"), inc="v")
                    S.op("vector", lambda e: e.tensor_tensor(
                        qt1[:, 512:1024], qt1[:, 0:512], qt2[:, 0:512], OP.mult),
                        waits=S.at("s"), inc="v")
                    am_s = S.op("scalar", lambda e: e.activation(
                        qt2[:, 512:1024], qt1[:, 512:1024], AF.Abs),
                        waits=S.at("v"), inc="s")
                    par = ngm[cb] % 2
                    S.op("vector", lambda e, cb=cb, par=par: e.tensor_tensor(
                        gmp[1 - par][:, 512 * cb:512 * (cb + 1)],
                        gmp[par][:, 512 * cb:512 * (cb + 1)],
                        qt2[:, 512:1024], OP.max), waits=[("s", am_s)], inc="v")
                    ngm[cb] += 1
                    midv = S.cnt["v"]
                    mid_d = S.dma(
                        "scalar", "dmid",
                        mid_dram[128 * fgl:128 * (fgl + 1), 512 * cb:512 * (cb + 1)],
                        qt1[:, 512:1024], waits=[("v", midv)])

        go_v, go_d = gamma_pass(gm)
        ldso = S.dma("sync", "db", sxq_bc, bass.AP(ln_dram, 3 * T, [[0, 128], [1, T]]),
                     waits=[go_d, ("v", go_v)])
        ldg5 = S.dma("sync", "db", grow_g, bass.AP(ln_dram, 4 * T, [[1, 1], [1, T]]),
                     waits=[go_d, ("v", go_v)])
        S.op("vector", lambda e: e.tensor_scalar(
            grow_v2, grow_g, ggam[0:1, 2:3], 1.0 / 127.0, OP.mult, OP.mult),
            waits=[ldg5, ldgw], inc="v")
        soo_d = S.dma("scalar", "dbs", ln_dram[5:6, :], grow_v2, waits=S.at("v"))
        ldsoo = S.dma("sync", "db", sg_bc, bass.AP(ln_dram, 5 * T, [[0, 128], [1, T]]),
                      waits=[soo_d, ("v", S.cnt["v"])])
        mi_v = 0
        mi_d = ("dmi", 0)
        mo_bfs = [ar(A_B5, 4096).bitcast(BF16),
                  ar(A_BC, 4096).bitcast(BF16)]
        mi_sb = [AR[0:128, A_B1 + 2048 * i:A_B1 + 2048 * (i + 1)].bitcast(BF16)
                 for i in range(2)]
        p7_done_v = S.cnt["v"]
        mo_free = [S.cnt["t"], S.cnt["t"]]
        moi = 0
        for cb in range(2):
            for oh in range(2):
                for kc in range(8):
                    if cb == 0 and oh == 0:
                        for ft8 in range(8):
                            ft = 8 * kc + ft8
                            ldm = S.dma("sync", "dy", qt2,
                                        mid_dram[128 * ft:128 * (ft + 1), :],
                                        waits=[mid_d, mi_d, ("v", mi_v)])
                            S.op("vector", lambda e: e.tensor_tensor(
                                qt1, qt2, sxq_bc, OP.mult),
                                waits=[ldm, ldso], inc="v")
                            S.op("vector", lambda e: e.tensor_scalar(
                                qt2, qt1, MAGIC, -MAGIC, OP.add, OP.add), inc="v")
                            S.op("vector", lambda e: e.tensor_scalar_max(
                                qt1, qt2, -128.0), inc="v")
                            mi_v = S.op("vector", lambda e: e.tensor_scalar_min(
                                qt2.bitcast(BF16)[:, 0:1024], qt1, 127.0), inc="v")
                            S.dma("scalar", "dmi",
                                  bass.AP(mi_dram,
                                          (2 * (ft // 8)) * 524288
                                          + (ft % 8) * 512,
                                          [[4096, 128], [1, 512]]),
                                  qt2.bitcast(BF16)[:, 0:512],
                                  waits=[("v", mi_v)])
                            mi_d = S.dma("scalar", "dmi",
                                         bass.AP(mi_dram,
                                                 (2 * (ft // 8) + 1) * 524288
                                                 + (ft % 8) * 512,
                                                 [[4096, 128], [1, 512]]),
                                         qt2.bitcast(BF16)[:, 512:1024],
                                         waits=[("v", mi_v)])
                    mo_bf = mo_bfs[moi % 2]
                    mw = [("cc", cc_m_n), ("t", mo_free[moi % 2])]
                    if moi < 2:
                        mw += [("dr", S.cnt["dr"]), ("drv", S.cnt["drv"]),
                               ("dr2", S.cnt["dr2"]), ("drv2", S.cnt["drv2"])]
                    ldmo = S.dma("sync", "dmo" if moi % 2 == 0 else "dmo2",
                                 mo_bf,
                                 bass.AP(mo_out,
                                         (128 * kc) * 16384 + 8192 * oh,
                                         [[16384, 128], [1, 8192]]),
                                 waits=mw)
                    miw = [mi_d, ("t", mo_free[kc % 2])]
                    if cb == 0 and oh == 0 and kc < 2:
                        miw += [("dx", S.cnt["dx"]), ("v", S.cnt["v"])]
                    ldmi = S.dma("sync", "dm0" if kc % 2 == 0 else "dm1",
                                 mi_sb[kc % 2],
                                 bass.AP(mi_dram, (2 * kc + cb) * 524288,
                                         [[4096, 128], [1, 4096]]),
                                 waits=miw)

                    def omm7(e, kc=kc, mo_bf=mo_bf):
                        last = None
                        for k8 in range(8):
                            for o in range(8):
                                ps = [pA, pB, pC, pD][o // 2]
                                last = e.matmul(
                                    ps[:, 512 * (o % 2):512 * (o % 2 + 1)],
                                    mo_bf[:, 1024 * k8 + 128 * o:
                                          1024 * k8 + 128 * (o + 1)],
                                    mi_sb[kc % 2][:, 512 * k8:512 * (k8 + 1)],
                                    start=(kc == 0 and k8 == 0),
                                    stop=(kc == 7 and k8 == 7))
                        return last
                    mo_free[moi % 2] = S.op(
                        "tensor", omm7,
                        waits=[ldmo, ldmi,
                               ("v", p7_done_v if kc == 0 else 0)],
                        inc="t")
                    moi += 1
                mo_free_t = mo_free[(moi - 1) % 2]
                for o in range(8):
                    ps = [pA, pB, pC, pD][o // 2]
                    og = 8 * oh + o

                    def fev(e, ps=ps, o=o, cb=cb):
                        return e.tensor_tensor(
                            qt1[:, 0:512], ps[:, 512 * (o % 2):512 * (o % 2 + 1)],
                            sg_bc[:, 512 * cb:512 * (cb + 1)], OP.mult)
                    S.op("vector", fev,
                         waits=[("t", mo_free_t), ldsoo], inc="v")
                    S.op("vector", lambda e, og=og: e.tensor_scalar_add(
                        qt2[:, 0:512], qt1[:, 0:512], fb_sb[:, 128 + og:129 + og]),
                        inc="v")

                    def fres(e, og=og, cb=cb):
                        xsl = xT_sb[:, og * T + 512 * cb: og * T + 512 * (cb + 1)]
                        return e.tensor_tensor(qt1[:, 512:1024], xsl,
                                               qt2[:, 0:512], OP.add)
                    S.op("vector", fres, inc="v")

                    def fres_c(e, og=og, cb=cb):
                        xsl = xT_sb[:, og * T + 512 * cb: og * T + 512 * (cb + 1)]
                        return e.tensor_copy(xsl, qt1[:, 512:1024])
                    S.op("vector", fres_c, inc="v")
                p7_done_v = S.cnt["v"]
        sqt2 = ar(A_B3 + 2048, 512)
        for cb in range(2):
            for k in range(KD):
                sq_s = S.op("scalar", lambda e, k=k, cb=cb: e.activation(
                    sqt2, xT_sb[:, k * T + 512 * cb: k * T + 512 * (cb + 1)],
                    AF.Square), waits=[("v", p7_done_v), ("t", S.cnt["t"])], inc="s")
                S.op("tensor", lambda e, k=k, cb=cb: e.matmul(
                    pA[0:1, 512 * cb:512 * (cb + 1)], ones_sb[:, 0:1],
                    xT_sb[:, k * T + 512 * cb: k * T + 512 * (cb + 1)],
                    start=(k == 0), stop=(k == KD - 1)),
                    waits=[("v", p7_done_v)], inc="t")
                S.op("tensor", lambda e, k=k, cb=cb: e.matmul(
                    pB[0:1, 512 * cb:512 * (cb + 1)], ones_sb[:, 0:1], sqt2,
                    start=(k == 0), stop=(k == KD - 1)),
                    waits=[("s", sq_s)], inc="t")
        ln2_t = S.cnt["t"]
        mu2 = ar(A_B4 + 1024, 1024, 1)
        t2r = ar(A_B4 + 2048, 1024, 1)
        r2r = ar(A_B4 + 3072, 1024, 1)
        S.op("vector", lambda e: e.tensor_scalar_mul(mu2, pA[0:1, 0:1024], 1.0 / DM),
             waits=[("t", ln2_t)], inc="v")
        S.op("vector", lambda e: e.tensor_scalar_mul(t2r, pB[0:1, 0:1024], 1.0 / DM),
             inc="v")
        S.op("vector", lambda e: e.tensor_tensor(r2r, mu2, mu2, OP.mult), inc="v")
        S.op("vector", lambda e: e.tensor_sub(t2r, t2r, r2r), inc="v")
        S.op("vector", lambda e: e.tensor_scalar_add(t2r, t2r, EPS), inc="v")
        S.op("vector", lambda e: e.reciprocal(r2r, t2r), inc="v")
        S.op("scalar", lambda e: e.activation(t2r, r2r, AF.Sqrt),
             waits=S.at("v"), inc="s")
        mu2_d = S.dma("scalar", "dbs", ln_dram[1:2, :], mu2, waits=S.at("v", "s"))
        rr2_d = S.dma("scalar", "dbs", ln_dram[2:3, :], t2r, waits=S.at("s"))
        mu_bc2 = ar(A_B2, 1024)
        r_bc2 = ar(A_B2 + 1024, 1024)
        ldmu2 = S.dma("sync", "db", mu_bc2, bass.AP(ln_dram, T, [[0, 128], [1, T]]),
                      waits=[mu2_d, ("v", S.cnt["v"])])
        ldrr2 = S.dma("sync", "db", r_bc2, bass.AP(ln_dram, 2 * T, [[0, 128], [1, T]]),
                      waits=[rr2_d, ("v", S.cnt["v"])])
        for k in range(KD):
            def d1f(e, k=k):
                xsl = xT_sb[:, k * T:(k + 1) * T]
                return e.tensor_tensor(xsl, xsl, mu_bc2, OP.subtract)
            S.op("vector", d1f, waits=[ldmu2, ldrr2], inc="v")

            def d2f(e, k=k):
                xsl = xT_sb[:, k * T:(k + 1) * T]
                return e.tensor_tensor(xsl, xsl, r_bc2, OP.mult)
            S.op("vector", d2f, inc="v")
        fin_v = S.cnt["v"]
        yout_d = S.dma("gpsimd", "dg",
                       bass.AP(yout, 0, [[T, 128], [128 * T, KD], [1, T]]),
                       xT_sb, waits=[("v", fin_v)])

        if "mid" in taps:
            for k in range(KD):
                S.dma("gpsimd", "dg", taps["mid"][128 * k:128 * (k + 1), :],
                      AR[0:128, A_XT + k * T:A_XT + (k + 1) * T],
                      waits=[("v", ln1_v)])
        if "ffnmid" in taps:
            for ft in range(FFT):
                S.dma("gpsimd", "dg", taps["ffnmid"][128 * ft:128 * (ft + 1), :],
                      mid_dram[128 * ft:128 * (ft + 1), :],
                      waits=[("dmid", S.cnt["dmid"])])
        if "acsT" in taps:
            S.dma("gpsimd", "dg", taps["acsT"][:, :], acsT[:, :],
                  waits=[("v", acs_v)])
        if "state" in taps:
            S.dma("gpsimd", "dg", taps["state"][:, :], s_run, waits=S.at("v"))
        if "y" in taps:
            for f in range(DIT):
                S.dma("gpsimd", "dg", taps["y"][128 * f:128 * (f + 1), :],
                      bass.AP(y_dram, 128 * f,
                              [[4096, 128], [524288, 8], [1, 128]]),
                      waits=[("dys", S.cnt["dys"])])
        if "xs" in taps:
            for f in range(32):
                S.dma("gpsimd", "dg", taps["xs"][128 * f:128 * (f + 1), :],
                      bass.AP(xs_dram, 128 * f,
                              [[4096, 128], [524288, 8], [1, 128]]),
                      waits=[patch_g, p1_dsc])
            for f in (32, 33):
                S.dma("gpsimd", "dg", taps["xs"][128 * f:128 * (f + 1), :],
                      BC[:, (f - 32) * T:(f - 31) * T],
                      waits=[("v", patch_v)])
        if "dt" in taps:
            S.dma("gpsimd", "dg", taps["dt"][:, :], dt_sb[:],
                  waits=[("v", dt_ready_v)])

        S.op("gpsimd", lambda e: e.nop(),
             waits=S.at("dg"), inc="g")

        block = E(nc.Block())
        S.emit(block, sems)
    return nc



def _stage(core, ins):
    b, q = core // 4, core % 4
    hs = ins["L0_attn_hidden_states"]
    xTa = np.ascontiguousarray(hs[b, q * T:(q + 1) * T, :].T).reshape(KD, 128, T)
    WT = np.ascontiguousarray(ins["L0_attn_in_proj_w"].T)
    wx = WT[:, DI:DI + 4352]
    w_xbc = np.ascontiguousarray(
        wx.reshape(KD, 128, FX, 128).transpose(2, 1, 0, 3)).reshape(FX, 128, KD * 128)
    wd = WT[:, DI + 4352:]
    w_dt = np.ascontiguousarray(
        wd.reshape(KD, 128, 64).transpose(1, 0, 2)).reshape(128, KD * 64)
    w_gate = np.ascontiguousarray(
        WT[:, :DI].reshape(KD, 128, DIT, 128).transpose(2, 1, 0, 3)).reshape(
        DIT, 128, KD * 128)
    w_out = np.ascontiguousarray(ins["L0_attn_out_proj_w"].T)
    conv_w = np.ascontiguousarray(
        ins["L0_attn_conv_w"].reshape(FX, 128, 4).transpose(1, 0, 2)).reshape(128, FX * 4)
    conv_b = np.ascontiguousarray(ins["L0_attn_conv_b"].reshape(FX, 128).T)
    state_w5 = np.zeros((5, 4), np.float32)
    for j in range(4):
        if j < q:
            for i in range(4):
                state_w5[i, j] = 1.0 if (j < i < q) else 0.0
        else:
            state_w5[4, j] = -1e30
    tail_mask = np.zeros((128, 4), np.float32)
    if q > 0:
        tail_mask[:, q - 1] = 1.0
    perm64 = np.zeros((64, 64), np.float32)
    for r in range(64):
        ph, hf, i = r // 32, (r % 32) // 16, r % 16
        perm64[32 * hf + 2 * i + ph, r] = 1.0
    f32 = np.float32
    return {
        "xT": xTa.astype(f32), "w_xbc": w_xbc.astype(f32), "w_dt": w_dt.astype(f32),
        "w_gate": w_gate.astype(f32), "w_out": w_out.astype(f32),
        "conv_w": conv_w.astype(f32), "conv_b": conv_b.astype(f32),
        "dt_bias": np.ascontiguousarray(ins["L0_attn_dt_bias"].reshape(64, 1)).astype(f32),
        "a_log": np.ascontiguousarray(ins["L0_attn_A_log"].reshape(64, 1)).astype(f32),
        "d_col": np.ascontiguousarray(
            np.repeat(ins["L0_attn_D"].reshape(DIT, 2).T, 64, axis=0)).astype(f32),
        "norm_w": np.ascontiguousarray(ins["L0_attn_norm_w"].reshape(DIT, 128).T).astype(f32),
        "gw_sl": np.ascontiguousarray(
            ins["L0_ffn_gate_w"].T[:, 1024 * core:1024 * (core + 1)]
            .reshape(16, 128, 1024).transpose(1, 0, 2).reshape(128, 16384)
        ).astype(f32),
        "vw_sl": np.ascontiguousarray(
            ins["L0_ffn_val_w"].T[:, 1024 * core:1024 * (core + 1)]
            .reshape(16, 128, 1024).transpose(1, 0, 2).reshape(128, 16384)
        ).astype(f32),
        "ow_sl": np.ascontiguousarray(
            ins["L0_ffn_out_w"].T[1024 * core:1024 * (core + 1), :]
            .reshape(8, 128, 2048).transpose(1, 0, 2).reshape(128, 16384)
        ).astype(f32),
        "gate_b": np.ascontiguousarray(ins["L0_ffn_gate_b"].reshape(FFT, 128).T).astype(f32),
        "val_b": np.ascontiguousarray(ins["L0_ffn_val_b"].reshape(FFT, 128).T).astype(f32),
        "out_b": np.ascontiguousarray(ins["L0_ffn_out_b"].reshape(KD, 128).T).astype(f32),
        "state_w5": state_w5, "tail_mask": tail_mask, "perm64": perm64,
    }


def kernel(**inputs):
    ins = {k: np.asarray(v) for k, v in inputs.items()}
    nc = build_program()
    in_maps = [_stage(c, ins) for c in range(8)]
    res = run_bass_kernel_spmd(nc, in_maps, core_ids=list(range(8)))
    out = np.empty((2, 4096, 2048), np.float32)
    for c in range(8):
        b, q = c // 4, c % 4
        out[b, q * T:(q + 1) * T, :] = res.results[c]["yout"].T
    return out


# version 07
# speedup vs baseline: 1.1416x; 1.0196x over previous
import numpy as np
import ml_dtypes
import concourse.bass as bass
import concourse.mybir as mybir
from concourse.bass_utils import run_bass_kernel_spmd

F32 = mybir.dt.float32
BF16 = mybir.dt.bfloat16
AF = mybir.ActivationFunctionType
OP = mybir.AluOpType
AX = mybir.AxisListType

T = 1024
NCH = 8
CH = 128
KD = 16
DM = 2048
FX = 34
NH = 64
NS = 128
DI = 4096
DIT = 32
FFH = 8192
FFT = 64
MAGIC = 12582912.0
EPS = 1e-5

DEV_PHASES = 99
DEV_TAPS = ()

A_XT = 0
A_B1 = 16384
A_B2 = 20480
A_B3 = 24576
A_B4 = 28672
A_B5 = 32768
A_BC = 36864
A_ACC = 38912
A_XSF = 39936
A_END = 40960
ARENA_N = A_END


class Sched:

    ENGS = ("sync", "tensor", "vector", "scalar", "gpsimd")

    def __init__(self):
        self.ops = {e: [] for e in self.ENGS}
        self.cnt = {}
        self.owner = {}
        self.selfgate = {}

    def sem(self, name, owner=None, selfgate=False):
        self.cnt[name] = 0
        self.owner[name] = owner
        self.selfgate[name] = selfgate

    SERIAL = {"vector": "v", "scalar": "s", "gpsimd": "g"}

    def op(self, eng, fn, waits=(), inc=None, n=1, is_dma=False):
        waits = list(waits)
        own = self.SERIAL.get(eng)
        if own is not None and self.cnt[own] > 0:
            waits.append((own, self.cnt[own]))
        for s, v in waits:
            assert v <= self.cnt[s], f"wait {s}>={v} but only {self.cnt[s]} emitted"
        self.ops[eng].append((fn, tuple(waits), inc, n, is_dma))
        if inc is not None:
            self.cnt[inc] += n
            return self.cnt[inc]
        return None

    def dma(self, eng, sem, out, in_, waits=()):
        def fn(e, out=out, in_=in_):
            return e.dma_start(out=out, in_=in_)
        waits = list(waits)
        if self.selfgate.get(sem):
            waits.append((sem, self.cnt[sem]))
        c = self.op(eng, fn, waits, inc=sem, n=16, is_dma=True)
        return (sem, c)

    def at(self, *names):
        return [(s, self.cnt[s]) for s in names]

    def emit(self, block, sems):
        for eng in self.ENGS:
            ops = self.ops[eng]
            if not ops:
                continue

            def body(e, eng=eng, ops=ops):
                last = {}
                for fn, waits, inc, n, is_dma in ops:
                    for s, v in waits:
                        if v <= 0 or last.get(s, 0) >= v:
                            continue
                        e.wait_ge(sems[s], v)
                        last[s] = v
                    inst = fn(e)
                    if inc is not None:
                        inst.then_inc(sems[inc], n)

            getattr(block, eng)(body)


def build_program():
    nc = bass.Bass("TRN2", target_bir_lowering=False)
    S = Sched()
    for name in ["t", "v", "s", "g"]:
        S.sem(name, owner=name if name != "t" else "tensor")
    S.owner["t"] = "tensor"
    S.owner["v"] = "vector"
    S.owner["s"] = "scalar"
    S.owner["g"] = "gpsimd"
    S.sem("cc")
    for name in ["dc", "dw0", "dw1", "dx", "dm0", "dm1", "dr", "drv", "dmo",
                 "dy", "dy0", "dy1", "dp0", "dp1", "dpm", "dxs", "dys", "dss",
                 "dmid", "dmi", "dyf", "dyf1", "dr2", "drv2", "dmo2",
                 "dp2", "dp3"]:
        S.sem(name)
    for name in ["db", "dbs", "dg"]:
        S.sem(name, selfgate=True)

    I = {}
    def inp(name, shape, dt=F32):
        I[name] = nc.dram_tensor(name, shape, dt, kind="ExternalInput")
        return I[name]

    xT = inp("xT", [KD, 128, T])
    w_xbc = inp("w_xbc", [FX, 128, KD * 128])
    w_dt = inp("w_dt", [128, KD * 64])
    w_gate = inp("w_gate", [DIT, 128, KD * 128])
    w_out = inp("w_out", [DI, DM])
    conv_w = inp("conv_w", [128, FX * 4])
    conv_b = inp("conv_b", [128, FX])
    dt_bias = inp("dt_bias", [64, 1])
    a_log = inp("a_log", [64, 1])
    d_col = inp("d_col", [128, DIT])
    norm_w = inp("norm_w", [128, DIT])
    gw_sl = inp("gw_sl", [128, 16384])
    vw_sl = inp("vw_sl", [128, 16384])
    ow_sl = inp("ow_sl", [128, 16384])
    gate_b = inp("gate_b", [128, FFT])
    val_b = inp("val_b", [128, FFT])
    out_b = inp("out_b", [128, KD])
    state_w5 = inp("state_w5", [5, 4])
    tail_mask = inp("tail_mask", [128, 4])
    perm64 = inp("perm64", [64, 64])

    yout = nc.dram_tensor("yout", [DM, T], F32, kind="ExternalOutput")

    xs_dram = nc.dram_tensor("xs_dram", [NCH * 128, DIT * 128], F32)
    y_dram = nc.dram_tensor("y_dram", [NCH * 128, DIT * 128], F32)
    yf_dram = nc.dram_tensor("yf_dram", [DI, T], F32)
    st_dram = nc.dram_tensor("st_dram", [NCH, 128, DI], F32)
    a_dram = nc.dram_tensor("a_dram", [NCH, 64 * 128], F32)
    a4_dram = nc.dram_tensor("a4_dram", [NCH, 64 * 128], F32)
    alast_dram = nc.dram_tensor("alast_dram", [1, 512], F32)
    misc_dram = nc.dram_tensor("misc_dram", [128, 64], F32)
    ln_dram = nc.dram_tensor("ln_dram", [8, T], F32)
    mid_dram = nc.dram_tensor("mid_dram", [FFH, T], F32)
    mi_dram = nc.dram_tensor("mi_dram", [16 * 128, 4096], BF16)
    tail_in = nc.dram_tensor("tail_in", [128, 102], F32)
    tail_out = nc.dram_tensor("tail_out", [512, 102], F32)
    sag_in = nc.dram_tensor("sag_in", [128, DI], F32)
    sag_out = nc.dram_tensor("sag_out", [512, DI], F32)
    asum_in = nc.dram_tensor("asum_in", [64, 1], F32)
    asum_out = nc.dram_tensor("asum_out", [256, 1], F32)
    gam_in = nc.dram_tensor("gam_in", [1, 4], F32)
    gam_out = nc.dram_tensor("gam_out", [1, 4], F32)
    mg_in = nc.dram_tensor("mg_in", [128, 16384], BF16)
    mv_in = nc.dram_tensor("mv_in", [128, 16384], BF16)
    mo_in = nc.dram_tensor("mo_in", [128, 16384], BF16)
    mg_out = nc.dram_tensor("mg_out", [1024, 16384], BF16)
    mv_out = nc.dram_tensor("mv_out", [1024, 16384], BF16)
    mo_out = nc.dram_tensor("mo_out", [1024, 16384], BF16)

    taps = {}
    for tname in DEV_TAPS:
        shp = {"xs": [DI + 256, T], "dt": [64, T], "acsT": [64, T], "y": [DI, T],
               "yf": [DI, T], "mid": [DM, T], "state": [128, DI],
               "ffnmid": [FFH, T]}[tname]
        taps[tname] = nc.dram_tensor("tap_" + tname, shp, F32, kind="ExternalOutput")

    from contextlib import ExitStack
    with ExitStack() as stk:
        E = stk.enter_context
        AR = E(nc.sbuf_tensor("arena", [128, ARENA_N], F32))

        def ar(off, n, rows=128):
            return AR[0:rows, off:off + n]

        wq = [E(nc.sbuf_tensor(f"wq{i}", [128, KD * 128], F32)) for i in range(2)]
        dt_sb = E(nc.sbuf_tensor("dt_sb", [64, T], F32))
        acsT = E(nc.sbuf_tensor("acsT", [64, T], F32))
        dt_tm = E(nc.sbuf_tensor("dt_tm", [128, NCH * 64], F32))
        adt_tm = E(nc.sbuf_tensor("adt_tm", [128, NCH * 64], F32))
        acs_col = E(nc.sbuf_tensor("acs_col", [128, NCH * 64], F32))
        alast_sb = E(nc.sbuf_tensor("alast_sb", [64, NCH + 8], F32))
        ds_all = E(nc.sbuf_tensor("ds_all", [128, NCH * 64], F32))
        dch_rep = E(nc.sbuf_tensor("dch_rep", [128, NCH * 64], F32))
        arep = E(nc.sbuf_tensor("arep", [128, 64], F32))
        dcol = E(nc.sbuf_tensor("dcol", [128, DIT], F32))
        nw_col = E(nc.sbuf_tensor("nw_col", [128, DIT], F32))
        cw_sb = E(nc.sbuf_tensor("cw_sb", [128, FX * 4], F32))
        cb_sb = E(nc.sbuf_tensor("cb_sb", [128, FX], F32))
        ones_sb = E(nc.sbuf_tensor("ones_sb", [128, 128], F32))
        u01 = E(nc.sbuf_tensor("u01", [128, 128], F32))
        ident = E(nc.sbuf_tensor("ident", [128, 128], F32))
        gu = E(nc.sbuf_tensor("gu", [128, 128], F32))
        accfix = E(nc.sbuf_tensor("accfix", [128, 102], F32))
        tail_sb = E(nc.sbuf_tensor("tail_sb", [128, 102], F32))
        tailprev = E(nc.sbuf_tensor("tailprev", [128, 102], F32))
        xsfix = E(nc.sbuf_tensor("xsfix", [128, 102], F32))
        tmask_sb = E(nc.sbuf_tensor("tmask_sb", [128, 4], F32))
        fb_sb = E(nc.sbuf_tensor("fb_sb", [128, 144], F32))
        small = E(nc.sbuf_tensor("small", [128, 64], F32))
        pm_sb = E(nc.sbuf_tensor("pm_sb", [64, 64], F32))
        ap64 = E(nc.sbuf_tensor("ap64", [64, T], F32))
        pA = E(nc.psum_tensor("pA", [128, 1024], F32))
        pB = E(nc.psum_tensor("pB", [128, 1024], F32))
        pC = E(nc.psum_tensor("pC", [128, 1024], F32))
        pD = E(nc.psum_tensor("pD", [128, 1024], F32))

        sems = {name: E(nc.semaphore(name)) for name in S.cnt}

        xT_sb = ar(A_XT, KD * T)
        acc = ar(A_ACC, T)
        xs_f = ar(A_XSF, T)
        BC = ar(A_BC, 2 * T)

        ld0 = S.dma("sync", "dc", xT_sb, bass.AP(xT, 0, [[T, 128], [128 * T, KD], [1, T]]))
        S.dma("sync", "dc", cw_sb[:], conv_w[:, :])
        S.dma("sync", "dc", cb_sb[:], conv_b[:, :])
        S.dma("sync", "dc", tmask_sb[:], tail_mask[:, :])
        S.dma("sync", "dc", small[0:64, 0:1], dt_bias[:, :])
        S.dma("sync", "dc", small[0:64, 1:2], a_log[:, :])
        S.dma("sync", "dc", nw_col[:], norm_w[:, :])
        S.dma("sync", "dc", fb_sb[:, 0:64], gate_b[:, :])
        S.dma("sync", "dc", fb_sb[:, 64:128], val_b[:, :])
        S.dma("sync", "dc", fb_sb[:, 128:144], out_b[:, :])
        S.dma("sync", "dc", pm_sb[:], perm64[:, :])
        ld_const = ("dc", S.cnt["dc"])
        S.op("vector", lambda e: e.memset(ones_sb[:], 1.0), inc="v")
        S.op("gpsimd", lambda e: e.affine_select(
            u01[:], ones_sb[:], [[1, 128]], OP.is_ge, 0.0, base=0,
            channel_multiplier=-1), waits=S.at("v"), inc="g")
        S.op("gpsimd", lambda e: e.affine_select(
            ident[:], ones_sb[:], [[1, 128]], OP.is_equal, 0.0, base=0,
            channel_multiplier=-1), waits=S.at("v"), inc="g")
        g_const = S.cnt["g"]
        S.op("scalar", lambda e: e.activation(small[0:64, 4:5], small[0:64, 1:2],
                                              AF.Exp),
             waits=[ld_const], inc="s")
        S.op("vector", lambda e: e.tensor_scalar_mul(small[0:64, 5:6],
                                                     small[0:64, 4:5], -1.0),
             waits=S.at("s"), inc="v")
        S.dma("scalar", "dbs", misc_dram[0:1, 0:64], small[0:64, 5:6], waits=S.at("v"))
        bounce1 = ("dbs", S.cnt["dbs"])
        S.dma("sync", "dc", arep[:, 0:64],
              bass.AP(misc_dram, 0, [[0, 128], [1, 64]]),
              waits=[bounce1])
        S.dma("sync", "dc", dcol[:], I["d_col"][:, :])

        dc_all = ("dc", S.cnt["dc"])
        wacc = E(nc.sbuf_tensor("wacc", [128, 8], F32))
        grow = E(nc.sbuf_tensor("grow", [1, 8], F32))
        rgc = E(nc.sbuf_tensor("rgc", [128, 4], F32))
        p0b = [ar(A_B1, 2048), ar(A_B1 + 2048, 2048)]
        p0t = ar(A_B2, 2048)
        p0m_bf = ar(A_B2 + 2048, 1024).bitcast(BF16)
        subs = []
        for mat, src, dst in [(0, gw_sl, mg_in), (1, vw_sl, mv_in), (2, ow_sl, mo_in)]:
            for r in range(src.shape[0] // 128):
                for c in range(src.shape[1] // 2048):
                    subs.append((mat, src, dst, r, c))
        S.op("gpsimd", lambda e: e.memset(wacc[:, 0:6], 0.0), waits=[dc_all],
             inc="g")
        p0_consumed = [0, 0]
        nsub = [0, 0, 0]
        for i, (mat, src, dst, r, c) in enumerate(subs):
            bb = p0b[i % 2]
            ld = S.dma("sync", "dp0" if i % 2 == 0 else "dp1", bb,
                       src[128 * r:128 * (r + 1), 2048 * c:2048 * (c + 1)],
                       waits=[("s", p0_consumed[i % 2])])
            S.op("scalar", lambda e, bb=bb: e.activation(
                p0t, bb, AF.Abs, accum_out=wacc[:, 6:7]),
                waits=[ld, dc_all, ("g", S.cnt["g"])], inc="s")
            par = nsub[mat] % 2
            S.op("gpsimd", lambda e, mat=mat, par=par: e.tensor_add(
                wacc[:, 3 * (1 - par) + mat:3 * (1 - par) + mat + 1],
                wacc[:, 3 * par + mat:3 * par + mat + 1], wacc[:, 6:7]),
                waits=S.at("s"), inc="g")
            nsub[mat] += 1
            p0_consumed[i % 2] = S.cnt["s"]
        S.op("tensor", lambda e: e.matmul(pC[0:1, 0:4], ones_sb[:, 0:1],
                                          wacc[:, 0:4], start=True, stop=True),
             waits=S.at("g"), inc="t")
        S.op("vector", lambda e: e.tensor_copy(grow[:, 0:4], pC[0:1, 0:4]),
             waits=S.at("t"), inc="v")
        p0a_dma = [("dp0", S.cnt["dp0"]), ("dp1", S.cnt["dp1"])]

        gsum_d = S.dma("gpsimd", "dg", gam_in[:, :], grow[0:1, 0:4], waits=S.at("v"))

        def cc_gam(e):
            return e.collective_compute(
                "AllReduce", OP.add, replica_groups=[list(range(8))],
                ins=[gam_in.ap().opt()], outs=[gam_out.ap().opt()])
        S.op("gpsimd", cc_gam, waits=[gsum_d], inc="cc")
        cc_gam_n = S.cnt["cc"]
        ldg = S.dma("sync", "db", grow[0:1, 4:8], gam_out[:, :],
                    waits=[("cc", cc_gam_n)])

        p0b4 = [ar(A_B1, 2048), ar(A_B1 + 2048, 2048),
                ar(A_B3, 2048), ar(A_B3 + 2048, 2048)]
        p0bs = {"i": 0, "iload": 0, "mdone": ("dpm", 0), "gam_bd": None,
                "ldrg": None, "pc": [S.cnt["v"]] * 4, "lds": {}}

        def p0b_head():
            S.op("vector", lambda e: e.tensor_scalar(
                grow[0:1, 0:4], grow[0:1, 4:8], 1.0 / 16777216.0, 1e-5,
                OP.mult, OP.max), waits=[ldg], inc="v")
            S.op("vector", lambda e: e.reciprocal(grow[0:1, 4:8],
                                                  grow[0:1, 0:4]), inc="v")
            S.dma("scalar", "dbs", misc_dram[6:7, 0:4], grow[0:1, 0:4],
                  waits=S.at("v"))
            p0bs["gam_bd"] = S.dma("scalar", "dbs", misc_dram[5:6, 0:4],
                                   grow[0:1, 4:8], waits=S.at("v"))
            p0bs["ldrg"] = S.dma("sync", "db", rgc[:, 0:4],
                                 bass.AP(misc_dram, 5 * 64, [[0, 128], [1, 4]]),
                                 waits=[p0bs["gam_bd"]])

        def p0b_load():
            i = p0bs["iload"]
            mat, src, dst, r, c = subs[i]
            p0bs["lds"][i] = S.dma(
                "sync", ["dp0", "dp1", "dp2", "dp3"][i % 4], p0b4[i % 4],
                src[128 * r:128 * (r + 1), 2048 * c:2048 * (c + 1)],
                waits=[("v", p0bs["pc"][i % 4]), p0bs["ldrg"]])
            p0bs["iload"] += 1

        def p0b_sub():
            i = p0bs["i"]
            mat, src, dst, r, c = subs[i]
            bb = p0b4[i % 4]
            S.op("vector", lambda e, bb=bb, mat=mat: e.tensor_scalar(
                p0t, bb, rgc[:, mat:mat + 1], MAGIC, OP.mult, OP.add),
                waits=[p0bs["lds"].pop(i), p0bs["mdone"]], inc="v")

            def bfap(off, freedims):
                b = p0m_bf
                return bass.AP(b.tensor, b.offset + off,
                               [list(b.ap[0])] + freedims)
            for hh in range(2):
                S.op("vector", lambda e, hh=hh: e.tensor_scalar(
                    ar(A_B2 + 3072, 1024), p0t[:, 1024 * hh:1024 * (hh + 1)],
                    MAGIC, -1.0, OP.subtract, OP.max), inc="v")
                if mat < 2:
                    mo_ap = bfap(256 * hh, [[512, 4], [1, 256]])
                else:
                    mo_ap = p0m_bf[:, 1024 * hh:1024 * (hh + 1)]
                S.op("vector", lambda e, mo_ap=mo_ap: e.tensor_scalar_min(
                    mo_ap, ar(A_B2 + 3072, 1024), 1.0), inc="v")
            p0bs["pc"][i % 4] = S.cnt["v"]
            if mat < 2:
                dst_ap = bass.AP(dst, 512 * c,
                                 [[16384, 128], [4096, 4], [1, 512]])
            else:
                dst_ap = bass.AP(dst, 1024 * c,
                                 [[16384, 128], [8192, 2], [1, 1024]])
            p0bs["mdone"] = S.dma("scalar", "dpm", dst_ap,
                                  p0m_bf[:, 0:2048], waits=S.at("v"))
            p0bs["i"] += 1

        wq_free = [0, 0]
        ps_free = [0, 0]
        last_silu = 0
        last_spill = ("dxs", 0)
        dt_ready_v = 0

        for f in range(FX + 1):
            buf = f % 2
            wsl = wq[buf]
            dwn = "dw0" if buf == 0 else "dw1"
            if f < FX:
                ldw = S.dma("sync", dwn, wsl[:], w_xbc[f, :, :],
                            waits=[("t", wq_free[buf])])
            else:
                ldw = S.dma("sync", dwn, wsl[:, 0:KD * 64], w_dt[:, :],
                            waits=[("t", wq_free[buf])])
            ps = [pA, pB][buf]
            rows = 128 if f < FX else 64

            def mm(e, ps=ps, wsl=wsl, rows=rows):
                last = None
                for cb in range(2):
                    for k in range(KD):
                        last = e.matmul(
                            ps[0:rows, cb * 512:(cb + 1) * 512],
                            wsl[:, k * rows:(k + 1) * rows],
                            xT_sb[:, k * T + cb * 512: k * T + (cb + 1) * 512],
                            start=(k == 0), stop=(k == KD - 1))
                return last
            tcnt = S.op("tensor", mm,
                        waits=[ldw, dc_all, ("v", ps_free[buf])],
                        inc="t")
            wq_free[buf] = tcnt

            if f < FX:
                pp = [acc, ar(A_B4, T)]

                def c0(e, ps=ps, f=f):
                    return e.tensor_scalar(pp[0], ps[:],
                                           cw_sb[:, 4 * f + 3:4 * f + 4],
                                           None, OP.mult)
                S.op("vector", c0, waits=[("t", tcnt), ("s", last_silu)], inc="v")
                for k in (2, 1, 0):
                    lo = 3 - k
                    src, dst2 = pp[(2 - k) % 2], pp[(3 - k) % 2]

                    def ck(e, ps=ps, f=f, k=k, lo=lo, src=src, dst2=dst2):
                        return e.scalar_tensor_tensor(
                            dst2[:, lo:T], ps[:, 0:T - lo],
                            cw_sb[:, 4 * f + k:4 * f + k + 1],
                            src[:, lo:T], OP.mult, OP.add)
                    S.op("vector", ck, inc="v")

                    def hd(e, lo=lo, src=src, dst2=dst2):
                        return e.tensor_copy(dst2[:, 0:lo], src[:, 0:lo])
                    S.op("vector", hd, inc="v")
                S.op("vector", lambda e, f=f: e.tensor_copy(
                    accfix[:, 3 * f:3 * f + 3], pp[1][:, 0:3]), inc="v")
                S.op("vector", lambda e, ps=ps, f=f: e.tensor_copy(
                    tail_sb[:, 3 * f:3 * f + 3], ps[:, T - 3:T]), inc="v")
                ps_free[buf] = S.cnt["v"]
                dst = xs_f if f < 32 else BC[:, (f - 32) * T:(f - 31) * T]

                def si(e, dst=dst, f=f):
                    return e.activation(dst, pp[1], AF.Silu, bias=cb_sb[:, f:f + 1])
                last_silu = S.op("scalar", si,
                                 waits=[("v", S.cnt["v"]), last_spill], inc="s")
                if f < 32:
                    last_spill = S.dma("scalar", "dxs",
                                       bass.AP(xs_dram, 128 * f,
                                               [[4096, 128], [524288, 8],
                                                [1, 128]]),
                                       xs_f, waits=[("s", last_silu)])
                if f == 13:
                    p0b_head()
                if 13 <= f <= 24:
                    p0b_load()
                    p0b_load()
                if 14 <= f <= 25:
                    p0b_sub()
                    p0b_sub()
            else:
                def d1(e, ps=ps):
                    return e.tensor_scalar(acc[0:64, :], ps[0:64, :],
                                           small[0:64, 0:1], None, OP.add)
                S.op("vector", d1, waits=[("t", tcnt), ("s", last_silu)]
                     + p0a_dma, inc="v")
                ps_free[buf] = S.cnt["v"]
                S.op("scalar", lambda e: e.activation(
                    ar(A_XSF, T, 64), acc[0:64, :], AF.Abs),
                    waits=[("v", S.cnt["v"]), last_spill], inc="s")
                S.op("scalar", lambda e: e.activation(
                    ar(A_B1, T, 64), ar(A_XSF, T, 64), AF.Exp, scale=-1.0), inc="s")
                S.op("vector", lambda e: e.tensor_scalar_max(
                    ar(A_B2 + 2048, T, 64), acc[0:64, :], 0.0),
                    waits=S.at("s") + [p0bs["mdone"]], inc="v")
                S.op("vector", lambda e: e.tensor_scalar_add(
                    ar(A_B2, T, 64), ar(A_B1, T, 64), 1.0), inc="v")
                S.op("scalar", lambda e: e.activation(
                    ar(A_B1, T, 64), ar(A_B2, T, 64), AF.Ln),
                    waits=S.at("v"), inc="s")
                S.op("vector", lambda e: e.tensor_add(
                    dt_sb[:], ar(A_B2 + 2048, T, 64), ar(A_B1, T, 64)),
                    waits=S.at("s"), inc="v")
                dt_ready_v = S.cnt["v"]

        p1_dsc = ("dxs", S.cnt["dxs"])

        assert p0bs["i"] == 24
        gam_bd = p0bs["gam_bd"]
        mdone = p0bs["mdone"]
        p0b_dma = [mdone, ("dp0", S.cnt["dp0"]), ("dp1", S.cnt["dp1"]),
                   ("dp2", S.cnt["dp2"]), ("dp3", S.cnt["dp3"])]

        S.dma("gpsimd", "dg", tail_in[:, :], tail_sb[:], waits=S.at("v"))

        def cc_tail(e):
            return e.collective_compute(
                "AllGather", OP.bypass, replica_groups=[[0, 1, 2, 3], [4, 5, 6, 7]],
                ins=[tail_in.ap().opt()], outs=[tail_out.ap().opt()])
        S.op("gpsimd", cc_tail, waits=S.at("dg"), inc="cc")
        cc_tail_n = S.cnt["cc"]
        S.op("vector", lambda e: e.memset(tailprev[:], 0.0), inc="v")
        tpp = [tailprev, tail_sb]
        for j in range(4):
            ldb = S.dma("sync", "db", xsfix[:], tail_out[128 * j:128 * (j + 1), :],
                        waits=[("cc", cc_tail_n), ("v", S.cnt["v"])])

            def addb(e, j=j):
                return e.scalar_tensor_tensor(tpp[(j + 1) % 2][:], xsfix[:],
                                              tmask_sb[:, j:j + 1], tpp[j % 2][:],
                                              OP.mult, OP.add)
            S.op("vector", addb, waits=[ldb], inc="v")
        for f in range(FX):
            S.op("vector", lambda e, f=f: e.scalar_tensor_tensor(
                xsfix[:, 3 * f:3 * f + 3], tail_sb[:, 3 * f:3 * f + 3],
                cw_sb[:, 4 * f + 0:4 * f + 1], accfix[:, 3 * f:3 * f + 3],
                OP.mult, OP.add), inc="v")
            S.op("vector", lambda e, f=f: e.scalar_tensor_tensor(
                tailprev[:, 3 * f:3 * f + 2], tail_sb[:, 3 * f + 1:3 * f + 3],
                cw_sb[:, 4 * f + 1:4 * f + 2], xsfix[:, 3 * f:3 * f + 2],
                OP.mult, OP.add), inc="v")
            S.op("vector", lambda e, f=f: e.tensor_copy(
                tailprev[:, 3 * f + 2:3 * f + 3], xsfix[:, 3 * f + 2:3 * f + 3]),
                inc="v")
            S.op("vector", lambda e, f=f: e.scalar_tensor_tensor(
                xsfix[:, 3 * f:3 * f + 1], tail_sb[:, 3 * f + 2:3 * f + 3],
                cw_sb[:, 4 * f + 2:4 * f + 3], tailprev[:, 3 * f:3 * f + 1],
                OP.mult, OP.add), inc="v")
            S.op("vector", lambda e, f=f: e.tensor_copy(
                xsfix[:, 3 * f + 1:3 * f + 3], tailprev[:, 3 * f + 1:3 * f + 3]),
                inc="v")

            def sif(e, f=f):
                return e.activation(tail_sb[:, 3 * f:3 * f + 3],
                                    xsfix[:, 3 * f:3 * f + 3],
                                    AF.Silu, bias=cb_sb[:, f:f + 1])
            S.op("scalar", sif, waits=S.at("v"), inc="s")
        fix_s = S.cnt["s"]
        for f in range(32):
            S.dma("gpsimd", "dg",
                  bass.AP(xs_dram, 128 * f, [[4096, 128], [1, 3]]),
                  tail_sb[:, 3 * f:3 * f + 3],
                  waits=[("s", fix_s), p1_dsc])
        for f in (32, 33):
            S.op("vector", lambda e, f=f: e.tensor_copy(
                BC[:, (f - 32) * T:(f - 32) * T + 3], tail_sb[:, 3 * f:3 * f + 3]),
                waits=[("s", fix_s)], inc="v")
        patch_g = ("dg", S.cnt["dg"])
        patch_v = S.cnt["v"]

        btm = E(nc.sbuf_tensor("btm", [128, 128], F32))
        facr = E(nc.sbuf_tensor("facr", [128, 64], F32))
        asw = E(nc.sbuf_tensor("asw", [5, 64], F32))
        w5_sb = E(nc.sbuf_tensor("w5_sb", [5, 4], F32))
        alr = E(nc.sbuf_tensor("alr", [128, 64], F32))
        asum_sb = E(nc.sbuf_tensor("asum_sb", [64, 1], F32))

        def mkap(x, off, freedims, parts=None):
            b = x if isinstance(x, bass.AP) else x[:]
            pd = list(b.ap[0])
            if parts is not None:
                pd = [pd[0], parts]
            return bass.AP(b.tensor, b.offset + off, [pd] + [list(d) for d in freedims])

        for m in range(NCH):
            S.op("tensor", lambda e, m=m: e.transpose(
                pC[0:128, 64 * m:64 * (m + 1)], dt_sb[:, 128 * m:128 * (m + 1)],
                ident[0:64, 0:64]),
                waits=[("v", dt_ready_v), ("g", g_const)], inc="t")
        S.op("vector", lambda e: e.tensor_copy(dt_tm[:, 0:512], pC[:, 0:512]),
             waits=S.at("t"), inc="v")
        S.op("vector", lambda e: e.tensor_tensor(
            adt_tm[:, 0:512], dt_tm[:, 0:512],
            mkap(arep, 0, [[0, NCH], [1, 64]]), OP.mult),
            waits=S.at("dc"), inc="v")
        adt_v = S.cnt["v"]
        for m in range(NCH):
            S.op("tensor", lambda e, m=m: e.matmul(
                pD[0:64, 128 * m:128 * (m + 1)], adt_tm[:, 64 * m:64 * (m + 1)],
                u01[:], start=True, stop=True), waits=[("v", adt_v)], inc="t")
            S.op("tensor", lambda e, m=m: e.matmul(
                pC[0:128, 512 + 64 * m:512 + 64 * (m + 1)], u01[:],
                adt_tm[:, 64 * m:64 * (m + 1)], start=True, stop=True), inc="t")
        acs_t = S.cnt["t"]
        S.op("vector", lambda e: e.tensor_copy(acsT[:, :], pD[0:64, 0:1024]),
             waits=[("t", acs_t)], inc="v")
        S.op("vector", lambda e: e.tensor_copy(acs_col[:, :], pC[:, 512:1024]),
             inc="v")
        acs_v = S.cnt["v"]
        acsT_d = S.dma("scalar", "dbs",
                       bass.AP(a_dram, 0, [[128, 64], [8192, 8], [1, 128]]),
                       acsT[:, :], waits=[("v", acs_v)])
        for cb2 in range(2):
            S.op("tensor", lambda e, cb2=cb2: e.matmul(
                pD[0:64, 512 * cb2:512 * (cb2 + 1)], pm_sb[:],
                acsT[:, 512 * cb2:512 * (cb2 + 1)], start=True, stop=True),
                waits=([("v", acs_v), dc_all] if cb2 == 0 else ()), inc="t")
        S.op("vector", lambda e: e.tensor_copy(ap64[:, :], pD[0:64, 0:1024]),
             waits=S.at("t"), inc="v")
        a4_d = S.dma("scalar", "dbs",
                     bass.AP(a4_dram, 0, [[128, 64], [8192, 8], [1, 128]]),
                     ap64[:, :], waits=[("v", S.cnt["v"])])
        alast_d = S.dma("scalar", "dbs", alast_dram[0:1, 0:512], acs_col[127:128, :],
                        waits=[("v", acs_v)])
        S.op("vector", lambda e: e.reduce_sum(
            asum_sb[:, 0:1], mkap(acsT, 127, [[128, NCH]], parts=64),
            axis=AX.X), waits=[("v", acs_v)], inc="v")
        asum_v = S.cnt["v"]
        for m in range(NCH):
            lda = S.dma("sync", "db", alr[:, 0:64],
                        bass.AP(alast_dram, 64 * m, [[0, 128], [1, 64]]),
                        waits=[alast_d, ("v", S.cnt["v"]),
                               ("s", S.cnt["s"])])
            S.op("vector", lambda e, m=m: e.tensor_sub(
                adt_tm[:, 64 * m:64 * (m + 1)], alr[:, 0:64],
                acs_col[:, 64 * m:64 * (m + 1)]), waits=[lda], inc="v")
            S.op("scalar", lambda e, m=m: e.activation(
                dch_rep[:, 64 * m:64 * (m + 1)], alr[:, 0:64], AF.Exp),
                waits=[lda], inc="s")
        S.op("scalar", lambda e: e.activation(
            ds_all[:, :], adt_tm[:, :], AF.Exp), waits=S.at("v"), inc="s")
        apath_s = S.cnt["s"]

        xs_ch = ar(A_B1, DI)
        xd = ar(A_B2, DI)
        y_sb = ar(A_B4, DI)
        s_run = ar(A_B5, DI)
        repA = ar(A_B3, 2048)
        repB = ar(A_B3 + 2048, 2048)
        S.op("vector", lambda e: e.memset(s_run, 0.0), inc="v")

        st_spill = [("dss", 0)]
        p2_y_d = []
        prev_t = 0
        for m in range(NCH):
            xw = [p1_dsc, ("t", prev_t), ("v", S.cnt["v"])]
            if m == 0:
                xw.append(patch_g)
                xw += p0b_dma
            ldx = S.dma("sync", "dx", xs_ch,
                        bass.AP(xs_dram, 524288 * m, [[4096, 128], [1, 4096]]),
                        waits=xw)
            BTm = BC[:, 128 * m:128 * (m + 1)]
            CTm = BC[:, T + 128 * m:T + 128 * (m + 1)]
            bcw = [("v", patch_v)] if m == 0 else []
            gt = S.op("tensor", lambda e, BTm=BTm, CTm=CTm: e.matmul(
                pD[:, 0:128], BTm, CTm, start=True, stop=True),
                waits=bcw + [("v", S.cnt["v"])], inc="t")
            bt = S.op("tensor", lambda e, BTm=BTm: e.transpose(
                pD[:, 128:256], BTm, ident[:]), inc="t")
            S.op("vector", lambda e: e.tensor_tensor(gu[:], pD[:, 0:128], u01[:],
                                                     OP.mult),
                 waits=[("t", bt)], inc="v")
            S.op("vector", lambda e: e.tensor_copy(btm[:], pD[:, 128:256]), inc="v")

            for hf in range(2):
                tws = [ldx, ("v", S.cnt["v"])]
                for i in range(16):
                    ps = pA if i < 8 else pB
                    S.op("tensor", lambda e, i=i, hf=hf, ps=ps: e.transpose(
                        ps[:, 128 * (i % 8):128 * (i % 8 + 1)],
                        xs_ch[:, 2048 * hf + 128 * i: 2048 * hf + 128 * (i + 1)],
                        ident[:]), waits=tws if i == 0 else (), inc="t")
                txd = S.cnt["t"]
                for h2 in range(2):
                    ps = [pA, pB][h2]

                    def evx(e, ps=ps, hf=hf, h2=h2, m=m):
                        in1 = mkap(dt_tm, 64 * m + 32 * hf + 16 * h2,
                                   [[2, 8], [1, 2], [0, 64]])
                        return e.tensor_tensor(
                            xd[:, 2048 * hf + 1024 * h2:2048 * hf + 1024 * (h2 + 1)],
                            ps[:], in1, OP.mult)
                    S.op("vector", evx, waits=[("t", txd)], inc="v")
                xd_v = S.cnt["v"]

                for qr in range(2):
                    hq = 32 * hf + 16 * qr
                    ldr = S.dma("sync", "dr", repA,
                                bass.AP(a_dram, 8192 * m + 128 * hq,
                                        [[0, 128], [1, 2048]]),
                                waits=[acsT_d, st_spill[-1],
                                       ("t", S.cnt["t"]), ("v", S.cnt["v"])])

                    def mtsub(e, hq=hq, m=m):
                        in1 = mkap(acs_col, 64 * m + hq, [[1, 16], [0, 128]])
                        return e.tensor_tensor(repB, repA, in1, OP.subtract)
                    S.op("vector", mtsub,
                         waits=[ldr, ("v", acs_v), st_spill[-1]], inc="v")
                    S.op("vector", lambda e: e.tensor_scalar_min(repA, repB, 0.0),
                         inc="v")
                    S.op("scalar", lambda e: e.activation(repB, repA, AF.Exp),
                         waits=S.at("v"), inc="s")

                    def mtgu(e):
                        in1 = mkap(gu, 0, [[0, 16], [1, 128]])
                        return e.tensor_tensor(repA, repB, in1, OP.mult)
                    S.op("vector", mtgu, waits=S.at("s"), inc="v")
                    mt_v = S.cnt["v"]

                    psq = [pC, pD][qr]
                    for hh in range(16):
                        h = hq + hh
                        slot = hh // 2
                        base = 64 * (h % 2)

                        def ymm(e, h=h, hh=hh, psq=psq, slot=slot, base=base):
                            return e.matmul(
                                psq[base:base + 64, 128 * slot:128 * (slot + 1)],
                                xd[:, 64 * h:64 * (h + 1)],
                                repA[:, 128 * hh:128 * (hh + 1)],
                                start=True, stop=True)
                        S.op("tensor", ymm,
                             waits=[("v", mt_v)] if hh == 0 else (), inc="t")
                ydiag_t = S.cnt["t"]

                def dtmp(e, hf=hf):
                    in1 = mkap(dcol, 16 * hf, [[1, 16], [0, 128]])
                    return e.tensor_tensor(repB, xs_ch[:, 2048 * hf:2048 * (hf + 1)],
                                           in1, OP.mult)
                S.op("vector", dtmp, waits=[("t", ydiag_t),
                                            ("dys", S.cnt["dys"])], inc="v")
                for h2 in range(2):
                    ps = [pC, pD][h2]

                    def yev(e, ps=ps, hf=hf, h2=h2):
                        return e.tensor_tensor(
                            y_sb[:, 2048 * hf + 1024 * h2:2048 * hf + 1024 * (h2 + 1)],
                            repB[:, 1024 * h2:1024 * (h2 + 1)], ps[:], OP.add)
                    S.op("vector", yev, inc="v")
                yv = S.cnt["v"]
                p2_y_d.append(S.dma(
                    "scalar", "dys",
                    bass.AP(y_dram, 524288 * m + 2048 * hf,
                            [[4096, 128], [1, 2048]]),
                    y_sb[:, 2048 * hf:2048 * (hf + 1)], waits=[("v", yv)]))

                def xd2f(e, hf=hf, m=m):
                    in1 = mkap(ds_all, 64 * m + 32 * hf, [[1, 32], [0, 64]])
                    return e.tensor_tensor(repB, xd[:, 2048 * hf:2048 * (hf + 1)],
                                           in1, OP.mult)
                S.op("vector", xd2f, waits=[("s", apath_s)], inc="v")
                xd2_v = S.cnt["v"]
                for i in range(4):
                    ps = [pA, pB][i // 2]

                    def smm(e, i=i, ps=ps):
                        return e.matmul(ps[:, 512 * (i % 2):512 * (i % 2 + 1)],
                                        btm[:], repB[:, 512 * i:512 * (i + 1)],
                                        start=True, stop=True)
                    S.op("tensor", smm,
                         waits=[("v", xd2_v)] if i == 0 else (), inc="t")
                prev_t = S.cnt["t"]
                for h2 in range(2):
                    ps = [pA, pB][h2]
                    S.op("vector", lambda e, ps=ps, h2=h2: e.tensor_copy(
                        repB[:, 1024 * h2:1024 * (h2 + 1)], ps[:]),
                        waits=[("t", prev_t)], inc="v")
                stv = S.cnt["v"]
                st_spill.append(S.dma(
                    "scalar", "dss", st_dram[m, :, 2048 * hf:2048 * (hf + 1)], repB,
                    waits=[("v", stv)]))
                tmpv = xd[:, 2048 * hf:2048 * (hf + 1)]

                def sdec(e, hf=hf, m=m, tmpv=tmpv):
                    in1 = mkap(dch_rep, 64 * m + 32 * hf, [[1, 32], [0, 64]])
                    return e.tensor_tensor(tmpv, s_run[:, 2048 * hf:2048 * (hf + 1)],
                                           in1, OP.mult)
                S.op("vector", sdec, waits=[("s", apath_s)], inc="v")
                S.op("vector", lambda e, hf=hf, tmpv=tmpv: e.tensor_add(
                    s_run[:, 2048 * hf:2048 * (hf + 1)], tmpv, repB), inc="v")

        sloc_d = S.dma("gpsimd", "dg", sag_in[:, :], s_run, waits=S.at("v"))
        asum_d = S.dma("gpsimd", "dg", asum_in[:, :], asum_sb[:, 0:1],
                       waits=[("v", asum_v)])

        def cc_st(e):
            return e.collective_compute(
                "AllGather", OP.bypass, replica_groups=[[0, 1, 2, 3], [4, 5, 6, 7]],
                ins=[sag_in.ap().opt()], outs=[sag_out.ap().opt()])
        S.op("gpsimd", cc_st, waits=[sloc_d], inc="cc")

        def cc_as(e):
            return e.collective_compute(
                "AllGather", OP.bypass, replica_groups=[[0, 1, 2, 3], [4, 5, 6, 7]],
                ins=[asum_in.ap().opt()], outs=[asum_out.ap().opt()])
        S.op("gpsimd", cc_as, waits=[asum_d], inc="cc")
        cc_state_n = S.cnt["cc"]
        def cc_ag(name_in, name_out):
            def f(e, name_in=name_in, name_out=name_out):
                return e.collective_compute(
                    "AllGather", OP.bypass, replica_groups=[list(range(8))],
                    ins=[name_in.ap().opt()], outs=[name_out.ap().opt()])
            return f
        S.op("gpsimd", cc_ag(mg_in, mg_out), waits=[mdone], inc="cc")
        S.op("gpsimd", cc_ag(mv_in, mv_out), inc="cc")
        S.op("gpsimd", cc_ag(mo_in, mo_out), inc="cc")
        cc_m_n = S.cnt["cc"]
        ld1r = S.dma("sync", "db", asw[4:5, :], ones_sb[0:1, 0:64],
                     waits=S.at("v"))
        ld5 = S.dma("sync", "db", asw[0:4, :],
                    bass.AP(asum_out, 0, [[64, 4], [1, 64]]),
                    waits=[("cc", cc_state_n)])
        ldw5 = S.dma("sync", "db", w5_sb[:, :], state_w5[:, :])
        fac_t = S.op("tensor", lambda e: e.matmul(
            pD[0:4, 0:64], w5_sb[:, :], asw[:, :], start=True, stop=True),
            waits=[ld5, ldw5, ld1r, ("v", S.cnt["v"])], inc="t")
        S.op("scalar", lambda e: e.activation(facr[0:4, 0:64], pD[0:4, 0:64],
                                              AF.Exp),
             waits=[("t", fac_t)], inc="s")
        facd = S.dma("scalar", "dbs", misc_dram[1:5, 0:64], facr[0:4, 0:64],
                     waits=S.at("s"))
        spp = [s_run, ar(A_B4, DI)]
        for j in range(4):
            ldb = S.dma("sync", "dx", xs_ch, sag_out[128 * j:128 * (j + 1), :],
                        waits=[("cc", cc_state_n), ("t", S.cnt["t"]),
                               ("v", S.cnt["v"]), sloc_d])
            ldf = S.dma("sync", "db", facr[:, 0:64],
                        bass.AP(misc_dram, 64 * (1 + j), [[0, 128], [1, 64]]),
                        waits=[facd, ("v", S.cnt["v"])])
            if j == 0:
                def sacc0(e):
                    in1 = mkap(facr, 0, [[1, 64], [0, 64]])
                    return e.tensor_tensor(spp[0], xs_ch, in1, OP.mult)
                S.op("vector", sacc0,
                     waits=[ldb, ldf, ("dys", S.cnt["dys"])], inc="v")
            else:
                def sacc(e):
                    in1 = mkap(facr, 0, [[1, 64], [0, 64]])
                    return e.tensor_tensor(xd, xs_ch, in1, OP.mult)
                S.op("vector", sacc, waits=[ldb, ldf], inc="v")
                S.op("vector", lambda e, j=j: e.tensor_add(
                    spp[j % 2], spp[(j + 1) % 2], xd), inc="v")
        S.op("vector", lambda e: e.tensor_copy(s_run, spp[1]), inc="v")
        sin_v = S.cnt["v"]

        hbuf = lambda base, p: ar(base + 2048 * p, 2048)
        ACX = ar(A_ACC, 2048)
        p4_y_d = [("dys", S.cnt["dys"])]
        dss_all = ("dss", S.cnt["dss"])
        yfree = [S.cnt["v"], S.cnt["v"]]
        stfree = [S.cnt["v"], S.cnt["v"]]
        erfree = [S.cnt["v"], S.cnt["v"]]
        erspill = [("dys", 0), ("dys", 0)]
        eefree = [S.cnt["v"], S.cnt["v"]]
        supd = [sin_v, sin_v]
        psum_free = [S.cnt["v"], S.cnt["v"]]
        for m in range(NCH):
            for hf in range(2):
                hc = 2 * m + hf
                p = hc % 2
                y_p, st_p = hbuf(A_B1, p), hbuf(A_B2, p)
                er_p, ee_p = hbuf(A_B3, p), hbuf(A_B4, p)
                pp0, pp1 = ([pA, pB], [pC, pD])[p]
                ldy = S.dma("sync", "dy0" if p == 0 else "dy1", y_p,
                            bass.AP(y_dram, 524288 * m + 2048 * hf,
                                    [[4096, 128], [1, 2048]]),
                            waits=[("dys", S.cnt["dys"]), ("v", yfree[p])])
                lds = S.dma("sync", "dm0" if p == 0 else "dm1", st_p,
                            st_dram[m, :, 2048 * hf:2048 * (hf + 1)],
                            waits=[dss_all, ("v", stfree[p])])
                for ph in range(2):
                    lde = S.dma("sync", "dr" if p == 0 else "drv",
                                AR[64 * ph:64 * (ph + 1),
                                   A_B3 + 2048 * p:A_B3 + 2048 * (p + 1)],
                                bass.AP(a4_dram,
                                        8192 * m + (32 * ph + 16 * hf) * 128,
                                        [[0, 64], [1, 2048]]),
                                waits=[a4_d, ("v", erfree[p]),
                                       erspill[p]])
                exp_s = S.op("scalar", lambda e, er_p=er_p, ee_p=ee_p: e.activation(
                    ee_p, er_p, AF.Exp),
                    waits=[lde, ("v", eefree[p])], inc="s")
                CTm = BC[:, T + 128 * m:T + 128 * (m + 1)]
                for j2r in range(16):
                    j2 = 16 * hf + j2r
                    ps = [pp0, pp1][j2r // 8]

                    def omm(e, j2=j2, j2r=j2r, ps=ps, CTm=CTm):
                        return e.matmul(ps[:, 128 * (j2r % 8):128 * (j2r % 8 + 1)],
                                        s_run[:, 128 * j2:128 * (j2 + 1)], CTm,
                                        start=True, stop=True)
                    S.op("tensor", omm,
                         waits=[("v", max(supd[hf], psum_free[p]))]
                         if j2r == 0 else (), inc="t")
                yoff_t = S.cnt["t"]
                for q2 in range(2):
                    ps = [pp0, pp1][q2]

                    def oev(e, q2=q2, ps=ps, ee_p=ee_p):
                        return e.tensor_tensor(
                            ACX[:, 1024 * q2:1024 * (q2 + 1)], ps[:],
                            ee_p[:, 1024 * q2:1024 * (q2 + 1)], OP.mult)
                    S.op("vector", oev,
                         waits=[("t", yoff_t), ("s", exp_s), ldy], inc="v")
                psum_free[p] = S.cnt["v"]
                S.op("vector", lambda e, er_p=er_p, y_p=y_p: e.tensor_add(
                    er_p, y_p, ACX), inc="v")
                yfree[p] = S.cnt["v"]
                erfree[p] = S.cnt["v"]
                erspill[p] = S.dma(
                    "scalar", "dys",
                    bass.AP(y_dram, 524288 * m + 2048 * hf,
                            [[4096, 128], [1, 2048]]),
                    er_p, waits=[("v", S.cnt["v"])])
                p4_y_d.append(erspill[p])

                def sup1(e, m=m, hf=hf, ee_p=ee_p):
                    in1 = mkap(dch_rep, 64 * m + 32 * hf, [[1, 32], [0, 64]])
                    return e.tensor_tensor(
                        ee_p, s_run[:, 2048 * hf:2048 * (hf + 1)], in1, OP.mult)
                S.op("vector", sup1, waits=[("t", yoff_t)], inc="v")

                def sup2(e, hf=hf, ee_p=ee_p, st_p=st_p):
                    return e.tensor_add(s_run[:, 2048 * hf:2048 * (hf + 1)],
                                        ee_p, st_p)
                S.op("vector", sup2, waits=[lds], inc="v")
                supd[hf] = S.cnt["v"]
                stfree[p] = S.cnt["v"]
                eefree[p] = S.cnt["v"]

        wq_free = [S.cnt["t"], S.cnt["t"]]
        ps_sfree = [S.cnt["s"], S.cnt["s"]]
        yf_dp = [("dyf", 0), ("dyf1", 0)]
        ydp_names = ["dy0", "dy1"]
        tt_p = [0, 0]
        sq_p = [0, 0]
        pend_sq = []
        for j in range(DIT):
            buf = j % 2
            wsl = wq[buf]
            gsil = ar(A_B3 + 1024 * buf, 1024)
            yj = ar(A_B3 + 2048 + 1024 * buf, 1024)
            yfb = ar(A_B2 + 1024 * buf, 1024)
            ldw = S.dma("sync", "dw0" if buf == 0 else "dw1", wsl[:],
                        w_gate[j, :, :], waits=[("t", wq_free[buf])])
            ps = [pA, pB][buf]

            def gmm(e, ps=ps, wsl=wsl):
                last = None
                for cb in range(2):
                    for k in range(KD):
                        last = e.matmul(
                            ps[:, cb * 512:(cb + 1) * 512],
                            wsl[:, k * 128:(k + 1) * 128],
                            xT_sb[:, k * T + cb * 512: k * T + (cb + 1) * 512],
                            start=(k == 0), stop=(k == KD - 1))
                return last
            gw_ = [ldw, ("s", ps_sfree[buf])]
            if j < 2:
                gw_.append(("v", S.cnt["v"]))
            tcnt = S.op("tensor", gmm, waits=gw_, inc="t")
            wq_free[buf] = tcnt
            while len(pend_sq) >= 3:
                q = pend_sq.pop(0)
                S.op("tensor", q[0], waits=[("s", q[1])], inc="t")
                S.op("tensor", q[2], inc="t")
            ldy = S.dma("sync", ydp_names[buf], yj,
                        bass.AP(y_dram, 128 * j,
                                [[4096, 128], [524288, 8], [1, 128]]),
                        waits=[p4_y_d[-1], ("v", tt_p[buf])])
            scnt = S.op("scalar", lambda e, ps=ps, gsil=gsil: e.activation(
                gsil, ps[:], AF.Silu),
                waits=[("t", tcnt), ("v", tt_p[buf])], inc="s")
            ps_sfree[buf] = scnt
            tt_p[buf] = S.op(
                "vector", lambda e, yfb=yfb, yj=yj, gsil=gsil: e.tensor_tensor(
                    yfb, yj, gsil, OP.mult),
                waits=[("s", max(scnt, sq_p[buf])), ldy, yf_dp[buf]], inc="v")
            yf_dp[buf] = S.dma("scalar", "dyf" if buf == 0 else "dyf1",
                               yf_dram[128 * j:128 * (j + 1), :], yfb,
                               waits=[("v", tt_p[buf])])
            sqb = ar(A_B4 + 1024 * (j % 4), 1024)
            sq_p[buf] = S.op("scalar", lambda e, yfb=yfb, sqb=sqb: e.activation(
                sqb, yfb, AF.Square),
                waits=[("v", tt_p[buf]), ("t", S.cnt["t"])], inc="s")

            def mk_sq(cb, j, sqb):
                def f(e, cb=cb, j=j, sqb=sqb):
                    return e.matmul(
                        pC[0:1, cb * 512:(cb + 1) * 512], ones_sb[:, 0:1],
                        sqb[:, cb * 512:(cb + 1) * 512],
                        start=(j == 0), stop=(j == DIT - 1))
                return f
            pend_sq.append((mk_sq(0, j, sqb), sq_p[buf], mk_sq(1, j, sqb)))
        for q in pend_sq:
            S.op("tensor", q[0], waits=[("s", q[1])], inc="t")
            S.op("tensor", q[2], inc="t")
        pend_sq = []
        ssq_t = S.cnt["t"]
        r_row = ar(A_B1, 1024, 1)
        r_row2 = ar(A_B1 + 1024, 1024, 1)
        S.op("vector", lambda e: e.tensor_scalar(
            r_row, pC[0:1, 0:1024], 1.0 / DI, EPS, OP.mult, OP.add),
            waits=[("t", ssq_t)], inc="v")
        S.op("vector", lambda e: e.reciprocal(r_row2, r_row), inc="v")
        S.op("scalar", lambda e: e.activation(r_row, r_row2, AF.Sqrt),
             waits=S.at("v"), inc="s")
        rbounce = S.dma("scalar", "dbs", ln_dram[0:1, :], r_row, waits=S.at("s"))
        r_bc = ar(A_B5, 1024)
        ldrbc = S.dma("sync", "db", r_bc, bass.AP(ln_dram, 0, [[0, 128], [1, T]]),
                      waits=[rbounce])

        yn_sb = ar(A_B1, 4 * DI)
        p6_done_v = S.cnt["v"]
        for cb in range(2):
            yn_v = 0
            for k in range(DIT):
                tb = ar(A_B5 + 1024 + 512 * (k % 2), 512)
                ldyf = S.dma("sync", "dy0" if k % 2 == 0 else "dy1", tb,
                             bass.AP(yf_dram, 128 * k * T + 512 * cb,
                                     [[T, 128], [1, 512]]),
                             waits=[yf_dp[0], yf_dp[1], ("v", S.cnt["v"])])

                def ynf(e, k=k, cb=cb, tb=tb):
                    return e.tensor_scalar(
                        yn_sb[:, 512 * k:512 * (k + 1)], tb,
                        nw_col[:, k:k + 1], None, OP.mult)
                yn_v = S.op("vector", ynf, waits=[ldyf], inc="v")
            for oh in range(2):
                for k in range(DIT):
                    buf = k % 2
                    wsl = wq[buf]
                    ldw = S.dma("sync", "dw0" if buf == 0 else "dw1",
                                wsl[:, 0:1024],
                                w_out[128 * k:128 * (k + 1), 1024 * oh:1024 * (oh + 1)],
                                waits=[("t", wq_free[buf])])

                    def pmm(e, k=k, wsl=wsl):
                        last = None
                        for o in range(8):
                            ps = [pA, pB, pC, pD][o // 2]
                            last = e.matmul(
                                ps[:, 512 * (o % 2):512 * (o % 2 + 1)],
                                wsl[:, 128 * o:128 * (o + 1)],
                                yn_sb[:, 512 * k:512 * (k + 1)],
                                start=(k == 0), stop=(k == DIT - 1))
                        return last
                    wq_free[buf] = S.op(
                        "tensor", pmm,
                        waits=[ldw, ("v", max(yn_v, p6_done_v) if k == 0 else 0)],
                        inc="t")
                op_t = S.cnt["t"]
                for o in range(8):
                    ps = [pA, pB, pC, pD][o // 2]
                    og = 8 * oh + o

                    def res_r(e, ps=ps, o=o, cb=cb):
                        return e.tensor_tensor(
                            ar(A_B5 + 2048, 512),
                            ps[:, 512 * (o % 2):512 * (o % 2 + 1)],
                            r_bc[:, 512 * cb:512 * (cb + 1)], OP.mult)
                    S.op("vector", res_r, waits=[("t", op_t), ldrbc], inc="v")

                    def res_f(e, og=og, cb=cb):
                        xsl = xT_sb[:, og * T + 512 * cb: og * T + 512 * (cb + 1)]
                        return e.tensor_tensor(
                            ar(A_B5 + 2560, 512), xsl, ar(A_B5 + 2048, 512),
                            OP.add)
                    S.op("vector", res_f, inc="v")

                    def res_c(e, og=og, cb=cb):
                        xsl = xT_sb[:, og * T + 512 * cb: og * T + 512 * (cb + 1)]
                        return e.tensor_copy(xsl, ar(A_B5 + 2560, 512))
                    S.op("vector", res_c, inc="v")
                p6_done_v = S.cnt["v"]
        sqt = ar(A_B5 + 1024, 512)
        for cb in range(2):
            for k in range(KD):
                sq_s = S.op("scalar", lambda e, k=k, cb=cb: e.activation(
                    sqt, xT_sb[:, k * T + 512 * cb: k * T + 512 * (cb + 1)],
                    AF.Square), waits=[("v", p6_done_v), ("t", S.cnt["t"])], inc="s")
                S.op("tensor", lambda e, k=k, cb=cb: e.matmul(
                    pA[0:1, 512 * cb:512 * (cb + 1)], ones_sb[:, 0:1],
                    xT_sb[:, k * T + 512 * cb: k * T + 512 * (cb + 1)],
                    start=(k == 0), stop=(k == KD - 1)),
                    waits=[("v", p6_done_v)], inc="t")
                S.op("tensor", lambda e, k=k, cb=cb: e.matmul(
                    pB[0:1, 512 * cb:512 * (cb + 1)], ones_sb[:, 0:1], sqt,
                    start=(k == 0), stop=(k == KD - 1)),
                    waits=[("s", sq_s)], inc="t")
        ln_t = S.cnt["t"]
        mu_row = ar(A_B1, 1024, 1)
        r1_row = ar(A_B1 + 1024, 1024, 1)
        t_row = ar(A_B1 + 2048, 1024, 1)
        t_row2 = ar(A_B1 + 3072, 1024, 1)
        S.op("vector", lambda e: e.tensor_scalar_mul(mu_row, pA[0:1, 0:1024],
                                                     1.0 / DM),
             waits=[("t", ln_t)], inc="v")
        S.op("vector", lambda e: e.tensor_scalar_mul(t_row, pB[0:1, 0:1024],
                                                     1.0 / DM), inc="v")
        S.op("vector", lambda e: e.tensor_tensor(r1_row, mu_row, mu_row, OP.mult),
             inc="v")
        S.op("vector", lambda e: e.tensor_sub(t_row2, t_row, r1_row), inc="v")
        S.op("vector", lambda e: e.tensor_scalar_add(t_row, t_row2, EPS), inc="v")
        S.op("vector", lambda e: e.reciprocal(r1_row, t_row), inc="v")
        S.op("scalar", lambda e: e.activation(t_row, r1_row, AF.Sqrt),
             waits=S.at("v"), inc="s")
        mu_d = S.dma("scalar", "dbs", ln_dram[1:2, :], mu_row, waits=S.at("v", "s"))
        rr_d = S.dma("scalar", "dbs", ln_dram[2:3, :], t_row, waits=S.at("s"))
        mu_bc = ar(A_B2, 1024)
        ldmu = S.dma("sync", "db", mu_bc, bass.AP(ln_dram, T, [[0, 128], [1, T]]),
                     waits=[mu_d, ("v", S.cnt["v"])])
        ldrr = S.dma("sync", "db", r_bc, bass.AP(ln_dram, 2 * T, [[0, 128], [1, T]]),
                     waits=[rr_d])
        ln1_v = 0
        for k in range(KD):
            def c1(e, k=k):
                xsl = xT_sb[:, k * T:(k + 1) * T]
                return e.tensor_tensor(ar(A_B1, T), xsl, mu_bc, OP.subtract)
            S.op("vector", c1, waits=[ldmu, ldrr], inc="v")

            def c2(e, k=k):
                xsl = xT_sb[:, k * T:(k + 1) * T]
                return e.tensor_tensor(xsl, ar(A_B1, T), r_bc, OP.mult)
            ln1_v = S.op("vector", c2, inc="v")

        xi_bf = AR[0:128, A_B1:A_B1 + 8192].bitcast(BF16)
        sg_bc = ar(A_B3, 1024)
        sv_bc = ar(A_B3 + 1024, 1024)
        qt1 = ar(A_B3 + 2048, 1024)
        qt2 = ar(A_B3 + 3072, 1024)
        sxq_bc = ar(A_B4, 1024)
        gm = ar(A_B4, 1024)
        grow_g = ar(A_B4 + 1024, 1024, 1)
        grow_v2 = ar(A_B4 + 2048, 1024, 1)
        gcol = ar(A_B4 + 3072, 8)
        gq = ar(A_B4 + 3080, 8)
        gr = ar(A_B4 + 3088, 8)
        ggam = ar(A_B4 + 3096, 4)

        maxt = ar(A_B2, 1024)
        absx = ar(A_B2 + 1024, 1024)
        S.op("scalar", lambda e: e.activation(maxt, xT_sb[:, 0:T], AF.Abs),
             waits=[("v", ln1_v), ("t", S.cnt["t"])], inc="s")
        for k in range(1, KD):
            ab_s = S.op("scalar", lambda e, k=k: e.activation(
                absx, xT_sb[:, k * T:(k + 1) * T], AF.Abs),
                waits=[("v", S.cnt["v"])], inc="s")
            S.op("vector", lambda e: e.tensor_tensor(maxt, maxt, absx, OP.max),
                 waits=[("s", ab_s)], inc="v")
        gx_v = S.cnt["v"]

        def gamma_pass(src, extra_tw=()):
            for c8 in range(8):
                S.op("tensor", lambda e, c8=c8, src=src: e.transpose(
                    pA[:, 128 * c8:128 * c8 + 128],
                    src[:, 128 * c8:128 * (c8 + 1)], ident[:]),
                    waits=([("v", S.cnt["v"])] + list(extra_tw)) if c8 == 0 else (),
                    inc="t")
            tt = S.cnt["t"]
            for c8 in range(8):
                S.op("vector", lambda e, c8=c8: e.reduce_max(
                    gcol[:, c8:c8 + 1], pA[:, 128 * c8:128 * (c8 + 1)], axis=AX.X),
                    waits=[("t", tt)] if c8 == 0 else (), inc="v")
            S.op("vector", lambda e: e.tensor_scalar_max(gcol, gcol, 1e-5), inc="v")
            S.op("vector", lambda e: e.reciprocal(gq, gcol), inc="v")
            S.op("vector", lambda e: e.tensor_scalar_mul(gr, gq, 127.0), inc="v")
            vq = S.cnt["v"]
            for c8 in range(8):
                S.dma("scalar", "dbs", ln_dram[3:4, 128 * c8:128 * (c8 + 1)],
                      gr[:, c8:c8 + 1], waits=[("v", vq)])
                dd = S.dma("scalar", "dbs", ln_dram[4:5, 128 * c8:128 * (c8 + 1)],
                           gcol[:, c8:c8 + 1], waits=[("v", vq)])
            return vq, dd

        gx2_v, gq_d = gamma_pass(maxt)
        ldsx = S.dma("sync", "db", sxq_bc, bass.AP(ln_dram, 3 * T, [[0, 128], [1, T]]),
                     waits=[gq_d, ("v", gx2_v)])
        ldgw = S.dma("sync", "db", ggam, bass.AP(misc_dram, 6 * 64, [[0, 128], [1, 4]]),
                     waits=[gam_bd, ("v", gx2_v)])
        ldg4 = S.dma("sync", "db", grow_g, bass.AP(ln_dram, 4 * T, [[1, 1], [1, T]]),
                     waits=[gq_d, ("v", gx2_v)])
        S.op("vector", lambda e: e.tensor_scalar(
            grow_v2, grow_g, ggam[0:1, 0:1], 1.0 / 127.0, OP.mult, OP.mult),
            waits=[ldgw, ldg4], inc="v")
        sgq_d = S.dma("scalar", "dbs", ln_dram[5:6, :], grow_v2, waits=S.at("v"))
        S.op("vector", lambda e: e.tensor_scalar(
            grow_v2, grow_g, ggam[0:1, 1:2], 1.0 / 127.0, OP.mult, OP.mult),
            waits=[sgq_d], inc="v")
        sgq_d = S.dma("scalar", "dbs", ln_dram[6:7, :], grow_v2, waits=S.at("v"))
        ldsg = S.dma("sync", "db", sg_bc, bass.AP(ln_dram, 5 * T, [[0, 128], [1, T]]),
                     waits=[sgq_d])
        ldsv = S.dma("sync", "db", sv_bc, bass.AP(ln_dram, 6 * T, [[0, 128], [1, T]]),
                     waits=[sgq_d])
        xi_v = 0
        for k in range(KD):
            S.op("vector", lambda e, k=k: e.tensor_tensor(
                qt1, xT_sb[:, k * T:(k + 1) * T], sxq_bc, OP.mult),
                waits=[ldsx] if k == 0 else (), inc="v")
            S.op("vector", lambda e: e.tensor_scalar(
                qt2, qt1, MAGIC, -MAGIC, OP.add, OP.add), inc="v")
            S.op("vector", lambda e: e.tensor_scalar_max(qt1, qt2, -128.0), inc="v")
            xi_v = S.op("vector", lambda e, k=k: e.tensor_scalar_min(
                xi_bf[:, k * T:(k + 1) * T], qt1, 127.0), inc="v")

        bandgs = [ar(A_B5, 2048).bitcast(BF16),
                  ar(A_BC, 2048).bitcast(BF16)]
        bandvs = [ar(A_B5 + 2048, 2048).bitcast(BF16),
                  ar(A_BC + 2048, 2048).bitcast(BF16)]
        gmp = [gm, ar(A_B4 + 1024, 1024)]
        S.op("vector", lambda e: e.memset(gm, 0.0), waits=[("v", xi_v)], inc="v")
        S.op("vector", lambda e: e.memset(gmp[1], 0.0), inc="v")
        gm0_v = S.cnt["v"]
        ngm = [0, 0]
        mid_d = ("dmid", 0)
        p7a_t0 = S.cnt["t"]
        bg_free = [p7a_t0, p7a_t0]
        bv_free = [p7a_t0, p7a_t0]
        psum_read_v = S.cnt["v"]
        for fb in range(32):
            par7 = fb % 2
            bandg = bandgs[par7]
            bandv = bandvs[par7]
            bw = [("cc", cc_m_n), ("t", bg_free[par7])]
            if fb == 0:
                bw += [("dy0", S.cnt["dy0"]), ("dy1", S.cnt["dy1"]),
                       ("db", S.cnt["db"]), ("dy", S.cnt["dy"])]
            if fb == 1:
                bw += [("v", psum_read_v)]
            boff = (128 * (fb // 4)) * 16384 + (fb % 4) * 4096
            ldbg = S.dma("sync", "dr" if par7 == 0 else "dr2", bandg,
                         bass.AP(mg_out, boff, [[16384, 128], [1, 4096]]),
                         waits=bw)
            ldbv = S.dma("sync", "drv" if par7 == 0 else "drv2", bandv,
                         bass.AP(mv_out, boff, [[16384, 128], [1, 4096]]),
                         waits=[("cc", cc_m_n), ("t", bv_free[par7])])

            def gvmm(e, band, pss):
                last = None
                for ft in range(2):
                    for cb in range(2):
                        ps = pss[ft]
                        for k in range(KD):
                            last = e.matmul(
                                ps[:, 512 * cb:512 * (cb + 1)],
                                band[:, 256 * k + 128 * ft:256 * k + 128 * (ft + 1)],
                                xi_bf[:, k * T + 512 * cb:k * T + 512 * (cb + 1)],
                                start=(k == 0), stop=(k == KD - 1))
                return last
            cur_bg = S.op("tensor",
                          lambda e, bandg=bandg: gvmm(e, bandg, [pA, pB]),
                          waits=[ldbg, ("v", psum_read_v)], inc="t")
            cur_bv = S.op("tensor",
                          lambda e, bandv=bandv: gvmm(e, bandv, [pC, pD]),
                          waits=[ldbv], inc="t")
            bg_free[par7] = cur_bg
            bv_free[par7] = cur_bv
            for ft in range(2):
                for cb in range(2):
                    psg, psv = [pA, pB][ft], [pC, pD][ft]
                    fgl = 2 * fb + ft

                    def sgm(e, psg=psg, cb=cb):
                        return e.tensor_tensor(
                            qt1[:, 0:512], psg[:, 512 * cb:512 * (cb + 1)],
                            sg_bc[:, 512 * cb:512 * (cb + 1)], OP.mult)
                    S.op("vector", sgm,
                         waits=[("t", cur_bv), ldsg, ldsv, mid_d], inc="v")
                    S.op("scalar", lambda e, fgl=fgl: e.activation(
                        qt2[:, 0:512], qt1[:, 0:512], AF.Silu,
                        bias=fb_sb[:, fgl:fgl + 1]), waits=S.at("v"), inc="s")

                    def svm(e, psv=psv, cb=cb):
                        return e.tensor_tensor(
                            qt1[:, 512:1024], psv[:, 512 * cb:512 * (cb + 1)],
                            sv_bc[:, 512 * cb:512 * (cb + 1)], OP.mult)
                    sv_cnt = S.op("vector", svm, inc="v")
                    if ft == 1 and cb == 1:
                        psum_read_v = sv_cnt
                    S.op("vector", lambda e, fgl=fgl: e.tensor_scalar_add(
                        qt1[:, 0:512], qt1[:, 512:1024],
                        fb_sb[:, 64 + fgl:64 + fgl + 1]),
                        waits=S.at("s"), inc="v")
                    S.op("vector", lambda e: e.tensor_tensor(
                        qt1[:, 512:1024], qt1[:, 0:512], qt2[:, 0:512], OP.mult),
                        waits=S.at("s"), inc="v")
                    am_s = S.op("scalar", lambda e: e.activation(
                        qt2[:, 512:1024], qt1[:, 512:1024], AF.Abs),
                        waits=S.at("v"), inc="s")
                    par = ngm[cb] % 2
                    S.op("vector", lambda e, cb=cb, par=par: e.tensor_tensor(
                        gmp[1 - par][:, 512 * cb:512 * (cb + 1)],
                        gmp[par][:, 512 * cb:512 * (cb + 1)],
                        qt2[:, 512:1024], OP.max), waits=[("s", am_s)], inc="v")
                    ngm[cb] += 1
                    midv = S.cnt["v"]
                    mid_d = S.dma(
                        "scalar", "dmid",
                        mid_dram[128 * fgl:128 * (fgl + 1), 512 * cb:512 * (cb + 1)],
                        qt1[:, 512:1024], waits=[("v", midv)])

        go_v, go_d = gamma_pass(gm)
        ldso = S.dma("sync", "db", sxq_bc, bass.AP(ln_dram, 3 * T, [[0, 128], [1, T]]),
                     waits=[go_d, ("v", go_v)])
        ldg5 = S.dma("sync", "db", grow_g, bass.AP(ln_dram, 4 * T, [[1, 1], [1, T]]),
                     waits=[go_d, ("v", go_v)])
        S.op("vector", lambda e: e.tensor_scalar(
            grow_v2, grow_g, ggam[0:1, 2:3], 1.0 / 127.0, OP.mult, OP.mult),
            waits=[ldg5, ldgw], inc="v")
        soo_d = S.dma("scalar", "dbs", ln_dram[5:6, :], grow_v2, waits=S.at("v"))
        ldsoo = S.dma("sync", "db", sg_bc, bass.AP(ln_dram, 5 * T, [[0, 128], [1, T]]),
                      waits=[soo_d, ("v", S.cnt["v"])])
        mi_v = 0
        mi_d = ("dmi", 0)
        mo_bfs = [ar(A_B5, 4096).bitcast(BF16),
                  ar(A_BC, 4096).bitcast(BF16)]
        mi_sb = [AR[0:128, A_B1 + 2048 * i:A_B1 + 2048 * (i + 1)].bitcast(BF16)
                 for i in range(2)]
        p7_done_v = S.cnt["v"]
        mo_free = [S.cnt["t"], S.cnt["t"]]
        moi = 0
        for cb in range(2):
            for oh in range(2):
                for kc in range(8):
                    if cb == 0 and oh == 0:
                        for ft8 in range(8):
                            ft = 8 * kc + ft8
                            ldm = S.dma("sync", "dy", qt2,
                                        mid_dram[128 * ft:128 * (ft + 1), :],
                                        waits=[mid_d, mi_d, ("v", mi_v)])
                            S.op("vector", lambda e: e.tensor_tensor(
                                qt1, qt2, sxq_bc, OP.mult),
                                waits=[ldm, ldso], inc="v")
                            S.op("vector", lambda e: e.tensor_scalar(
                                qt2, qt1, MAGIC, -MAGIC, OP.add, OP.add), inc="v")
                            S.op("vector", lambda e: e.tensor_scalar_max(
                                qt1, qt2, -128.0), inc="v")
                            mi_v = S.op("vector", lambda e: e.tensor_scalar_min(
                                qt2.bitcast(BF16)[:, 0:1024], qt1, 127.0), inc="v")
                            S.dma("scalar", "dmi",
                                  bass.AP(mi_dram,
                                          (2 * (ft // 8)) * 524288
                                          + (ft % 8) * 512,
                                          [[4096, 128], [1, 512]]),
                                  qt2.bitcast(BF16)[:, 0:512],
                                  waits=[("v", mi_v)])
                            mi_d = S.dma("scalar", "dmi",
                                         bass.AP(mi_dram,
                                                 (2 * (ft // 8) + 1) * 524288
                                                 + (ft % 8) * 512,
                                                 [[4096, 128], [1, 512]]),
                                         qt2.bitcast(BF16)[:, 512:1024],
                                         waits=[("v", mi_v)])
                    mo_bf = mo_bfs[moi % 2]
                    mw = [("cc", cc_m_n), ("t", mo_free[moi % 2])]
                    if moi < 2:
                        mw += [("dr", S.cnt["dr"]), ("drv", S.cnt["drv"]),
                               ("dr2", S.cnt["dr2"]), ("drv2", S.cnt["drv2"])]
                    ldmo = S.dma("sync", "dmo" if moi % 2 == 0 else "dmo2",
                                 mo_bf,
                                 bass.AP(mo_out,
                                         (128 * kc) * 16384 + 8192 * oh,
                                         [[16384, 128], [1, 8192]]),
                                 waits=mw)
                    miw = [mi_d, ("t", mo_free[kc % 2])]
                    if cb == 0 and oh == 0 and kc < 2:
                        miw += [("dx", S.cnt["dx"]), ("v", S.cnt["v"])]
                    ldmi = S.dma("sync", "dm0" if kc % 2 == 0 else "dm1",
                                 mi_sb[kc % 2],
                                 bass.AP(mi_dram, (2 * kc + cb) * 524288,
                                         [[4096, 128], [1, 4096]]),
                                 waits=miw)

                    def omm7(e, kc=kc, mo_bf=mo_bf):
                        last = None
                        for k8 in range(8):
                            for o in range(8):
                                ps = [pA, pB, pC, pD][o // 2]
                                last = e.matmul(
                                    ps[:, 512 * (o % 2):512 * (o % 2 + 1)],
                                    mo_bf[:, 1024 * k8 + 128 * o:
                                          1024 * k8 + 128 * (o + 1)],
                                    mi_sb[kc % 2][:, 512 * k8:512 * (k8 + 1)],
                                    start=(kc == 0 and k8 == 0),
                                    stop=(kc == 7 and k8 == 7))
                        return last
                    mo_free[moi % 2] = S.op(
                        "tensor", omm7,
                        waits=[ldmo, ldmi,
                               ("v", p7_done_v if kc == 0 else 0)],
                        inc="t")
                    moi += 1
                mo_free_t = mo_free[(moi - 1) % 2]
                for o in range(8):
                    ps = [pA, pB, pC, pD][o // 2]
                    og = 8 * oh + o

                    def fev(e, ps=ps, o=o, cb=cb):
                        return e.tensor_tensor(
                            qt1[:, 0:512], ps[:, 512 * (o % 2):512 * (o % 2 + 1)],
                            sg_bc[:, 512 * cb:512 * (cb + 1)], OP.mult)
                    S.op("vector", fev,
                         waits=[("t", mo_free_t), ldsoo], inc="v")
                    S.op("vector", lambda e, og=og: e.tensor_scalar_add(
                        qt2[:, 0:512], qt1[:, 0:512], fb_sb[:, 128 + og:129 + og]),
                        inc="v")

                    def fres(e, og=og, cb=cb):
                        xsl = xT_sb[:, og * T + 512 * cb: og * T + 512 * (cb + 1)]
                        return e.tensor_tensor(qt1[:, 512:1024], xsl,
                                               qt2[:, 0:512], OP.add)
                    S.op("vector", fres, inc="v")

                    def fres_c(e, og=og, cb=cb):
                        xsl = xT_sb[:, og * T + 512 * cb: og * T + 512 * (cb + 1)]
                        return e.tensor_copy(xsl, qt1[:, 512:1024])
                    S.op("vector", fres_c, inc="v")
                p7_done_v = S.cnt["v"]
        sqt2 = ar(A_B3 + 2048, 512)
        for cb in range(2):
            for k in range(KD):
                sq_s = S.op("scalar", lambda e, k=k, cb=cb: e.activation(
                    sqt2, xT_sb[:, k * T + 512 * cb: k * T + 512 * (cb + 1)],
                    AF.Square), waits=[("v", p7_done_v), ("t", S.cnt["t"])], inc="s")
                S.op("tensor", lambda e, k=k, cb=cb: e.matmul(
                    pA[0:1, 512 * cb:512 * (cb + 1)], ones_sb[:, 0:1],
                    xT_sb[:, k * T + 512 * cb: k * T + 512 * (cb + 1)],
                    start=(k == 0), stop=(k == KD - 1)),
                    waits=[("v", p7_done_v)], inc="t")
                S.op("tensor", lambda e, k=k, cb=cb: e.matmul(
                    pB[0:1, 512 * cb:512 * (cb + 1)], ones_sb[:, 0:1], sqt2,
                    start=(k == 0), stop=(k == KD - 1)),
                    waits=[("s", sq_s)], inc="t")
        ln2_t = S.cnt["t"]
        mu2 = ar(A_B4 + 1024, 1024, 1)
        t2r = ar(A_B4 + 2048, 1024, 1)
        r2r = ar(A_B4 + 3072, 1024, 1)
        S.op("vector", lambda e: e.tensor_scalar_mul(mu2, pA[0:1, 0:1024], 1.0 / DM),
             waits=[("t", ln2_t)], inc="v")
        S.op("vector", lambda e: e.tensor_scalar_mul(t2r, pB[0:1, 0:1024], 1.0 / DM),
             inc="v")
        S.op("vector", lambda e: e.tensor_tensor(r2r, mu2, mu2, OP.mult), inc="v")
        S.op("vector", lambda e: e.tensor_sub(t2r, t2r, r2r), inc="v")
        S.op("vector", lambda e: e.tensor_scalar_add(t2r, t2r, EPS), inc="v")
        S.op("vector", lambda e: e.reciprocal(r2r, t2r), inc="v")
        S.op("scalar", lambda e: e.activation(t2r, r2r, AF.Sqrt),
             waits=S.at("v"), inc="s")
        mu2_d = S.dma("scalar", "dbs", ln_dram[1:2, :], mu2, waits=S.at("v", "s"))
        rr2_d = S.dma("scalar", "dbs", ln_dram[2:3, :], t2r, waits=S.at("s"))
        mu_bc2 = ar(A_B2, 1024)
        r_bc2 = ar(A_B2 + 1024, 1024)
        ldmu2 = S.dma("sync", "db", mu_bc2, bass.AP(ln_dram, T, [[0, 128], [1, T]]),
                      waits=[mu2_d, ("v", S.cnt["v"])])
        ldrr2 = S.dma("sync", "db", r_bc2, bass.AP(ln_dram, 2 * T, [[0, 128], [1, T]]),
                      waits=[rr2_d, ("v", S.cnt["v"])])
        for k in range(KD):
            def d1f(e, k=k):
                xsl = xT_sb[:, k * T:(k + 1) * T]
                return e.tensor_tensor(xsl, xsl, mu_bc2, OP.subtract)
            S.op("vector", d1f, waits=[ldmu2, ldrr2], inc="v")

            def d2f(e, k=k):
                xsl = xT_sb[:, k * T:(k + 1) * T]
                return e.tensor_tensor(xsl, xsl, r_bc2, OP.mult)
            S.op("vector", d2f, inc="v")
        fin_v = S.cnt["v"]
        yout_d = S.dma("gpsimd", "dg",
                       bass.AP(yout, 0, [[T, 128], [128 * T, KD], [1, T]]),
                       xT_sb, waits=[("v", fin_v)])

        if "mid" in taps:
            for k in range(KD):
                S.dma("gpsimd", "dg", taps["mid"][128 * k:128 * (k + 1), :],
                      AR[0:128, A_XT + k * T:A_XT + (k + 1) * T],
                      waits=[("v", ln1_v)])
        if "ffnmid" in taps:
            for ft in range(FFT):
                S.dma("gpsimd", "dg", taps["ffnmid"][128 * ft:128 * (ft + 1), :],
                      mid_dram[128 * ft:128 * (ft + 1), :],
                      waits=[("dmid", S.cnt["dmid"])])
        if "acsT" in taps:
            S.dma("gpsimd", "dg", taps["acsT"][:, :], acsT[:, :],
                  waits=[("v", acs_v)])
        if "state" in taps:
            S.dma("gpsimd", "dg", taps["state"][:, :], s_run, waits=S.at("v"))
        if "y" in taps:
            for f in range(DIT):
                S.dma("gpsimd", "dg", taps["y"][128 * f:128 * (f + 1), :],
                      bass.AP(y_dram, 128 * f,
                              [[4096, 128], [524288, 8], [1, 128]]),
                      waits=[("dys", S.cnt["dys"])])
        if "xs" in taps:
            for f in range(32):
                S.dma("gpsimd", "dg", taps["xs"][128 * f:128 * (f + 1), :],
                      bass.AP(xs_dram, 128 * f,
                              [[4096, 128], [524288, 8], [1, 128]]),
                      waits=[patch_g, p1_dsc])
            for f in (32, 33):
                S.dma("gpsimd", "dg", taps["xs"][128 * f:128 * (f + 1), :],
                      BC[:, (f - 32) * T:(f - 31) * T],
                      waits=[("v", patch_v)])
        if "dt" in taps:
            S.dma("gpsimd", "dg", taps["dt"][:, :], dt_sb[:],
                  waits=[("v", dt_ready_v)])

        S.op("gpsimd", lambda e: e.nop(),
             waits=S.at("dg"), inc="g")

        block = E(nc.Block())
        S.emit(block, sems)
    return nc



def _stage(core, ins):
    b, q = core // 4, core % 4
    hs = ins["L0_attn_hidden_states"]
    xTa = np.ascontiguousarray(hs[b, q * T:(q + 1) * T, :].T).reshape(KD, 128, T)
    WT = np.ascontiguousarray(ins["L0_attn_in_proj_w"].T)
    wx = WT[:, DI:DI + 4352]
    w_xbc = np.ascontiguousarray(
        wx.reshape(KD, 128, FX, 128).transpose(2, 1, 0, 3)).reshape(FX, 128, KD * 128)
    wd = WT[:, DI + 4352:]
    w_dt = np.ascontiguousarray(
        wd.reshape(KD, 128, 64).transpose(1, 0, 2)).reshape(128, KD * 64)
    w_gate = np.ascontiguousarray(
        WT[:, :DI].reshape(KD, 128, DIT, 128).transpose(2, 1, 0, 3)).reshape(
        DIT, 128, KD * 128)
    w_out = np.ascontiguousarray(ins["L0_attn_out_proj_w"].T)
    conv_w = np.ascontiguousarray(
        ins["L0_attn_conv_w"].reshape(FX, 128, 4).transpose(1, 0, 2)).reshape(128, FX * 4)
    conv_b = np.ascontiguousarray(ins["L0_attn_conv_b"].reshape(FX, 128).T)
    state_w5 = np.zeros((5, 4), np.float32)
    for j in range(4):
        if j < q:
            for i in range(4):
                state_w5[i, j] = 1.0 if (j < i < q) else 0.0
        else:
            state_w5[4, j] = -1e30
    tail_mask = np.zeros((128, 4), np.float32)
    if q > 0:
        tail_mask[:, q - 1] = 1.0
    perm64 = np.zeros((64, 64), np.float32)
    for r in range(64):
        ph, hf, i = r // 32, (r % 32) // 16, r % 16
        perm64[32 * hf + 2 * i + ph, r] = 1.0
    f32 = np.float32
    return {
        "xT": xTa.astype(f32), "w_xbc": w_xbc.astype(f32), "w_dt": w_dt.astype(f32),
        "w_gate": w_gate.astype(f32), "w_out": w_out.astype(f32),
        "conv_w": conv_w.astype(f32), "conv_b": conv_b.astype(f32),
        "dt_bias": np.ascontiguousarray(ins["L0_attn_dt_bias"].reshape(64, 1)).astype(f32),
        "a_log": np.ascontiguousarray(ins["L0_attn_A_log"].reshape(64, 1)).astype(f32),
        "d_col": np.ascontiguousarray(
            np.repeat(ins["L0_attn_D"].reshape(DIT, 2).T, 64, axis=0)).astype(f32),
        "norm_w": np.ascontiguousarray(ins["L0_attn_norm_w"].reshape(DIT, 128).T).astype(f32),
        "gw_sl": np.ascontiguousarray(
            ins["L0_ffn_gate_w"].T[:, 1024 * core:1024 * (core + 1)]
            .reshape(16, 128, 1024).transpose(1, 0, 2).reshape(128, 16384)
        ).astype(f32),
        "vw_sl": np.ascontiguousarray(
            ins["L0_ffn_val_w"].T[:, 1024 * core:1024 * (core + 1)]
            .reshape(16, 128, 1024).transpose(1, 0, 2).reshape(128, 16384)
        ).astype(f32),
        "ow_sl": np.ascontiguousarray(
            ins["L0_ffn_out_w"].T[1024 * core:1024 * (core + 1), :]
            .reshape(8, 128, 2048).transpose(1, 0, 2).reshape(128, 16384)
        ).astype(f32),
        "gate_b": np.ascontiguousarray(ins["L0_ffn_gate_b"].reshape(FFT, 128).T).astype(f32),
        "val_b": np.ascontiguousarray(ins["L0_ffn_val_b"].reshape(FFT, 128).T).astype(f32),
        "out_b": np.ascontiguousarray(ins["L0_ffn_out_b"].reshape(KD, 128).T).astype(f32),
        "state_w5": state_w5, "tail_mask": tail_mask, "perm64": perm64,
    }


def kernel(**inputs):
    ins = {k: np.asarray(v) for k, v in inputs.items()}
    nc = build_program()
    in_maps = [_stage(c, ins) for c in range(8)]
    res = run_bass_kernel_spmd(nc, in_maps, core_ids=list(range(8)))
    out = np.empty((2, 4096, 2048), np.float32)
    for c in range(8):
        b, q = c // 4, c % 4
        out[b, q * T:(q + 1) * T, :] = res.results[c]["yout"].T
    return out
